# version 06
# speedup vs baseline: 1.6808x; 1.0763x over previous
import math
from contextlib import ExitStack

import numpy as np
import ml_dtypes

import concourse.bass as bass
import concourse.mybir as mybir
from concourse.bass_utils import run_bass_kernel_spmd

F32 = mybir.dt.float32
BF16 = mybir.dt.bfloat16
I32 = mybir.dt.int32
AF = mybir.ActivationFunctionType
ALU = mybir.AluOpType
AX = mybir.AxisListType.X
BF = ml_dtypes.bfloat16

B, T, D = 8, 2048, 1024
DBG_L1A = False
NCORE = 8
PB = 512
NG = T // PB

ENGS = ("sync", "gpsimd", "tensor", "vector", "scalar")


class Tok:
    __slots__ = ("sem", "val", "order")

    def __init__(self, sem, val, order):
        self.sem, self.val, self.order = sem, val, order


class Prog:
    def __init__(self, nc):
        self.nc = nc
        self.q = {e: [] for e in ENGS}
        self.sems = {}
        self.cnt = {}
        self.order = 0
        self.bar_round = 0

    def sem(self, name):
        if name not in self.sems:
            self.sems[name] = self.nc.alloc_semaphore(name)
            self.cnt[name] = 0
        return self.sems[name]

    def emit(self, eng, fn, wait=(), inc=None, inc_by=1, drain=False):
        self.order += 1
        o = self.order
        wsd = {}
        for t in wait:
            if t is None:
                continue
            assert t.order < o, "wait on later-planned token"
            wsd[t.sem] = max(wsd.get(t.sem, 0), t.val)
        ws = list(wsd.items())
        tok = None
        if inc is not None:
            self.sem(inc)
            self.cnt[inc] += inc_by
            tok = Tok(inc, self.cnt[inc], o)
        self.q[eng].append((ws, fn, drain, (inc, inc_by) if inc else None))
        return tok

    def barrier(self):
        self.bar_round += 1
        bar = "bar"
        self.sem(bar)
        for e in ENGS:
            self.order += 1
            self.cnt[bar] += 1
            self.q[e].append(([], lambda eng: eng.drain(), False, (bar, 1)))
        tgt = 5 * self.bar_round
        for e in ENGS:
            self.order += 1
            self.q[e].append(([(bar, tgt)], None, False, None))

    def finalize(self, block):
        for name in ENGS:
            ops = self.q[name]

            def run(eng, ops=ops):
                seen = {}
                for ws, fn, drain, inc in ops:
                    for (s, v) in ws:
                        if seen.get(s, -1) < v:
                            eng.wait_ge(self.sems[s], v)
                            seen[s] = v
                    if drain:
                        eng.drain()
                    if fn is None:
                        continue
                    instr = fn(eng)
                    if inc is not None:
                        s, by = inc
                        instr.then_inc(self.sems[s], by)

            getattr(block, name)(run)

    def dma(self, queue, out, in_, wait=(), sem="dL"):
        if not hasattr(self, "_last_dma"):
            self._last_dma = {}
        w = list(wait) + [self._last_dma.get(sem)]
        tok = self.emit(queue, lambda e, o=out, i=in_: e.dma_start(out=o, in_=i),
                        wait=w, inc=sem, inc_by=16)
        self._last_dma[sem] = tok
        return tok


class Arena:
    def __init__(self, handle, nbytes):
        self.h = handle
        self.nbytes = nbytes
        self.off = 0
        self.marks = []

    def take(self, cols, dtype, name=""):
        isz = {F32: 4, BF16: 2, I32: 4}[dtype]
        nb = cols * isz
        off = (self.off + 63) & ~63
        assert off + nb <= self.nbytes, f"arena overflow at {name}: {off + nb}>{self.nbytes}"
        self.off = off + nb
        return self.h[:, off:off + nb].bitcast(dtype)

    def push(self):
        self.marks.append(self.off)

    def pop(self):
        self.off = self.marks.pop()



def _prep_weights(I):
    W = {}
    f32 = lambda a: np.ascontiguousarray(np.asarray(a, dtype=np.float32))

    qhead = lambda cch, half: (cch // 4) * 8 + (cch % 4) + 4 * half
    qperm = np.empty(1024, np.int64)
    for cch in range(8):
        for half in range(2):
            h = qhead(cch, half)
            qperm[cch * 128 + half * 64: cch * 128 + half * 64 + 64] = \
                np.arange(h * 64, (h + 1) * 64)
    W["l0f_wq"] = f32(np.asarray(I["L0_ffn_wq"]).T[:, qperm])
    W["l0f_wk"] = f32(np.asarray(I["L0_ffn_wk"]).T)
    W["l0f_wv"] = f32(np.asarray(I["L0_ffn_wv"]).T)
    W["l0f_wo"] = f32(np.asarray(I["L0_ffn_wo"]).T[qperm, :])
    qg = np.asarray(I["L0_ffn_q_gain"], np.float32)
    W["l0f_qgain"] = f32(np.stack([[qg[qhead(cch, half)] for cch in range(8)]
                                   for half in range(2)]))
    W["l0f_kgain"] = f32(np.asarray(I["L0_ffn_k_gain"]).reshape(2, 2).T)
    W["l0f_fc_logits"] = f32(I["L0_ffn_fc_logits"]).reshape(1, 8)
    W["l0f_pj_logits"] = f32(I["L0_ffn_pj_logits"]).reshape(1, 8)

    s1 = np.asarray(I["L0_ffn_fc_s1"], np.float32)
    s2 = np.asarray(I["L0_ffn_fc_s2"], np.float32)
    mix = np.asarray(I["L0_ffn_fc_mix"], np.float32)
    p1 = np.asarray(I["L0_ffn_pj_s1"], np.float32)
    p2 = np.asarray(I["L0_ffn_pj_s2"], np.float32)
    pmx = np.asarray(I["L0_ffn_pj_mix"], np.float32)

    fcAT = np.zeros((8, 8, 128, 128), np.float32)
    s1r = s1.reshape(8, 16, 64, 64)
    for a in range(8):
        for gg in range(2):
            fcAT[:, a, gg * 64:(gg + 1) * 64, gg * 64:(gg + 1) * 64] = \
                s1r[:, 2 * a + gg].transpose(0, 2, 1)
    W["fcAT"] = f32(fcAT)
    u = np.arange(16)[:, None]
    cc = np.arange(64)[None, :]
    fcP = np.empty((8, 1024, 4096), np.float32)
    for j in range(8):
        perm = (cc * 16 + (u + j) % 16).reshape(-1)
        fcP[j] = mix[j].T[perm]
    W["fcP"] = f32(fcP)
    s2r = s2.reshape(8, 64, 64, 64)
    fcB = np.zeros((8, 32, 128, 128), np.float32)
    for dch in range(32):
        for qq in range(2):
            fcB[:, dch, qq * 64:(qq + 1) * 64, qq * 64:(qq + 1) * 64] = s2r[:, 2 * dch + qq]
    W["fcB"] = f32(fcB)

    pjAT = np.zeros((8, 32, 128, 128), np.float32)
    p1r = p1.reshape(8, 64, 64, 64)
    for a in range(32):
        for gg in range(2):
            pjAT[:, a, gg * 64:(gg + 1) * 64, gg * 64:(gg + 1) * 64] = \
                p1r[:, 2 * a + gg].transpose(0, 2, 1)
    W["pjAT"] = f32(pjAT)
    u = np.arange(64)[:, None]
    pjP = np.empty((8, 4096, 1024), np.float32)
    for j in range(8):
        perm = (cc * 64 + (u + j) % 64).reshape(-1)
        pjP[j] = pmx[j].T[perm]
    W["pjP"] = f32(pjP)
    p2r = p2.reshape(8, 16, 64, 64)
    pjB = np.zeros((8, 8, 128, 128), np.float32)
    for dch in range(8):
        for qq in range(2):
            pjB[:, dch, qq * 64:(qq + 1) * 64, qq * 64:(qq + 1) * 64] = p2r[:, 2 * dch + qq]
    W["pjB"] = f32(pjB)

    W["l0a_inproj"] = f32(I["L0_attn_in_proj_w"])
    W["l0a_outproj"] = f32(I["L0_attn_out_proj_w"])
    W["l0a_convk"] = f32(np.asarray(I["L0_attn_conv_k"]).reshape(4, 4128).T)
    W["l0a_convb"] = f32(I["L0_attn_conv_b"]).reshape(4128, 1)
    W["l0a_theta"] = f32(np.asarray(I["L0_attn_theta"]).reshape(2048, 1))
    W["l0a_decay"] = f32(I["L0_attn_decay_slopes"]).reshape(1, 28)
    W["l0a_anchor"] = f32(I["L0_attn_anchor_slopes"]).reshape(1, 4)
    W["l0a_sscale"] = f32(np.asarray(I["L0_attn_score_scale"]).reshape(32, 1))
    W["l0a_dlog"] = f32(I["L0_attn_deriv_logits"]).reshape(1, 3)
    W["l0a_wre"] = f32(I["L0_attn_W_re"])
    W["l0a_wim"] = f32(I["L0_attn_W_im"])
    nsc = np.asarray(I["L0_attn_norm_scale"], np.float32)
    W["l0a_nsc_re"] = f32(np.tile(nsc[:64], 2).reshape(128, 1))
    W["l0a_nsc_im"] = f32(np.tile(nsc[64:], 2).reshape(128, 1))

    for k in ("W_kv", "W_kdec", "W_vdec", "W_q", "W_qdec", "W_o"):
        W["l1a_" + k] = f32(I["L1_attn_" + k])
    W["l1a_g_kv"] = f32(np.asarray(I["L1_attn_g_kv"]).reshape(2, 128).T)

    def rope_cols(wsrc):
        src = f32(I[wsrc])
        dst = np.zeros((src.shape[0], 2 * 384), np.float32)
        for h in range(16):
            half, hh = divmod(h, 8)
            cidx, r = divmod(hh, 3)
            dst[:, half * 384 + cidx * 128 + r * 32: half * 384 + cidx * 128 + r * 32 + 32] = \
                src[:, h * 32:(h + 1) * 32]
        return np.ascontiguousarray(dst)
    W["l1a_W_krope"] = rope_cols("L1_attn_W_krope")
    W["l1a_W_qrope"] = rope_cols("L1_attn_W_qrope")

    W["l1f_colembT"] = f32(np.asarray(I["L1_ffn_col_emb"]).T)
    W["l1f_gw1"] = f32(I["L1_ffn_gate_w1"])
    W["l1f_gb1"] = f32(np.asarray(I["L1_ffn_gate_b1"]).reshape(4, 128).T)
    W["l1f_gw2"] = f32(I["L1_ffn_gate_w2"])
    W["l1f_gb2"] = f32(np.asarray(I["L1_ffn_gate_b2"]).reshape(1, 16))
    W["l1f_Wr"] = f32(np.asarray(I["L1_ffn_W_cols"]).transpose(1, 0, 2).reshape(1024, 4096))
    W["l1f_bcols"] = f32(np.asarray(I["L1_ffn_b_cols"]).reshape(32, 128).T)
    W["l1f_projw"] = f32(I["L1_ffn_proj_w"])
    W["l1f_projb"] = f32(np.asarray(I["L1_ffn_proj_b"]).reshape(8, 128).T)
    W["l1f_lng"] = f32(np.asarray(I["L1_ffn_ln_g"]).reshape(8, 128).T)
    W["l1f_lnb"] = f32(np.asarray(I["L1_ffn_ln_b"]).reshape(8, 128).T)
    return W


_INPUT_SHAPES = {
    "xT": (D, T),
    "l0f_wq": (1024, 1024), "l0f_wk": (1024, 256), "l0f_wv": (1024, 256),
    "l0f_wo": (1024, 1024), "l0f_qgain": (2, 8), "l0f_kgain": (2, 2),
    "l0f_fc_logits": (1, 8), "l0f_pj_logits": (1, 8),
    "fcAT": (8, 8, 128, 128), "fcP": (8, 1024, 4096), "fcB": (8, 32, 128, 128),
    "pjAT": (8, 32, 128, 128), "pjP": (8, 4096, 1024), "pjB": (8, 8, 128, 128),
    "l0a_inproj": (1024, 4128), "l0a_outproj": (2048, 1024),
    "l0a_convk": (4128, 4), "l0a_convb": (4128, 1), "l0a_theta": (2048, 1),
    "l0a_decay": (1, 28), "l0a_anchor": (1, 4), "l0a_sscale": (32, 1),
    "l0a_dlog": (1, 3), "l0a_wre": (64, 64), "l0a_wim": (64, 64),
    "l0a_nsc_re": (128, 1), "l0a_nsc_im": (128, 1),
    "l1a_W_kv": (1024, 256), "l1a_W_kdec": (256, 1024), "l1a_W_vdec": (256, 1024),
    "l1a_W_q": (1024, 512), "l1a_W_qdec": (512, 1024), "l1a_W_o": (1024, 1024),
    "l1a_g_kv": (128, 2), "l1a_W_krope": (1024, 768), "l1a_W_qrope": (512, 768),
    "l1f_colembT": (1024, 16), "l1f_gw1": (1024, 512), "l1f_gb1": (128, 4),
    "l1f_gw2": (512, 16), "l1f_gb2": (1, 16), "l1f_Wr": (1024, 4096),
    "l1f_bcols": (128, 32), "l1f_projw": (4096, 1024), "l1f_projb": (128, 8),
    "l1f_lng": (128, 8), "l1f_lnb": (128, 8),
}


def _host_consts():
    C = {}
    C["ident_bf"] = np.eye(128, dtype=BF)
    C["ident_f32"] = np.eye(128, dtype=np.float32)
    C["ones_col_bf"] = np.ones((128, 1), dtype=BF)
    C["ones_row_bf"] = np.ones((1, 128), dtype=BF)
    C["ones_row_f32"] = np.ones((1, 128), dtype=np.float32)
    r = np.arange(128)
    C["diagmask_bf"] = (r[None, :] <= r[:, None]).astype(BF)

    hd = 64
    inv = 1.0 / (10000.0 ** (np.arange(0, hd, 2, dtype=np.float64) / hd))
    fr = np.outer(np.arange(T, dtype=np.float64), inv)
    cosv = np.cos(fr).astype(np.float32).T
    sinv = np.sin(fr).astype(np.float32).T
    C["l0f_ropeC"] = np.ascontiguousarray(np.tile(np.concatenate([cosv, cosv], 0), (2, 1)))
    C["l0f_ropeS"] = np.ascontiguousarray(np.tile(np.concatenate([sinv, sinv], 0), (2, 1)))
    M64 = np.zeros((64, 64), np.float32)
    for j in range(32):
        M64[j, 32 + j] = 1.0
        M64[32 + j, j] = -1.0
    M128 = np.zeros((128, 128), np.float32)
    M128[:64, :64] = M64
    M128[64:, 64:] = M64
    C["l0f_rotT"] = np.ascontiguousarray(M128.T).astype(BF)

    rd = 32
    inv1 = 1.0 / (10000.0 ** (np.arange(0, rd, 2, dtype=np.float64) / rd))
    fr1 = np.outer(np.arange(T, dtype=np.float64), inv1)
    c1 = np.cos(fr1).astype(np.float32).T
    s1 = np.sin(fr1).astype(np.float32).T
    c32 = np.concatenate([c1, c1], 0)
    s32 = np.concatenate([s1, s1], 0)
    padC = np.zeros((128, T), np.float32)
    padS = np.zeros((128, T), np.float32)
    for rr in range(3):
        padC[rr * 32:(rr + 1) * 32] = c32
        padS[rr * 32:(rr + 1) * 32] = s32
    C["l1a_ropeC"] = np.ascontiguousarray(padC)
    C["l1a_ropeS"] = np.ascontiguousarray(padS)
    M32 = np.zeros((32, 32), np.float32)
    for j in range(16):
        M32[j, 16 + j] = -1.0
        M32[16 + j, j] = 1.0
    M1 = np.zeros((128, 128), np.float32)
    for rr in range(3):
        M1[rr * 32:(rr + 1) * 32, rr * 32:(rr + 1) * 32] = M32
    C["l1a_rotT"] = np.ascontiguousarray(M1.T).astype(BF)
    C["l1a_rotT_f32"] = np.ascontiguousarray(M1.T)

    hs = np.zeros((128, 2), np.float32)
    hs[:64, 0] = 1.0
    hs[64:, 1] = 1.0
    C["headsel_bf"] = hs.astype(BF)
    C["headbc_bf"] = np.ascontiguousarray(hs.T).astype(BF)
    C["headsel_f32"] = hs
    C["headbc_f32"] = np.ascontiguousarray(hs.T)
    C["l0f_rotT_f32"] = np.ascontiguousarray(M128.T)
    C["trow_f32"] = np.arange(T, dtype=np.float32).reshape(1, T)
    C["drow_f32"] = (T - 1.0 - np.arange(T, dtype=np.float32)).reshape(1, T)
    repl = np.zeros((32, 16 * 128), np.float32)
    for hp in range(16):
        repl[2 * hp, hp * 128:hp * 128 + 64] = 1.0
        repl[2 * hp + 1, hp * 128 + 64:(hp + 1) * 128] = 1.0
    C["replc_bf"] = repl.astype(BF)
    return C



class Ctx:
    pass


def build_program(phases, dbg=()):
    nc = bass.Bass("TRN2", target_bir_lowering=False, dynamic_dma_scratch_size=16384)
    prog = Prog(nc)
    c = Ctx()
    c.nc, c.prog = nc, prog
    c.used = set()
    c.hostc = _host_consts()
    c.inline = {}
    c.dbg_req = list(dbg)
    c.dbg_out = {}

    def din(name):
        if name not in c.used:
            c.used.add(name)
            c._dins[name] = nc.dram_tensor(name, list(_INPUT_SHAPES[name]), F32,
                                           kind="ExternalInput").ap()
        return c._dins[name]
    c._dins = {}
    c.din = din

    def const(name):
        if name not in c.inline:
            arr = np.ascontiguousarray(c.hostc[name])
            c.inline[name] = nc.inline_tensor(arr, name).ap()
        return c.inline[name]
    c.const = const

    c._scratch = {}

    def scratch(name, shape, dtype):
        if name not in c._scratch:
            c._scratch[name] = nc.dram_tensor(name, list(shape), dtype,
                                              kind="Internal").ap()
        return c._scratch[name]
    c.scratch = scratch

    c.out = nc.dram_tensor("out", [T, D], F32, kind="ExternalOutput").ap()

    with ExitStack() as stack:
        ec = stack.enter_context
        c.X = [ec(nc.sbuf_tensor(f"X{i}", [128, T], F32)) for i in range(8)]
        c.epscol = ec(nc.sbuf_tensor("epscol", [1, 1], F32))
        c.ident = ec(nc.sbuf_tensor("identt", [128, 128], BF16))
        c.identf = ec(nc.sbuf_tensor("identtf", [128, 128], F32))
        c.onescol = ec(nc.sbuf_tensor("onescol", [128, 1], BF16))
        c.onesrow = ec(nc.sbuf_tensor("onesrow", [1, 128], BF16))
        c.onesrowf = ec(nc.sbuf_tensor("onesrowf", [1, 128], F32))
        c.diagmask = ec(nc.sbuf_tensor("diagmask", [128, 128], BF16))
        c.smallf = ec(nc.sbuf_tensor("smallf", [128, 160], F32))
        c.headself = ec(nc.sbuf_tensor("headself", [128, 2], F32))
        c.headbcf = ec(nc.sbuf_tensor("headbcf", [2, 128], F32))
        c.rotTf = ec(nc.sbuf_tensor("rotTf", [128, 128], F32))
        c.ftmp = [ec(nc.sbuf_tensor(f"ftmp{i}", [128, PB], F32)) for i in range(2)]
        ARENA_BYTES = 136 * 1024
        c.ar = Arena(ec(nc.sbuf_tensor("arena", [128, ARENA_BYTES], mybir.dt.uint8)),
                     ARENA_BYTES)
        c.pS = ec(nc.psum_tensor("pS", [128, 4 * PB], F32))
        c.p4 = ec(nc.psum_tensor("p4", [128, PB], F32))
        c.p5 = ec(nc.psum_tensor("p5", [128, PB], F32))
        c.p6 = ec(nc.psum_tensor("p6", [128, PB], F32))
        c.p7 = ec(nc.psum_tensor("p7", [128, PB], F32))
        c.pb = [c.pS[:, 0:PB], c.pS[:, PB:2 * PB], c.pS[:, 2 * PB:3 * PB],
                c.pS[:, 3 * PB:4 * PB], c.p4[:, :], c.p5[:, :], c.p6[:, :], c.p7[:, :]]

        block = ec(nc.Block())

        _emit_preamble(c)
        for ph in phases:
            PHASE_EMITTERS[ph](c)
        _emit_output(c)

        prog.finalize(block)

    return nc, sorted(c.used), dict(c.dbg_out)


def _dbg_dump(c, name, src_ap, shape):
    if name not in c.dbg_req:
        return
    o = c.nc.dram_tensor("dbg_" + name, list(shape), F32, kind="ExternalOutput").ap()
    c.dbg_out[name] = "dbg_" + name
    c.prog.barrier()
    t = c.prog.dma("sync", o, src_ap, sem="dS")
    c.prog.emit("sync", None, wait=(t,))
    c.prog.barrier()


def _emit_preamble(c):
    p = c.prog
    toks = [
        p.dma("sync", c.ident[:], c.const("ident_bf")),
        p.dma("sync", c.identf[:], c.const("ident_f32")),
        p.dma("sync", c.onescol[:], c.const("ones_col_bf")),
        p.dma("sync", c.onesrow[:], c.const("ones_row_bf")),
        p.dma("sync", c.onesrowf[:], c.const("ones_row_f32")),
        p.dma("sync", c.diagmask[:], c.const("diagmask_bf")),
        p.dma("sync", c.headself[:], c.const("headsel_f32")),
        p.dma("sync", c.headbcf[:], c.const("headbc_f32")),
        p.dma("sync", c.rotTf[:], c.const("l0f_rotT_f32")),
    ]
    xr = c.din("xT").rearrange("(c p) t -> c p t", p=128)
    for i in range(8):
        toks.append(p.dma("sync", c.X[i][:], xr[i]))
    p.emit("vector", lambda e: e.memset(c.epscol[:], 1e-5), inc="ve")
    tok = toks[-1]
    for e in ENGS:
        p.emit(e, None, wait=(tok,))
    p.barrier()


def _emit_output(c):
    p = c.prog
    p.barrier()
    outr = c.out.rearrange("(tt p) d -> tt p d", p=128)
    evac = [c.ftmp[0], c.ftmp[1]]
    last = [None, None]
    tv_prev = [None, None]
    for tt in range(16):
        for half in range(2):
            bank = (c.p4 if half == 0 else c.p5)
            ev = evac[half]
            tk = None
            for k in range(4):
                ch = half * 4 + k
                tk = p.emit("tensor",
                            lambda e, ch=ch, tt=tt, bank=bank, k=k:
                            e.matmul(bank[:, k * 128:(k + 1) * 128],
                                     c.X[ch][:, tt * 128:(tt + 1) * 128],
                                     c.identf[:], start=True, stop=True),
                            wait=(tv_prev[half],), inc="pe")
            tv = p.emit("vector", lambda e, bank=bank, ev=ev: e.tensor_copy(ev[:], bank[:, 0:PB]),
                        wait=(tk, last[half]), inc="ve")
            tv_prev[half] = tv
            last[half] = p.dma("sync", outr[tt][:, half * 512:(half + 1) * 512], ev[:],
                               wait=(tv,), sem="dSo%d" % half)
    p.emit("sync", None, wait=(last[0], last[1]))



def emit_layernorm(c, xn_chunks, st, scratch4=None, eps=1e-5):
    p = c.prog
    own = scratch4 is None
    if own:
        c.ar.push()
        scratch4 = [c.ar.take(T, BF16, f"ln_s{i}") for i in range(4)]
    xb = scratch4[0:2]
    sq = scratch4[2:4]
    p.barrier()
    sqb = [c.p4, c.p5, c.p6, c.p7]
    last_use = [None, None]
    tsum = None
    for ch in range(8):
        pa = ch % 2
        t1 = p.emit("vector", lambda e, ch=ch, pa=pa: e.tensor_copy(xb[pa], c.X[ch][:]),
                    wait=(last_use[pa],), inc="ve")
        t2 = p.emit("scalar", lambda e, pa=pa: e.activation(sq[pa], xb[pa], AF.Square),
                    wait=(t1,), inc="se")
        for g in range(4):
            s = slice(g * PB, (g + 1) * PB)
            p.emit("tensor", lambda e, s=s, ch=ch, g=g, pa=pa: e.matmul(
                c.pS[0:1, s.start:s.stop], c.onescol[:], xb[pa][:, s],
                start=(ch == 0), stop=(ch == 7)), wait=(t1,), inc="pe")
            tsum = p.emit("tensor", lambda e, s=s, ch=ch, g=g, pa=pa: e.matmul(
                sqb[g][0:1, 0:PB], c.onescol[:], sq[pa][:, s],
                start=(ch == 0), stop=(ch == 7)), wait=(t2,), inc="pe")
        last_use[pa] = tsum
    r1 = c.ftmp[0][0:1, 0:PB]
    r2 = c.ftmp[1][0:1, 0:PB]
    tg_done = None
    mu, nmu, rstd = st["mu"], st["nmu"], st["rstd"]
    for g in range(4):
        s = slice(g * PB, (g + 1) * PB)
        t1 = p.emit("vector", lambda e, s=s: e.tensor_scalar_mul(
            mu[:, s], c.pS[0:1, s.start:s.stop], 1.0 / D), wait=(tsum, tg_done), inc="ve")
        t1b = p.emit("vector", lambda e, s=s, g=g: e.tensor_scalar_mul(
            r1, sqb[g][0:1, 0:PB], 1.0 / D), wait=(tsum, t1), inc="ve")
        t2 = p.emit("scalar", lambda e, s=s: e.activation(r2, mu[:, s], AF.Square),
                    wait=(t1, tg_done), inc="se")
        t3 = p.emit("vector", lambda e: e.tensor_sub(r1, r1, r2),
                    wait=(t2, t1b), inc="ve")
        t4 = p.emit("scalar", lambda e: e.activation(r2, r1, AF.Ln, bias=c.epscol[0:1, 0:1]),
                    wait=(t3,), inc="se")
        t5 = p.emit("scalar", lambda e, s=s: e.activation(rstd[:, s], r2, AF.Exp, scale=-0.5),
                    wait=(t4,), inc="se")
        tg_done = p.emit("vector", lambda e, s=s: e.tensor_scalar_mul(
            nmu[:, s], mu[:, s], -1.0), wait=(t5,), inc="ve")
    p.barrier()
    for g in range(4):
        s = slice(g * PB, (g + 1) * PB)
        tb1 = p.emit("tensor", lambda e, s=s: e.matmul(
            c.p4[:, 0:PB], c.onesrowf[:], nmu[0:1, s], start=True, stop=True), inc="pe")
        tb2 = p.emit("tensor", lambda e, s=s: e.matmul(
            c.p5[:, 0:PB], c.onesrowf[:], rstd[0:1, s], start=True, stop=True), inc="pe")
        tvs = [None, None]
        for ch in range(8):
            ft = c.ftmp[ch % 2]
            t1 = p.emit("vector", lambda e, ch=ch, s=s, ft=ft: e.tensor_add(
                ft[:], c.X[ch][:, s], c.p4[:, 0:PB]),
                wait=(tb1, tvs[ch % 2]), inc="ve")
            tvs[ch % 2] = p.emit("vector", lambda e, ch=ch, s=s, ft=ft: e.tensor_mul(
                xn_chunks[ch][:, s], ft[:], c.p5[:, 0:PB]),
                wait=(tb2, t1), inc="ve")
        p.barrier()
    if own:
        c.ar.pop()


def alloc_stats(c):
    return {"mu": c.ar.take(T, F32, "st_mu")[0:1, :],
            "nmu": c.ar.take(T, F32, "st_nmu")[0:1, :],
            "rstd": c.ar.take(T, F32, "st_rstd")[0:1, :]}



def emit_alpha(c, logits_name, out_cols, wait=()):
    p = c.prog
    nc = c.nc
    c.ar.push()
    row = c.ar.take(64, F32, "al_rows")
    lg = row[0:1, 0:8]
    m1 = row[0:1, 8:9]
    nm1 = row[0:1, 9:10]
    eq1 = row[0:1, 16:24]
    t1 = row[0:1, 24:32]
    t2 = row[0:1, 32:40]
    m2 = row[0:1, 10:11]
    mask = row[0:1, 40:48]
    expv = row[0:1, 48:56]
    ssum = row[0:1, 11:12]
    rsum = row[0:1, 12:13]
    alpha = row[0:1, 56:64]

    tl = p.dma("sync", lg, c.din(logits_name), wait=wait)
    v = lambda fn, wait, inc="ve": p.emit("vector", fn, wait=wait, inc=inc)
    t = v(lambda e: e.reduce_max(m1, lg, axis=AX), (tl,))
    t = v(lambda e: e.tensor_scalar(eq1, lg, m1, None, ALU.is_equal), (t,))
    t = v(lambda e: e.tensor_mul(t1, eq1, lg), (t,))
    t = v(lambda e: e.tensor_sub(t1, lg, t1), (t,))
    t = v(lambda e: e.tensor_scalar(t2, eq1, 1e38, None, ALU.mult), (t,))
    t = v(lambda e: e.tensor_sub(t2, t1, t2), (t,))
    t = v(lambda e: e.reduce_max(m2, t2, axis=AX), (t,))
    tmask = v(lambda e: e.tensor_scalar(mask, lg, m2, None, ALU.is_ge), (t,))
    tneg = v(lambda e: e.tensor_scalar_mul(nm1, m1, -1.0), (tmask,))
    te = p.emit("scalar", lambda e: e.activation(expv, lg, AF.Exp, bias=nm1),
                wait=(tneg,), inc="se")
    t = v(lambda e: e.tensor_mul(expv, expv, mask), (te,))
    t = v(lambda e: e.reduce_sum(ssum, expv, axis=AX), (t,))
    t = v(lambda e: e.reciprocal(rsum, ssum), (t,))
    t = v(lambda e: e.tensor_scalar(alpha, expv, rsum, None, ALU.mult), (t,))
    tb = p.emit("tensor", lambda e: e.matmul(c.p7[:, 0:8], c.onesrowf[:], alpha,
                                             start=True, stop=True), wait=(t,), inc="pe")
    tout = p.emit("vector", lambda e: e.tensor_copy(out_cols, c.p7[:, 0:8]),
                  wait=(tb,), inc="ve")
    c.ar.pop()
    return tout


def emit_meff_build(c):
    p = c.prog
    nc = c.nc
    mfc = c.scratch("Mfc_t", (128, 32768), BF16)
    mpj = c.scratch("Mpj_t", (128, 32768), BF16)
    afc = c.smallf[:, 0:8]
    apj = c.smallf[:, 8:16]
    p.barrier()

    c.ar.push()
    atbF = c.ar.take(8 * 8 * 128, BF16, "bd_atF")
    atbP = c.ar.take(32 * 8 * 128, BF16, "bd_atP")
    bb = [c.ar.take(8 * 512, BF16, "bd_b0"), c.ar.take(8 * 512, BF16, "bd_b1")]
    NPR = 10
    pring = [c.ar.take(512, BF16, "bd_p%d" % i) for i in range(NPR)]
    ztb = [c.ar.take(512, BF16, "bd_zt0"), c.ar.take(512, BF16, "bd_zt1")]
    mo = [c.ar.take(512, BF16, "bd_mo0"), c.ar.take(512, BF16, "bd_mo1")]

    t_at = {}
    for which, atb, NA in (("fc", atbF, 8), ("pj", atbP, 32)):
        AT_in = c.din("fcAT" if which == "fc" else "pjAT")
        for a in range(NA):
            t_at[(which, a)] = p.dma(
                "gpsimd", atb[:, 1024 * a:1024 * (a + 1)].rearrange("p (j q) -> p j q", j=8),
                AT_in[:, a].rearrange("j p q -> p j q"), sem="dW_at%d" % (a % 4))

    t_afc = emit_alpha(c, "l0f_fc_logits", afc)
    t_apj = emit_alpha(c, "l0f_pj_logits", apj, wait=(t_afc,))
    p.barrier()

    cfgs = {
        "fc": (c.din("fcP"), c.din("fcB"), atbF, afc, t_afc, 8, 8,
               mfc.rearrange("p (uc i q) -> p uc i q", uc=32, i=8, q=128)),
        "pj": (c.din("pjP"), c.din("pjB"), atbP, apj, t_apj, 32, 2,
               mpj.rearrange("p (o a q) -> p o a q", o=8, a=32, q=128)),
    }
    its = []
    dglist = []
    for which in ("fc", "pj"):
        _, _, _, _, _, NA, NDG, _ = cfgs[which]
        for dg in range(NDG):
            gdg = len(dglist)
            dglist.append((which, dg))
            for a in range(NA):
                its.append((which, dg, a, gdg))
    flat = []
    gacnt = 0
    for which, dg, a, gdg in its:
        for j in range(8):
            flat.append((which, dg, a, j, gdg, gacnt))
        gacnt += 1
    NI = len(flat)

    tstore = [None, None]
    tbload = {}
    tbscale = {}
    tbbfree = [None, None]
    tpl = {}
    tpfree = {}
    tev = {}
    ts2 = {}
    tmo_t = {}

    def bb_load(gdg):
        which, dg = dglist[gdg]
        _, B_in, _, acol, t_alpha, _, _, _ = cfgs[which]
        par = gdg % 2
        for j in range(8):
            tbload[(gdg, j)] = p.dma(
                "gpsimd", bb[par][:, 512 * j:512 * (j + 1)].rearrange(
                    "p (d q) -> p d q", d=4),
                B_in[j, 4 * dg:4 * dg + 4].rearrange("d p q -> p d q"),
                wait=(tbbfree[par],), sem="dW_b%d" % par)
        tsc = None
        for j in range(8):
            tsc = p.emit("vector", lambda e, par=par, j=j: e.tensor_scalar(
                bb[par][:, 512 * j:512 * (j + 1)], bb[par][:, 512 * j:512 * (j + 1)],
                acol[:, j:j + 1], None, ALU.mult),
                wait=(tbload[(gdg, j)], t_alpha), inc="ve")
        tbscale[gdg] = tsc

    def p_load(gi):
        which, dg, a, j, _, _ = flat[gi]
        P_in = cfgs[which][0]
        slot = gi % NPR
        tpl[gi] = p.dma("gpsimd", pring[slot],
                        P_in[j, 128 * a:128 * (a + 1), 512 * dg:512 * (dg + 1)],
                        wait=(tpfree.get(slot),), sem="dW_p%d" % slot)

    PF = 8
    bb_load(0)
    bb_load(1)
    for gi in range(PF):
        p_load(gi)

    for gi in range(NI + 1):
        if gi < NI:
            which, dg, a, j, gdg, gac = flat[gi]
            atb = cfgs[which][2]
            slot = gi % NPR
            ps1 = c.p4 if gi % 2 == 0 else c.p5
            t1 = None
            for tt in range(4):
                t1 = p.emit("tensor", lambda e, tt=tt, ps1=ps1, slot=slot, atb=atb, a=a, j=j:
                            e.matmul(ps1[:, 128 * tt:128 * (tt + 1)],
                                     pring[slot][:, 128 * tt:128 * (tt + 1)],
                                     atb[:, 1024 * a + 128 * j:1024 * a + 128 * (j + 1)],
                                     start=True, stop=True),
                            wait=(tpl[gi], t_at[(which, a)], tev.get(gi - 2)), inc="pe")
            tpfree[slot] = t1
            if gi + PF < NI:
                p_load(gi + PF)
            tev[gi] = p.emit("vector", lambda e, gi=gi, ps1=ps1: e.tensor_copy(
                ztb[gi % 2], ps1[:, 0:PB]), wait=(t1, ts2.get(gi - 2)), inc="ve")
        if gi >= 1:
            which1, dg1, a1, j1, gdg1, gac1 = flat[gi - 1]
            Mv = cfgs[which1][7]
            bank = c.p6 if gac1 % 2 == 0 else c.p7
            tl = None
            for tt in range(4):
                tl = p.emit("tensor", lambda e, tt=tt, bank=bank, gi=gi, j1=j1, gdg1=gdg1:
                            e.matmul(bank[:, 128 * tt:128 * (tt + 1)],
                                     ztb[(gi - 1) % 2][:, 128 * tt:128 * (tt + 1)],
                                     bb[gdg1 % 2][:, 512 * j1 + 128 * tt:512 * j1 + 128 * (tt + 1)],
                                     start=(j1 == 0 and tt == 0),
                                     stop=(j1 == 7 and tt == 3)),
                            wait=(tev[gi - 1], tbscale[gdg1],
                                  tmo_t.get(gac1 - 2)), inc="pe")
            ts2[gi - 1] = tl
            if j1 == 7:
                last_a = (gi == NI) or (flat[gi][4] != gdg1)
                if last_a:
                    tbbfree[gdg1 % 2] = tl
                    if gdg1 + 2 < len(dglist):
                        bb_load(gdg1 + 2)
                ms = gac1 % 2
                tmo = p.emit("vector", lambda e, ms=ms, bank=bank: e.tensor_copy(
                    mo[ms], bank[:, 0:PB]), wait=(tl, tstore[ms]), inc="ve")
                tmo_t[gac1] = tmo
                dst = Mv[:, 4 * dg1:4 * dg1 + 4, a1, :]
                src = mo[ms].rearrange("p (u q) -> p u q", q=128)
                tstore[ms] = p.dma("sync", dst, src, wait=(tmo,), sem="dS_mo%d" % ms)
    p.emit("sync", None, wait=(tstore[0], tstore[1]))
    p.barrier()
    c.ar.pop()


def phase_meff(c):
    emit_meff_build(c)



def emit_l0ffn(c, tag):
    p = c.prog
    nc = c.nc
    c.ar.push()
    st = alloc_stats(c)
    xnO = [c.ar.take(T, BF16, "xnO") for _ in range(8)]
    qTi = [c.ar.take(T, BF16, "qTi") for _ in range(8)]
    kxh = [c.ar.take(T, BF16, "kxh") for _ in range(2)]
    WV = 264
    Vb = c.ar.take(16 * WV, BF16, "Vb")
    sh = [c.ar.take(T, BF16, "sh") for _ in range(2)]
    at2 = c.ar.take(PB, BF16, "at2")
    sqf = c.ar.take(PB, F32, "sqf")
    qrawf = c.ar.take(PB, F32, "qrawf")
    qnf = c.ar.take(PB, F32, "qnf")
    avsf = c.ar.take(256, F32, "avsf")
    row1 = c.ar.take(PB, F32, "row1")
    row2 = c.ar.take(PB, F32, "row2")
    colsc = c.ar.take(64, F32, "colsc")
    wq2ab = c.ar.take(2048, BF16, "wq2ab")
    wq2 = [wq2ab[:, 0:1024], wq2ab[:, 1024:2048]]
    wkv = [c.ar.take(2048, BF16, "wkva")]

    emit_layernorm(c, xnO, st, scratch4=qTi[0:4])
    p.barrier()

    sfx = tag
    tg2 = p.dma("sync", c.smallf[0:2, 16:24], c.din("l0f_qgain"), sem="dL")
    tg3 = p.dma("sync", c.smallf[0:2, 24:26], c.din("l0f_kgain"), sem="dL")
    trC = p.dma("gpsimd", sh[0], c.const("l0f_ropeC"), sem="dW_rc" + sfx)
    trS = p.dma("gpsimd", sh[1], c.const("l0f_ropeS"), sem="dW_rs" + sfx)
    te6 = p.emit("vector", lambda e: e.memset(colsc[:, 0:1], 1e-6), inc="ve")
    qg2 = c.smallf[0:2, 16:24]
    kg2 = c.smallf[0:2, 24:26]
    eps6 = colsc[0:2, 0:1]

    def proj_normed(kind, tprev):
        if kind == "q":
            nco, wname, gcol, dst = 8, "l0f_wq", qg2, qTi
        else:
            nco, wname, gcol, dst = 2, "l0f_wk", kg2, kxh
        w_in = c.din(wname)
        tw = [None, None]
        guards = [tprev.get("wq0"), tprev.get("wq1")]
        tlast = None
        for oc in range(nco):
            pa = oc % 2
            wb = wq2[pa]
            tw[pa] = p.dma("gpsimd", wb.rearrange("p (i q) -> p i q", i=8),
                           w_in[:, 128 * oc:128 * (oc + 1)].rearrange("(i p) q -> p i q", p=128),
                           wait=(guards[pa],), sem=f"dW_{kind}{pa}{sfx}")
            for tgi in range(4):
                s = slice(tgi * PB, (tgi + 1) * PB)
                tq = None
                for ic in range(8):
                    tq = p.emit("tensor", lambda e, ic=ic, s=s, wb=wb:
                                e.matmul(c.p4[:, 0:PB], wb[:, 128 * ic:128 * (ic + 1)],
                                         xnO[ic][:, s], start=(ic == 0), stop=(ic == 7)),
                                wait=(tw[pa], tprev.get("p4")), inc="pe")
                guards[pa] = tq
                ts = p.emit("scalar", lambda e: e.activation(sqf, c.p4[:, 0:PB], AF.Square),
                            wait=(tq, tprev.get("sqf")), inc="se")
                tq2 = p.emit("scalar", lambda e: e.copy(qrawf, c.p4[:, 0:PB]),
                             wait=(tq, tprev.get("qrawf")), inc="se")
                tprev["p4"] = tq2
                tm = p.emit("tensor", lambda e: e.matmul(c.p5[0:2, 0:PB], c.headself[:],
                                                         sqf, start=True, stop=True),
                            wait=(ts, tprev.get("p5")), inc="pe")
                t1 = p.emit("vector", lambda e: e.tensor_scalar_mul(
                    row1[0:2, :], c.p5[0:2, 0:PB], 1.0 / 64),
                    wait=(tm, tprev.get("row1")), inc="ve")
                tprev["p5"] = t1
                t2 = p.emit("scalar", lambda e: e.activation(
                    row2[0:2, :], row1[0:2, :], AF.Ln, bias=eps6),
                    wait=(t1, te6, tprev.get("row2")), inc="se")
                t3 = p.emit("scalar", lambda e: e.activation(
                    row1[0:2, :], row2[0:2, :], AF.Exp, scale=-0.5),
                    wait=(t2,), inc="se")
                tprev["row2"] = t3
                t4 = p.emit("vector", lambda e, oc=oc, gcol=gcol: e.tensor_scalar(
                    row1[0:2, :], row1[0:2, :], gcol[:, oc:oc + 1], None, ALU.mult),
                    wait=(t3, tg2, tg3), inc="ve")
                tprev["row1"] = t4
                t5 = p.emit("tensor", lambda e: e.matmul(c.p6[:, 0:PB], c.headbcf[:],
                                                         row1[0:2, :], start=True, stop=True),
                            wait=(t4, tprev.get("p6")), inc="pe")
                t6 = p.emit("vector", lambda e: e.tensor_mul(qnf, qrawf, c.p6[:, 0:PB]),
                            wait=(t5, tq2, tprev.get("qnf")), inc="ve")
                tprev["p6"] = t6
                t7 = p.emit("tensor", lambda e: e.matmul(c.p7[:, 0:PB], c.rotTf[:], qnf,
                                                         start=True, stop=True),
                            wait=(t6, tprev.get("p7")), inc="pe")
                t8 = p.emit("vector", lambda e, s=s: e.tensor_mul(qrawf, qnf, sh[0][:, s]),
                            wait=(t6, trC, tq2, ts), inc="ve")
                tprev["qnf"] = t8
                t9 = p.emit("vector", lambda e, s=s: e.tensor_mul(sqf, c.p7[:, 0:PB], sh[1][:, s]),
                            wait=(t7, trS, ts), inc="ve")
                tprev["p7"] = t9
                tlast = p.emit("vector", lambda e, oc=oc, s=s, dst=dst: e.tensor_add(
                    dst[oc][:, s], qrawf, sqf), wait=(t8, t9), inc="ve")
                tprev["qrawf"] = tlast
                tprev["sqf"] = tlast
        tprev["wq0"], tprev["wq1"] = guards
        return tlast

    tprev = {}
    tq_done = proj_normed("q", tprev)
    tk_done = proj_normed("k", tprev)

    twv = p.dma("gpsimd", wkv[0].rearrange("p (i q) -> p i q", i=8),
                c.din("l0f_wv").rearrange("(i p) q -> p i q", p=128),
                wait=(tk_done,), sem="dW_wv" + sfx)
    tvslot = [None, None]
    tve_t = {}
    tvev = {}

    def v_evac(tt):
        off = 512 * (tt % 2)
        t1 = p.emit("vector", lambda e, tt=tt, off=off: e.tensor_copy(
            Vb[:, tt * WV:tt * WV + 260].rearrange("p (h d) -> p h d", d=65)[:, :, 0:64],
            c.pS[:, off:off + 256].rearrange("p (h d) -> p h d", d=64)),
            wait=(tve_t[tt],), inc="ve")
        tvslot[tt % 2] = t1
        tvev[tt] = p.emit("vector", lambda e, tt=tt: e.memset(
            Vb[:, tt * WV + 64: tt * WV + 261:65], 1.0), wait=(t1,), inc="ve")

    for tt in range(16):
        off = 512 * (tt % 2)
        tvm = None
        for ic in range(8):
            tvm = p.emit("tensor", lambda e, ic=ic, tt=tt, off=off: e.matmul(
                c.pS[:, off:off + 256], xnO[ic][:, 128 * tt:128 * (tt + 1)],
                wkv[0][:, 256 * ic:256 * (ic + 1)],
                start=(ic == 0), stop=(ic == 7)),
                wait=(twv, tvslot[tt % 2] if ic == 0 else None), inc="pe")
        tve_t[tt] = tvm
        if tt >= 1:
            v_evac(tt - 1)
    v_evac(15)
    tv_done = tvev[15]
    txn_free = tvev[15]
    tpS_init = (tvev[14], tvev[15])

    A4 = [sh[0], sh[1], wq2ab, wkv[0]]
    aits = [(h, tp) for h in range(16) for tp in range(8)]
    NIT = len(aits)
    scl = 0.125

    texp = {}
    tmask = {}
    tABr = {(-2, 0): tk_done, (-2, 1): tk_done,
            (-1, 0): tk_done, (-1, 1): tvev[15]}
    tsc_t = {}
    ttr = {0: tprev.get("p4"), 1: tprev.get("p7")}
    tat2r = [None, None]
    tavfin = {}
    trecip = {-2: tprev.get("p5"), -1: tprev.get("p6")}
    tavs = {}
    toutm = {}
    tbc_t = {}
    tO_last = txn_free
    tscore_last = None

    def ageo(n):
        h, tp = aits[n]
        kvh = h // 4
        return (h, tp, kvh, (h // 8) * 4 + h % 4, (kvh % 2) * 64,
                kvh // 2, (kvh % 2) * 64, 2 * tp + 2)

    def emit_scores(n, ti, qmat, qrow0, kmat, krow0, ext, exts, cm, Ab, w0):
        tsc = None
        tmq = []
        ngr = (exts + PB - 1) // PB
        qt = 2 * aits[n][1] + ti
        for g in range(ngr):
            ws = min(PB, exts - g * PB)
            tsc = p.emit("tensor", lambda e, g=g, ws=ws, qt=qt, qmat=qmat,
                         qrow0=qrow0, kmat=kmat, krow0=krow0: e.matmul(
                c.pS[:, g * PB:g * PB + ws],
                qmat[qrow0:qrow0 + 64, 128 * qt:128 * (qt + 1)],
                kmat[krow0:krow0 + 64, g * PB:g * PB + ws],
                start=True, stop=True),
                wait=w0 if g == 0 else (), inc="pe")
            tmq.append(p.emit("vector", lambda e, g=g, ws=ws, cm=cm: e.reduce_max(
                colsc[:, cm + 2 + g:cm + 3 + g], c.pS[:, g * PB:g * PB + ws], axis=AX),
                wait=(tsc, texp.get((n - 2, ti))), inc="ve"))
        tsc_t[(n, ti)] = tsc
        tmx = p.emit("vector", lambda e, cm=cm, ngr=ngr: e.reduce_max(
            colsc[:, cm:cm + 1], colsc[:, cm + 2:cm + 2 + ngr], axis=AX, negate=True),
            wait=tuple(tmq), inc="ve")
        tmx2 = p.emit("vector", lambda e, cm=cm: e.tensor_scalar_mul(
            colsc[:, cm + 1:cm + 2], colsc[:, cm:cm + 1], scl), wait=(tmx,), inc="ve")
        texp[(n, ti)] = p.emit("scalar", lambda e, exts=exts, Ab=Ab, cm=cm: e.activation(
            Ab[:, 0:exts], c.pS[:, 0:exts], AF.Exp, bias=colsc[:, cm + 1:cm + 2],
            scale=scl), wait=(tmx2, tABr[(n - 2, ti)]), inc="se")
        return tsc

    def prod_first(n):
        h, tp, kvh, qch, qoff, kch, koff, nkt = ageo(n)
        ext = 128 * nkt
        pr = n % 2
        wprev = texp.get((n - 1, 1))
        w0 = (tq_done, tk_done, wprev) if wprev is not None else \
            (tq_done, tk_done) + tpS_init
        emit_scores(n, 0, qTi[qch], qoff, kxh[kch], koff, ext,
                    ext - 128 if nkt > 1 else ext, 1 + 14 * pr, A4[2 * pr], w0)

    def prod_second(n):
        nonlocal tscore_last
        h, tp, kvh, qch, qoff, kch, koff, nkt = ageo(n)
        ext = 128 * nkt
        pr = n % 2
        Ab0, Ab1 = A4[2 * pr], A4[2 * pr + 1]
        qt0, qt1 = 2 * tp, 2 * tp + 1
        tscore_last = emit_scores(n, 1, qTi[qch], qoff, kxh[kch], koff, ext, ext,
                                  8 + 14 * pr, Ab1, (texp[(n, 0)],))
        tms0 = p.emit("gpsimd", lambda e, qt=qt0, Ab=Ab0: e.tensor_mul(
            Ab[:, 128 * qt:128 * (qt + 1)], Ab[:, 128 * qt:128 * (qt + 1)],
            c.diagmask[:]), wait=(texp[(n, 0)],), inc="ge")
        tmask[(n, 0)] = p.emit("gpsimd", lambda e, qt=qt0, ext=ext, Ab=Ab0: e.memset(
            Ab[:, 128 * (qt + 1):ext], 0.0), wait=(tms0,), inc="ge")
        tmask[(n, 1)] = p.emit("gpsimd", lambda e, qt=qt1, Ab=Ab1: e.tensor_mul(
            Ab[:, 128 * qt:128 * (qt + 1)], Ab[:, 128 * qt:128 * (qt + 1)],
            c.diagmask[:]), wait=(texp[(n, 1)],), inc="ge")


    cstate = {}

    def cons_kt(n, kt, nkt, kvh, pr):
        Ab0, Ab1 = A4[2 * pr], A4[2 * pr + 1]
        tb = c.p4 if kt % 2 == 0 else c.p7
        to1 = p.emit("tensor", lambda e, kt=kt, tb=tb, Ab0=Ab0: e.matmul(
            tb[:, 0:128], Ab0[:, 128 * kt:128 * (kt + 1)], c.ident[:],
            start=True, stop=True), wait=(tmask[(n, 0)], ttr[kt % 2]), inc="pe")
        to2 = p.emit("tensor", lambda e, kt=kt, tb=tb, Ab1=Ab1: e.matmul(
            tb[:, 128:256], Ab1[:, 128 * kt:128 * (kt + 1)], c.ident[:],
            start=True, stop=True), wait=(tmask[(n, 1)],), inc="pe")
        tABr[(n, 0)] = to1
        tABr[(n, 1)] = to2
        if kt % 2 == 0:
            tev = p.emit("vector", lambda e, kt=kt, tb=tb: e.tensor_copy(
                at2[:, 256 * (kt % 2):256 * (kt % 2) + 256], tb[:, 0:256]),
                wait=(to1, to2, tat2r[kt % 2]), inc="ve")
        else:
            tev = p.emit("scalar", lambda e, kt=kt, tb=tb: e.copy(
                at2[:, 256 * (kt % 2):256 * (kt % 2) + 256], tb[:, 0:256]),
                wait=(to1, to2, tat2r[kt % 2]), inc="se")
        ttr[kt % 2] = tev
        cstate[(n, kt)] = tev

    def cons_av(n, kt, nkt, kvh, pr):
        avbank = c.p5 if pr == 0 else c.p6
        w = [cstate.pop((n, kt)), tv_done]
        if kt == 0:
            w += [trecip.get(n - 2), tavs.get(n - 2)]
        tav = p.emit("tensor", lambda e, kt=kt, kvh=kvh, nkt=nkt, avbank=avbank: e.matmul(
            avbank[0:65, 0:256], Vb[:, kt * WV + kvh * 65: kt * WV + kvh * 65 + 65],
            at2[:, 256 * (kt % 2):256 * (kt % 2) + 256],
            start=(kt == 0), stop=(kt == nkt - 1)), wait=tuple(w), inc="pe")
        tat2r[kt % 2] = tav
        if kt == nkt - 1:
            tavfin[n] = tav

    def cons(n, half):
        h, tp, kvh, qch, qoff, kch, koff, nkt = ageo(n)
        pr = n % 2
        h1 = (nkt + 1) // 2
        kts = range(0, h1) if half == 0 else range(h1, nkt)
        for kt in kts:
            cons_kt(n, kt, nkt, kvh, pr)
            if kt >= 1:
                cons_av(n, kt - 1, nkt, kvh, pr)
        if half == 1:
            cons_av(n, nkt - 1, nkt, kvh, pr)
            avbank = c.p5 if pr == 0 else c.p6
            rsl = row1[0:1, 256 * pr:256 * pr + 256]
            tln = p.emit("scalar", lambda e, rsl=rsl, avbank=avbank: e.activation(
                rsl, avbank[64:65, 0:256], AF.Ln),
                wait=(tavfin[n], tbc_t.get(n - 2)), inc="se")
            tiv = p.emit("scalar", lambda e, rsl=rsl: e.activation(
                rsl, rsl, AF.Exp, scale=-1.0), wait=(tln,), inc="se", drain=True)
            trecip[n] = tiv
            avs = avsf[0:64, 0:256] if pr == 0 else row2[0:64, 0:256]
            tas = p.emit("scalar", lambda e, avs=avs, avbank=avbank: e.copy(
                avs, avbank[0:64, 0:256]),
                wait=(tavfin[n], toutm.get(n - 2)), inc="se")
            tavs[n] = tas

    def cons_tail(n):
        h, tp, kvh, qch, qoff, kch, koff, nkt = ageo(n)
        pr = n % 2
        avbank = c.p5 if pr == 0 else c.p6
        rsl = row1[0:1, 256 * pr:256 * pr + 256]
        avs = avsf[0:64, 0:256] if pr == 0 else row2[0:64, 0:256]
        tbc = p.emit("tensor", lambda e, rsl=rsl, avbank=avbank: e.matmul(
            avbank[0:64, 256:512], c.onesrowf[0:1, 0:64], rsl,
            start=True, stop=True),
            wait=(trecip[n], tavs[n], toutm.get(n - 2)), inc="pe")
        tbc_t[n] = tbc
        toutm[n] = p.emit("vector", lambda e, tp=tp, qch=qch, qoff=qoff,
                          avs=avs, avbank=avbank: e.tensor_mul(
            xnO[qch][qoff:qoff + 64, 256 * tp:256 * (tp + 1)],
            avs, avbank[0:64, 256:512]), wait=(tbc,), inc="ve")

    for n in range(NIT + 2):
        if n < NIT:
            prod_first(n)
        if n >= 2:
            cons_tail(n - 2)
        if 1 <= n <= NIT:
            cons(n - 1, 0)
        if n < NIT:
            prod_second(n)
        if 1 <= n <= NIT:
            cons(n - 1, 1)
    tO_done = toutm[NIT - 1]

    w_in = c.din("l0f_wo")
    tw = [None, None]
    twg = [tO_done, tO_done]
    tup = {"p4": tO_done, "p5": tO_done, "p6": tO_done}
    tinner_last = None
    tu5 = None
    for oc in range(8):
        pa = oc % 2
        tw[pa] = p.dma("gpsimd", wq2[pa].rearrange("p (i q) -> p i q", i=8),
                       w_in[:, 128 * oc:128 * (oc + 1)].rearrange("(i p) q -> p i q", p=128),
                       wait=(twg[pa],), sem=f"dW_wo{pa}{sfx}")
        for tgi in range(4):
            s = slice(tgi * PB, (tgi + 1) * PB)
            tb1 = p.emit("tensor", lambda e, s=s: e.matmul(
                c.p5[:, 0:PB], c.onesrowf[:], st["nmu"][0:1, s], start=True, stop=True),
                wait=(tup.get("p5"),), inc="pe")
            tb2 = p.emit("tensor", lambda e, s=s: e.matmul(
                c.p6[:, 0:PB], c.onesrowf[:], st["rstd"][0:1, s], start=True, stop=True),
                wait=(tup.get("p6"),), inc="pe")
            tmm = None
            for ic in range(8):
                tmm = p.emit("tensor", lambda e, ic=ic, s=s, pa=pa: e.matmul(
                    c.p4[:, 0:PB], wq2[pa][:, 128 * ic:128 * (ic + 1)], xnO[ic][:, s],
                    start=(ic == 0), stop=(ic == 7)),
                    wait=(tw[pa], tup.get("p4"), tO_done), inc="pe")
            twg[pa] = tmm
            ft = c.ftmp[pa]
            tu1 = p.emit("vector", lambda e, oc=oc, s=s, ft=ft: e.tensor_add(
                ft[:], c.X[oc][:, s], c.p5[:, 0:PB]),
                wait=(tb1, tup.get("ft%d" % pa)), inc="ve")
            tup["p5"] = tu1
            tu2 = p.emit("vector", lambda e, ft=ft: e.tensor_mul(
                ft[:], ft[:], c.p6[:, 0:PB]), wait=(tb2, tu1), inc="ve")
            tup["p6"] = tu2
            tu3 = p.emit("vector", lambda e, ft=ft: e.tensor_add(
                ft[:], ft[:], c.p4[:, 0:PB]), wait=(tmm, tu2), inc="ve")
            tup["p4"] = tu3
            ti4 = p.emit("vector", lambda e, oc=oc, s=s, ft=ft: e.tensor_copy(
                qTi[oc][:, s], ft[:]), wait=(tu3, tscore_last), inc="ve")
            tinner_last = ti4
            tu5 = p.emit("vector", lambda e, oc=oc, s=s, ft=ft: e.scalar_tensor_tensor(
                c.X[oc][:, s], ft[:], 0.5, c.X[oc][:, s], ALU.mult, ALU.add),
                wait=(ti4,), inc="ve")
            tup["ft%d" % pa] = tu5
    p.barrier()

    inner = qTi
    mfc = c.scratch("Mfc_t", (128, 32768), BF16)
    mpj = c.scratch("Mpj_t", (128, 32768), BF16)
    msq_row = st["mu"][0:1, :]
    rms_row = st["rstd"][0:1, :]
    tmsq = None
    tsq_guard = [None, None]
    sqbf = [A4[0][:, 0:PB], A4[0][:, PB:2 * PB]]
    for chn in range(8):
        pa = chn % 2
        sb = sqbf[pa]
        for g in range(4):
            sg = slice(g * PB, (g + 1) * PB)
            tsq = p.emit("scalar", lambda e, chn=chn, sg=sg, sb=sb: e.activation(
                sb, inner[chn][:, sg], AF.Square), wait=(tinner_last, tsq_guard[pa]), inc="se")
            tmsq = p.emit("tensor", lambda e, g=g, chn=chn, sb=sb: e.matmul(
                c.pS[0:1, g * PB:(g + 1) * PB], c.onescol[:], sb,
                start=(chn == 0), stop=(chn == 7)), wait=(tsq,), inc="pe")
            tsq_guard[pa] = tmsq
    t = p.emit("vector", lambda e: e.tensor_scalar_mul(msq_row, c.pS[0:1, 0:4 * PB], 1.0 / D),
               wait=(tmsq,), inc="ve")
    t = p.emit("scalar", lambda e: e.activation(st["nmu"][0:1, :], msq_row, AF.Ln, bias=eps6[0:1, 0:1]),
               wait=(t,), inc="se")
    trms = p.emit("scalar", lambda e: e.activation(rms_row, st["nmu"][0:1, :], AF.Exp, scale=-0.5),
                  wait=(t,), inc="se")

    def xh_ap(ic):
        return kxh[ic // 4][:, (ic % 4) * PB:(ic % 4 + 1) * PB]

    u2v = []
    for uc in range(32):
        u2v.append(xnO[uc // 4][:, (uc % 4) * PB:(uc % 4 + 1) * PB])

    NMR = 8
    mring = [Vb[:, 1024 * i:1024 * (i + 1)] for i in range(4)] + \
            [A4[2][:, 0:1024], A4[2][:, 1024:2048],
             A4[3][:, 0:1024], A4[3][:, 1024:2048]]
    mloads = []
    for tgi in range(4):
        for uc in range(32):
            mloads.append(mfc[:, 1024 * uc:1024 * (uc + 1)])
        for oc in range(8):
            for ug4 in range(4):
                mloads.append(mpj[:, 4096 * oc + 1024 * ug4:4096 * oc + 1024 * (ug4 + 1)])
    tmld = {}
    tmread = {}
    nxt = [0]

    def m_load():
        k = nxt[0]
        nxt[0] += 1
        sl = k % NMR
        tmld[k] = p.dma("sync", mring[sl], mloads[k],
                        wait=(tmread.get(k - NMR, tO_done),), sem=f"dM{sl}{sfx}")

    for _ in range(6):
        m_load()

    sq2 = [sqf, qnf]
    tmg = {"b0": trms, "b1": trms, "b2": trms, "b3": trms,
           "sq0": None, "sq1": None}
    tX_last = tu5
    for tgi in range(4):
        s = slice(tgi * PB, (tgi + 1) * PB)
        tb = p.emit("tensor", lambda e, s=s: e.matmul(
            c.p7[:, 0:PB], c.onesrowf[:], rms_row[0:1, s], start=True, stop=True),
            wait=(trms, tmg.get("p7")), inc="pe")
        txh = None
        for ic in range(8):
            txh = p.emit("vector", lambda e, ic=ic, s=s: e.tensor_mul(
                xh_ap(ic), inner[ic][:, s], c.p7[:, 0:PB]),
                wait=(tb, tmg.get("xh"), tscore_last), inc="ve")
        tmg["p7"] = txh
        tlast_u2 = None
        for uc in range(32):
            k = tgi * 64 + uc
            sl = k % NMR
            bi = uc % 4
            bk = c.pb[bi]
            sb = sq2[uc % 2]
            tfc = None
            for ic in range(8):
                tfc = p.emit("tensor", lambda e, ic=ic, sl=sl, bk=bk: e.matmul(
                    bk, mring[sl][:, 128 * ic:128 * (ic + 1)], xh_ap(ic),
                    start=(ic == 0), stop=(ic == 7)),
                    wait=(txh, tmld[k], tmg.get("b%d" % bi)), inc="pe")
            tmread[k] = tfc
            if nxt[0] < len(mloads):
                m_load()
            trl = p.emit("scalar", lambda e, bk=bk, sb=sb: e.activation(sb, bk, AF.Relu),
                         wait=(tfc, tmg.get("sq%d" % (uc % 2))), inc="se")
            tu2 = p.emit("gpsimd", lambda e, uc=uc, sb=sb: e.tensor_mul(
                u2v[uc], sb, sb), wait=(trl, tmg.get("u2")), inc="ge")
            tmg["sq%d" % (uc % 2)] = tu2
            tmg["b%d" % bi] = tu2
            tlast_u2 = tu2
        for oc in range(8):
            pa = oc % 2
            bk = c.p5 if pa == 0 else c.p6
            tpj = None
            for ug4 in range(4):
                k = tgi * 64 + 32 + oc * 4 + ug4
                sl = k % NMR
                for du in range(8):
                    uc = ug4 * 8 + du
                    tpj = p.emit("tensor", lambda e, du=du, sl=sl, bk=bk, uc=uc: e.matmul(
                        bk[:, 0:PB], mring[sl][:, 128 * du:128 * (du + 1)],
                        u2v[uc], start=(uc == 0), stop=(uc == 31)),
                        wait=(tmld[k], tlast_u2, tmg.get("pj%d" % pa)), inc="pe")
                tmread[k] = tpj
                if nxt[0] < len(mloads):
                    m_load()
            tX = p.emit("vector", lambda e, oc=oc, s=s, bk=bk: e.scalar_tensor_tensor(
                c.X[oc][:, s], bk[:, 0:PB], 0.5, c.X[oc][:, s], ALU.mult, ALU.add),
                wait=(tpj, tX_last), inc="ve")
            tmg["pj%d" % pa] = tX
            tX_last = tX
        tmg["xh"] = tX_last
        tmg["u2"] = tX_last
    p.barrier()
    c.ar.pop()


def phase_l0ffn(c):
    emit_l0ffn(c, "l0f")



def emit_l0attn(c):
    p = c.prog
    nc = c.nc
    c.ar.push()
    xn = [c.ar.take(T, BF16, "a_xn") for _ in range(8)]
    c.ar.push()
    st = alloc_stats(c)
    emit_layernorm(c, xn, st)
    c.ar.pop()
    p.barrier()
    yg = [c.ar.take(T, BF16, "a_yg") for _ in range(8)]
    zin = c.ar.take(T + 16, F32, "a_zin")
    xval = c.ar.take(T, F32, "a_xval")
    W1 = c.ar.take(T, F32, "a_w1")
    W2 = c.ar.take(T, F32, "a_w2")
    W3 = c.ar.take(T, F32, "a_w3")
    tA_ = c.ar.take(T, F32, "a_p")
    tbf = c.ar.take(2 * T, BF16, "a_pbf")
    replc = c.ar.take(2048, BF16, "a_replc")
    wre2 = c.ar.take(64, BF16, "a_wre2")
    wim2 = c.ar.take(64, BF16, "a_wim2")
    scrb = [zin[:, 0:256].bitcast(BF16), zin[:, 256:512].bitcast(BF16)]
    wst = [c.ar.take(1024, BF16, "a_ws0"), c.ar.take(1024, BF16, "a_ws1")]
    wrow = c.smallf[:, 96:160]
    pA = tA_[0:32, :]
    pB = xval[0:32, :]
    pbf = tbf[0:32, 0:T]
    invbf = tbf[0:32, T:2 * T]

    inproj = c.din("l0a_inproj")
    CQ = c.smallf
    t1 = p.emit("vector", lambda e: e.memset(CQ[:, 13:14], 1e-5), inc="ve")
    t1b = p.emit("vector", lambda e: e.memset(CQ[:, 14:15], 1.0), wait=(t1,), inc="ve")
    trepl = p.dma("gpsimd", replc[0:32, :], c.const("replc_bf"), sem="dWrp")
    ttrow = p.dma("sync", c.ftmp[0][0:1, 0:PB] if False else wrow[0:1, 0:3], c.din("l0a_dlog"), sem="dL")

    r = wrow
    t = p.emit("vector", lambda e: e.reduce_max(r[0:1, 8:9], r[0:1, 0:3], axis=AX),
               wait=(ttrow,), inc="ve")
    t = p.emit("vector", lambda e: e.tensor_scalar_mul(r[0:1, 9:10], r[0:1, 8:9], -1.0),
               wait=(t,), inc="ve")
    t = p.emit("scalar", lambda e: e.activation(r[0:1, 4:7], r[0:1, 0:3], AF.Exp,
                                                bias=r[0:1, 9:10]), wait=(t,), inc="se")
    t = p.emit("vector", lambda e: e.reduce_sum(r[0:1, 8:9], r[0:1, 4:7], axis=AX),
               wait=(t,), inc="ve")
    t = p.emit("vector", lambda e: e.reciprocal(r[0:1, 9:10], r[0:1, 8:9]),
               wait=(t,), inc="ve")
    t = p.emit("vector", lambda e: e.tensor_scalar(r[0:1, 4:7], r[0:1, 4:7],
                                                   r[0:1, 9:10], None, ALU.mult),
               wait=(t,), inc="ve")
    t = p.emit("vector", lambda e: e.tensor_scalar_mul(r[0:1, 6:7], r[0:1, 6:7], -1.0),
               wait=(t,), inc="ve")
    tbw = p.emit("tensor", lambda e: e.matmul(c.p7[:, 0:3], c.onesrowf[:], r[0:1, 4:7],
                                              start=True, stop=True), wait=(t,), inc="pe")
    twcols = p.emit("vector", lambda e: e.tensor_copy(CQ[:, 10:13], c.p7[:, 0:3]),
                    wait=(tbw,), inc="ve")

    tdec = p.dma("sync", r[0:1, 32:60], c.din("l0a_decay"), wait=(t,), sem="dL")
    tanc = p.dma("sync", r[0:1, 60:64], c.din("l0a_anchor"), wait=(tdec,), sem="dL")
    t = p.emit("scalar", lambda e: e.activation(r[0:1, 0:32], r[0:1, 32:64], AF.Abs),
               wait=(tanc, twcols), inc="se")
    t = p.emit("scalar", lambda e: e.activation(r[0:1, 0:32], r[0:1, 0:32], AF.Exp,
                                                scale=-1.0), wait=(t,), inc="se", drain=True)
    t = p.emit("scalar", lambda e: e.activation(r[0:1, 0:32], r[0:1, 0:32], AF.Ln,
                                                bias=CQ[0:1, 14:15]), wait=(t1b, t), inc="se",
               drain=True)
    t2 = p.emit("vector", lambda e: e.tensor_scalar_max(r[0:1, 32:64], r[0:1, 32:64], 0.0),
                wait=(t,), inc="ve")
    tsp = p.emit("vector", lambda e: e.tensor_add(r[0:1, 32:64], r[0:1, 32:64], r[0:1, 0:32]),
                 wait=(t2,), inc="ve")
    t = p.emit("vector", lambda e: e.memset(r[0:1, 0:32], 0.0), wait=(tsp,), inc="ve")
    t = p.emit("vector", lambda e: e.tensor_copy(r[0:1, 0:28], r[0:1, 32:60]),
               wait=(t,), inc="ve")
    tspd = p.emit("vector", lambda e: e.memset(r[0:1, 32:60], 0.0), wait=(tsp, t), inc="ve")
    tdr = p.dma("sync", c.ftmp[0][0:1, 0:T] if False else zin[0:1, 0:T], c.const("drow_f32"), sem="dL")
    ttr = p.dma("sync", zin[1:2, 0:T] if False else xval[0:1, 0:T], c.const("trow_f32"), sem="dL")
    targ = None
    for g in range(4):
        s = slice(g * PB, (g + 1) * PB)
        p.emit("tensor", lambda e, s=s: e.matmul(
            c.pS[0:32, s.start:s.stop], r[0:1, 0:32], zin[0:1, s],
            start=True, stop=False), wait=(t, tdr, targ), inc="pe")
        targ = p.emit("tensor", lambda e, s=s: e.matmul(
            c.pS[0:32, s.start:s.stop], r[0:1, 32:64], xval[0:1, s],
            start=False, stop=True), wait=(tspd, ttr), inc="pe")
    ttw = p.emit("scalar", lambda e: e.activation(pB, c.pS[0:32, 0:T], AF.Exp, scale=-1.0),
                 wait=(targ,), inc="se")
    tzs = emit_inproj_chunk(c, zin, xn, 4096, wst[0], "dWsr", wait=(ttw,), rows=32)
    tck = p.dma("sync", CQ[0:32, 4:8], c.din("l0a_convk")[4096:4128, :],
                sem="dL")
    tcb = p.dma("sync", CQ[0:32, 8:9], c.din("l0a_convb")[4096:4128, :],
                sem="dL")
    tss = p.dma("sync", CQ[0:32, 9:10], c.din("l0a_sscale"), sem="dL")
    tcv = emit_conv4(c, pA, zin[0:32, :], CQ[0:32, 4:8], CQ[0:32, 8:9],
                     wait=(tzs, tck, tcb), rows=32)
    t = p.emit("vector", lambda e: e.tensor_scalar(pA, pA, CQ[0:32, 9:10], None,
                                                   ALU.mult),
               wait=(tcv, tss), inc="ve", drain=True)
    t = p.emit("vector", lambda e: e.tensor_scalar_min(pA, pA, 20.0), wait=(t,), inc="ve")
    t = p.emit("vector", lambda e: e.tensor_scalar_max(pA, pA, -20.0), wait=(t,), inc="ve")
    t = p.emit("scalar", lambda e: e.activation(pA, pA, AF.Exp), wait=(t,), inc="se")
    tp_ = p.emit("vector", lambda e: e.tensor_mul(pA, pA, pB), wait=(t, ttw), inc="ve")
    tone = p.emit("vector", lambda e: e.memset(W1[0:32, :], 1.0), inc="ve")
    t = p.emit("vector", lambda e: e.tensor_tensor_scan(pB, W1[0:32, :], pA, 0.0,
                                                        ALU.mult, ALU.add),
               wait=(tp_, tone, ttw), inc="ve", drain=True)
    t = p.emit("vector", lambda e: e.tensor_scalar_max(pB, pB, 1e-4), wait=(t,), inc="ve",
               drain=True)
    t = p.emit("vector", lambda e: e.reciprocal(pB, pB), wait=(t,), inc="ve", drain=True)
    tinv = p.emit("vector", lambda e: e.tensor_copy(invbf, pB), wait=(t,), inc="ve")
    tpbf = p.emit("vector", lambda e: e.tensor_copy(pbf, pA), wait=(tp_,), inc="ve")

    tnre = p.dma("sync", CQ[:, 15:16], c.din("l0a_nsc_re"), sem="dL")
    tnim = p.dma("sync", CQ[:, 16:17], c.din("l0a_nsc_im"), sem="dL")
    twr1 = p.dma("sync", W2[0:64, 0:64].bitcast(F32) if False else W2[0:64, 0:64], c.din("l0a_wre"), sem="dL")
    twr2 = p.dma("sync", W2[64:128, 0:64], c.din("l0a_wre"), sem="dL")
    twi1 = p.dma("sync", W3[0:64, 0:64], c.din("l0a_wim"), sem="dL")
    twi2 = p.dma("sync", W3[64:128, 0:64], c.din("l0a_wim"), sem="dL")
    t = p.emit("vector", lambda e: e.tensor_scalar(W2[:, 0:64], W2[:, 0:64],
                                                   CQ[:, 15:16], None, ALU.mult),
               wait=(twr2, tnre), inc="ve")
    twre = p.emit("vector", lambda e: e.tensor_copy(wre2, W2[:, 0:64]), wait=(t,), inc="ve")
    t = p.emit("vector", lambda e: e.tensor_scalar(W3[:, 0:64], W3[:, 0:64],
                                                   CQ[:, 16:17], None, ALU.mult),
               wait=(twi2, tnim), inc="ve")
    twim = p.emit("vector", lambda e: e.tensor_copy(wim2, W3[:, 0:64]), wait=(t,), inc="ve")
    p.barrier()

    opj = c.din("l0a_outproj")
    tw_g = [None, None]
    for half in range(2):
        tyg_done = [None] * 8
        for hpi in range(8):
            hp = half * 8 + hpi
            gch = 2048 + 128 * hp
            vch = 128 * hp
            gd = {}
            tck1 = p.dma("sync", CQ[:, 4:8],
                         c.din("l0a_convk")[gch:gch + 128, :],
                         wait=(tw_g[0],), sem="dLck")
            tcb1 = p.dma("sync", CQ[:, 8:9],
                         c.din("l0a_convb")[gch:gch + 128, :],
                         wait=(tck1,), sem="dLck")
            tz = emit_inproj_chunk(c, zin, xn, gch, wst[0], "dWip0", wait=(), rows=128)
            tcv = emit_conv4(c, W1, zin, CQ[:, 4:8], CQ[:, 8:9],
                             wait=(tz, tck1, tcb1), rows=128)
            tsg = p.emit("scalar", lambda e: e.activation(W2, W1, AF.Sigmoid),
                         wait=(tcv,), inc="se")
            tyg = p.emit("vector", lambda e, hpi=hpi: e.tensor_mul(yg[hpi], W1, W2),
                         wait=(tsg, tyg_done[hpi]), inc="ve")
            tck2 = p.dma("sync", CQ[:, 4:8],
                         c.din("l0a_convk")[vch:vch + 128, :],
                         wait=(tcv,), sem="dLck")
            tcb2 = p.dma("sync", CQ[:, 8:9],
                         c.din("l0a_convb")[vch:vch + 128, :],
                         wait=(tck2,), sem="dLck")
            tth = p.dma("sync", CQ[:, 9:10],
                        c.din("l0a_theta")[vch:vch + 128, :], wait=(tss,), sem="dLck")
            tz2 = emit_inproj_chunk(c, zin, xn, vch, wst[1], "dWip1", wait=(tcv,), rows=128)
            tcv2 = emit_conv4(c, xval, zin, CQ[:, 4:8], CQ[:, 8:9],
                              wait=(tz2, tck2, tcb2, tyg), rows=128)
            t = p.emit("vector", lambda e: e.tensor_scalar(W1, xval, CQ[:, 9:10], None,
                                                           ALU.mult), wait=(tcv2, tth, tsg), inc="ve")
            t = p.emit("vector", lambda e: e.tensor_scalar_mul(W2, W1, 1.0 / (2 * np.pi)),
                       wait=(t,), inc="ve")
            ti = p.emit("vector", lambda e: e.tensor_copy(W3.bitcast(I32), W2), wait=(t,),
                        inc="ve", drain=True)
            t = p.emit("vector", lambda e: e.tensor_copy(W1, W3.bitcast(I32)), wait=(ti,),
                       inc="ve", drain=True)
            t = p.emit("vector", lambda e: e.tensor_sub(W2, W2, W1), wait=(t,), inc="ve")
            t = p.emit("vector", lambda e: e.tensor_scalar_mul(W2, W2, 2 * np.pi),
                       wait=(t,), inc="ve")
            t = p.emit("vector", lambda e: e.tensor_scalar(W1, W2, np.pi, None, ALU.is_gt),
                       wait=(t,), inc="ve")
            t = p.emit("vector", lambda e: e.tensor_scalar_mul(W1, W1, -2 * np.pi),
                       wait=(t,), inc="ve")
            t = p.emit("vector", lambda e: e.tensor_add(W2, W2, W1), wait=(t,), inc="ve")
            t = p.emit("vector", lambda e: e.tensor_scalar(W1, W2, -np.pi, None, ALU.is_lt),
                       wait=(t,), inc="ve")
            t = p.emit("vector", lambda e: e.tensor_scalar_mul(W1, W1, 2 * np.pi),
                       wait=(t,), inc="ve")
            t = p.emit("vector", lambda e: e.tensor_add(W2, W2, W1), wait=(t,), inc="ve")
            tsn = p.emit("scalar", lambda e: e.activation(W3, W2, AF.Sin), wait=(t,), inc="se")
            t2 = p.emit("vector", lambda e: e.tensor_scalar_add(W1, W2, np.pi / 2),
                        wait=(tsn,), inc="ve")
            t2 = p.emit("vector", lambda e: e.tensor_scalar(W2, W1, np.pi, None, ALU.is_gt),
                        wait=(t2,), inc="ve")
            t2 = p.emit("vector", lambda e: e.tensor_scalar_mul(W2, W2, -2 * np.pi),
                        wait=(t2,), inc="ve")
            t2 = p.emit("vector", lambda e: e.tensor_add(W1, W1, W2), wait=(t2,), inc="ve")
            tcs = p.emit("scalar", lambda e: e.activation(W2, W1, AF.Sin), wait=(t2,), inc="se")
            tq = p.emit("scalar", lambda e: e.activation(W1, xval, AF.Square),
                        wait=(t2, tcs), inc="se")
            t3 = p.emit("vector", lambda e: e.tensor_scalar(xval, xval, CQ[:, 11:12],
                                                            CQ[:, 10:11], ALU.mult, ALU.add),
                        wait=(tq, twcols), inc="ve", drain=True)
            t3 = p.emit("vector", lambda e: e.scalar_tensor_tensor(
                xval, W1, CQ[:, 12:13], xval, ALU.mult, ALU.add), wait=(t3,), inc="ve")
            t4 = p.emit("vector", lambda e: e.tensor_mul(W2, W2, xval), wait=(t3, tcs), inc="ve")
            t5 = p.emit("vector", lambda e: e.tensor_mul(W3, W3, xval), wait=(t3, tsn), inc="ve")
            tlast = None
            for g in range(4):
                s = slice(g * PB, (g + 1) * PB)
                tb = p.emit("tensor", lambda e, s=s, hp=hp: e.matmul(
                    c.p4[:, 0:PB], replc[0:32, 128 * hp:128 * (hp + 1)], pbf[:, s],
                    start=True, stop=True), wait=(trepl, tpbf, t4, t5, gd.get("p4b")),
                    inc="pe")
                t6 = p.emit("vector", lambda e, s=s: e.tensor_mul(W2[:, s], W2[:, s],
                                                                 c.p4[:, 0:PB]),
                            wait=(tb,), inc="ve")
                tlast = p.emit("vector", lambda e, s=s: e.tensor_mul(W3[:, s], W3[:, s],
                                                                    c.p4[:, 0:PB]),
                               wait=(tb, t6), inc="ve")
                gd["p4b"] = tlast
            tone2 = p.emit("vector", lambda e: e.memset(W1, 1.0), wait=(tlast,), inc="ve")
            tsc1 = p.emit("vector", lambda e: e.tensor_tensor_scan(xval, W1, W2, 0.0,
                                                                   ALU.mult, ALU.add),
                          wait=(tone2,), inc="ve", drain=True)
            tsc2 = p.emit("vector", lambda e: e.tensor_tensor_scan(W2, W1, W3, 0.0,
                                                                   ALU.mult, ALU.add),
                          wait=(tsc1,), inc="ve", drain=True)
            tl2 = None
            for g in range(4):
                s = slice(g * PB, (g + 1) * PB)
                tb = p.emit("tensor", lambda e, s=s, hp=hp: e.matmul(
                    c.p4[:, 0:PB], replc[0:32, 128 * hp:128 * (hp + 1)], invbf[:, s],
                    start=True, stop=True), wait=(tinv, tsc2, gd.get("p4c")), inc="pe")
                t6 = p.emit("vector", lambda e, s=s: e.tensor_mul(xval[:, s], xval[:, s],
                                                                 c.p4[:, 0:PB]),
                            wait=(tb, tsc1), inc="ve")
                tl2 = p.emit("vector", lambda e, s=s: e.tensor_mul(W2[:, s], W2[:, s],
                                                                  c.p4[:, 0:PB]),
                             wait=(tb, t6), inc="ve")
                gd["p4c"] = tl2
            tprev_sq = tl2
            for g in range(4):
                s = slice(g * PB, (g + 1) * PB)
                tq1 = p.emit("scalar", lambda e, s=s: e.activation(W1[:, 0:PB], xval[:, s],
                                                                   AF.Square),
                             wait=(tl2, gd.get("W1s"),), inc="se")
                tm1 = p.emit("tensor", lambda e: e.matmul(c.p5[0:2, 0:PB], c.headself[:],
                                                          W1[:, 0:PB], start=True, stop=False),
                             wait=(tq1, gd.get("p5g")), inc="pe")
                tq2 = p.emit("scalar", lambda e, s=s: e.activation(W1[:, 0:PB], W2[:, s],
                                                                   AF.Square),
                             wait=(tm1,), inc="se")
                tm2 = p.emit("tensor", lambda e: e.matmul(c.p5[0:2, 0:PB], c.headself[:],
                                                          W1[:, 0:PB], start=False, stop=True),
                             wait=(tq2,), inc="pe")
                gd["W1s"] = tm2
                tr = p.emit("vector", lambda e: e.tensor_scalar_mul(
                    row_a(c)[0:2, :], c.p5[0:2, 0:PB], 1.0 / 128),
                    wait=(tm2, gd.get("rowa")), inc="ve")
                tr = p.emit("vector", lambda e: e.tensor_scalar(
                    row_a(c)[0:2, :], row_a(c)[0:2, :], CQ[0:2, 13:14], None, ALU.add),
                    wait=(tr,), inc="ve", drain=True)
                gd["p5g"] = tr
                tr = p.emit("scalar", lambda e: e.activation(row_b(c)[0:2, :], row_a(c)[0:2, :],
                                                             AF.Ln), wait=(tr,), inc="se")
                tr = p.emit("scalar", lambda e: e.activation(row_a(c)[0:2, :], row_b(c)[0:2, :],
                                                             AF.Exp, scale=-0.5),
                            wait=(tr,), inc="se", drain=True)
                gd["rowa"] = tr
                tbs = p.emit("tensor", lambda e: e.matmul(c.p6[:, 0:PB], c.headbcf[:],
                                                          row_a(c)[0:2, :], start=True, stop=True),
                             wait=(tr, gd.get("p6g")), inc="pe")
                tn1 = p.emit("vector", lambda e, s=s: e.tensor_mul(scrb[0], xval[:, s],
                                                                  c.p6[:, 0:PB]),
                             wait=(tbs, gd.get("sb0")), inc="ve")
                tn2 = p.emit("vector", lambda e, s=s: e.tensor_mul(scrb[1], W2[:, s],
                                                                  c.p6[:, 0:PB]),
                             wait=(tbs, tn1, gd.get("sb1")), inc="ve")
                gd["p6g"] = tn2
                tyy = None
                for off in (0, 64):
                    tyy = p.emit("tensor", lambda e, off=off: e.matmul(
                        c.p7[off:off + 64, 0:PB], wre2[off:off + 64, :],
                        scrb[0][off:off + 64, :], start=True, stop=False),
                        wait=(twre, tn1, gd.get("p7g")), inc="pe")
                    tyy = p.emit("tensor", lambda e, off=off: e.matmul(
                        c.p7[off:off + 64, 0:PB], wim2[off:off + 64, :],
                        scrb[1][off:off + 64, :], start=False, stop=True),
                        wait=(twim, tn2, tyy), inc="pe")
                gd["sb0"] = tyy
                gd["sb1"] = tyy
                tyg2 = p.emit("vector", lambda e, s=s, hpi=hpi: e.tensor_mul(
                    yg[hpi][:, s], yg[hpi][:, s], c.p7[:, 0:PB]),
                    wait=(tyy, tyg), inc="ve")
                gd["p7g"] = tyg2
                tyg_done[hpi] = tyg2
            p.barrier()
        p.barrier()
        tpx = {}
        for oc in range(8):
            pa = oc % 2
            tw_g[pa] = p.dma("gpsimd", wst[pa].rearrange("p (i q) -> p i q", i=8),
                             opj[1024 * half:1024 * (half + 1),
                                 128 * oc:128 * (oc + 1)].rearrange("(i p) q -> p i q", p=128),
                             wait=(tpx.get("w%d" % pa),), sem=f"dWop{pa}")
            for g in range(4):
                s = slice(g * PB, (g + 1) * PB)
                tm = None
                for ic in range(8):
                    tm = p.emit("tensor", lambda e, ic=ic, s=s, pa=pa: e.matmul(
                        c.p4[:, 0:PB], wst[pa][:, 128 * ic:128 * (ic + 1)], yg[ic][:, s],
                        start=(ic == 0), stop=(ic == 7)),
                        wait=(tw_g[pa], tpx.get("p4")), inc="pe")
                tpx["w%d" % pa] = tm
                tX = p.emit("vector", lambda e, oc=oc, s=s: e.tensor_add(
                    c.X[oc][:, s], c.X[oc][:, s], c.p4[:, 0:PB]),
                    wait=(tm, tpx.get("X")), inc="ve")
                tpx["p4"] = tX
                tpx["X"] = tX
        p.barrier()
    c.ar.pop()


def row_a(c):
    return c.ftmp[0][0:2, 0:PB]


def row_b(c):
    return c.ftmp[1][0:2, 0:PB]


def emit_inproj_chunk(c, dst_zin, xn, col0, wbuf, sem, wait=(), rows=128):
    p = c.prog
    inproj = c.din("l0a_inproj")
    tw = p.dma("gpsimd", wbuf[:, 0:8 * rows].rearrange("p (i q) -> p i q", i=8),
               inproj[:, col0:col0 + rows].rearrange("(i p) q -> p i q", p=128),
               wait=wait, sem=sem)
    tzp = p.emit("vector", lambda e: e.memset(dst_zin[0:rows, 0:3], 0.0), wait=wait, inc="ve")
    tz = None
    tprev = None
    for g in range(4):
        s = slice(g * PB, (g + 1) * PB)
        tm = None
        for ic in range(8):
            tm = p.emit("tensor", lambda e, ic=ic, s=s: e.matmul(
                c.p4[0:rows, 0:PB], wbuf[:, rows * ic:rows * (ic + 1)], xn[ic][:, s],
                start=(ic == 0), stop=(ic == 7)), wait=(tw, tprev), inc="pe")
        tz = p.emit("scalar", lambda e, s=s: e.copy(
            dst_zin[0:rows, 3 + s.start:3 + s.stop], c.p4[0:rows, 0:PB]),
            wait=(tm,), inc="se")
        tprev = tz
    p.emit("vector", None, wait=(tzp,))
    return tz


def emit_conv4(c, out, zin, taps, bias, wait=(), rows=128):
    p = c.prog
    ra = out[0:rows, :] if out.shape[0] != rows else out
    t = p.emit("vector", lambda e: e.tensor_scalar(ra, zin[0:rows, 0:T], taps[0:rows, 0:1],
                                                   None, ALU.mult), wait=wait, inc="ve")
    for tau in range(1, 4):
        t = p.emit("vector", lambda e, tau=tau: e.scalar_tensor_tensor(
            ra, zin[0:rows, tau:T + tau], taps[0:rows, tau:tau + 1], ra,
            ALU.mult, ALU.add), wait=(t,), inc="ve", drain=True)
    t = p.emit("vector", lambda e: e.tensor_scalar(ra, ra, bias[0:rows, 0:1], None,
                                                   ALU.add), wait=(t,), inc="ve", drain=True)
    return t


def phase_l0attn(c):
    emit_l0attn(c)



def emit_l1ffn(c, tag):
    p = c.prog
    nc = c.nc
    sfx = tag
    c.ar.push()
    st = alloc_stats(c)
    xn = [c.ar.take(T, BF16, "f_xn") for _ in range(8)]
    emit_layernorm(c, xn, st)
    p.barrier()
    colsT = [c.ar.take(T, BF16, "f_cols") for _ in range(8)]
    h1all = c.ar.take(4 * T, BF16, "f_h1")
    h1 = [h1all[:, T * i:T * (i + 1)] for i in range(4)]
    cembb = c.ar.take(256, BF16, "f_cemb")
    wTs = c.ar.take(128, BF16, "f_wTs")
    wTrow = c.ar.take(PB, BF16, "f_wTrow")
    smallv = c.ar.take(16, F32, "f_smallv")
    CA = c.ar.take(64, F32, "f_CA")
    wst = [c.ar.take(1024, BF16, "f_ws0"), c.ar.take(1024, BF16, "f_ws1")]
    yf0 = h1all.bitcast(F32)
    yf = [yf0[:, 0:T], yf0[:, T:2 * T]]
    sq1 = c.ar.take(PB, BF16, "f_sq")
    sq2 = c.ar.take(PB, BF16, "f_sq2")
    wtile = c.ar.take(16, BF16, "f_wtile")
    rowbf = c.ar.take(T, BF16, "f_rowbf")
    rrow = c.ar.take(PB, F32, "f_rrow")

    def col_u2(uc):
        return colsT[uc // 4][:, (uc % 4) * PB:(uc % 4 + 1) * PB]

    def y_ap(oc):
        return yf[oc // 4][:, (oc % 4) * PB:(oc % 4 + 1) * PB]

    tbc = p.dma("sync", CA[:, 0:32], c.din("l1f_bcols"), sem="dL")
    tpb = p.dma("sync", CA[:, 32:40], c.din("l1f_projb"), sem="dL")
    tlg = p.dma("sync", CA[:, 40:48], c.din("l1f_lng"), sem="dL")
    tlb = p.dma("sync", CA[:, 48:56], c.din("l1f_lnb"), sem="dL")
    tg1 = p.dma("sync", CA[:, 56:60], c.din("l1f_gb1"), sem="dL")
    te24 = p.emit("vector", lambda e: e.memset(CA[:, 60:61], 1e-24), inc="ve")

    tce = p.dma("gpsimd", cembb[:, 0:128].rearrange("p (i q) -> p i q", i=8),
                c.din("l1f_colembT").rearrange("(i p) q -> p i q", p=128), sem="dC" + sfx)
    tgw2 = p.dma("gpsimd", cembb[:, 128:192].rearrange("p (i q) -> p i q", i=4),
                 c.din("l1f_gw2").rearrange("(i p) q -> p i q", p=128), sem="dC" + sfx)
    tsq = None
    tss = None
    for ic in range(8):
        tss = p.emit("scalar", lambda e, ic=ic: e.activation(
            sq1[:, 0:16], cembb[:, 16 * ic:16 * (ic + 1)], AF.Square),
            wait=(tce, tsq), inc="se")
        tsq = p.emit("tensor", lambda e, ic=ic: e.matmul(
            c.p6[0:1, 0:16], c.onescol[:], sq1[:, 0:16],
            start=(ic == 0), stop=(ic == 7)), wait=(tss,), inc="pe")
    r1 = c.ftmp[0][0:1, 0:16]
    t = p.emit("scalar", lambda e: e.activation(r1, c.p6[0:1, 0:16], AF.Ln,
                                                bias=CA[0:1, 60:61]), wait=(tsq, te24), inc="se")
    t = p.emit("scalar", lambda e: e.activation(r1, r1, AF.Exp, scale=-0.5),
               wait=(t,), inc="se", drain=True)
    tinvc = p.emit("tensor", lambda e: e.matmul(c.p7[:, 0:16], c.onesrowf[:], r1,
                                                start=True, stop=True), wait=(t,), inc="pe")
    invcb = c.ftmp[1][:, 0:16]
    ticb = p.emit("vector", lambda e: e.tensor_copy(invcb, c.p7[:, 0:16]),
                  wait=(tinvc,), inc="ve")

    tms = None
    tsqg = None
    for ch in range(8):
        for g in range(4):
            s = slice(g * PB, (g + 1) * PB)
            ts_ = p.emit("scalar", lambda e, ch=ch, s=s: e.activation(
                sq1, xn[ch][:, s], AF.Square), wait=(tsqg, tss), inc="se")
            tms = p.emit("tensor", lambda e, g=g, ch=ch: e.matmul(
                c.pS[0:1, g * PB:(g + 1) * PB], c.onescol[:], sq1,
                start=(ch == 0), stop=(ch == 7)), wait=(ts_,), inc="pe")
            tsqg = tms
    t = p.emit("scalar", lambda e: e.activation(rowbf[0:1, :], c.pS[0:1, 0:4 * PB], AF.Ln,
                                                bias=CA[0:1, 60:61]), wait=(tms, te24), inc="se")
    tinvn = p.emit("scalar", lambda e: e.activation(rowbf[0:1, :], rowbf[0:1, :], AF.Exp,
                                                    scale=-0.5), wait=(t,), inc="se", drain=True)

    tw = [None, None]
    twg = [None, None]
    th1 = None
    for hc in range(4):
        pa = hc % 2
        tw[pa] = p.dma("gpsimd", wst[pa].rearrange("p (i q) -> p i q", i=8),
                       c.din("l1f_gw1")[:, 128 * hc:128 * (hc + 1)].rearrange(
                           "(i p) q -> p i q", p=128), wait=(twg[pa],), sem=f"dG{pa}{sfx}")
        for g in range(4):
            s = slice(g * PB, (g + 1) * PB)
            tm = None
            for ic in range(8):
                tm = p.emit("tensor", lambda e, ic=ic, s=s, pa=pa: e.matmul(
                    c.p4[:, 0:PB], wst[pa][:, 128 * ic:128 * (ic + 1)], xn[ic][:, s],
                    start=(ic == 0), stop=(ic == 7)), wait=(tw[pa], th1), inc="pe")
            th1 = p.emit("scalar", lambda e, hc=hc, s=s: e.activation(
                h1[hc][:, s], c.p4[:, 0:PB], AF.Gelu, bias=CA[:, 56 + hc:57 + hc]),
                wait=(tm, tg1), inc="se")
        twg[pa] = th1
    p.barrier()

    tgb2 = p.dma("sync", c.ftmp[0][0:1, 16:32], c.din("l1f_gb2"), sem="dL")
    tb2 = p.emit("tensor", lambda e: e.matmul(c.p7[:, 0:16], c.onesrowf[:],
                                              c.ftmp[0][0:1, 16:32], start=True, stop=True),
                 wait=(tgb2,), inc="pe")
    gb2b = c.ftmp[0][:, 32:48]
    tgb2b = p.emit("vector", lambda e: e.tensor_copy(gb2b, c.p7[:, 0:16]),
                   wait=(tb2,), inc="ve")
    gd = {}
    t15 = None
    for tt in range(16):
        tm1 = None
        for ic in range(8):
            tm1 = p.emit("tensor", lambda e, ic=ic, tt=tt: e.matmul(
                c.p4[:, 0:16], xn[ic][:, 128 * tt:128 * (tt + 1)],
                cembb[:, 16 * ic:16 * (ic + 1)], start=(ic == 0), stop=(ic == 7)),
                wait=(tce, gd.get("p4"),), inc="pe")
        tm2 = None
        for hc in range(4):
            tm2 = p.emit("tensor", lambda e, hc=hc, tt=tt: e.matmul(
                c.p5[:, 0:16], h1[hc][:, 128 * tt:128 * (tt + 1)],
                cembb[:, 128 + 16 * hc:128 + 16 * (hc + 1)],
                start=(hc == 0), stop=(hc == 3)), wait=(tgw2, th1, gd.get("p5"),), inc="pe")
        ttr = p.emit("tensor", lambda e, tt=tt: e.matmul(
            c.p6[0:128, 0:1], rowbf[0:1, 128 * tt:128 * (tt + 1)], c.onescol[0:1, 0:1],
            start=True, stop=True), wait=(tinvn, gd.get("p6"),), inc="pe")
        tcol = p.emit("vector", lambda e: e.tensor_copy(smallv[:, 0:1], c.p6[0:128, 0:1]),
                      wait=(ttr, gd.get("sv"),), inc="ve")
        f1 = c.ftmp[1][:, 16:32]
        f2 = c.ftmp[1][:, 32:48]
        t1 = p.emit("vector", lambda e: e.tensor_scalar(f1, c.p4[:, 0:16], smallv[:, 0:1],
                                                        None, ALU.mult),
                    wait=(tm1, tcol, gd.get("f1"),), inc="ve")
        gd["p4"] = t1
        t2 = p.emit("vector", lambda e: e.tensor_mul(f1, f1, invcb), wait=(t1, ticb), inc="ve")
        t3 = p.emit("vector", lambda e: e.tensor_add(f2, c.p5[:, 0:16], gb2b),
                    wait=(tm2, tgb2b, gd.get("f2"),), inc="ve")
        gd["p5"] = t3
        t4 = p.emit("scalar", lambda e: e.activation(f2, f2, AF.Sigmoid), wait=(t3,), inc="se")
        t5 = p.emit("vector", lambda e: e.tensor_add(f1, f1, f2), wait=(t2, t4), inc="ve")
        t6 = p.emit("vector", lambda e: e.reduce_max(smallv[:, 1:2], f1, axis=AX,
                                                     negate=True), wait=(t5,), inc="ve",
                    drain=True)
        t7 = p.emit("scalar", lambda e: e.activation(f2, f1, AF.Exp, bias=smallv[:, 1:2]),
                    wait=(t6,), inc="se")
        t8 = p.emit("vector", lambda e: e.tensor_scalar_mul(smallv[:, 2:3], smallv[:, 1:2],
                                                            -1.0), wait=(t7,), inc="ve",
                    drain=True)
        t9 = p.emit("vector", lambda e: e.tensor_scalar(f1, f1, smallv[:, 2:3], None,
                                                        ALU.is_ge), wait=(t8,), inc="ve")
        t10 = p.emit("vector", lambda e: e.reduce_sum(smallv[:, 3:4], f2, axis=AX),
                     wait=(t7, t9), inc="ve", drain=True)
        t11 = p.emit("vector", lambda e: e.reciprocal(smallv[:, 4:5], smallv[:, 3:4]),
                     wait=(t10,), inc="ve", drain=True)
        t12 = p.emit("vector", lambda e: e.tensor_mul(f2, f2, f1), wait=(t9, t11), inc="ve")
        t13 = p.emit("vector", lambda e: e.tensor_scalar(wtile, f2, smallv[:, 4:5], None,
                                                         ALU.mult), wait=(t12,), inc="ve")
        gd["f1"] = t13
        gd["f2"] = t13
        gd["sv"] = t13
        t14 = p.emit("tensor", lambda e, tt=tt: e.matmul(
            c.p6[0:16, 0:128], wtile, c.ident[:], start=True, stop=True),
            wait=(t13, ttr), inc="pe")
        t15 = p.emit("vector", lambda e: e.tensor_copy(
            wTs[0:16, 0:128], c.p6[0:16, 0:128]),
            wait=(t14, gd.get("wTg"),), inc="ve")
        gd["p6"] = t15
        gd["wTg"] = p.dma("sync", c.scratch("wTd", (16, T), BF16)[:, 128 * tt:128 * (tt + 1)],
                          wTs[0:16, 0:128], wait=(t15,), sem="dWT")
    twT = gd["wTg"]
    p.barrier()

    Wr = c.din("l1f_Wr")
    pw = c.din("l1f_projw")
    wring = wst + [c.ar.take(1024, BF16, "f_ws2"), c.ar.take(1024, BF16, "f_ws3")]
    wloads = []
    for tg in range(4):
        for uc in range(32):
            wloads.append(("wr", uc, 0))
        for oc in range(8):
            for ug4 in range(4):
                wloads.append(("pj", oc, ug4))
    twld = {}
    twread = {}
    wnxt = [0]

    def w_load():
        k = wnxt[0]
        wnxt[0] += 1
        sl = k % 4
        d = wloads[k]
        if d[0] == "wr":
            src = Wr[:, 128 * d[1]:128 * (d[1] + 1)].rearrange("(i p) q -> p i q", p=128)
        else:
            src = pw[1024 * d[2]:1024 * (d[2] + 1),
                     128 * d[1]:128 * (d[1] + 1)].rearrange("(u p) q -> p u q", p=128)
        twld[k] = p.dma("gpsimd", wring[sl].rearrange("p (i q) -> p i q", i=8), src,
                        wait=(twread.get(k - 4),), sem=f"dR{sl}{sfx}")

    for _ in range(3):
        w_load()
    wtrows = c.ar.take(4096, BF16, "f_wtr")
    tslotr = {}

    def wt_load(cc, s):
        sl8 = cc % 8
        return p.dma("sync", wtrows[0:1, 512 * sl8:512 * sl8 + 512],
                     c.scratch("wTd", (16, T), BF16)[cc:cc + 1, s],
                     wait=(twT, tslotr.get(sl8)), sem="dWT%d" % (cc % 4))

    for tg in range(4):
        s = slice(tg * PB, (tg + 1) * PB)
        gm = {}
        twl = {}
        for cc in range(8):
            twl[cc] = wt_load(cc, s)
        for uc in range(32):
            k = tg * 64 + uc
            sl = k % 4
            bi = uc % 4
            bk = c.pb[bi]
            tm = None
            for ic in range(8):
                tm = p.emit("tensor", lambda e, ic=ic, sl=sl, s=s, bk=bk: e.matmul(
                    bk, wring[sl][:, 128 * ic:128 * (ic + 1)], xn[ic][:, s],
                    start=(ic == 0), stop=(ic == 7)),
                    wait=(twld[k], gm.get("b%d" % bi),), inc="pe")
            twread[k] = tm
            if wnxt[0] < len(wloads):
                w_load()
            tcl = p.emit("scalar", lambda e, uc=uc, bk=bk: e.activation(
                col_u2(uc), bk, AF.Gelu, bias=CA[:, uc:uc + 1]),
                wait=(tm, tbc), inc="se")
            gm["b%d" % bi] = tcl
            gm["c%d" % uc] = tcl
        tww = None
        for cc in range(16):
            sl8 = cc % 8
            tb = p.emit("tensor", lambda e, sl8=sl8: e.matmul(
                c.p6[:, 0:PB], c.onesrow[:], wtrows[0:1, 512 * sl8:512 * sl8 + 512],
                start=True, stop=True),
                wait=(twl[cc], gm.get("p6w"),), inc="pe")
            tslotr[sl8] = tb
            if cc < 8:
                twl[cc + 8] = wt_load(cc + 8, s)
            for du in range(2):
                uc = 2 * cc + du
                tww = p.emit("vector", lambda e, uc=uc: e.tensor_mul(
                    col_u2(uc), col_u2(uc), c.p6[:, 0:PB]),
                    wait=(tb, gm.get("c%d" % uc),), inc="ve")
                gm["c%d" % uc] = tww
            gm["p6w"] = tww
        tb1 = p.emit("tensor", lambda e, s=s: e.matmul(
            c.p6[:, 0:PB], c.onesrowf[:], st["nmu"][0:1, s], start=True, stop=True),
            wait=(tww,), inc="pe")
        tb2 = p.emit("tensor", lambda e, s=s: e.matmul(
            c.p7[:, 0:PB], c.onesrowf[:], st["rstd"][0:1, s], start=True, stop=True),
            wait=(tinvc, tb1), inc="pe")
        tyl = None
        for oc in range(8):
            pa = oc % 2
            bk = c.p4 if pa == 0 else c.p5
            tpj = None
            for ug4 in range(4):
                k = tg * 64 + 32 + oc * 4 + ug4
                sl = k % 4
                for du in range(8):
                    uc = ug4 * 8 + du
                    tpj = p.emit("tensor", lambda e, du=du, sl=sl, bk=bk, uc=uc: e.matmul(
                        bk[:, 0:PB], wring[sl][:, 128 * du:128 * (du + 1)], col_u2(uc),
                        start=(uc == 0), stop=(uc == 31)),
                        wait=(twld[k], tww, gm.get("pj%d" % pa),), inc="pe")
                twread[k] = tpj
                if wnxt[0] < len(wloads):
                    w_load()
            t1 = p.emit("vector", lambda e, oc=oc, s=s: e.tensor_add(
                y_ap(oc), c.X[oc][:, s], c.p6[:, 0:PB]), wait=(tb1, gm.get("y%d" % oc),),
                inc="ve")
            t2 = p.emit("vector", lambda e, oc=oc: e.tensor_mul(
                y_ap(oc), y_ap(oc), c.p7[:, 0:PB]), wait=(tb2, t1), inc="ve")
            t3 = p.emit("vector", lambda e, oc=oc: e.tensor_scalar(
                y_ap(oc), y_ap(oc), CA[:, 32 + oc:33 + oc], None, ALU.add),
                wait=(t2, tpb), inc="ve")
            tyl = p.emit("vector", lambda e, oc=oc, bk=bk: e.tensor_add(
                y_ap(oc), y_ap(oc), bk[:, 0:PB]), wait=(t3, tpj), inc="ve")
            gm["pj%d" % pa] = tyl
            gm["y%d" % oc] = tyl
        tst = None
        for oc in range(8):
            tyb = p.emit("vector", lambda e, oc=oc: e.tensor_copy(sq1, y_ap(oc)),
                         wait=(gm.get("y%d" % oc), gm.get("sq1"),), inc="ve")
            tsq2 = p.emit("scalar", lambda e: e.activation(sq2, sq1, AF.Square),
                          wait=(tyb, gm.get("sq2"),), inc="se")
            p.emit("tensor", lambda e, oc=oc: e.matmul(
                c.pS[0:1, 0:PB], c.onescol[:], sq1, start=(oc == 0), stop=(oc == 7)),
                wait=(tyb, gm.get("st1"),), inc="pe")
            tst = p.emit("tensor", lambda e, oc=oc: e.matmul(
                c.pS[0:1, PB:2 * PB], c.onescol[:], sq2,
                start=(oc == 0), stop=(oc == 7)), wait=(tsq2,), inc="pe")
            gm["sq1"] = tst
            gm["sq2"] = tst
        gm["st1"] = tst
        ra = c.ftmp[1][0:1, 0:PB]
        rb = c.ftmp[0][0:1, 0:PB]
        rc = rrow[0:1, 0:PB]
        t1 = p.emit("vector", lambda e: e.tensor_scalar_mul(ra, c.pS[0:1, 0:PB], 1.0 / D),
                    wait=(tst,), inc="ve", drain=True)
        t2 = p.emit("vector", lambda e: e.tensor_scalar_mul(rb, c.pS[0:1, PB:2 * PB], 1.0 / D),
                    wait=(tst, t1), inc="ve", drain=True)
        t3 = p.emit("scalar", lambda e: e.activation(rc, ra, AF.Square),
                    wait=(t1, t2), inc="se")
        t4 = p.emit("vector", lambda e: e.tensor_sub(rb, rb, rc), wait=(t3,), inc="ve",
                    drain=True)
        t5 = p.emit("scalar", lambda e: e.activation(rb, rb, AF.Ln, bias=c.epscol[0:1, 0:1]),
                    wait=(t4,), inc="se", drain=True)
        t6 = p.emit("scalar", lambda e: e.activation(rb, rb, AF.Exp, scale=-0.5),
                    wait=(t5,), inc="se", drain=True)
        t7 = p.emit("vector", lambda e: e.tensor_scalar_mul(ra, ra, -1.0), wait=(t6,),
                    inc="ve", drain=True)
        tb3 = p.emit("tensor", lambda e: e.matmul(c.p6[:, 0:PB], c.onesrowf[:], ra,
                                                  start=True, stop=True),
                     wait=(t7, tb2, tyl), inc="pe")
        tb4 = p.emit("tensor", lambda e: e.matmul(c.p7[:, 0:PB], c.onesrowf[:], rb,
                                                  start=True, stop=True),
                     wait=(t6, tb3), inc="pe")
        for oc in range(8):
            t1 = p.emit("vector", lambda e, oc=oc: e.tensor_add(
                y_ap(oc), y_ap(oc), c.p6[:, 0:PB]), wait=(tb3, gm.get("yo%d" % oc),), inc="ve")
            t2 = p.emit("vector", lambda e, oc=oc: e.tensor_mul(
                y_ap(oc), y_ap(oc), c.p7[:, 0:PB]), wait=(tb4, t1), inc="ve")
            t3 = p.emit("vector", lambda e, oc=oc: e.tensor_scalar(
                y_ap(oc), y_ap(oc), CA[:, 40 + oc:41 + oc], CA[:, 48 + oc:49 + oc],
                ALU.mult, ALU.add), wait=(t2, tlg, tlb), inc="ve")
            p.emit("vector", lambda e, oc=oc, s=s: e.scalar_tensor_tensor(
                c.X[oc][:, s], y_ap(oc), 0.5, c.X[oc][:, s], ALU.mult, ALU.add),
                wait=(t3,), inc="ve")
        p.barrier()
    c.ar.pop()


def phase_l1ffn(c):
    emit_l1ffn(c, "f1")



def emit_l1attn(c):
    p = c.prog
    nc = c.nc
    sfx = "a1"
    c.ar.push()
    xn = [c.ar.take(T, BF16, "b_xn") for _ in range(8)]
    c.ar.push()
    st = alloc_stats(c)
    emit_layernorm(c, xn, st)
    c.ar.pop()
    p.barrier()
    kvn = [c.ar.take(T, BF16, "b_kvn") for _ in range(2)]
    qcT = [c.ar.take(T, BF16, "b_qc") for _ in range(4)]
    krT = [c.ar.take(T, BF16, "b_kr") for _ in range(3)]
    qrT = [c.ar.take(T, BF16, "b_qr") for _ in range(3)]
    WV = 66 * 8
    Vb = c.ar.take(16 * WV, BF16, "b_V")
    sh = [c.ar.take(T, BF16, "b_sh") for _ in range(2)]
    Ob = [c.ar.take(T, BF16, "b_O") for _ in range(4)]
    at2 = c.ar.take(PB, BF16, "b_at2")
    scr = c.ar.take(PB, F32, "b_scr")
    rotf = c.ar.take(128, F32, "b_rotf")
    wstab = c.ar.take(2048, BF16, "b_wsab")
    wst = [wstab[:, 0:1024], wstab[:, 1024:2048]]
    Ab3 = c.ar.take(T, BF16, "b_Ab3")
    colsc = c.ar.take(64, F32, "b_colsc")
    krow = Vb[:, 0:4096].bitcast(F32)

    CQ = c.smallf
    trot = p.dma("sync", rotf, c.const("l1a_rotT_f32"), sem="dL")
    tbl = {}
    tbl["C"] = p.dma("gpsimd", sh[0], c.const("l1a_ropeC"), sem="dTb" + sfx)
    tbl["S"] = p.dma("gpsimd", sh[1], c.const("l1a_ropeS"), sem="dTb" + sfx)
    tgk = p.dma("sync", CQ[:, 30:32], c.din("l1a_g_kv"), sem="dL")
    te6 = p.emit("vector", lambda e: e.memset(colsc[:, 0:1], 1e-6), inc="ve")

    tw = [None, None]
    wguard = [None, None]
    tkv = None
    for oc in range(2):
        pa = oc % 2
        tw[pa] = p.dma("gpsimd", wst[pa].rearrange("p (i q) -> p i q", i=8),
                       c.din("l1a_W_kv")[:, 128 * oc:128 * (oc + 1)].rearrange(
                           "(i p) q -> p i q", p=128), wait=(wguard[pa],), sem=f"dKV{pa}{sfx}")
        for g in range(4):
            s = slice(g * PB, (g + 1) * PB)
            tm = None
            for ic in range(8):
                tm = p.emit("tensor", lambda e, ic=ic, s=s, pa=pa: e.matmul(
                    c.p4[:, 0:PB], wst[pa][:, 128 * ic:128 * (ic + 1)], xn[ic][:, s],
                    start=(ic == 0), stop=(ic == 7)), wait=(tw[pa], tkv), inc="pe")
            tkv = p.emit("scalar", lambda e, oc=oc, s=s: e.copy(kvn[oc][:, s], c.p4[:, 0:PB]),
                         wait=(tm,), inc="se")
            wguard[pa] = tm
    tms = None
    tsg = None
    for oc in range(2):
        for g in range(4):
            s = slice(g * PB, (g + 1) * PB)
            ts_ = p.emit("scalar", lambda e, oc=oc, s=s: e.activation(
                at2, kvn[oc][:, s], AF.Square), wait=(tkv, tsg), inc="se")
            tms = p.emit("tensor", lambda e, g=g, oc=oc: e.matmul(
                c.pS[0:1, g * PB:(g + 1) * PB], c.onescol[:], at2,
                start=(oc == 0), stop=(oc == 1)), wait=(ts_,), inc="pe")
            tsg = tms
    r1 = krow[0:1, 0:T]
    t = p.emit("vector", lambda e: e.tensor_scalar_mul(r1, c.pS[0:1, 0:4 * PB], 1.0 / 256),
               wait=(tms, te6), inc="ve", drain=True)
    t = p.emit("vector", lambda e: e.tensor_scalar(r1, r1, colsc[0:1, 0:1], None, ALU.add),
               wait=(t,), inc="ve", drain=True)
    t = p.emit("scalar", lambda e: e.activation(r1, r1, AF.Ln), wait=(t,), inc="se")
    trs = p.emit("scalar", lambda e: e.activation(r1, r1, AF.Exp, scale=-0.5),
                 wait=(t,), inc="se", drain=True)
    tkn = None
    for g in range(4):
        s = slice(g * PB, (g + 1) * PB)
        tb = p.emit("tensor", lambda e, s=s: e.matmul(
            c.p5[:, 0:PB], c.onesrowf[:], r1[0:1, s], start=True, stop=True),
            wait=(trs, tkn), inc="pe")
        for oc in range(2):
            t1 = p.emit("vector", lambda e, oc=oc, s=s: e.tensor_mul(
                kvn[oc][:, s], kvn[oc][:, s], c.p5[:, 0:PB]), wait=(tb, tkv, tkn), inc="ve")
            tkn = p.emit("vector", lambda e, oc=oc, s=s: e.tensor_scalar(
                kvn[oc][:, s], kvn[oc][:, s], CQ[:, 30 + oc:31 + oc], None, ALU.mult),
                wait=(t1, tgk), inc="ve")

    tqc = None
    for oc in range(4):
        pa = oc % 2
        tw[pa] = p.dma("gpsimd", wst[pa].rearrange("p (i q) -> p i q", i=8),
                       c.din("l1a_W_q")[:, 128 * oc:128 * (oc + 1)].rearrange(
                           "(i p) q -> p i q", p=128), wait=(wguard[pa],), sem=f"dKV{pa}{sfx}")
        for g in range(4):
            s = slice(g * PB, (g + 1) * PB)
            tm = None
            for ic in range(8):
                tm = p.emit("tensor", lambda e, ic=ic, s=s, pa=pa: e.matmul(
                    c.p4[:, 0:PB], wst[pa][:, 128 * ic:128 * (ic + 1)], xn[ic][:, s],
                    start=(ic == 0), stop=(ic == 7)), wait=(tw[pa], tqc), inc="pe")
            tqc = p.emit("scalar", lambda e, oc=oc, s=s: e.copy(qcT[oc][:, s], c.p4[:, 0:PB]),
                         wait=(tm,), inc="se")
            wguard[pa] = tm
    p.barrier()

    krd = c.scratch("krTd", (384, T), BF16)

    def emit_rope(dst3, w_name, src_chunks, nsrc, half, sem):
        tlast = None
        tww = [None, None]
        wgr = [wguard[0], wguard[1]]
        for ch3 in range(3):
            pa = ch3 % 2
            tww[pa] = p.dma("gpsimd", wst[pa][:, 0:nsrc * 128].rearrange(
                "p (i q) -> p i q", i=nsrc),
                c.din(w_name)[:, 384 * half + 128 * ch3:384 * half + 128 * (ch3 + 1)].rearrange(
                    "(i p) q -> p i q", p=128),
                wait=(wgr[pa],), sem=f"dRp{pa}{sfx}")
            for g in range(4):
                s = slice(g * PB, (g + 1) * PB)
                tm = None
                for ic in range(nsrc):
                    tm = p.emit("tensor", lambda e, ic=ic, s=s, pa=pa: e.matmul(
                        c.p4[:, 0:PB], wst[pa][:, 128 * ic:128 * (ic + 1)],
                        src_chunks[ic][:, s], start=(ic == 0), stop=(ic == nsrc - 1)),
                        wait=(tww[pa], tlast), inc="pe")
                wgr[pa] = tm
                wguard[pa] = tm
                t1 = p.emit("scalar", lambda e: e.copy(scr, c.p4[:, 0:PB]),
                            wait=(tm, tlast), inc="se")
                t2 = p.emit("tensor", lambda e: e.matmul(c.p7[:, 0:PB], rotf, scr,
                                                         start=True, stop=True),
                            wait=(t1, trot, tlast), inc="pe")
                t3 = p.emit("vector", lambda e, s=s: e.tensor_mul(
                    c.ftmp[1][:, 0:PB], scr, sh[0][:, s]), wait=(t1, tbl["C"], tlast), inc="ve")
                t4 = p.emit("vector", lambda e, s=s: e.tensor_mul(
                    c.ftmp[0][:, 0:PB], c.p7[:, 0:PB], sh[1][:, s]),
                    wait=(t2, tbl["S"], t3), inc="ve")
                tlast = p.emit("vector", lambda e, ch3=ch3, s=s, dst3=dst3: e.tensor_add(
                    dst3[ch3][:, s], c.ftmp[1][:, 0:PB], c.ftmp[0][:, 0:PB]),
                    wait=(t3, t4), inc="ve")
        return tlast

    tk1 = emit_rope(krT, "l1a_W_krope", xn, 8, 1, "k1")
    tkd = None
    for ch3 in range(3):
        tkd = p.dma("sync", krd[128 * ch3:128 * (ch3 + 1), :], krT[ch3][:, :],
                    wait=(tk1,), sem="dKd")
    for _e in ENGS:
        p.emit(_e, None, wait=(tkd,))
    p.barrier()
    tk0 = emit_rope(krT, "l1a_W_krope", xn, 8, 0, "k0")
    p.barrier()
    tq0 = emit_rope(qrT, "l1a_W_qrope", qcT, 4, 0, "q0")
    p.barrier()

    if DBG_L1A:
        p.barrier()
        for nm, buf in (("kvn0", kvn[0]), ("qc0", qcT[0]), ("kr0", krT[0]),
                        ("qr0", qrT[0])):
            tdmp = p.dma("sync", c.scratch("dbg_" + nm, (128, T), BF16), buf[:, :], sem="dDBG")
        p.emit("sync", None, wait=(tdmp,))
        p.barrier()

    opj = c.din("l1a_W_o")
    scl1 = 1.0 / math.sqrt(96.0)
    for half in range(2):
        if half == 1:
            tkr = None
            for ch3 in range(3):
                tkr = p.dma("gpsimd", krT[ch3][:, :], krd[128 * ch3:128 * (ch3 + 1), :],
                            wait=(tkd,), sem="dKd2")
            p.barrier()
            tbl["C"] = p.dma("gpsimd", sh[0], c.const("l1a_ropeC"), sem="dTb" + sfx)
            tbl["S"] = p.dma("gpsimd", sh[1], c.const("l1a_ropeS"), sem="dTb" + sfx)
            tq1 = emit_rope(qrT, "l1a_W_qrope", qcT, 4, 1, "q1")
            p.emit("tensor", None, wait=(tkr, tq1))
            p.barrier()
            if DBG_L1A:
                tdmp = p.dma("sync", c.scratch("dbg_qrfresh", (128, T), BF16),
                             qrT[0][:, :], sem="dDBG")
                p.emit("sync", None, wait=(tdmp,))
                p.barrier()
        kc = xn[0:4]
        qc2 = xn[4:8]
        tdec = None
        for (dst, wname, srcs, nsrc) in ((kc, "l1a_W_kdec", kvn, 2),
                                         (qc2, "l1a_W_qdec", qcT, 4)):
            for oc in range(4):
                pa = oc % 2
                tw[pa] = p.dma("gpsimd", wst[pa][:, 0:nsrc * 128].rearrange(
                    "p (i q) -> p i q", i=nsrc),
                    c.din(wname)[:, 512 * half + 128 * oc:512 * half + 128 * (oc + 1)].rearrange(
                        "(i p) q -> p i q", p=128),
                    wait=(wguard[pa],), sem=f"dKV{pa}{sfx}")
                for g in range(4):
                    s = slice(g * PB, (g + 1) * PB)
                    tm = None
                    for ic in range(nsrc):
                        tm = p.emit("tensor", lambda e, ic=ic, s=s, pa=pa, srcs=srcs, nsrc=nsrc:
                                    e.matmul(c.p4[:, 0:PB],
                                             wst[pa][:, 128 * ic:128 * (ic + 1)],
                                             srcs[ic][:, s], start=(ic == 0),
                                             stop=(ic == nsrc - 1)),
                                    wait=(tw[pa], tdec), inc="pe")
                    tdec = p.emit("scalar", lambda e, dst=dst, oc=oc, s=s: e.copy(
                        dst[oc][:, s], c.p4[:, 0:PB]), wait=(tm,), inc="se")
                    wguard[pa] = tm
        twv = p.dma("gpsimd", wst[0][:, 0:1024].rearrange("p (i q) -> p i q", i=2),
                    c.din("l1a_W_vdec")[:, 512 * half:512 * (half + 1)].rearrange(
                        "(i p) q -> p i q", p=128), wait=(wguard[0],), sem=f"dKV0{sfx}")
        tvm = tdec
        tves = None
        for tt in range(16):
            tm = None
            for ic in range(2):
                tm = p.emit("tensor", lambda e, ic=ic, tt=tt: e.matmul(
                    c.p4[:, 0:PB], kvn[ic][:, 128 * tt:128 * (tt + 1)],
                    wst[0][:, 512 * ic:512 * (ic + 1)],
                    start=(ic == 0), stop=(ic == 1)), wait=(twv, tvm), inc="pe")
            tve = p.emit("vector", lambda e, tt=tt: e.tensor_copy(
                Vb[:, tt * WV:tt * WV + 8 * 66].rearrange("p (h d) -> p h d", d=66)[:, :, 0:64],
                c.p4[:, 0:PB].rearrange("p (h d) -> p h d", d=64)),
                wait=(tm, tves), inc="ve")
            tves = p.emit("vector", lambda e, tt=tt: e.memset(
                Vb[:, tt * WV + 64: tt * WV + 8 * 66:66], 1.0), wait=(tve,), inc="ve")
            tvm = tve
            wguard[0] = tm
        tv_done = tves
        if DBG_L1A:
            p.barrier()
            for nm, buf in (("kc0", kc[0]), ("qc20", qc2[0]), ("kr0x", krT[0]),
                            ("qr0x", qrT[0])):
                tdmp = p.dma("sync", c.scratch("dbg_%sh%d" % (nm, half), (128, T), BF16),
                             buf[:, :], sem="dDBG")
            tdmp = p.dma("sync", c.scratch("dbg_V0h%d" % half, (128, 512), BF16),
                         Vb[:, 0:528].rearrange("p (h d) -> p h d", d=66)[:, :, 0:64], sem="dDBG")
            p.emit("sync", None, wait=(tdmp,))
        p.barrier()
        A4 = [sh[0], sh[1], wstab, Ab3]
        aits = [(hh, tp) for hh in range(8) for tp in range(8)]
        NIT = len(aits)
        texp = {}
        tmask = {}
        tABr = {(-2, 0): None, (-2, 1): None, (-1, 0): None, (-1, 1): None}
        tsc_t = {}
        ttr = {0: None, 1: None}
        tat2r = [None, None]
        tavfin = {}
        trecip = {-2: None, -1: None}
        tavs = {}
        toutm = {}
        tbc_t = {}
        tscore_last = None

        def ageo(n):
            hh, tp = aits[n]
            return (hh, tp, hh // 2, (hh % 2) * 64, hh // 3, (hh % 3) * 32,
                    2 * tp + 2)

        def emit_scores(n, ti, ext, exts, cm, Ab, w0):
            nonlocal tscore_last
            hh, tp, cch, coff, rch, roff, nkt = ageo(n)
            qt = 2 * tp + ti
            tsc = None
            tmq = []
            ngr = (exts + PB - 1) // PB
            for g in range(ngr):
                ws = min(PB, exts - g * PB)
                tsc = p.emit("tensor", lambda e, g=g, ws=ws, qt=qt, cch=cch,
                             coff=coff: e.matmul(
                    c.pS[:, g * PB:g * PB + ws],
                    qc2[cch][coff:coff + 64, 128 * qt:128 * (qt + 1)],
                    kc[cch][coff:coff + 64, g * PB:g * PB + ws],
                    start=True, stop=False),
                    wait=(w0 if g == 0 else ()), inc="pe")
                tsc = p.emit("tensor", lambda e, g=g, ws=ws, qt=qt, rch=rch,
                             roff=roff: e.matmul(
                    c.pS[:, g * PB:g * PB + ws],
                    qrT[rch][roff:roff + 32, 128 * qt:128 * (qt + 1)],
                    krT[rch][roff:roff + 32, g * PB:g * PB + ws],
                    start=False, stop=True),
                    wait=(tsc,), inc="pe")
                tmq.append(p.emit("vector", lambda e, g=g, ws=ws, cm=cm: e.reduce_max(
                    colsc[:, cm + 2 + g:cm + 3 + g], c.pS[:, g * PB:g * PB + ws],
                    axis=AX), wait=(tsc, texp.get((n - 2, ti))), inc="ve"))
            tsc_t[(n, ti)] = tsc
            tscore_last = tsc
            tmx = p.emit("vector", lambda e, cm=cm, ngr=ngr: e.reduce_max(
                colsc[:, cm:cm + 1], colsc[:, cm + 2:cm + 2 + ngr], axis=AX,
                negate=True), wait=tuple(tmq), inc="ve")
            tmx2 = p.emit("vector", lambda e, cm=cm: e.tensor_scalar_mul(
                colsc[:, cm + 1:cm + 2], colsc[:, cm:cm + 1], scl1), wait=(tmx,), inc="ve")
            texp[(n, ti)] = p.emit("scalar", lambda e, exts=exts, Ab=Ab, cm=cm:
                                   e.activation(Ab[:, 0:exts], c.pS[:, 0:exts], AF.Exp,
                                                bias=colsc[:, cm + 1:cm + 2], scale=scl1),
                                   wait=(tmx2, tABr[(n - 2, ti)]), inc="se")
            return tsc

        def prod_first(n):
            hh, tp, cch, coff, rch, roff, nkt = ageo(n)
            ext = 128 * nkt
            pr = n % 2
            wprev = texp.get((n - 1, 1))
            w0 = (tdec, tv_done, wprev)
            emit_scores(n, 0, ext, ext - 128, 1 + 14 * pr, A4[2 * pr], w0)

        def prod_second(n):
            hh, tp, cch, coff, rch, roff, nkt = ageo(n)
            ext = 128 * nkt
            pr = n % 2
            Ab0, Ab1 = A4[2 * pr], A4[2 * pr + 1]
            qt0, qt1 = 2 * tp, 2 * tp + 1
            emit_scores(n, 1, ext, ext, 8 + 14 * pr, Ab1, (texp[(n, 0)],))
            tms0 = p.emit("gpsimd", lambda e, qt=qt0, Ab=Ab0: e.tensor_mul(
                Ab[:, 128 * qt:128 * (qt + 1)], Ab[:, 128 * qt:128 * (qt + 1)],
                c.diagmask[:]), wait=(texp[(n, 0)],), inc="ge")
            tmask[(n, 0)] = p.emit("gpsimd", lambda e, qt=qt0, ext=ext, Ab=Ab0:
                                   e.memset(Ab[:, 128 * (qt + 1):ext], 0.0),
                                   wait=(tms0,), inc="ge")
            tmask[(n, 1)] = p.emit("gpsimd", lambda e, qt=qt1, Ab=Ab1: e.tensor_mul(
                Ab[:, 128 * qt:128 * (qt + 1)], Ab[:, 128 * qt:128 * (qt + 1)],
                c.diagmask[:]), wait=(texp[(n, 1)],), inc="ge")

        cstate = {}

        def cons_kt(n, kt, pr):
            Ab0, Ab1 = A4[2 * pr], A4[2 * pr + 1]
            tb = c.p4 if kt % 2 == 0 else c.p7
            to1 = p.emit("tensor", lambda e, kt=kt, tb=tb, Ab0=Ab0: e.matmul(
                tb[:, 0:128], Ab0[:, 128 * kt:128 * (kt + 1)], c.ident[:],
                start=True, stop=True), wait=(tmask[(n, 0)], ttr[kt % 2]), inc="pe")
            to2 = p.emit("tensor", lambda e, kt=kt, tb=tb, Ab1=Ab1: e.matmul(
                tb[:, 128:256], Ab1[:, 128 * kt:128 * (kt + 1)], c.ident[:],
                start=True, stop=True), wait=(tmask[(n, 1)],), inc="pe")
            tABr[(n, 0)] = to1
            tABr[(n, 1)] = to2
            if kt % 2 == 0:
                tev = p.emit("vector", lambda e, kt=kt, tb=tb: e.tensor_copy(
                    at2[:, 256 * (kt % 2):256 * (kt % 2) + 256], tb[:, 0:256]),
                    wait=(to1, to2, tat2r[kt % 2]), inc="ve")
            else:
                tev = p.emit("scalar", lambda e, kt=kt, tb=tb: e.copy(
                    at2[:, 256 * (kt % 2):256 * (kt % 2) + 256], tb[:, 0:256]),
                    wait=(to1, to2, tat2r[kt % 2]), inc="se")
            ttr[kt % 2] = tev
            cstate[(n, kt)] = tev

        def cons_av(n, kt, nkt, hh, pr):
            avbank = c.p5 if pr == 0 else c.p6
            w = [cstate.pop((n, kt)), tv_done]
            if kt == 0:
                w += [trecip.get(n - 2), tavs.get(n - 2)]
            tav = p.emit("tensor", lambda e, kt=kt, hh=hh, nkt=nkt, avbank=avbank:
                         e.matmul(avbank[0:65, 0:256],
                                  Vb[:, kt * WV + hh * 66: kt * WV + hh * 66 + 65],
                                  at2[:, 256 * (kt % 2):256 * (kt % 2) + 256],
                                  start=(kt == 0), stop=(kt == nkt - 1)),
                         wait=tuple(w), inc="pe")
            tat2r[kt % 2] = tav
            if kt == nkt - 1:
                tavfin[n] = tav

        def cons(n, half_):
            hh, tp, cch, coff, rch, roff, nkt = ageo(n)
            pr = n % 2
            h1 = (nkt + 1) // 2
            kts = range(0, h1) if half_ == 0 else range(h1, nkt)
            for kt in kts:
                cons_kt(n, kt, pr)
                if kt >= 1:
                    cons_av(n, kt - 1, nkt, hh, pr)
            if half_ == 1:
                cons_av(n, nkt - 1, nkt, hh, pr)
                avbank = c.p5 if pr == 0 else c.p6
                rsl = c.ftmp[0][0:1, 256 * pr:256 * pr + 256]
                avs = scr[0:64, 0:256] if pr == 0 else scr[0:64, 256:512]
                tln = p.emit("scalar", lambda e, rsl=rsl, avbank=avbank: e.activation(
                    rsl, avbank[64:65, 0:256], AF.Ln),
                    wait=(tavfin[n], tbc_t.get(n - 2)), inc="se")
                tiv = p.emit("scalar", lambda e, rsl=rsl: e.activation(
                    rsl, rsl, AF.Exp, scale=-1.0), wait=(tln,), inc="se", drain=True)
                trecip[n] = tiv
                tas = p.emit("scalar", lambda e, avs=avs, avbank=avbank: e.copy(
                    avs, avbank[0:64, 0:256]),
                    wait=(tavfin[n], toutm.get(n - 2)), inc="se")
                tavs[n] = tas

        def cons_tail(n):
            hh, tp, cch, coff, rch, roff, nkt = ageo(n)
            pr = n % 2
            avbank = c.p5 if pr == 0 else c.p6
            rsl = c.ftmp[0][0:1, 256 * pr:256 * pr + 256]
            avs = scr[0:64, 0:256] if pr == 0 else scr[0:64, 256:512]
            tbc = p.emit("tensor", lambda e, rsl=rsl, avbank=avbank: e.matmul(
                avbank[0:64, 256:512], c.onesrowf[0:1, 0:64], rsl,
                start=True, stop=True),
                wait=(trecip[n], tavs[n], toutm.get(n - 2)), inc="pe")
            tbc_t[n] = tbc
            toutm[n] = p.emit("vector", lambda e, tp=tp, cch=cch, coff=coff,
                              avs=avs, avbank=avbank: e.tensor_mul(
                Ob[cch][coff:coff + 64, 256 * tp:256 * (tp + 1)],
                avs, avbank[0:64, 256:512]), wait=(tbc,), inc="ve")

        for n in range(NIT + 2):
            if n < NIT:
                prod_first(n)
            if n >= 2:
                cons_tail(n - 2)
            if 1 <= n <= NIT:
                cons(n - 1, 0)
            if n < NIT:
                prod_second(n)
            if 1 <= n <= NIT:
                cons(n - 1, 1)
        tO_done = toutm[NIT - 1]
        tp4_guard = tO_done
        if DBG_L1A:
            p.barrier()
            tdmp = p.dma("sync", c.scratch("dbg_O0h%d" % half, (128, T), BF16),
                         Ob[0][:, :], sem="dDBG")
            tdmp = p.dma("sync", c.scratch("dbg_O1h%d" % half, (128, T), BF16),
                         Ob[1][:, :], sem="dDBG")
            p.emit("sync", None, wait=(tdmp,))
            p.barrier()
        twg = [tO_done, tO_done]
        tpx = {"p4": tp4_guard}
        tXl = None
        for oc in range(8):
            pa = oc % 2
            tw[pa] = p.dma("gpsimd", wst[pa][:, 0:512].rearrange("p (i q) -> p i q", i=4),
                           opj[512 * half:512 * (half + 1),
                               128 * oc:128 * (oc + 1)].rearrange("(i p) q -> p i q", p=128),
                           wait=(twg[pa], wguard[pa]), sem=f"dWo{pa}{sfx}")
            for g in range(4):
                s = slice(g * PB, (g + 1) * PB)
                tm = None
                for ic in range(4):
                    tm = p.emit("tensor", lambda e, ic=ic, s=s, pa=pa: e.matmul(
                        c.p4[:, 0:PB], wst[pa][:, 128 * ic:128 * (ic + 1)], Ob[ic][:, s],
                        start=(ic == 0), stop=(ic == 3)),
                        wait=(tw[pa], tpx.get("p4"), tO_done), inc="pe")
                twg[pa] = tm
                wguard[pa] = tm
                tXl = p.emit("vector", lambda e, oc=oc, s=s: e.tensor_add(
                    c.X[oc][:, s], c.X[oc][:, s], c.p4[:, 0:PB]),
                    wait=(tm, tpx.get("X")), inc="ve")
                tpx["p4"] = tXl
                tpx["X"] = tXl
        p.barrier()
    c.ar.pop()


def phase_l1attn(c):
    emit_l1attn(c)



def phase_dumpx(c):
    p = c.prog
    nc = c.nc
    p.barrier()
    oX = nc.dram_tensor("dbg_X0", [128, T], F32, kind="ExternalOutput").ap()
    oI = nc.dram_tensor("dbg_idf", [128, 128], F32, kind="ExternalOutput").ap()
    oS = nc.dram_tensor("dbg_smallf", [128, 64], F32, kind="ExternalOutput").ap()
    t = p.dma("sync", oX, c.X[0][:], sem="dDBG")
    t = p.dma("sync", oI, c.identf[:], sem="dDBG")
    t = p.dma("sync", oS, c.smallf[:], sem="dDBG")
    p.emit("sync", None, wait=(t,))
    p.barrier()


def phase_lntest(c):
    p = c.prog
    c.ar.push()
    st = alloc_stats(c)
    xn = [c.ar.take(T, BF16, f"xn{i}") for i in range(8)]
    emit_layernorm(c, xn, st)
    for ch in range(8):
        p.emit("vector", lambda e, ch=ch: e.tensor_copy(c.X[ch][:], xn[ch]), inc="ve")
    p.barrier()
    c.ar.pop()


PHASE_EMITTERS = {
    "lntest": phase_lntest,
    "meff": phase_meff,
    "l0ffn": phase_l0ffn,
    "l0attn": phase_l0attn,
    "l1ffn": phase_l1ffn,
    "l1attn": phase_l1attn,
    "dumpx": phase_dumpx,
}



ALL_PHASES = ["meff", "l0ffn", "l0attn", "l0ffn", "l1ffn", "l1attn", "l1ffn"]
LAUNCH_GROUPS = [ALL_PHASES]


def kernel(**inputs):
    W = _prep_weights(inputs)
    x = np.asarray(inputs["L0_attn_x"], np.float32)
    cur = [np.ascontiguousarray(x[b].T) for b in range(NCORE)]

    outs = None
    for phases in LAUNCH_GROUPS:
        nc, used, _dbg = build_program(phases)
        in_maps = []
        for core in range(NCORE):
            m = {}
            for name in used:
                m[name] = cur[core] if name == "xT" else W[name]
            in_maps.append(m)
        res = run_bass_kernel_spmd(nc, in_maps, core_ids=list(range(NCORE)))
        outs = [np.asarray(res.results[i]["out"]) for i in range(NCORE)]
        cur = [np.ascontiguousarray(o.T) for o in outs]

    return np.stack(outs, axis=0)


# version 08
# speedup vs baseline: 1.6900x; 1.0055x over previous
import math
from contextlib import ExitStack

import numpy as np
import ml_dtypes

import concourse.bass as bass
import concourse.mybir as mybir
from concourse.bass_utils import run_bass_kernel_spmd

F32 = mybir.dt.float32
BF16 = mybir.dt.bfloat16
I32 = mybir.dt.int32
AF = mybir.ActivationFunctionType
ALU = mybir.AluOpType
AX = mybir.AxisListType.X
BF = ml_dtypes.bfloat16

B, T, D = 8, 2048, 1024
DBG_L1A = False
NCORE = 8
PB = 512
NG = T // PB

ENGS = ("sync", "gpsimd", "tensor", "vector", "scalar")


class Tok:
    __slots__ = ("sem", "val", "order")

    def __init__(self, sem, val, order):
        self.sem, self.val, self.order = sem, val, order


class Prog:
    def __init__(self, nc):
        self.nc = nc
        self.q = {e: [] for e in ENGS}
        self.sems = {}
        self.cnt = {}
        self.order = 0
        self.bar_round = 0

    def sem(self, name):
        if name not in self.sems:
            self.sems[name] = self.nc.alloc_semaphore(name)
            self.cnt[name] = 0
        return self.sems[name]

    def emit(self, eng, fn, wait=(), inc=None, inc_by=1, drain=False):
        self.order += 1
        o = self.order
        wsd = {}
        for t in wait:
            if t is None:
                continue
            assert t.order < o, "wait on later-planned token"
            wsd[t.sem] = max(wsd.get(t.sem, 0), t.val)
        ws = list(wsd.items())
        tok = None
        if inc is not None:
            self.sem(inc)
            self.cnt[inc] += inc_by
            tok = Tok(inc, self.cnt[inc], o)
        self.q[eng].append((ws, fn, drain, (inc, inc_by) if inc else None))
        return tok

    def barrier(self):
        self.bar_round += 1
        bar = "bar"
        self.sem(bar)
        for e in ENGS:
            self.order += 1
            self.cnt[bar] += 1
            self.q[e].append(([], lambda eng: eng.drain(), False, (bar, 1)))
        tgt = 5 * self.bar_round
        for e in ENGS:
            self.order += 1
            self.q[e].append(([(bar, tgt)], None, False, None))

    def finalize(self, block):
        for name in ENGS:
            ops = self.q[name]

            def run(eng, ops=ops):
                seen = {}
                for ws, fn, drain, inc in ops:
                    for (s, v) in ws:
                        if seen.get(s, -1) < v:
                            eng.wait_ge(self.sems[s], v)
                            seen[s] = v
                    if drain:
                        eng.drain()
                    if fn is None:
                        continue
                    instr = fn(eng)
                    if inc is not None:
                        s, by = inc
                        instr.then_inc(self.sems[s], by)

            getattr(block, name)(run)

    def dma(self, queue, out, in_, wait=(), sem="dL"):
        if not hasattr(self, "_last_dma"):
            self._last_dma = {}
        w = list(wait) + [self._last_dma.get(sem)]
        tok = self.emit(queue, lambda e, o=out, i=in_: e.dma_start(out=o, in_=i),
                        wait=w, inc=sem, inc_by=16)
        self._last_dma[sem] = tok
        return tok


class Arena:
    def __init__(self, handle, nbytes):
        self.h = handle
        self.nbytes = nbytes
        self.off = 0
        self.marks = []

    def take(self, cols, dtype, name=""):
        isz = {F32: 4, BF16: 2, I32: 4}[dtype]
        nb = cols * isz
        off = (self.off + 63) & ~63
        assert off + nb <= self.nbytes, f"arena overflow at {name}: {off + nb}>{self.nbytes}"
        self.off = off + nb
        return self.h[:, off:off + nb].bitcast(dtype)

    def push(self):
        self.marks.append(self.off)

    def pop(self):
        self.off = self.marks.pop()



def _prep_weights(I):
    W = {}
    f32 = lambda a: np.ascontiguousarray(np.asarray(a, dtype=np.float32))

    qhead = lambda cch, half: (cch // 4) * 8 + (cch % 4) + 4 * half
    qperm = np.empty(1024, np.int64)
    for cch in range(8):
        for half in range(2):
            h = qhead(cch, half)
            qperm[cch * 128 + half * 64: cch * 128 + half * 64 + 64] = \
                np.arange(h * 64, (h + 1) * 64)
    W["l0f_wq"] = f32(np.asarray(I["L0_ffn_wq"]).T[:, qperm])
    W["l0f_wk"] = f32(np.asarray(I["L0_ffn_wk"]).T)
    W["l0f_wv"] = f32(np.asarray(I["L0_ffn_wv"]).T)
    W["l0f_wo"] = f32(np.asarray(I["L0_ffn_wo"]).T[qperm, :])
    qg = np.asarray(I["L0_ffn_q_gain"], np.float32)
    W["l0f_qgain"] = f32(np.stack([[qg[qhead(cch, half)] for cch in range(8)]
                                   for half in range(2)]))
    W["l0f_kgain"] = f32(np.asarray(I["L0_ffn_k_gain"]).reshape(2, 2).T)
    W["l0f_fc_logits"] = f32(I["L0_ffn_fc_logits"]).reshape(1, 8)
    W["l0f_pj_logits"] = f32(I["L0_ffn_pj_logits"]).reshape(1, 8)

    s1 = np.asarray(I["L0_ffn_fc_s1"], np.float32)
    s2 = np.asarray(I["L0_ffn_fc_s2"], np.float32)
    mix = np.asarray(I["L0_ffn_fc_mix"], np.float32)
    p1 = np.asarray(I["L0_ffn_pj_s1"], np.float32)
    p2 = np.asarray(I["L0_ffn_pj_s2"], np.float32)
    pmx = np.asarray(I["L0_ffn_pj_mix"], np.float32)

    fcAT = np.zeros((8, 8, 128, 128), np.float32)
    s1r = s1.reshape(8, 16, 64, 64)
    for a in range(8):
        for gg in range(2):
            fcAT[:, a, gg * 64:(gg + 1) * 64, gg * 64:(gg + 1) * 64] = \
                s1r[:, 2 * a + gg].transpose(0, 2, 1)
    W["fcAT"] = f32(fcAT)
    u = np.arange(16)[:, None]
    cc = np.arange(64)[None, :]
    fcP = np.empty((8, 1024, 4096), np.float32)
    for j in range(8):
        perm = (cc * 16 + (u + j) % 16).reshape(-1)
        fcP[j] = mix[j].T[perm]
    W["fcP"] = f32(fcP)
    s2r = s2.reshape(8, 64, 64, 64)
    fcB = np.zeros((8, 32, 128, 128), np.float32)
    for dch in range(32):
        for qq in range(2):
            fcB[:, dch, qq * 64:(qq + 1) * 64, qq * 64:(qq + 1) * 64] = s2r[:, 2 * dch + qq]
    W["fcB"] = f32(fcB)

    pjAT = np.zeros((8, 32, 128, 128), np.float32)
    p1r = p1.reshape(8, 64, 64, 64)
    for a in range(32):
        for gg in range(2):
            pjAT[:, a, gg * 64:(gg + 1) * 64, gg * 64:(gg + 1) * 64] = \
                p1r[:, 2 * a + gg].transpose(0, 2, 1)
    W["pjAT"] = f32(pjAT)
    u = np.arange(64)[:, None]
    pjP = np.empty((8, 4096, 1024), np.float32)
    for j in range(8):
        perm = (cc * 64 + (u + j) % 64).reshape(-1)
        pjP[j] = pmx[j].T[perm]
    W["pjP"] = f32(pjP)
    p2r = p2.reshape(8, 16, 64, 64)
    pjB = np.zeros((8, 8, 128, 128), np.float32)
    for dch in range(8):
        for qq in range(2):
            pjB[:, dch, qq * 64:(qq + 1) * 64, qq * 64:(qq + 1) * 64] = p2r[:, 2 * dch + qq]
    W["pjB"] = f32(pjB)

    W["l0a_inproj"] = f32(I["L0_attn_in_proj_w"])
    W["l0a_outproj"] = f32(I["L0_attn_out_proj_w"])
    W["l0a_convk"] = f32(np.asarray(I["L0_attn_conv_k"]).reshape(4, 4128).T)
    W["l0a_convb"] = f32(I["L0_attn_conv_b"]).reshape(4128, 1)
    W["l0a_theta"] = f32(np.asarray(I["L0_attn_theta"]).reshape(2048, 1))
    W["l0a_decay"] = f32(I["L0_attn_decay_slopes"]).reshape(1, 28)
    W["l0a_anchor"] = f32(I["L0_attn_anchor_slopes"]).reshape(1, 4)
    W["l0a_sscale"] = f32(np.asarray(I["L0_attn_score_scale"]).reshape(32, 1))
    W["l0a_dlog"] = f32(I["L0_attn_deriv_logits"]).reshape(1, 3)
    W["l0a_wre"] = f32(I["L0_attn_W_re"])
    W["l0a_wim"] = f32(I["L0_attn_W_im"])
    nsc = np.asarray(I["L0_attn_norm_scale"], np.float32)
    W["l0a_nsc_re"] = f32(np.tile(nsc[:64], 2).reshape(128, 1))
    W["l0a_nsc_im"] = f32(np.tile(nsc[64:], 2).reshape(128, 1))

    for k in ("W_kv", "W_kdec", "W_vdec", "W_q", "W_qdec", "W_o"):
        W["l1a_" + k] = f32(I["L1_attn_" + k])
    W["l1a_g_kv"] = f32(np.asarray(I["L1_attn_g_kv"]).reshape(2, 128).T)

    def rope_cols(wsrc):
        src = f32(I[wsrc])
        dst = np.zeros((src.shape[0], 2 * 384), np.float32)
        for h in range(16):
            half, hh = divmod(h, 8)
            cidx, r = divmod(hh, 3)
            dst[:, half * 384 + cidx * 128 + r * 32: half * 384 + cidx * 128 + r * 32 + 32] = \
                src[:, h * 32:(h + 1) * 32]
        return np.ascontiguousarray(dst)
    W["l1a_W_krope"] = rope_cols("L1_attn_W_krope")
    W["l1a_W_qrope"] = rope_cols("L1_attn_W_qrope")

    W["l1f_colembT"] = f32(np.asarray(I["L1_ffn_col_emb"]).T)
    W["l1f_gw1"] = f32(I["L1_ffn_gate_w1"])
    W["l1f_gb1"] = f32(np.asarray(I["L1_ffn_gate_b1"]).reshape(4, 128).T)
    W["l1f_gw2"] = f32(I["L1_ffn_gate_w2"])
    W["l1f_gb2"] = f32(np.asarray(I["L1_ffn_gate_b2"]).reshape(1, 16))
    W["l1f_Wr"] = f32(np.asarray(I["L1_ffn_W_cols"]).transpose(1, 0, 2).reshape(1024, 4096))
    W["l1f_bcols"] = f32(np.asarray(I["L1_ffn_b_cols"]).reshape(32, 128).T)
    W["l1f_projw"] = f32(I["L1_ffn_proj_w"])
    W["l1f_projb"] = f32(np.asarray(I["L1_ffn_proj_b"]).reshape(8, 128).T)
    W["l1f_lng"] = f32(np.asarray(I["L1_ffn_ln_g"]).reshape(8, 128).T)
    W["l1f_lnb"] = f32(np.asarray(I["L1_ffn_ln_b"]).reshape(8, 128).T)
    return W


_INPUT_SHAPES = {
    "xT": (D, T),
    "l0f_wq": (1024, 1024), "l0f_wk": (1024, 256), "l0f_wv": (1024, 256),
    "l0f_wo": (1024, 1024), "l0f_qgain": (2, 8), "l0f_kgain": (2, 2),
    "l0f_fc_logits": (1, 8), "l0f_pj_logits": (1, 8),
    "fcAT": (8, 8, 128, 128), "fcP": (8, 1024, 4096), "fcB": (8, 32, 128, 128),
    "pjAT": (8, 32, 128, 128), "pjP": (8, 4096, 1024), "pjB": (8, 8, 128, 128),
    "l0a_inproj": (1024, 4128), "l0a_outproj": (2048, 1024),
    "l0a_convk": (4128, 4), "l0a_convb": (4128, 1), "l0a_theta": (2048, 1),
    "l0a_decay": (1, 28), "l0a_anchor": (1, 4), "l0a_sscale": (32, 1),
    "l0a_dlog": (1, 3), "l0a_wre": (64, 64), "l0a_wim": (64, 64),
    "l0a_nsc_re": (128, 1), "l0a_nsc_im": (128, 1),
    "l1a_W_kv": (1024, 256), "l1a_W_kdec": (256, 1024), "l1a_W_vdec": (256, 1024),
    "l1a_W_q": (1024, 512), "l1a_W_qdec": (512, 1024), "l1a_W_o": (1024, 1024),
    "l1a_g_kv": (128, 2), "l1a_W_krope": (1024, 768), "l1a_W_qrope": (512, 768),
    "l1f_colembT": (1024, 16), "l1f_gw1": (1024, 512), "l1f_gb1": (128, 4),
    "l1f_gw2": (512, 16), "l1f_gb2": (1, 16), "l1f_Wr": (1024, 4096),
    "l1f_bcols": (128, 32), "l1f_projw": (4096, 1024), "l1f_projb": (128, 8),
    "l1f_lng": (128, 8), "l1f_lnb": (128, 8),
}


def _host_consts():
    C = {}
    C["ident_bf"] = np.eye(128, dtype=BF)
    C["ident_f32"] = np.eye(128, dtype=np.float32)
    C["ones_col_bf"] = np.ones((128, 1), dtype=BF)
    C["ones_row_bf"] = np.ones((1, 128), dtype=BF)
    C["ones_row_f32"] = np.ones((1, 128), dtype=np.float32)
    r = np.arange(128)
    C["diagmask_bf"] = (r[None, :] <= r[:, None]).astype(BF)

    hd = 64
    inv = 1.0 / (10000.0 ** (np.arange(0, hd, 2, dtype=np.float64) / hd))
    fr = np.outer(np.arange(T, dtype=np.float64), inv)
    cosv = np.cos(fr).astype(np.float32).T
    sinv = np.sin(fr).astype(np.float32).T
    C["l0f_ropeC"] = np.ascontiguousarray(np.tile(np.concatenate([cosv, cosv], 0), (2, 1)))
    C["l0f_ropeS"] = np.ascontiguousarray(np.tile(np.concatenate([sinv, sinv], 0), (2, 1)))
    M64 = np.zeros((64, 64), np.float32)
    for j in range(32):
        M64[j, 32 + j] = 1.0
        M64[32 + j, j] = -1.0
    M128 = np.zeros((128, 128), np.float32)
    M128[:64, :64] = M64
    M128[64:, 64:] = M64
    C["l0f_rotT"] = np.ascontiguousarray(M128.T).astype(BF)

    rd = 32
    inv1 = 1.0 / (10000.0 ** (np.arange(0, rd, 2, dtype=np.float64) / rd))
    fr1 = np.outer(np.arange(T, dtype=np.float64), inv1)
    c1 = np.cos(fr1).astype(np.float32).T
    s1 = np.sin(fr1).astype(np.float32).T
    c32 = np.concatenate([c1, c1], 0)
    s32 = np.concatenate([s1, s1], 0)
    padC = np.zeros((128, T), np.float32)
    padS = np.zeros((128, T), np.float32)
    for rr in range(3):
        padC[rr * 32:(rr + 1) * 32] = c32
        padS[rr * 32:(rr + 1) * 32] = s32
    C["l1a_ropeC"] = np.ascontiguousarray(padC)
    C["l1a_ropeS"] = np.ascontiguousarray(padS)
    M32 = np.zeros((32, 32), np.float32)
    for j in range(16):
        M32[j, 16 + j] = -1.0
        M32[16 + j, j] = 1.0
    M1 = np.zeros((128, 128), np.float32)
    for rr in range(3):
        M1[rr * 32:(rr + 1) * 32, rr * 32:(rr + 1) * 32] = M32
    C["l1a_rotT"] = np.ascontiguousarray(M1.T).astype(BF)
    C["l1a_rotT_f32"] = np.ascontiguousarray(M1.T)

    hs = np.zeros((128, 2), np.float32)
    hs[:64, 0] = 1.0
    hs[64:, 1] = 1.0
    C["headsel_bf"] = hs.astype(BF)
    C["headbc_bf"] = np.ascontiguousarray(hs.T).astype(BF)
    C["headsel_f32"] = hs
    C["headbc_f32"] = np.ascontiguousarray(hs.T)
    C["l0f_rotT_f32"] = np.ascontiguousarray(M128.T)
    C["trow_f32"] = np.arange(T, dtype=np.float32).reshape(1, T)
    C["drow_f32"] = (T - 1.0 - np.arange(T, dtype=np.float32)).reshape(1, T)
    repl = np.zeros((32, 16 * 128), np.float32)
    for hp in range(16):
        repl[2 * hp, hp * 128:hp * 128 + 64] = 1.0
        repl[2 * hp + 1, hp * 128 + 64:(hp + 1) * 128] = 1.0
    C["replc_bf"] = repl.astype(BF)
    return C



class Ctx:
    pass


def build_program(phases, dbg=()):
    nc = bass.Bass("TRN2", target_bir_lowering=False, dynamic_dma_scratch_size=16384)
    prog = Prog(nc)
    c = Ctx()
    c.nc, c.prog = nc, prog
    c.used = set()
    c.hostc = _host_consts()
    c.inline = {}
    c.dbg_req = list(dbg)
    c.dbg_out = {}

    def din(name):
        if name not in c.used:
            c.used.add(name)
            c._dins[name] = nc.dram_tensor(name, list(_INPUT_SHAPES[name]), F32,
                                           kind="ExternalInput").ap()
        return c._dins[name]
    c._dins = {}
    c.din = din

    def const(name):
        if name not in c.inline:
            arr = np.ascontiguousarray(c.hostc[name])
            c.inline[name] = nc.inline_tensor(arr, name).ap()
        return c.inline[name]
    c.const = const

    c._scratch = {}

    def scratch(name, shape, dtype):
        if name not in c._scratch:
            c._scratch[name] = nc.dram_tensor(name, list(shape), dtype,
                                              kind="Internal").ap()
        return c._scratch[name]
    c.scratch = scratch

    c.out = nc.dram_tensor("out", [T, D], F32, kind="ExternalOutput").ap()

    with ExitStack() as stack:
        ec = stack.enter_context
        c.X = [ec(nc.sbuf_tensor(f"X{i}", [128, T], F32)) for i in range(8)]
        c.epscol = ec(nc.sbuf_tensor("epscol", [1, 1], F32))
        c.ident = ec(nc.sbuf_tensor("identt", [128, 128], BF16))
        c.identf = ec(nc.sbuf_tensor("identtf", [128, 128], F32))
        c.onescol = ec(nc.sbuf_tensor("onescol", [128, 1], BF16))
        c.onesrow = ec(nc.sbuf_tensor("onesrow", [1, 128], BF16))
        c.onesrowf = ec(nc.sbuf_tensor("onesrowf", [1, 128], F32))
        c.diagmask = ec(nc.sbuf_tensor("diagmask", [128, 128], BF16))
        c.smallf = ec(nc.sbuf_tensor("smallf", [128, 160], F32))
        c.headself = ec(nc.sbuf_tensor("headself", [128, 2], F32))
        c.headbcf = ec(nc.sbuf_tensor("headbcf", [2, 128], F32))
        c.rotTf = ec(nc.sbuf_tensor("rotTf", [128, 128], F32))
        c.ftmp = [ec(nc.sbuf_tensor(f"ftmp{i}", [128, PB], F32)) for i in range(2)]
        ARENA_BYTES = 136 * 1024
        c.ar = Arena(ec(nc.sbuf_tensor("arena", [128, ARENA_BYTES], mybir.dt.uint8)),
                     ARENA_BYTES)
        c.pS = ec(nc.psum_tensor("pS", [128, 4 * PB], F32))
        c.p4 = ec(nc.psum_tensor("p4", [128, PB], F32))
        c.p5 = ec(nc.psum_tensor("p5", [128, PB], F32))
        c.p6 = ec(nc.psum_tensor("p6", [128, PB], F32))
        c.p7 = ec(nc.psum_tensor("p7", [128, PB], F32))
        c.pb = [c.pS[:, 0:PB], c.pS[:, PB:2 * PB], c.pS[:, 2 * PB:3 * PB],
                c.pS[:, 3 * PB:4 * PB], c.p4[:, :], c.p5[:, :], c.p6[:, :], c.p7[:, :]]

        block = ec(nc.Block())

        _emit_preamble(c)
        for ph in phases:
            PHASE_EMITTERS[ph](c)
        _emit_output(c)

        prog.finalize(block)

    return nc, sorted(c.used), dict(c.dbg_out)


def _dbg_dump(c, name, src_ap, shape):
    if name not in c.dbg_req:
        return
    o = c.nc.dram_tensor("dbg_" + name, list(shape), F32, kind="ExternalOutput").ap()
    c.dbg_out[name] = "dbg_" + name
    c.prog.barrier()
    t = c.prog.dma("sync", o, src_ap, sem="dS")
    c.prog.emit("sync", None, wait=(t,))
    c.prog.barrier()


def _emit_preamble(c):
    p = c.prog
    toks = [
        p.dma("sync", c.ident[:], c.const("ident_bf")),
        p.dma("sync", c.identf[:], c.const("ident_f32")),
        p.dma("sync", c.onescol[:], c.const("ones_col_bf")),
        p.dma("sync", c.onesrow[:], c.const("ones_row_bf")),
        p.dma("sync", c.onesrowf[:], c.const("ones_row_f32")),
        p.dma("sync", c.diagmask[:], c.const("diagmask_bf")),
        p.dma("sync", c.headself[:], c.const("headsel_f32")),
        p.dma("sync", c.headbcf[:], c.const("headbc_f32")),
        p.dma("sync", c.rotTf[:], c.const("l0f_rotT_f32")),
    ]
    xr = c.din("xT").rearrange("(c p) t -> c p t", p=128)
    for i in range(8):
        toks.append(p.dma("sync", c.X[i][:], xr[i]))
    p.emit("vector", lambda e: e.memset(c.epscol[:], 1e-5), inc="ve")
    tok = toks[-1]
    for e in ENGS:
        p.emit(e, None, wait=(tok,))
    p.barrier()


def _emit_output(c):
    p = c.prog
    p.barrier()
    outr = c.out.rearrange("(tt p) d -> tt p d", p=128)
    evac = [c.ftmp[0], c.ftmp[1]]
    last = [None, None]
    tv_prev = [None, None]
    for tt in range(16):
        for half in range(2):
            bank = (c.p4 if half == 0 else c.p5)
            ev = evac[half]
            tk = None
            for k in range(4):
                ch = half * 4 + k
                tk = p.emit("tensor",
                            lambda e, ch=ch, tt=tt, bank=bank, k=k:
                            e.matmul(bank[:, k * 128:(k + 1) * 128],
                                     c.X[ch][:, tt * 128:(tt + 1) * 128],
                                     c.identf[:], start=True, stop=True),
                            wait=(tv_prev[half],), inc="pe")
            tv = p.emit("vector", lambda e, bank=bank, ev=ev: e.tensor_copy(ev[:], bank[:, 0:PB]),
                        wait=(tk, last[half]), inc="ve")
            tv_prev[half] = tv
            last[half] = p.dma("sync", outr[tt][:, half * 512:(half + 1) * 512], ev[:],
                               wait=(tv,), sem="dSo%d" % half)
    p.emit("sync", None, wait=(last[0], last[1]))



def emit_layernorm(c, xn_chunks, st, scratch4=None, eps=1e-5):
    p = c.prog
    own = scratch4 is None
    if own:
        c.ar.push()
        scratch4 = [c.ar.take(T, BF16, f"ln_s{i}") for i in range(4)]
    xb = scratch4[0:2]
    sq = scratch4[2:4]
    p.barrier()
    sqb = [c.p4, c.p5, c.p6, c.p7]
    last_use = [None, None]
    tsum = None
    for ch in range(8):
        pa = ch % 2
        t1 = p.emit("vector", lambda e, ch=ch, pa=pa: e.tensor_copy(xb[pa], c.X[ch][:]),
                    wait=(last_use[pa],), inc="ve")
        t2 = p.emit("scalar", lambda e, pa=pa: e.activation(sq[pa], xb[pa], AF.Square),
                    wait=(t1,), inc="se")
        for g in range(4):
            s = slice(g * PB, (g + 1) * PB)
            p.emit("tensor", lambda e, s=s, ch=ch, g=g, pa=pa: e.matmul(
                c.pS[0:1, s.start:s.stop], c.onescol[:], xb[pa][:, s],
                start=(ch == 0), stop=(ch == 7)), wait=(t1,), inc="pe")
            tsum = p.emit("tensor", lambda e, s=s, ch=ch, g=g, pa=pa: e.matmul(
                sqb[g][0:1, 0:PB], c.onescol[:], sq[pa][:, s],
                start=(ch == 0), stop=(ch == 7)), wait=(t2,), inc="pe")
        last_use[pa] = tsum
    r1 = c.ftmp[0][0:1, 0:PB]
    r2 = c.ftmp[1][0:1, 0:PB]
    tg_done = None
    mu, nmu, rstd = st["mu"], st["nmu"], st["rstd"]
    for g in range(4):
        s = slice(g * PB, (g + 1) * PB)
        t1 = p.emit("vector", lambda e, s=s: e.tensor_scalar_mul(
            mu[:, s], c.pS[0:1, s.start:s.stop], 1.0 / D), wait=(tsum, tg_done), inc="ve")
        t1b = p.emit("vector", lambda e, s=s, g=g: e.tensor_scalar_mul(
            r1, sqb[g][0:1, 0:PB], 1.0 / D), wait=(tsum, t1), inc="ve")
        t2 = p.emit("scalar", lambda e, s=s: e.activation(r2, mu[:, s], AF.Square),
                    wait=(t1, tg_done), inc="se")
        t3 = p.emit("vector", lambda e: e.tensor_sub(r1, r1, r2),
                    wait=(t2, t1b), inc="ve")
        t4 = p.emit("scalar", lambda e: e.activation(r2, r1, AF.Ln, bias=c.epscol[0:1, 0:1]),
                    wait=(t3,), inc="se")
        t5 = p.emit("scalar", lambda e, s=s: e.activation(rstd[:, s], r2, AF.Exp, scale=-0.5),
                    wait=(t4,), inc="se")
        tg_done = p.emit("vector", lambda e, s=s: e.tensor_scalar_mul(
            nmu[:, s], mu[:, s], -1.0), wait=(t5,), inc="ve")
    p.barrier()
    for g in range(4):
        s = slice(g * PB, (g + 1) * PB)
        tb1 = p.emit("tensor", lambda e, s=s: e.matmul(
            c.p4[:, 0:PB], c.onesrowf[:], nmu[0:1, s], start=True, stop=True), inc="pe")
        tb2 = p.emit("tensor", lambda e, s=s: e.matmul(
            c.p5[:, 0:PB], c.onesrowf[:], rstd[0:1, s], start=True, stop=True), inc="pe")
        tvs = [None, None]
        for ch in range(8):
            ft = c.ftmp[ch % 2]
            t1 = p.emit("vector", lambda e, ch=ch, s=s, ft=ft: e.tensor_add(
                ft[:], c.X[ch][:, s], c.p4[:, 0:PB]),
                wait=(tb1, tvs[ch % 2]), inc="ve")
            tvs[ch % 2] = p.emit("vector", lambda e, ch=ch, s=s, ft=ft: e.tensor_mul(
                xn_chunks[ch][:, s], ft[:], c.p5[:, 0:PB]),
                wait=(tb2, t1), inc="ve")
        p.barrier()
    if own:
        c.ar.pop()


def alloc_stats(c):
    return {"mu": c.ar.take(T, F32, "st_mu")[0:1, :],
            "nmu": c.ar.take(T, F32, "st_nmu")[0:1, :],
            "rstd": c.ar.take(T, F32, "st_rstd")[0:1, :]}



def emit_alpha(c, logits_name, out_cols, wait=()):
    p = c.prog
    nc = c.nc
    c.ar.push()
    row = c.ar.take(64, F32, "al_rows")
    lg = row[0:1, 0:8]
    m1 = row[0:1, 8:9]
    nm1 = row[0:1, 9:10]
    eq1 = row[0:1, 16:24]
    t1 = row[0:1, 24:32]
    t2 = row[0:1, 32:40]
    m2 = row[0:1, 10:11]
    mask = row[0:1, 40:48]
    expv = row[0:1, 48:56]
    ssum = row[0:1, 11:12]
    rsum = row[0:1, 12:13]
    alpha = row[0:1, 56:64]

    tl = p.dma("sync", lg, c.din(logits_name), wait=wait)
    v = lambda fn, wait, inc="ve": p.emit("vector", fn, wait=wait, inc=inc)
    t = v(lambda e: e.reduce_max(m1, lg, axis=AX), (tl,))
    t = v(lambda e: e.tensor_scalar(eq1, lg, m1, None, ALU.is_equal), (t,))
    t = v(lambda e: e.tensor_mul(t1, eq1, lg), (t,))
    t = v(lambda e: e.tensor_sub(t1, lg, t1), (t,))
    t = v(lambda e: e.tensor_scalar(t2, eq1, 1e38, None, ALU.mult), (t,))
    t = v(lambda e: e.tensor_sub(t2, t1, t2), (t,))
    t = v(lambda e: e.reduce_max(m2, t2, axis=AX), (t,))
    tmask = v(lambda e: e.tensor_scalar(mask, lg, m2, None, ALU.is_ge), (t,))
    tneg = v(lambda e: e.tensor_scalar_mul(nm1, m1, -1.0), (tmask,))
    te = p.emit("scalar", lambda e: e.activation(expv, lg, AF.Exp, bias=nm1),
                wait=(tneg,), inc="se")
    t = v(lambda e: e.tensor_mul(expv, expv, mask), (te,))
    t = v(lambda e: e.reduce_sum(ssum, expv, axis=AX), (t,))
    t = v(lambda e: e.reciprocal(rsum, ssum), (t,))
    t = v(lambda e: e.tensor_scalar(alpha, expv, rsum, None, ALU.mult), (t,))
    tb = p.emit("tensor", lambda e: e.matmul(c.p7[:, 0:8], c.onesrowf[:], alpha,
                                             start=True, stop=True), wait=(t,), inc="pe")
    tout = p.emit("vector", lambda e: e.tensor_copy(out_cols, c.p7[:, 0:8]),
                  wait=(tb,), inc="ve")
    c.ar.pop()
    return tout


def emit_meff_build(c):
    p = c.prog
    nc = c.nc
    mfc = c.scratch("Mfc_t", (128, 32768), BF16)
    mpj = c.scratch("Mpj_t", (128, 32768), BF16)
    afc = c.smallf[:, 0:8]
    apj = c.smallf[:, 8:16]
    p.barrier()

    c.ar.push()
    atbF = c.ar.take(8 * 8 * 128, BF16, "bd_atF")
    atbP = c.ar.take(32 * 8 * 128, BF16, "bd_atP")
    bb = [c.ar.take(8 * 512, BF16, "bd_b0"), c.ar.take(8 * 512, BF16, "bd_b1")]
    NPR = 10
    pring = [c.ar.take(512, BF16, "bd_p%d" % i) for i in range(NPR)]
    ztb = [c.ar.take(512, BF16, "bd_zt0"), c.ar.take(512, BF16, "bd_zt1")]
    mo = [c.ar.take(512, BF16, "bd_mo0"), c.ar.take(512, BF16, "bd_mo1")]

    t_at = {}
    for which, atb, NA in (("fc", atbF, 8), ("pj", atbP, 32)):
        AT_in = c.din("fcAT" if which == "fc" else "pjAT")
        for a in range(NA):
            t_at[(which, a)] = p.dma(
                "gpsimd", atb[:, 1024 * a:1024 * (a + 1)].rearrange("p (j q) -> p j q", j=8),
                AT_in[:, a].rearrange("j p q -> p j q"), sem="dW_at%d" % (a % 4))

    t_afc = emit_alpha(c, "l0f_fc_logits", afc)
    t_apj = emit_alpha(c, "l0f_pj_logits", apj, wait=(t_afc,))
    p.barrier()

    cfgs = {
        "fc": (c.din("fcP"), c.din("fcB"), atbF, afc, t_afc, 8, 8,
               mfc.rearrange("p (uc i q) -> p uc i q", uc=32, i=8, q=128)),
        "pj": (c.din("pjP"), c.din("pjB"), atbP, apj, t_apj, 32, 2,
               mpj.rearrange("p (o a q) -> p o a q", o=8, a=32, q=128)),
    }
    its = []
    dglist = []
    for which in ("fc", "pj"):
        _, _, _, _, _, NA, NDG, _ = cfgs[which]
        for dg in range(NDG):
            gdg = len(dglist)
            dglist.append((which, dg))
            for a in range(NA):
                its.append((which, dg, a, gdg))
    flat = []
    gacnt = 0
    for which, dg, a, gdg in its:
        for j in range(8):
            flat.append((which, dg, a, j, gdg, gacnt))
        gacnt += 1
    NI = len(flat)

    tstore = [None, None]
    tbload = {}
    tbscale = {}
    tbbfree = [None, None]
    tpl = {}
    tpfree = {}
    tev = {}
    ts2 = {}
    tmo_t = {}

    def bb_load(gdg):
        which, dg = dglist[gdg]
        _, B_in, _, acol, t_alpha, _, _, _ = cfgs[which]
        par = gdg % 2
        for j in range(8):
            tbload[(gdg, j)] = p.dma(
                "gpsimd", bb[par][:, 512 * j:512 * (j + 1)].rearrange(
                    "p (d q) -> p d q", d=4),
                B_in[j, 4 * dg:4 * dg + 4].rearrange("d p q -> p d q"),
                wait=(tbbfree[par],), sem="dW_b%d" % par)
        tsc = None
        for j in range(8):
            tsc = p.emit("vector", lambda e, par=par, j=j: e.tensor_scalar(
                bb[par][:, 512 * j:512 * (j + 1)], bb[par][:, 512 * j:512 * (j + 1)],
                acol[:, j:j + 1], None, ALU.mult),
                wait=(tbload[(gdg, j)], t_alpha), inc="ve")
        tbscale[gdg] = tsc

    def p_load(gi):
        which, dg, a, j, _, _ = flat[gi]
        P_in = cfgs[which][0]
        slot = gi % NPR
        tpl[gi] = p.dma("gpsimd", pring[slot],
                        P_in[j, 128 * a:128 * (a + 1), 512 * dg:512 * (dg + 1)],
                        wait=(tpfree.get(slot),), sem="dW_p%d" % slot)

    PF = 8
    bb_load(0)
    bb_load(1)
    for gi in range(PF):
        p_load(gi)

    for gi in range(NI + 1):
        if gi < NI:
            which, dg, a, j, gdg, gac = flat[gi]
            atb = cfgs[which][2]
            slot = gi % NPR
            ps1 = c.p4 if gi % 2 == 0 else c.p5
            t1 = None
            for tt in range(4):
                t1 = p.emit("tensor", lambda e, tt=tt, ps1=ps1, slot=slot, atb=atb, a=a, j=j:
                            e.matmul(ps1[:, 128 * tt:128 * (tt + 1)],
                                     pring[slot][:, 128 * tt:128 * (tt + 1)],
                                     atb[:, 1024 * a + 128 * j:1024 * a + 128 * (j + 1)],
                                     start=True, stop=True),
                            wait=(tpl[gi], t_at[(which, a)], tev.get(gi - 2)), inc="pe")
            tpfree[slot] = t1
            if gi + PF < NI:
                p_load(gi + PF)
            tev[gi] = p.emit("vector", lambda e, gi=gi, ps1=ps1: e.tensor_copy(
                ztb[gi % 2], ps1[:, 0:PB]), wait=(t1, ts2.get(gi - 2)), inc="ve")
        if gi >= 1:
            which1, dg1, a1, j1, gdg1, gac1 = flat[gi - 1]
            Mv = cfgs[which1][7]
            bank = c.p6 if gac1 % 2 == 0 else c.p7
            tl = None
            for tt in range(4):
                tl = p.emit("tensor", lambda e, tt=tt, bank=bank, gi=gi, j1=j1, gdg1=gdg1:
                            e.matmul(bank[:, 128 * tt:128 * (tt + 1)],
                                     ztb[(gi - 1) % 2][:, 128 * tt:128 * (tt + 1)],
                                     bb[gdg1 % 2][:, 512 * j1 + 128 * tt:512 * j1 + 128 * (tt + 1)],
                                     start=(j1 == 0 and tt == 0),
                                     stop=(j1 == 7 and tt == 3)),
                            wait=(tev[gi - 1], tbscale[gdg1],
                                  tmo_t.get(gac1 - 2)), inc="pe")
            ts2[gi - 1] = tl
            if j1 == 7:
                last_a = (gi == NI) or (flat[gi][4] != gdg1)
                if last_a:
                    tbbfree[gdg1 % 2] = tl
                    if gdg1 + 2 < len(dglist):
                        bb_load(gdg1 + 2)
                ms = gac1 % 2
                tmo = p.emit("vector", lambda e, ms=ms, bank=bank: e.tensor_copy(
                    mo[ms], bank[:, 0:PB]), wait=(tl, tstore[ms]), inc="ve")
                tmo_t[gac1] = tmo
                dst = Mv[:, 4 * dg1:4 * dg1 + 4, a1, :]
                src = mo[ms].rearrange("p (u q) -> p u q", q=128)
                tstore[ms] = p.dma("sync", dst, src, wait=(tmo,), sem="dS_mo%d" % ms)
    p.emit("sync", None, wait=(tstore[0], tstore[1]))
    p.barrier()
    c.ar.pop()


def phase_meff(c):
    emit_meff_build(c)



def emit_l0ffn(c, tag):
    p = c.prog
    nc = c.nc
    c.ar.push()
    st = alloc_stats(c)
    xnO = [c.ar.take(T, BF16, "xnO") for _ in range(8)]
    qTi = [c.ar.take(T, BF16, "qTi") for _ in range(8)]
    kxh = [c.ar.take(T, BF16, "kxh") for _ in range(2)]
    WV = 264
    Vb = c.ar.take(16 * WV, BF16, "Vb")
    sh = [c.ar.take(T, BF16, "sh") for _ in range(2)]
    at2 = c.ar.take(PB, BF16, "at2")
    sqf = c.ar.take(PB, F32, "sqf")
    qrawf = c.ar.take(PB, F32, "qrawf")
    qnf = c.ar.take(PB, F32, "qnf")
    avsf = c.ar.take(256, F32, "avsf")
    row1 = c.ar.take(PB, F32, "row1")
    row2 = c.ar.take(PB, F32, "row2")
    colsc = c.ar.take(64, F32, "colsc")
    wq2ab = c.ar.take(2048, BF16, "wq2ab")
    wq2 = [wq2ab[:, 0:1024], wq2ab[:, 1024:2048]]
    wkv = [c.ar.take(2048, BF16, "wkva")]

    emit_layernorm(c, xnO, st, scratch4=qTi[0:4])
    p.barrier()

    sfx = tag
    tg2 = p.dma("sync", c.smallf[0:2, 16:24], c.din("l0f_qgain"), sem="dL")
    tg3 = p.dma("sync", c.smallf[0:2, 24:26], c.din("l0f_kgain"), sem="dL")
    trC = p.dma("gpsimd", sh[0], c.const("l0f_ropeC"), sem="dW_rc" + sfx)
    trS = p.dma("gpsimd", sh[1], c.const("l0f_ropeS"), sem="dW_rs" + sfx)
    te6 = p.emit("vector", lambda e: e.memset(colsc[:, 0:1], 1e-6), inc="ve")
    qg2 = c.smallf[0:2, 16:24]
    kg2 = c.smallf[0:2, 24:26]
    eps6 = colsc[0:2, 0:1]

    def proj_normed(kind, tprev):
        if kind == "q":
            nco, wname, gcol, dst = 8, "l0f_wq", qg2, qTi
        else:
            nco, wname, gcol, dst = 2, "l0f_wk", kg2, kxh
        w_in = c.din(wname)
        tw = [None, None]
        guards = [tprev.get("wq0"), tprev.get("wq1")]
        tlast = None
        for oc in range(nco):
            pa = oc % 2
            wb = wq2[pa]
            tw[pa] = p.dma("gpsimd", wb.rearrange("p (i q) -> p i q", i=8),
                           w_in[:, 128 * oc:128 * (oc + 1)].rearrange("(i p) q -> p i q", p=128),
                           wait=(guards[pa],), sem=f"dW_{kind}{pa}{sfx}")
            for tgi in range(4):
                s = slice(tgi * PB, (tgi + 1) * PB)
                tq = None
                for ic in range(8):
                    tq = p.emit("tensor", lambda e, ic=ic, s=s, wb=wb:
                                e.matmul(c.p4[:, 0:PB], wb[:, 128 * ic:128 * (ic + 1)],
                                         xnO[ic][:, s], start=(ic == 0), stop=(ic == 7)),
                                wait=(tw[pa], tprev.get("p4")), inc="pe")
                guards[pa] = tq
                ts = p.emit("scalar", lambda e: e.activation(sqf, c.p4[:, 0:PB], AF.Square),
                            wait=(tq, tprev.get("sqf")), inc="se")
                tq2 = p.emit("scalar", lambda e: e.copy(qrawf, c.p4[:, 0:PB]),
                             wait=(tq, tprev.get("qrawf")), inc="se")
                tprev["p4"] = tq2
                tm = p.emit("tensor", lambda e: e.matmul(c.p5[0:2, 0:PB], c.headself[:],
                                                         sqf, start=True, stop=True),
                            wait=(ts, tprev.get("p5")), inc="pe")
                t1 = p.emit("vector", lambda e: e.tensor_scalar_mul(
                    row1[0:2, :], c.p5[0:2, 0:PB], 1.0 / 64),
                    wait=(tm, tprev.get("row1")), inc="ve")
                tprev["p5"] = t1
                t2 = p.emit("scalar", lambda e: e.activation(
                    row2[0:2, :], row1[0:2, :], AF.Ln, bias=eps6),
                    wait=(t1, te6, tprev.get("row2")), inc="se")
                t3 = p.emit("scalar", lambda e: e.activation(
                    row1[0:2, :], row2[0:2, :], AF.Exp, scale=-0.5),
                    wait=(t2,), inc="se")
                tprev["row2"] = t3
                t4 = p.emit("vector", lambda e, oc=oc, gcol=gcol: e.tensor_scalar(
                    row1[0:2, :], row1[0:2, :], gcol[:, oc:oc + 1], None, ALU.mult),
                    wait=(t3, tg2, tg3), inc="ve")
                tprev["row1"] = t4
                t5 = p.emit("tensor", lambda e: e.matmul(c.p6[:, 0:PB], c.headbcf[:],
                                                         row1[0:2, :], start=True, stop=True),
                            wait=(t4, tprev.get("p6")), inc="pe")
                t6 = p.emit("vector", lambda e: e.tensor_mul(qnf, qrawf, c.p6[:, 0:PB]),
                            wait=(t5, tq2, tprev.get("qnf")), inc="ve")
                tprev["p6"] = t6
                t7 = p.emit("tensor", lambda e: e.matmul(c.p7[:, 0:PB], c.rotTf[:], qnf,
                                                         start=True, stop=True),
                            wait=(t6, tprev.get("p7")), inc="pe")
                t8 = p.emit("vector", lambda e, s=s: e.tensor_mul(qrawf, qnf, sh[0][:, s]),
                            wait=(t6, trC, tq2, ts), inc="ve")
                tprev["qnf"] = t8
                t9 = p.emit("vector", lambda e, s=s: e.tensor_mul(sqf, c.p7[:, 0:PB], sh[1][:, s]),
                            wait=(t7, trS, ts), inc="ve")
                tprev["p7"] = t9
                tlast = p.emit("vector", lambda e, oc=oc, s=s, dst=dst: e.tensor_add(
                    dst[oc][:, s], qrawf, sqf), wait=(t8, t9), inc="ve")
                tprev["qrawf"] = tlast
                tprev["sqf"] = tlast
        tprev["wq0"], tprev["wq1"] = guards
        return tlast

    tprev = {}
    tq_done = proj_normed("q", tprev)
    tk_done = proj_normed("k", tprev)

    twv = p.dma("gpsimd", wkv[0].rearrange("p (i q) -> p i q", i=8),
                c.din("l0f_wv").rearrange("(i p) q -> p i q", p=128),
                wait=(tk_done,), sem="dW_wv" + sfx)
    tvslot = [None, None]
    tve_t = {}
    tvev = {}

    def v_evac(tt):
        off = 512 * (tt % 2)
        t1 = p.emit("vector", lambda e, tt=tt, off=off: e.tensor_copy(
            Vb[:, tt * WV:tt * WV + 260].rearrange("p (h d) -> p h d", d=65)[:, :, 0:64],
            c.pS[:, off:off + 256].rearrange("p (h d) -> p h d", d=64)),
            wait=(tve_t[tt],), inc="ve")
        tvslot[tt % 2] = t1
        tvev[tt] = p.emit("vector", lambda e, tt=tt: e.memset(
            Vb[:, tt * WV + 64: tt * WV + 261:65], 1.0), wait=(t1,), inc="ve")

    for tt in range(16):
        off = 512 * (tt % 2)
        tvm = None
        for ic in range(8):
            tvm = p.emit("tensor", lambda e, ic=ic, tt=tt, off=off: e.matmul(
                c.pS[:, off:off + 256], xnO[ic][:, 128 * tt:128 * (tt + 1)],
                wkv[0][:, 256 * ic:256 * (ic + 1)],
                start=(ic == 0), stop=(ic == 7)),
                wait=(twv, tvslot[tt % 2] if ic == 0 else None), inc="pe")
        tve_t[tt] = tvm
        if tt >= 1:
            v_evac(tt - 1)
    v_evac(15)
    tv_done = tvev[15]
    txn_free = tvev[15]
    tpS_init = (tvev[14], tvev[15])

    A4 = [sh[0], sh[1], wq2ab, wkv[0]]
    aits = [(h, tp) for h in range(16) for tp in range(8)]
    NIT = len(aits)
    scl = 0.125

    texp = {}
    tmask = {}
    tABr = {(-2, 0): tk_done, (-2, 1): tk_done,
            (-1, 0): tk_done, (-1, 1): tvev[15]}
    tsc_t = {}
    ttr = {0: tprev.get("p4"), 1: tprev.get("p7")}
    tat2r = [None, None]
    tavfin = {}
    trecip = {-2: tprev.get("p5"), -1: tprev.get("p6")}
    tavs = {}
    toutm = {}
    tbc_t = {}
    tO_last = txn_free
    tscore_last = None

    def ageo(n):
        h, tp = aits[n]
        kvh = h // 4
        return (h, tp, kvh, (h // 8) * 4 + h % 4, (kvh % 2) * 64,
                kvh // 2, (kvh % 2) * 64, 2 * tp + 2)

    def emit_scores(n, ti, qmat, qrow0, kmat, krow0, ext, exts, cm, Ab, w0):
        tsc = None
        tmq = []
        ngr = (exts + PB - 1) // PB
        qt = 2 * aits[n][1] + ti
        for g in range(ngr):
            ws = min(PB, exts - g * PB)
            tsc = p.emit("tensor", lambda e, g=g, ws=ws, qt=qt, qmat=qmat,
                         qrow0=qrow0, kmat=kmat, krow0=krow0: e.matmul(
                c.pS[:, g * PB:g * PB + ws],
                qmat[qrow0:qrow0 + 64, 128 * qt:128 * (qt + 1)],
                kmat[krow0:krow0 + 64, g * PB:g * PB + ws],
                start=True, stop=True),
                wait=w0 if g == 0 else (), inc="pe")
            tmq.append(p.emit("vector", lambda e, g=g, ws=ws, cm=cm: e.reduce_max(
                colsc[:, cm + 2 + g:cm + 3 + g], c.pS[:, g * PB:g * PB + ws], axis=AX),
                wait=(tsc, texp.get((n - 2, ti))), inc="ve"))
        tsc_t[(n, ti)] = tsc
        tmx = p.emit("vector", lambda e, cm=cm, ngr=ngr: e.reduce_max(
            colsc[:, cm:cm + 1], colsc[:, cm + 2:cm + 2 + ngr], axis=AX, negate=True),
            wait=tuple(tmq), inc="ve")
        tmx2 = p.emit("vector", lambda e, cm=cm: e.tensor_scalar_mul(
            colsc[:, cm + 1:cm + 2], colsc[:, cm:cm + 1], scl), wait=(tmx,), inc="ve")
        texp[(n, ti)] = p.emit("scalar", lambda e, exts=exts, Ab=Ab, cm=cm: e.activation(
            Ab[:, 0:exts], c.pS[:, 0:exts], AF.Exp, bias=colsc[:, cm + 1:cm + 2],
            scale=scl), wait=(tmx2, tABr[(n - 2, ti)]), inc="se")
        return tsc

    def prod_first(n):
        h, tp, kvh, qch, qoff, kch, koff, nkt = ageo(n)
        ext = 128 * nkt
        pr = n % 2
        wprev = texp.get((n - 1, 1))
        w0 = (tq_done, tk_done, wprev) if wprev is not None else \
            (tq_done, tk_done) + tpS_init
        emit_scores(n, 0, qTi[qch], qoff, kxh[kch], koff, ext,
                    ext - 128 if nkt > 1 else ext, 1 + 14 * pr, A4[2 * pr], w0)

    def prod_second(n):
        nonlocal tscore_last
        h, tp, kvh, qch, qoff, kch, koff, nkt = ageo(n)
        ext = 128 * nkt
        pr = n % 2
        Ab0, Ab1 = A4[2 * pr], A4[2 * pr + 1]
        qt0, qt1 = 2 * tp, 2 * tp + 1
        tscore_last = emit_scores(n, 1, qTi[qch], qoff, kxh[kch], koff, ext, ext,
                                  8 + 14 * pr, Ab1, (texp[(n, 0)],))
        tms0 = p.emit("gpsimd", lambda e, qt=qt0, Ab=Ab0: e.tensor_mul(
            Ab[:, 128 * qt:128 * (qt + 1)], Ab[:, 128 * qt:128 * (qt + 1)],
            c.diagmask[:]), wait=(texp[(n, 0)],), inc="ge")
        tmask[(n, 0)] = p.emit("gpsimd", lambda e, qt=qt0, ext=ext, Ab=Ab0: e.memset(
            Ab[:, 128 * (qt + 1):ext], 0.0), wait=(tms0,), inc="ge")
        tmask[(n, 1)] = p.emit("gpsimd", lambda e, qt=qt1, Ab=Ab1: e.tensor_mul(
            Ab[:, 128 * qt:128 * (qt + 1)], Ab[:, 128 * qt:128 * (qt + 1)],
            c.diagmask[:]), wait=(texp[(n, 1)],), inc="ge")


    cstate = {}

    def cons_kt(n, kt, nkt, kvh, pr):
        Ab0, Ab1 = A4[2 * pr], A4[2 * pr + 1]
        tb = c.p4 if kt % 2 == 0 else c.p7
        to1 = p.emit("tensor", lambda e, kt=kt, tb=tb, Ab0=Ab0: e.matmul(
            tb[:, 0:128], Ab0[:, 128 * kt:128 * (kt + 1)], c.ident[:],
            start=True, stop=True), wait=(tmask[(n, 0)], ttr[kt % 2]), inc="pe")
        to2 = p.emit("tensor", lambda e, kt=kt, tb=tb, Ab1=Ab1: e.matmul(
            tb[:, 128:256], Ab1[:, 128 * kt:128 * (kt + 1)], c.ident[:],
            start=True, stop=True), wait=(tmask[(n, 1)],), inc="pe")
        tABr[(n, 0)] = to1
        tABr[(n, 1)] = to2
        if kt % 2 == 0:
            tev = p.emit("vector", lambda e, kt=kt, tb=tb: e.tensor_copy(
                at2[:, 256 * (kt % 2):256 * (kt % 2) + 256], tb[:, 0:256]),
                wait=(to1, to2, tat2r[kt % 2]), inc="ve")
        else:
            tev = p.emit("scalar", lambda e, kt=kt, tb=tb: e.copy(
                at2[:, 256 * (kt % 2):256 * (kt % 2) + 256], tb[:, 0:256]),
                wait=(to1, to2, tat2r[kt % 2]), inc="se")
        ttr[kt % 2] = tev
        cstate[(n, kt)] = tev

    def cons_av(n, kt, nkt, kvh, pr):
        avbank = c.p5 if pr == 0 else c.p6
        w = [cstate.pop((n, kt)), tv_done]
        if kt == 0:
            w += [trecip.get(n - 2), tavs.get(n - 2)]
        tav = p.emit("tensor", lambda e, kt=kt, kvh=kvh, nkt=nkt, avbank=avbank: e.matmul(
            avbank[0:65, 0:256], Vb[:, kt * WV + kvh * 65: kt * WV + kvh * 65 + 65],
            at2[:, 256 * (kt % 2):256 * (kt % 2) + 256],
            start=(kt == 0), stop=(kt == nkt - 1)), wait=tuple(w), inc="pe")
        tat2r[kt % 2] = tav
        if kt == nkt - 1:
            tavfin[n] = tav

    def cons(n, half):
        h, tp, kvh, qch, qoff, kch, koff, nkt = ageo(n)
        pr = n % 2
        h1 = (nkt + 1) // 2
        kts = range(0, h1) if half == 0 else range(h1, nkt)
        for kt in kts:
            cons_kt(n, kt, nkt, kvh, pr)
            if kt >= 1:
                cons_av(n, kt - 1, nkt, kvh, pr)
        if half == 1:
            cons_av(n, nkt - 1, nkt, kvh, pr)
            avbank = c.p5 if pr == 0 else c.p6
            rsl = row1[0:1, 256 * pr:256 * pr + 256]
            tln = p.emit("scalar", lambda e, rsl=rsl, avbank=avbank: e.activation(
                rsl, avbank[64:65, 0:256], AF.Ln),
                wait=(tavfin[n], tbc_t.get(n - 2)), inc="se")
            tiv = p.emit("scalar", lambda e, rsl=rsl: e.activation(
                rsl, rsl, AF.Exp, scale=-1.0), wait=(tln,), inc="se", drain=True)
            trecip[n] = tiv
            avs = avsf[0:64, 0:256] if pr == 0 else row2[0:64, 0:256]
            tas = p.emit("scalar", lambda e, avs=avs, avbank=avbank: e.copy(
                avs, avbank[0:64, 0:256]),
                wait=(tavfin[n], toutm.get(n - 2)), inc="se")
            tavs[n] = tas

    def cons_tail(n):
        h, tp, kvh, qch, qoff, kch, koff, nkt = ageo(n)
        pr = n % 2
        avbank = c.p5 if pr == 0 else c.p6
        rsl = row1[0:1, 256 * pr:256 * pr + 256]
        avs = avsf[0:64, 0:256] if pr == 0 else row2[0:64, 0:256]
        tbc = p.emit("tensor", lambda e, rsl=rsl, avbank=avbank: e.matmul(
            avbank[0:64, 256:512], c.onesrowf[0:1, 0:64], rsl,
            start=True, stop=True),
            wait=(trecip[n], tavs[n], toutm.get(n - 2)), inc="pe")
        tbc_t[n] = tbc
        toutm[n] = p.emit("vector", lambda e, tp=tp, qch=qch, qoff=qoff,
                          avs=avs, avbank=avbank: e.tensor_mul(
            xnO[qch][qoff:qoff + 64, 256 * tp:256 * (tp + 1)],
            avs, avbank[0:64, 256:512]), wait=(tbc,), inc="ve")

    for n in range(NIT + 2):
        if n < NIT:
            prod_first(n)
        if n >= 2:
            cons_tail(n - 2)
        if 1 <= n <= NIT:
            cons(n - 1, 0)
        if n < NIT:
            prod_second(n)
        if 1 <= n <= NIT:
            cons(n - 1, 1)
    tO_done = toutm[NIT - 1]

    w_in = c.din("l0f_wo")
    tw = [None, None]
    twg = [tO_done, tO_done]
    tup = {"p4": tO_done, "p5": tO_done, "p6": tO_done}
    tinner_last = None
    tu5 = None
    for oc in range(8):
        pa = oc % 2
        tw[pa] = p.dma("gpsimd", wq2[pa].rearrange("p (i q) -> p i q", i=8),
                       w_in[:, 128 * oc:128 * (oc + 1)].rearrange("(i p) q -> p i q", p=128),
                       wait=(twg[pa],), sem=f"dW_wo{pa}{sfx}")
        for tgi in range(4):
            s = slice(tgi * PB, (tgi + 1) * PB)
            tb1 = p.emit("tensor", lambda e, s=s: e.matmul(
                c.p5[:, 0:PB], c.onesrowf[:], st["nmu"][0:1, s], start=True, stop=True),
                wait=(tup.get("p5"),), inc="pe")
            tb2 = p.emit("tensor", lambda e, s=s: e.matmul(
                c.p6[:, 0:PB], c.onesrowf[:], st["rstd"][0:1, s], start=True, stop=True),
                wait=(tup.get("p6"),), inc="pe")
            tmm = None
            for ic in range(8):
                tmm = p.emit("tensor", lambda e, ic=ic, s=s, pa=pa: e.matmul(
                    c.p4[:, 0:PB], wq2[pa][:, 128 * ic:128 * (ic + 1)], xnO[ic][:, s],
                    start=(ic == 0), stop=(ic == 7)),
                    wait=(tw[pa], tup.get("p4"), tO_done), inc="pe")
            twg[pa] = tmm
            ft = c.ftmp[pa]
            tu1 = p.emit("vector", lambda e, oc=oc, s=s, ft=ft: e.tensor_add(
                ft[:], c.X[oc][:, s], c.p5[:, 0:PB]),
                wait=(tb1, tup.get("ft%d" % pa)), inc="ve")
            tup["p5"] = tu1
            tu2 = p.emit("vector", lambda e, ft=ft: e.tensor_mul(
                ft[:], ft[:], c.p6[:, 0:PB]), wait=(tb2, tu1), inc="ve")
            tup["p6"] = tu2
            tu3 = p.emit("vector", lambda e, ft=ft: e.tensor_add(
                ft[:], ft[:], c.p4[:, 0:PB]), wait=(tmm, tu2), inc="ve")
            tup["p4"] = tu3
            ti4 = p.emit("vector", lambda e, oc=oc, s=s, ft=ft: e.tensor_copy(
                qTi[oc][:, s], ft[:]), wait=(tu3, tscore_last), inc="ve")
            tinner_last = ti4
            tu5 = p.emit("vector", lambda e, oc=oc, s=s, ft=ft: e.scalar_tensor_tensor(
                c.X[oc][:, s], ft[:], 0.5, c.X[oc][:, s], ALU.mult, ALU.add),
                wait=(ti4,), inc="ve")
            tup["ft%d" % pa] = tu5
    p.barrier()

    inner = qTi
    mfc = c.scratch("Mfc_t", (128, 32768), BF16)
    mpj = c.scratch("Mpj_t", (128, 32768), BF16)
    msq_row = st["mu"][0:1, :]
    rms_row = st["rstd"][0:1, :]
    tmsq = None
    tsq_guard = [None, None]
    sqbf = [A4[0][:, 0:PB], A4[0][:, PB:2 * PB]]
    for chn in range(8):
        pa = chn % 2
        sb = sqbf[pa]
        for g in range(4):
            sg = slice(g * PB, (g + 1) * PB)
            tsq = p.emit("scalar", lambda e, chn=chn, sg=sg, sb=sb: e.activation(
                sb, inner[chn][:, sg], AF.Square), wait=(tinner_last, tsq_guard[pa]), inc="se")
            tmsq = p.emit("tensor", lambda e, g=g, chn=chn, sb=sb: e.matmul(
                c.pS[0:1, g * PB:(g + 1) * PB], c.onescol[:], sb,
                start=(chn == 0), stop=(chn == 7)), wait=(tsq,), inc="pe")
            tsq_guard[pa] = tmsq
    t = p.emit("vector", lambda e: e.tensor_scalar_mul(msq_row, c.pS[0:1, 0:4 * PB], 1.0 / D),
               wait=(tmsq,), inc="ve")
    t = p.emit("scalar", lambda e: e.activation(st["nmu"][0:1, :], msq_row, AF.Ln, bias=eps6[0:1, 0:1]),
               wait=(t,), inc="se")
    trms = p.emit("scalar", lambda e: e.activation(rms_row, st["nmu"][0:1, :], AF.Exp, scale=-0.5),
                  wait=(t,), inc="se")

    def xh_ap(ic):
        return kxh[ic // 4][:, (ic % 4) * PB:(ic % 4 + 1) * PB]

    u2v = []
    for uc in range(32):
        u2v.append(xnO[uc // 4][:, (uc % 4) * PB:(uc % 4 + 1) * PB])

    NMR = 8
    mring = [Vb[:, 1024 * i:1024 * (i + 1)] for i in range(4)] + \
            [A4[2][:, 0:1024], A4[2][:, 1024:2048],
             A4[3][:, 0:1024], A4[3][:, 1024:2048]]
    mloads = []
    for tgi in range(4):
        for uc in range(32):
            mloads.append(mfc[:, 1024 * uc:1024 * (uc + 1)])
        for oc in range(8):
            for ug4 in range(4):
                mloads.append(mpj[:, 4096 * oc + 1024 * ug4:4096 * oc + 1024 * (ug4 + 1)])
    tmld = {}
    tmread = {}
    nxt = [0]

    def m_load():
        k = nxt[0]
        nxt[0] += 1
        sl = k % NMR
        tmld[k] = p.dma("sync", mring[sl], mloads[k],
                        wait=(tmread.get(k - NMR, tO_done),), sem=f"dM{sl}{sfx}")

    for _ in range(6):
        m_load()

    sq2 = [sqf, qnf]
    tmg = {"b0": trms, "b1": trms, "b2": trms, "b3": trms,
           "sq0": None, "sq1": None}
    tX_last = tu5
    for tgi in range(4):
        s = slice(tgi * PB, (tgi + 1) * PB)
        tb = p.emit("tensor", lambda e, s=s: e.matmul(
            c.p7[:, 0:PB], c.onesrowf[:], rms_row[0:1, s], start=True, stop=True),
            wait=(trms, tmg.get("p7")), inc="pe")
        txh = None
        for ic in range(8):
            txh = p.emit("vector", lambda e, ic=ic, s=s: e.tensor_mul(
                xh_ap(ic), inner[ic][:, s], c.p7[:, 0:PB]),
                wait=(tb, tmg.get("xh"), tscore_last), inc="ve")
        tmg["p7"] = txh
        tlast_u2 = None
        for uc in range(32):
            k = tgi * 64 + uc
            sl = k % NMR
            bi = uc % 4
            bk = c.pb[bi]
            sb = sq2[uc % 2]
            tfc = None
            for ic in range(8):
                tfc = p.emit("tensor", lambda e, ic=ic, sl=sl, bk=bk: e.matmul(
                    bk, mring[sl][:, 128 * ic:128 * (ic + 1)], xh_ap(ic),
                    start=(ic == 0), stop=(ic == 7)),
                    wait=(txh, tmld[k], tmg.get("b%d" % bi)), inc="pe")
            tmread[k] = tfc
            if nxt[0] < len(mloads):
                m_load()
            trl = p.emit("scalar", lambda e, bk=bk, sb=sb: e.activation(sb, bk, AF.Relu),
                         wait=(tfc, tmg.get("sq%d" % (uc % 2))), inc="se")
            tu2 = p.emit("gpsimd", lambda e, uc=uc, sb=sb: e.tensor_mul(
                u2v[uc], sb, sb), wait=(trl, tmg.get("u2")), inc="ge")
            tmg["sq%d" % (uc % 2)] = tu2
            tmg["b%d" % bi] = tu2
            tlast_u2 = tu2
        for oc in range(8):
            pa = oc % 2
            bk = c.p5 if pa == 0 else c.p6
            tpj = None
            for ug4 in range(4):
                k = tgi * 64 + 32 + oc * 4 + ug4
                sl = k % NMR
                for du in range(8):
                    uc = ug4 * 8 + du
                    tpj = p.emit("tensor", lambda e, du=du, sl=sl, bk=bk, uc=uc: e.matmul(
                        bk[:, 0:PB], mring[sl][:, 128 * du:128 * (du + 1)],
                        u2v[uc], start=(uc == 0), stop=(uc == 31)),
                        wait=(tmld[k], tlast_u2, tmg.get("pj%d" % pa)), inc="pe")
                tmread[k] = tpj
                if nxt[0] < len(mloads):
                    m_load()
            tX = p.emit("vector", lambda e, oc=oc, s=s, bk=bk: e.scalar_tensor_tensor(
                c.X[oc][:, s], bk[:, 0:PB], 0.5, c.X[oc][:, s], ALU.mult, ALU.add),
                wait=(tpj, tX_last), inc="ve")
            tmg["pj%d" % pa] = tX
            tX_last = tX
        tmg["xh"] = tX_last
        tmg["u2"] = tX_last
    p.barrier()
    c.ar.pop()


def phase_l0ffn(c):
    emit_l0ffn(c, "l0f")



def emit_l0attn(c):
    p = c.prog
    nc = c.nc
    c.ar.push()
    xn = [c.ar.take(T, BF16, "a_xn") for _ in range(8)]
    c.ar.push()
    st = alloc_stats(c)
    emit_layernorm(c, xn, st)
    c.ar.pop()
    p.barrier()
    yg = [c.ar.take(T, BF16, "a_yg") for _ in range(8)]
    zin = c.ar.take(T + 16, F32, "a_zin")
    xval = c.ar.take(T, F32, "a_xval")
    W1 = c.ar.take(T, F32, "a_w1")
    W2 = c.ar.take(T, F32, "a_w2")
    W3 = c.ar.take(T, F32, "a_w3")
    tA_ = c.ar.take(T, F32, "a_p")
    tbf = c.ar.take(2 * T, BF16, "a_pbf")
    replc = c.ar.take(2048, BF16, "a_replc")
    wre2 = c.ar.take(64, BF16, "a_wre2")
    wim2 = c.ar.take(64, BF16, "a_wim2")
    scrb = [zin[:, 0:256].bitcast(BF16), zin[:, 256:512].bitcast(BF16)]
    wst = [c.ar.take(1024, BF16, "a_ws0"), c.ar.take(1024, BF16, "a_ws1")]
    wrow = c.smallf[:, 96:160]
    pA = tA_[0:32, :]
    pB = xval[0:32, :]
    pbf = tbf[0:32, 0:T]
    invbf = tbf[0:32, T:2 * T]

    inproj = c.din("l0a_inproj")
    CQ = c.smallf
    t1 = p.emit("vector", lambda e: e.memset(CQ[:, 13:14], 1e-5), inc="ve")
    t1b = p.emit("vector", lambda e: e.memset(CQ[:, 14:15], 1.0), wait=(t1,), inc="ve")
    trepl = p.dma("gpsimd", replc[0:32, :], c.const("replc_bf"), sem="dWrp")
    ttrow = p.dma("sync", c.ftmp[0][0:1, 0:PB] if False else wrow[0:1, 0:3], c.din("l0a_dlog"), sem="dL")

    r = wrow
    t = p.emit("vector", lambda e: e.reduce_max(r[0:1, 8:9], r[0:1, 0:3], axis=AX),
               wait=(ttrow,), inc="ve")
    t = p.emit("vector", lambda e: e.tensor_scalar_mul(r[0:1, 9:10], r[0:1, 8:9], -1.0),
               wait=(t,), inc="ve")
    t = p.emit("scalar", lambda e: e.activation(r[0:1, 4:7], r[0:1, 0:3], AF.Exp,
                                                bias=r[0:1, 9:10]), wait=(t,), inc="se")
    t = p.emit("vector", lambda e: e.reduce_sum(r[0:1, 8:9], r[0:1, 4:7], axis=AX),
               wait=(t,), inc="ve")
    t = p.emit("vector", lambda e: e.reciprocal(r[0:1, 9:10], r[0:1, 8:9]),
               wait=(t,), inc="ve")
    t = p.emit("vector", lambda e: e.tensor_scalar(r[0:1, 4:7], r[0:1, 4:7],
                                                   r[0:1, 9:10], None, ALU.mult),
               wait=(t,), inc="ve")
    t = p.emit("vector", lambda e: e.tensor_scalar_mul(r[0:1, 6:7], r[0:1, 6:7], -1.0),
               wait=(t,), inc="ve")
    tbw = p.emit("tensor", lambda e: e.matmul(c.p7[:, 0:3], c.onesrowf[:], r[0:1, 4:7],
                                              start=True, stop=True), wait=(t,), inc="pe")
    twcols = p.emit("vector", lambda e: e.tensor_copy(CQ[:, 10:13], c.p7[:, 0:3]),
                    wait=(tbw,), inc="ve")

    tdec = p.dma("sync", r[0:1, 32:60], c.din("l0a_decay"), wait=(t,), sem="dL")
    tanc = p.dma("sync", r[0:1, 60:64], c.din("l0a_anchor"), wait=(tdec,), sem="dL")
    t = p.emit("scalar", lambda e: e.activation(r[0:1, 0:32], r[0:1, 32:64], AF.Abs),
               wait=(tanc, twcols), inc="se")
    t = p.emit("scalar", lambda e: e.activation(r[0:1, 0:32], r[0:1, 0:32], AF.Exp,
                                                scale=-1.0), wait=(t,), inc="se", drain=True)
    t = p.emit("scalar", lambda e: e.activation(r[0:1, 0:32], r[0:1, 0:32], AF.Ln,
                                                bias=CQ[0:1, 14:15]), wait=(t1b, t), inc="se",
               drain=True)
    t2 = p.emit("vector", lambda e: e.tensor_scalar_max(r[0:1, 32:64], r[0:1, 32:64], 0.0),
                wait=(t,), inc="ve")
    tsp = p.emit("vector", lambda e: e.tensor_add(r[0:1, 32:64], r[0:1, 32:64], r[0:1, 0:32]),
                 wait=(t2,), inc="ve")
    t = p.emit("vector", lambda e: e.memset(r[0:1, 0:32], 0.0), wait=(tsp,), inc="ve")
    t = p.emit("vector", lambda e: e.tensor_copy(r[0:1, 0:28], r[0:1, 32:60]),
               wait=(t,), inc="ve")
    tspd = p.emit("vector", lambda e: e.memset(r[0:1, 32:60], 0.0), wait=(tsp, t), inc="ve")
    tdr = p.dma("sync", c.ftmp[0][0:1, 0:T] if False else zin[0:1, 0:T], c.const("drow_f32"), sem="dL")
    ttr = p.dma("sync", zin[1:2, 0:T] if False else xval[0:1, 0:T], c.const("trow_f32"), sem="dL")
    targ = None
    for g in range(4):
        s = slice(g * PB, (g + 1) * PB)
        p.emit("tensor", lambda e, s=s: e.matmul(
            c.pS[0:32, s.start:s.stop], r[0:1, 0:32], zin[0:1, s],
            start=True, stop=False), wait=(t, tdr, targ), inc="pe")
        targ = p.emit("tensor", lambda e, s=s: e.matmul(
            c.pS[0:32, s.start:s.stop], r[0:1, 32:64], xval[0:1, s],
            start=False, stop=True), wait=(tspd, ttr), inc="pe")
    ttw = p.emit("scalar", lambda e: e.activation(pB, c.pS[0:32, 0:T], AF.Exp, scale=-1.0),
                 wait=(targ,), inc="se")
    tzs = emit_inproj_chunk(c, zin, xn, 4096, wst[0], "dWsr", wait=(ttw,), rows=32)
    tck = p.dma("sync", CQ[0:32, 4:8], c.din("l0a_convk")[4096:4128, :],
                sem="dL")
    tcb = p.dma("sync", CQ[0:32, 8:9], c.din("l0a_convb")[4096:4128, :],
                sem="dL")
    tss = p.dma("sync", CQ[0:32, 9:10], c.din("l0a_sscale"), sem="dL")
    tcv = emit_conv4(c, pA, zin[0:32, :], CQ[0:32, 4:8], CQ[0:32, 8:9],
                     wait=(tzs, tck, tcb), rows=32)
    t = p.emit("vector", lambda e: e.tensor_scalar(pA, pA, CQ[0:32, 9:10], None,
                                                   ALU.mult),
               wait=(tcv, tss), inc="ve", drain=True)
    t = p.emit("vector", lambda e: e.tensor_scalar_min(pA, pA, 20.0), wait=(t,), inc="ve")
    t = p.emit("vector", lambda e: e.tensor_scalar_max(pA, pA, -20.0), wait=(t,), inc="ve")
    t = p.emit("scalar", lambda e: e.activation(pA, pA, AF.Exp), wait=(t,), inc="se")
    tp_ = p.emit("vector", lambda e: e.tensor_mul(pA, pA, pB), wait=(t, ttw), inc="ve")
    tone = p.emit("vector", lambda e: e.memset(W1[0:32, :], 1.0), inc="ve")
    t = p.emit("vector", lambda e: e.tensor_tensor_scan(pB, W1[0:32, :], pA, 0.0,
                                                        ALU.mult, ALU.add),
               wait=(tp_, tone, ttw), inc="ve", drain=True)
    t = p.emit("vector", lambda e: e.tensor_scalar_max(pB, pB, 1e-4), wait=(t,), inc="ve",
               drain=True)
    t = p.emit("vector", lambda e: e.reciprocal(pB, pB), wait=(t,), inc="ve", drain=True)
    tinv = p.emit("vector", lambda e: e.tensor_copy(invbf, pB), wait=(t,), inc="ve")
    tpbf = p.emit("vector", lambda e: e.tensor_copy(pbf, pA), wait=(tp_,), inc="ve")

    tnre = p.dma("sync", CQ[:, 15:16], c.din("l0a_nsc_re"), sem="dL")
    tnim = p.dma("sync", CQ[:, 16:17], c.din("l0a_nsc_im"), sem="dL")
    twr1 = p.dma("sync", W2[0:64, 0:64].bitcast(F32) if False else W2[0:64, 0:64], c.din("l0a_wre"), sem="dL")
    twr2 = p.dma("sync", W2[64:128, 0:64], c.din("l0a_wre"), sem="dL")
    twi1 = p.dma("sync", W3[0:64, 0:64], c.din("l0a_wim"), sem="dL")
    twi2 = p.dma("sync", W3[64:128, 0:64], c.din("l0a_wim"), sem="dL")
    t = p.emit("vector", lambda e: e.tensor_scalar(W2[:, 0:64], W2[:, 0:64],
                                                   CQ[:, 15:16], None, ALU.mult),
               wait=(twr2, tnre), inc="ve")
    twre = p.emit("vector", lambda e: e.tensor_copy(wre2, W2[:, 0:64]), wait=(t,), inc="ve")
    t = p.emit("vector", lambda e: e.tensor_scalar(W3[:, 0:64], W3[:, 0:64],
                                                   CQ[:, 16:17], None, ALU.mult),
               wait=(twi2, tnim), inc="ve")
    twim = p.emit("vector", lambda e: e.tensor_copy(wim2, W3[:, 0:64]), wait=(t,), inc="ve")
    p.barrier()

    opj = c.din("l0a_outproj")
    tw_g = [None, None]
    W4 = c.ar.take(1024, F32, "a_w4")
    scrb = [c.ar.take(512, BF16, "a_sc0"), c.ar.take(512, BF16, "a_sc1")]
    xvbuf = [xval, tA_]
    GT = (4, 17)
    VT = (23, 28)
    TH = (33, 34)

    for half in range(2):
        gg = {}
        st1 = {}
        tzpad = p.emit("vector", lambda e: e.memset(zin[:, 0:3], 0.0), inc="ve")
        gg["zin"] = tzpad

        def inproj2(col0, wbuf, sem, wguard, zguard):
            tw = p.dma("gpsimd", wbuf.rearrange("p (i q) -> p i q", i=8),
                       inproj[:, col0:col0 + 128].rearrange("(i p) q -> p i q", p=128),
                       wait=(wguard,), sem=sem)
            gtoks = []
            tmlast = None
            for g in range(4):
                s = slice(g * PB, (g + 1) * PB)
                po = (g % 2) * PB
                tm = None
                for ic in range(8):
                    tm = p.emit("tensor", lambda e, ic=ic, s=s, po=po, wbuf=wbuf:
                                e.matmul(c.pS[:, po:po + PB],
                                         wbuf[:, 128 * ic:128 * (ic + 1)], xn[ic][:, s],
                                         start=(ic == 0), stop=(ic == 7)),
                                wait=(tw, gg.get("pS%d" % (g % 2)), zguard), inc="pe")
                tmlast = tm
                tz = p.emit("scalar", lambda e, s=s, po=po: e.copy(
                    zin[:, 3 + s.start:3 + s.stop], c.pS[:, po:po + PB]),
                    wait=(tm, zguard), inc="se")
                gg["pS%d" % (g % 2)] = tz
                gtoks.append(tz)
            return gtoks, tmlast

        def conv_pool(out, lo, hi, taps, bias, wait):
            t = p.emit("vector", lambda e, out=out, lo=lo, hi=hi, taps=taps: e.tensor_scalar(
                out, zin[:, lo:hi], taps[:, 0:1], None, ALU.mult),
                wait=wait, inc="ve")
            for tau in range(1, 4):
                t = p.emit("vector", lambda e, out=out, lo=lo, hi=hi, taps=taps, tau=tau:
                           e.scalar_tensor_tensor(out, zin[:, lo + tau:hi + tau],
                                                  taps[:, tau:tau + 1], out,
                                                  ALU.mult, ALU.add),
                           wait=(t,), inc="ve", drain=True)
            t = p.emit("vector", lambda e, out=out, bias=bias: e.tensor_scalar(
                out, out, bias[:, 0:1], None, ALU.add), wait=(t,), inc="ve", drain=True)
            return t

        def stage1(hp):
            hpi = hp % 8
            par = hp % 2
            gch = 2048 + 128 * hp
            vch = 128 * hp
            gt0, vt0, th = GT[par], VT[par], TH[par]
            sem = "dLck%d" % par
            tck1 = p.dma("sync", CQ[:, gt0:gt0 + 4], c.din("l0a_convk")[gch:gch + 128, :],
                         wait=(gg.get("tapg%d" % par),), sem=sem)
            tcb1 = p.dma("sync", CQ[:, gt0 + 4:gt0 + 5],
                         c.din("l0a_convb")[gch:gch + 128, :], wait=(tck1,), sem=sem)
            tck2 = p.dma("sync", CQ[:, vt0:vt0 + 4], c.din("l0a_convk")[vch:vch + 128, :],
                         wait=(gg.get("tapv%d" % par), tcb1), sem=sem)
            tcb2 = p.dma("sync", CQ[:, vt0 + 4:vt0 + 5],
                         c.din("l0a_convb")[vch:vch + 128, :], wait=(tck2,), sem=sem)
            tth = p.dma("sync", CQ[:, th:th + 1], c.din("l0a_theta")[vch:vch + 128, :],
                        wait=(gg.get("th%d" % par), tcb2), sem=sem)
            gz, wl = inproj2(gch, wst[0], "dWip0", gg.get("wg0"), gg.get("zin"))
            gg["wg0"] = wl
            tsilu = []
            tcvg = None
            for g in range(4):
                sg = slice(g * PB, (g + 1) * PB)
                w4 = W4[:, (g % 2) * PB:(g % 2) * PB + PB]
                tcv = conv_pool(w4, sg.start, sg.stop, CQ[:, gt0:gt0 + 5],
                                CQ[:, gt0 + 4:gt0 + 5],
                                (gz[g], tck1, tcb1, tcvg,
                                 tsilu[g - 2] if g >= 2 else gg.get("w4%d" % (g % 2))))
                tcvg = tcv
                tsl = p.emit("scalar", lambda e, sg=sg, w4=w4, hpi=hpi: e.activation(
                    yg[hpi][:, sg], w4, AF.Silu), wait=(tcv,), inc="se")
                tsilu.append(tsl)
                gg["w4%d" % (g % 2)] = tsl
            gg["tapg%d" % par] = tcvg
            vz, wl2 = inproj2(vch, wst[1], "dWip1", gg.get("wg1"), tcvg)
            gg["wg1"] = wl2
            gg["zin"] = None
            xv = xvbuf[par]
            tcv2 = conv_pool(xv, 0, T, CQ[:, vt0:vt0 + 5], CQ[:, vt0 + 4:vt0 + 5],
                             (vz[3], tck2, tcb2, gg.get("xv%d" % par)))
            gg["tapv%d" % par] = tcv2
            gg["zin"] = tcv2
            st1[hp] = (tcv2, tsilu, tth, th)

        def stage2(hp):
            hpi = hp % 8
            par = hp % 2
            tcv2, tsilu, tth, th = st1.pop(hp)
            xv = xvbuf[par]
            gd = {"p4b": gg.get("p4"), "p5g": gg.get("p5"), "p6g": gg.get("p6"),
                  "p7g": gg.get("p7"), "rowa": gg.get("rowa"),
                  "sb0": gg.get("scrb"), "sb1": gg.get("scrb")}
            t = p.emit("vector", lambda e, xv=xv, th=th: e.tensor_scalar(
                W1, xv, CQ[:, th:th + 1], None, ALU.mult),
                wait=(tcv2, tth, gg.get("W1")), inc="ve")
            gg["th%d" % par] = t
            t = p.emit("vector", lambda e: e.tensor_scalar_mul(W2, W1, 1.0 / (2 * np.pi)),
                       wait=(t, gg.get("W2")), inc="ve")
            ti = p.emit("vector", lambda e: e.tensor_copy(W3.bitcast(I32), W2), wait=(t, gg.get("W3")),
                        inc="ve", drain=True)
            t = p.emit("vector", lambda e: e.tensor_copy(W1, W3.bitcast(I32)), wait=(ti,),
                       inc="ve", drain=True)
            t = p.emit("gpsimd", lambda e: e.tensor_sub(W2, W2, W1), wait=(t,), inc="ge")
            t = p.emit("vector", lambda e: e.tensor_scalar_mul(W2, W2, 2 * np.pi),
                       wait=(t,), inc="ve")
            t = p.emit("vector", lambda e: e.tensor_scalar(W1, W2, np.pi, None, ALU.is_gt),
                       wait=(t,), inc="ve")
            t = p.emit("vector", lambda e: e.tensor_scalar_mul(W1, W1, -2 * np.pi),
                       wait=(t,), inc="ve")
            t = p.emit("gpsimd", lambda e: e.tensor_add(W2, W2, W1), wait=(t,), inc="ge")
            t = p.emit("vector", lambda e: e.tensor_scalar(W1, W2, -np.pi, None, ALU.is_lt),
                       wait=(t,), inc="ve")
            t = p.emit("vector", lambda e: e.tensor_scalar_mul(W1, W1, 2 * np.pi),
                       wait=(t,), inc="ve")
            t = p.emit("gpsimd", lambda e: e.tensor_add(W2, W2, W1), wait=(t,), inc="ge")
            tsn = p.emit("scalar", lambda e: e.activation(W3, W2, AF.Sin), wait=(t,), inc="se")
            t2 = p.emit("vector", lambda e: e.tensor_scalar_add(W1, W2, np.pi / 2),
                        wait=(tsn,), inc="ve")
            t2 = p.emit("vector", lambda e: e.tensor_scalar(W2, W1, np.pi, None, ALU.is_gt),
                        wait=(t2,), inc="ve")
            t2 = p.emit("vector", lambda e: e.tensor_scalar_mul(W2, W2, -2 * np.pi),
                        wait=(t2,), inc="ve")
            t2 = p.emit("gpsimd", lambda e: e.tensor_add(W1, W1, W2), wait=(t2,), inc="ge")
            tcs = p.emit("scalar", lambda e: e.activation(W2, W1, AF.Sin), wait=(t2,), inc="se")
            tq = p.emit("scalar", lambda e, xv=xv: e.activation(W1, xv, AF.Square),
                        wait=(t2, tcs), inc="se")
            t3 = p.emit("vector", lambda e, xv=xv: e.tensor_scalar(xv, xv, CQ[:, 11:12],
                                                                   CQ[:, 10:11], ALU.mult, ALU.add),
                        wait=(tq, twcols), inc="ve", drain=True)
            t3 = p.emit("vector", lambda e, xv=xv: e.scalar_tensor_tensor(
                xv, W1, CQ[:, 12:13], xv, ALU.mult, ALU.add), wait=(t3,), inc="ve")
            t4 = p.emit("gpsimd", lambda e, xv=xv: e.tensor_mul(W2, W2, xv), wait=(t3, tcs), inc="ge")
            t5 = p.emit("gpsimd", lambda e, xv=xv: e.tensor_mul(W3, W3, xv), wait=(t3, tsn, t4), inc="ge")
            tlast = None
            for g in range(4):
                s = slice(g * PB, (g + 1) * PB)
                tb = p.emit("tensor", lambda e, s=s, hp=hp: e.matmul(
                    c.p4[:, 0:PB], replc[0:32, 128 * hp:128 * (hp + 1)], pbf[:, s],
                    start=True, stop=True), wait=(trepl, tpbf, t4, t5, gd.get("p4b")),
                    inc="pe")
                t6 = p.emit("vector", lambda e, s=s: e.tensor_mul(W2[:, s], W2[:, s],
                                                                 c.p4[:, 0:PB]),
                            wait=(tb,), inc="ve")
                tlast = p.emit("vector", lambda e, s=s: e.tensor_mul(W3[:, s], W3[:, s],
                                                                    c.p4[:, 0:PB]),
                               wait=(tb, t6), inc="ve")
                gd["p4b"] = tlast
            tone2 = p.emit("vector", lambda e: e.memset(W1, 1.0), wait=(tlast,), inc="ve")
            tsc1 = p.emit("vector", lambda e, xv=xv: e.tensor_tensor_scan(xv, W1, W2, 0.0,
                                                                          ALU.mult, ALU.add),
                          wait=(tone2,), inc="ve", drain=True)
            tsc2 = p.emit("vector", lambda e: e.tensor_tensor_scan(W2, W1, W3, 0.0,
                                                                   ALU.mult, ALU.add),
                          wait=(tsc1,), inc="ve", drain=True)
            gg["W3"] = tsc2
            tl2 = None
            for g in range(4):
                s = slice(g * PB, (g + 1) * PB)
                tb = p.emit("tensor", lambda e, s=s, hp=hp: e.matmul(
                    c.p4[:, 0:PB], replc[0:32, 128 * hp:128 * (hp + 1)], invbf[:, s],
                    start=True, stop=True), wait=(tinv, tsc2, gd.get("p4c")), inc="pe")
                t6 = p.emit("vector", lambda e, s=s, xv=xv: e.tensor_mul(xv[:, s], xv[:, s],
                                                                        c.p4[:, 0:PB]),
                            wait=(tb, tsc1), inc="ve")
                tl2 = p.emit("vector", lambda e, s=s: e.tensor_mul(W2[:, s], W2[:, s],
                                                                  c.p4[:, 0:PB]),
                             wait=(tb, t6), inc="ve")
                gd["p4c"] = tl2
            gg["p4"] = tl2
            for g in range(4):
                s = slice(g * PB, (g + 1) * PB)
                tq1 = p.emit("scalar", lambda e, s=s, xv=xv: e.activation(W1[:, 0:PB], xv[:, s],
                                                                          AF.Square),
                             wait=(tl2, gd.get("W1s"),), inc="se")
                tm1 = p.emit("tensor", lambda e: e.matmul(c.p5[0:2, 0:PB], c.headself[:],
                                                          W1[:, 0:PB], start=True, stop=False),
                             wait=(tq1, gd.get("p5g")), inc="pe")
                tq2 = p.emit("scalar", lambda e, s=s: e.activation(W1[:, 0:PB], W2[:, s],
                                                                   AF.Square),
                             wait=(tm1,), inc="se")
                tm2 = p.emit("tensor", lambda e: e.matmul(c.p5[0:2, 0:PB], c.headself[:],
                                                          W1[:, 0:PB], start=False, stop=True),
                             wait=(tq2,), inc="pe")
                gd["W1s"] = tm2
                gg["W1"] = tm2
                tr = p.emit("vector", lambda e: e.tensor_scalar_mul(
                    row_a(c)[0:2, :], c.p5[0:2, 0:PB], 1.0 / 128),
                    wait=(tm2, gd.get("rowa")), inc="ve")
                gg["p5"] = tr
                tr = p.emit("vector", lambda e: e.tensor_scalar(
                    row_a(c)[0:2, :], row_a(c)[0:2, :], CQ[0:2, 13:14], None, ALU.add),
                    wait=(tr,), inc="ve", drain=True)
                gd["p5g"] = tr
                tr = p.emit("scalar", lambda e: e.activation(row_b(c)[0:2, :], row_a(c)[0:2, :],
                                                             AF.Ln), wait=(tr,), inc="se")
                tr = p.emit("scalar", lambda e: e.activation(row_a(c)[0:2, :], row_b(c)[0:2, :],
                                                             AF.Exp, scale=-0.5),
                            wait=(tr,), inc="se", drain=True)
                gd["rowa"] = tr
                tbs = p.emit("tensor", lambda e: e.matmul(c.p6[:, 0:PB], c.headbcf[:],
                                                          row_a(c)[0:2, :], start=True, stop=True),
                             wait=(tr, gd.get("p6g")), inc="pe")
                gg["rowa"] = tbs
                tn1 = p.emit("vector", lambda e, s=s, xv=xv: e.tensor_mul(scrb[0], xv[:, s],
                                                                          c.p6[:, 0:PB]),
                             wait=(tbs, gd.get("sb0")), inc="ve")
                tn2 = p.emit("vector", lambda e, s=s: e.tensor_mul(scrb[1], W2[:, s],
                                                                  c.p6[:, 0:PB]),
                             wait=(tbs, tn1, gd.get("sb1")), inc="ve")
                gd["p6g"] = tn2
                gg["p6"] = tn2
                gg["W2"] = tn2
                gg["xv%d" % par] = tn1
                tyy = None
                for off in (0, 64):
                    tyy = p.emit("tensor", lambda e, off=off: e.matmul(
                        c.p7[off:off + 64, 0:PB], wre2[off:off + 64, :],
                        scrb[0][off:off + 64, :], start=True, stop=False),
                        wait=(twre, tn1, gd.get("p7g")), inc="pe")
                    tyy = p.emit("tensor", lambda e, off=off: e.matmul(
                        c.p7[off:off + 64, 0:PB], wim2[off:off + 64, :],
                        scrb[1][off:off + 64, :], start=False, stop=True),
                        wait=(twim, tn2, tyy), inc="pe")
                gd["sb0"] = tyy
                gd["sb1"] = tyy
                gg["scrb"] = tyy
                tyg2 = p.emit("vector", lambda e, s=s, hpi=hpi: e.tensor_mul(
                    yg[hpi][:, s], yg[hpi][:, s], c.p7[:, 0:PB]),
                    wait=(tyy, tsilu[g]), inc="ve")
                gd["p7g"] = tyg2
                gg["p7"] = tyg2

        for hpi in range(8):
            stage1(half * 8 + hpi)
            if hpi >= 1:
                stage2(half * 8 + hpi - 1)
        stage2(half * 8 + 7)
        p.barrier()
        p.barrier()
        tpx = {}
        for oc in range(8):
            pa = oc % 2
            tw_g[pa] = p.dma("gpsimd", wst[pa].rearrange("p (i q) -> p i q", i=8),
                             opj[1024 * half:1024 * (half + 1),
                                 128 * oc:128 * (oc + 1)].rearrange("(i p) q -> p i q", p=128),
                             wait=(tpx.get("w%d" % pa),), sem=f"dWop{pa}")
            for g in range(4):
                s = slice(g * PB, (g + 1) * PB)
                tm = None
                for ic in range(8):
                    tm = p.emit("tensor", lambda e, ic=ic, s=s, pa=pa: e.matmul(
                        c.p4[:, 0:PB], wst[pa][:, 128 * ic:128 * (ic + 1)], yg[ic][:, s],
                        start=(ic == 0), stop=(ic == 7)),
                        wait=(tw_g[pa], tpx.get("p4")), inc="pe")
                tpx["w%d" % pa] = tm
                tX = p.emit("vector", lambda e, oc=oc, s=s: e.tensor_add(
                    c.X[oc][:, s], c.X[oc][:, s], c.p4[:, 0:PB]),
                    wait=(tm, tpx.get("X")), inc="ve")
                tpx["p4"] = tX
                tpx["X"] = tX
        p.barrier()
    c.ar.pop()


def row_a(c):
    return c.ftmp[0][0:2, 0:PB]


def row_b(c):
    return c.ftmp[1][0:2, 0:PB]


def emit_inproj_chunk(c, dst_zin, xn, col0, wbuf, sem, wait=(), rows=128):
    p = c.prog
    inproj = c.din("l0a_inproj")
    tw = p.dma("gpsimd", wbuf[:, 0:8 * rows].rearrange("p (i q) -> p i q", i=8),
               inproj[:, col0:col0 + rows].rearrange("(i p) q -> p i q", p=128),
               wait=wait, sem=sem)
    tzp = p.emit("vector", lambda e: e.memset(dst_zin[0:rows, 0:3], 0.0), wait=wait, inc="ve")
    tz = None
    tprev = None
    for g in range(4):
        s = slice(g * PB, (g + 1) * PB)
        tm = None
        for ic in range(8):
            tm = p.emit("tensor", lambda e, ic=ic, s=s: e.matmul(
                c.p4[0:rows, 0:PB], wbuf[:, rows * ic:rows * (ic + 1)], xn[ic][:, s],
                start=(ic == 0), stop=(ic == 7)), wait=(tw, tprev), inc="pe")
        tz = p.emit("scalar", lambda e, s=s: e.copy(
            dst_zin[0:rows, 3 + s.start:3 + s.stop], c.p4[0:rows, 0:PB]),
            wait=(tm,), inc="se")
        tprev = tz
    p.emit("vector", None, wait=(tzp,))
    return tz


def emit_conv4(c, out, zin, taps, bias, wait=(), rows=128):
    p = c.prog
    ra = out[0:rows, :] if out.shape[0] != rows else out
    t = p.emit("vector", lambda e: e.tensor_scalar(ra, zin[0:rows, 0:T], taps[0:rows, 0:1],
                                                   None, ALU.mult), wait=wait, inc="ve")
    for tau in range(1, 4):
        t = p.emit("vector", lambda e, tau=tau: e.scalar_tensor_tensor(
            ra, zin[0:rows, tau:T + tau], taps[0:rows, tau:tau + 1], ra,
            ALU.mult, ALU.add), wait=(t,), inc="ve", drain=True)
    t = p.emit("vector", lambda e: e.tensor_scalar(ra, ra, bias[0:rows, 0:1], None,
                                                   ALU.add), wait=(t,), inc="ve", drain=True)
    return t


def phase_l0attn(c):
    emit_l0attn(c)



def emit_l1ffn(c, tag):
    p = c.prog
    nc = c.nc
    sfx = tag
    c.ar.push()
    st = alloc_stats(c)
    xn = [c.ar.take(T, BF16, "f_xn") for _ in range(8)]
    emit_layernorm(c, xn, st)
    p.barrier()
    colsT = [c.ar.take(T, BF16, "f_cols") for _ in range(8)]
    h1all = c.ar.take(4 * T, BF16, "f_h1")
    h1 = [h1all[:, T * i:T * (i + 1)] for i in range(4)]
    cembb = c.ar.take(256, BF16, "f_cemb")
    wTs = c.ar.take(128, BF16, "f_wTs")
    wTrow = c.ar.take(PB, BF16, "f_wTrow")
    smallv = c.ar.take(16, F32, "f_smallv")
    CA = c.ar.take(64, F32, "f_CA")
    wst = [c.ar.take(1024, BF16, "f_ws0"), c.ar.take(1024, BF16, "f_ws1")]
    yf0 = h1all.bitcast(F32)
    yf = [yf0[:, 0:T], yf0[:, T:2 * T]]
    sq1 = c.ar.take(PB, BF16, "f_sq")
    sq2 = c.ar.take(PB, BF16, "f_sq2")
    wtile = c.ar.take(16, BF16, "f_wtile")
    rowbf = c.ar.take(T, BF16, "f_rowbf")
    rrow = c.ar.take(PB, F32, "f_rrow")

    def col_u2(uc):
        return colsT[uc // 4][:, (uc % 4) * PB:(uc % 4 + 1) * PB]

    def y_ap(oc):
        return yf[oc // 4][:, (oc % 4) * PB:(oc % 4 + 1) * PB]

    tbc = p.dma("sync", CA[:, 0:32], c.din("l1f_bcols"), sem="dL")
    tpb = p.dma("sync", CA[:, 32:40], c.din("l1f_projb"), sem="dL")
    tlg = p.dma("sync", CA[:, 40:48], c.din("l1f_lng"), sem="dL")
    tlb = p.dma("sync", CA[:, 48:56], c.din("l1f_lnb"), sem="dL")
    tg1 = p.dma("sync", CA[:, 56:60], c.din("l1f_gb1"), sem="dL")
    te24 = p.emit("vector", lambda e: e.memset(CA[:, 60:61], 1e-24), inc="ve")

    tce = p.dma("gpsimd", cembb[:, 0:128].rearrange("p (i q) -> p i q", i=8),
                c.din("l1f_colembT").rearrange("(i p) q -> p i q", p=128), sem="dC" + sfx)
    tgw2 = p.dma("gpsimd", cembb[:, 128:192].rearrange("p (i q) -> p i q", i=4),
                 c.din("l1f_gw2").rearrange("(i p) q -> p i q", p=128), sem="dC" + sfx)
    tsq = None
    tss = None
    for ic in range(8):
        tss = p.emit("scalar", lambda e, ic=ic: e.activation(
            sq1[:, 0:16], cembb[:, 16 * ic:16 * (ic + 1)], AF.Square),
            wait=(tce, tsq), inc="se")
        tsq = p.emit("tensor", lambda e, ic=ic: e.matmul(
            c.p6[0:1, 0:16], c.onescol[:], sq1[:, 0:16],
            start=(ic == 0), stop=(ic == 7)), wait=(tss,), inc="pe")
    r1 = c.ftmp[0][0:1, 0:16]
    t = p.emit("scalar", lambda e: e.activation(r1, c.p6[0:1, 0:16], AF.Ln,
                                                bias=CA[0:1, 60:61]), wait=(tsq, te24), inc="se")
    t = p.emit("scalar", lambda e: e.activation(r1, r1, AF.Exp, scale=-0.5),
               wait=(t,), inc="se", drain=True)
    tinvc = p.emit("tensor", lambda e: e.matmul(c.p7[:, 0:16], c.onesrowf[:], r1,
                                                start=True, stop=True), wait=(t,), inc="pe")
    invcb = c.ftmp[1][:, 0:16]
    ticb = p.emit("vector", lambda e: e.tensor_copy(invcb, c.p7[:, 0:16]),
                  wait=(tinvc,), inc="ve")

    tms = None
    tsqg = None
    for ch in range(8):
        for g in range(4):
            s = slice(g * PB, (g + 1) * PB)
            ts_ = p.emit("scalar", lambda e, ch=ch, s=s: e.activation(
                sq1, xn[ch][:, s], AF.Square), wait=(tsqg, tss), inc="se")
            tms = p.emit("tensor", lambda e, g=g, ch=ch: e.matmul(
                c.pS[0:1, g * PB:(g + 1) * PB], c.onescol[:], sq1,
                start=(ch == 0), stop=(ch == 7)), wait=(ts_,), inc="pe")
            tsqg = tms
    t = p.emit("scalar", lambda e: e.activation(rowbf[0:1, :], c.pS[0:1, 0:4 * PB], AF.Ln,
                                                bias=CA[0:1, 60:61]), wait=(tms, te24), inc="se")
    tinvn = p.emit("scalar", lambda e: e.activation(rowbf[0:1, :], rowbf[0:1, :], AF.Exp,
                                                    scale=-0.5), wait=(t,), inc="se", drain=True)

    tw = [None, None]
    twg = [None, None]
    th1 = None
    for hc in range(4):
        pa = hc % 2
        tw[pa] = p.dma("gpsimd", wst[pa].rearrange("p (i q) -> p i q", i=8),
                       c.din("l1f_gw1")[:, 128 * hc:128 * (hc + 1)].rearrange(
                           "(i p) q -> p i q", p=128), wait=(twg[pa],), sem=f"dG{pa}{sfx}")
        for g in range(4):
            s = slice(g * PB, (g + 1) * PB)
            tm = None
            for ic in range(8):
                tm = p.emit("tensor", lambda e, ic=ic, s=s, pa=pa: e.matmul(
                    c.p4[:, 0:PB], wst[pa][:, 128 * ic:128 * (ic + 1)], xn[ic][:, s],
                    start=(ic == 0), stop=(ic == 7)), wait=(tw[pa], th1), inc="pe")
            th1 = p.emit("scalar", lambda e, hc=hc, s=s: e.activation(
                h1[hc][:, s], c.p4[:, 0:PB], AF.Gelu, bias=CA[:, 56 + hc:57 + hc]),
                wait=(tm, tg1), inc="se")
        twg[pa] = th1
    p.barrier()

    tgb2 = p.dma("sync", c.ftmp[0][0:1, 16:32], c.din("l1f_gb2"), sem="dL")
    tb2 = p.emit("tensor", lambda e: e.matmul(c.p7[:, 0:16], c.onesrowf[:],
                                              c.ftmp[0][0:1, 16:32], start=True, stop=True),
                 wait=(tgb2,), inc="pe")
    gb2b = c.ftmp[0][:, 32:48]
    tgb2b = p.emit("vector", lambda e: e.tensor_copy(gb2b, c.p7[:, 0:16]),
                   wait=(tb2,), inc="ve")
    gd = {}
    t15 = None
    for tt in range(16):
        tm1 = None
        for ic in range(8):
            tm1 = p.emit("tensor", lambda e, ic=ic, tt=tt: e.matmul(
                c.p4[:, 0:16], xn[ic][:, 128 * tt:128 * (tt + 1)],
                cembb[:, 16 * ic:16 * (ic + 1)], start=(ic == 0), stop=(ic == 7)),
                wait=(tce, gd.get("p4"),), inc="pe")
        tm2 = None
        for hc in range(4):
            tm2 = p.emit("tensor", lambda e, hc=hc, tt=tt: e.matmul(
                c.p5[:, 0:16], h1[hc][:, 128 * tt:128 * (tt + 1)],
                cembb[:, 128 + 16 * hc:128 + 16 * (hc + 1)],
                start=(hc == 0), stop=(hc == 3)), wait=(tgw2, th1, gd.get("p5"),), inc="pe")
        ttr = p.emit("tensor", lambda e, tt=tt: e.matmul(
            c.p6[0:128, 0:1], rowbf[0:1, 128 * tt:128 * (tt + 1)], c.onescol[0:1, 0:1],
            start=True, stop=True), wait=(tinvn, gd.get("p6"),), inc="pe")
        tcol = p.emit("vector", lambda e: e.tensor_copy(smallv[:, 0:1], c.p6[0:128, 0:1]),
                      wait=(ttr, gd.get("sv"),), inc="ve")
        f1 = c.ftmp[1][:, 16:32]
        f2 = c.ftmp[1][:, 32:48]
        t1 = p.emit("vector", lambda e: e.tensor_scalar(f1, c.p4[:, 0:16], smallv[:, 0:1],
                                                        None, ALU.mult),
                    wait=(tm1, tcol, gd.get("f1"),), inc="ve")
        gd["p4"] = t1
        t2 = p.emit("vector", lambda e: e.tensor_mul(f1, f1, invcb), wait=(t1, ticb), inc="ve")
        t3 = p.emit("vector", lambda e: e.tensor_add(f2, c.p5[:, 0:16], gb2b),
                    wait=(tm2, tgb2b, gd.get("f2"),), inc="ve")
        gd["p5"] = t3
        t4 = p.emit("scalar", lambda e: e.activation(f2, f2, AF.Sigmoid), wait=(t3,), inc="se")
        t5 = p.emit("vector", lambda e: e.tensor_add(f1, f1, f2), wait=(t2, t4), inc="ve")
        t6 = p.emit("vector", lambda e: e.reduce_max(smallv[:, 1:2], f1, axis=AX,
                                                     negate=True), wait=(t5,), inc="ve",
                    drain=True)
        t7 = p.emit("scalar", lambda e: e.activation(f2, f1, AF.Exp, bias=smallv[:, 1:2]),
                    wait=(t6,), inc="se")
        t8 = p.emit("vector", lambda e: e.tensor_scalar_mul(smallv[:, 2:3], smallv[:, 1:2],
                                                            -1.0), wait=(t7,), inc="ve",
                    drain=True)
        t9 = p.emit("vector", lambda e: e.tensor_scalar(f1, f1, smallv[:, 2:3], None,
                                                        ALU.is_ge), wait=(t8,), inc="ve")
        t10 = p.emit("vector", lambda e: e.reduce_sum(smallv[:, 3:4], f2, axis=AX),
                     wait=(t7, t9), inc="ve", drain=True)
        t11 = p.emit("vector", lambda e: e.reciprocal(smallv[:, 4:5], smallv[:, 3:4]),
                     wait=(t10,), inc="ve", drain=True)
        t12 = p.emit("vector", lambda e: e.tensor_mul(f2, f2, f1), wait=(t9, t11), inc="ve")
        t13 = p.emit("vector", lambda e: e.tensor_scalar(wtile, f2, smallv[:, 4:5], None,
                                                         ALU.mult), wait=(t12,), inc="ve")
        gd["f1"] = t13
        gd["f2"] = t13
        gd["sv"] = t13
        t14 = p.emit("tensor", lambda e, tt=tt: e.matmul(
            c.p6[0:16, 0:128], wtile, c.ident[:], start=True, stop=True),
            wait=(t13, ttr), inc="pe")
        t15 = p.emit("vector", lambda e: e.tensor_copy(
            wTs[0:16, 0:128], c.p6[0:16, 0:128]),
            wait=(t14, gd.get("wTg"),), inc="ve")
        gd["p6"] = t15
        gd["wTg"] = p.dma("sync", c.scratch("wTd", (16, T), BF16)[:, 128 * tt:128 * (tt + 1)],
                          wTs[0:16, 0:128], wait=(t15,), sem="dWT")
    twT = gd["wTg"]
    p.barrier()

    Wr = c.din("l1f_Wr")
    pw = c.din("l1f_projw")
    wring = wst + [c.ar.take(1024, BF16, "f_ws2"), c.ar.take(1024, BF16, "f_ws3")]
    wloads = []
    for tg in range(4):
        for uc in range(32):
            wloads.append(("wr", uc, 0))
        for oc in range(8):
            for ug4 in range(4):
                wloads.append(("pj", oc, ug4))
    twld = {}
    twread = {}
    wnxt = [0]

    def w_load():
        k = wnxt[0]
        wnxt[0] += 1
        sl = k % 4
        d = wloads[k]
        if d[0] == "wr":
            src = Wr[:, 128 * d[1]:128 * (d[1] + 1)].rearrange("(i p) q -> p i q", p=128)
        else:
            src = pw[1024 * d[2]:1024 * (d[2] + 1),
                     128 * d[1]:128 * (d[1] + 1)].rearrange("(u p) q -> p u q", p=128)
        twld[k] = p.dma("gpsimd", wring[sl].rearrange("p (i q) -> p i q", i=8), src,
                        wait=(twread.get(k - 4),), sem=f"dR{sl}{sfx}")

    for _ in range(3):
        w_load()
    wtrows = c.ar.take(4096, BF16, "f_wtr")
    tslotr = {}

    def wt_load(cc, s):
        sl8 = cc % 8
        return p.dma("sync", wtrows[0:1, 512 * sl8:512 * sl8 + 512],
                     c.scratch("wTd", (16, T), BF16)[cc:cc + 1, s],
                     wait=(twT, tslotr.get(sl8)), sem="dWT%d" % (cc % 4))

    for tg in range(4):
        s = slice(tg * PB, (tg + 1) * PB)
        gm = {}
        twl = {}
        for cc in range(8):
            twl[cc] = wt_load(cc, s)
        for uc in range(32):
            k = tg * 64 + uc
            sl = k % 4
            bi = uc % 4
            bk = c.pb[bi]
            tm = None
            for ic in range(8):
                tm = p.emit("tensor", lambda e, ic=ic, sl=sl, s=s, bk=bk: e.matmul(
                    bk, wring[sl][:, 128 * ic:128 * (ic + 1)], xn[ic][:, s],
                    start=(ic == 0), stop=(ic == 7)),
                    wait=(twld[k], gm.get("b%d" % bi),), inc="pe")
            twread[k] = tm
            if wnxt[0] < len(wloads):
                w_load()
            tcl = p.emit("scalar", lambda e, uc=uc, bk=bk: e.activation(
                col_u2(uc), bk, AF.Gelu, bias=CA[:, uc:uc + 1]),
                wait=(tm, tbc), inc="se")
            gm["b%d" % bi] = tcl
            gm["c%d" % uc] = tcl
        tww = None
        for cc in range(16):
            sl8 = cc % 8
            tb = p.emit("tensor", lambda e, sl8=sl8: e.matmul(
                c.p6[:, 0:PB], c.onesrow[:], wtrows[0:1, 512 * sl8:512 * sl8 + 512],
                start=True, stop=True),
                wait=(twl[cc], gm.get("p6w"),), inc="pe")
            tslotr[sl8] = tb
            if cc < 8:
                twl[cc + 8] = wt_load(cc + 8, s)
            for du in range(2):
                uc = 2 * cc + du
                tww = p.emit("vector", lambda e, uc=uc: e.tensor_mul(
                    col_u2(uc), col_u2(uc), c.p6[:, 0:PB]),
                    wait=(tb, gm.get("c%d" % uc),), inc="ve")
                gm["c%d" % uc] = tww
            gm["p6w"] = tww
        tb1 = p.emit("tensor", lambda e, s=s: e.matmul(
            c.p6[:, 0:PB], c.onesrowf[:], st["nmu"][0:1, s], start=True, stop=True),
            wait=(tww,), inc="pe")
        tb2 = p.emit("tensor", lambda e, s=s: e.matmul(
            c.p7[:, 0:PB], c.onesrowf[:], st["rstd"][0:1, s], start=True, stop=True),
            wait=(tinvc, tb1), inc="pe")
        tyl = None
        for oc in range(8):
            pa = oc % 2
            bk = c.p4 if pa == 0 else c.p5
            tpj = None
            for ug4 in range(4):
                k = tg * 64 + 32 + oc * 4 + ug4
                sl = k % 4
                for du in range(8):
                    uc = ug4 * 8 + du
                    tpj = p.emit("tensor", lambda e, du=du, sl=sl, bk=bk, uc=uc: e.matmul(
                        bk[:, 0:PB], wring[sl][:, 128 * du:128 * (du + 1)], col_u2(uc),
                        start=(uc == 0), stop=(uc == 31)),
                        wait=(twld[k], tww, gm.get("pj%d" % pa),), inc="pe")
                twread[k] = tpj
                if wnxt[0] < len(wloads):
                    w_load()
            t1 = p.emit("vector", lambda e, oc=oc, s=s: e.tensor_add(
                y_ap(oc), c.X[oc][:, s], c.p6[:, 0:PB]), wait=(tb1, gm.get("y%d" % oc),),
                inc="ve")
            t2 = p.emit("vector", lambda e, oc=oc: e.tensor_mul(
                y_ap(oc), y_ap(oc), c.p7[:, 0:PB]), wait=(tb2, t1), inc="ve")
            t3 = p.emit("vector", lambda e, oc=oc: e.tensor_scalar(
                y_ap(oc), y_ap(oc), CA[:, 32 + oc:33 + oc], None, ALU.add),
                wait=(t2, tpb), inc="ve")
            tyl = p.emit("vector", lambda e, oc=oc, bk=bk: e.tensor_add(
                y_ap(oc), y_ap(oc), bk[:, 0:PB]), wait=(t3, tpj), inc="ve")
            gm["pj%d" % pa] = tyl
            gm["y%d" % oc] = tyl
        tst = None
        for oc in range(8):
            tyb = p.emit("vector", lambda e, oc=oc: e.tensor_copy(sq1, y_ap(oc)),
                         wait=(gm.get("y%d" % oc), gm.get("sq1"),), inc="ve")
            tsq2 = p.emit("scalar", lambda e: e.activation(sq2, sq1, AF.Square),
                          wait=(tyb, gm.get("sq2"),), inc="se")
            p.emit("tensor", lambda e, oc=oc: e.matmul(
                c.pS[0:1, 0:PB], c.onescol[:], sq1, start=(oc == 0), stop=(oc == 7)),
                wait=(tyb, gm.get("st1"),), inc="pe")
            tst = p.emit("tensor", lambda e, oc=oc: e.matmul(
                c.pS[0:1, PB:2 * PB], c.onescol[:], sq2,
                start=(oc == 0), stop=(oc == 7)), wait=(tsq2,), inc="pe")
            gm["sq1"] = tst
            gm["sq2"] = tst
        gm["st1"] = tst
        ra = c.ftmp[1][0:1, 0:PB]
        rb = c.ftmp[0][0:1, 0:PB]
        rc = rrow[0:1, 0:PB]
        t1 = p.emit("vector", lambda e: e.tensor_scalar_mul(ra, c.pS[0:1, 0:PB], 1.0 / D),
                    wait=(tst,), inc="ve", drain=True)
        t2 = p.emit("vector", lambda e: e.tensor_scalar_mul(rb, c.pS[0:1, PB:2 * PB], 1.0 / D),
                    wait=(tst, t1), inc="ve", drain=True)
        t3 = p.emit("scalar", lambda e: e.activation(rc, ra, AF.Square),
                    wait=(t1, t2), inc="se")
        t4 = p.emit("vector", lambda e: e.tensor_sub(rb, rb, rc), wait=(t3,), inc="ve",
                    drain=True)
        t5 = p.emit("scalar", lambda e: e.activation(rb, rb, AF.Ln, bias=c.epscol[0:1, 0:1]),
                    wait=(t4,), inc="se", drain=True)
        t6 = p.emit("scalar", lambda e: e.activation(rb, rb, AF.Exp, scale=-0.5),
                    wait=(t5,), inc="se", drain=True)
        t7 = p.emit("vector", lambda e: e.tensor_scalar_mul(ra, ra, -1.0), wait=(t6,),
                    inc="ve", drain=True)
        tb3 = p.emit("tensor", lambda e: e.matmul(c.p6[:, 0:PB], c.onesrowf[:], ra,
                                                  start=True, stop=True),
                     wait=(t7, tb2, tyl), inc="pe")
        tb4 = p.emit("tensor", lambda e: e.matmul(c.p7[:, 0:PB], c.onesrowf[:], rb,
                                                  start=True, stop=True),
                     wait=(t6, tb3), inc="pe")
        for oc in range(8):
            t1 = p.emit("vector", lambda e, oc=oc: e.tensor_add(
                y_ap(oc), y_ap(oc), c.p6[:, 0:PB]), wait=(tb3, gm.get("yo%d" % oc),), inc="ve")
            t2 = p.emit("vector", lambda e, oc=oc: e.tensor_mul(
                y_ap(oc), y_ap(oc), c.p7[:, 0:PB]), wait=(tb4, t1), inc="ve")
            t3 = p.emit("vector", lambda e, oc=oc: e.tensor_scalar(
                y_ap(oc), y_ap(oc), CA[:, 40 + oc:41 + oc], CA[:, 48 + oc:49 + oc],
                ALU.mult, ALU.add), wait=(t2, tlg, tlb), inc="ve")
            p.emit("vector", lambda e, oc=oc, s=s: e.scalar_tensor_tensor(
                c.X[oc][:, s], y_ap(oc), 0.5, c.X[oc][:, s], ALU.mult, ALU.add),
                wait=(t3,), inc="ve")
        p.barrier()
    c.ar.pop()


def phase_l1ffn(c):
    emit_l1ffn(c, "f1")



def emit_l1attn(c):
    p = c.prog
    nc = c.nc
    sfx = "a1"
    c.ar.push()
    xn = [c.ar.take(T, BF16, "b_xn") for _ in range(8)]
    c.ar.push()
    st = alloc_stats(c)
    emit_layernorm(c, xn, st)
    c.ar.pop()
    p.barrier()
    kvn = [c.ar.take(T, BF16, "b_kvn") for _ in range(2)]
    qcT = [c.ar.take(T, BF16, "b_qc") for _ in range(4)]
    krT = [c.ar.take(T, BF16, "b_kr") for _ in range(3)]
    qrT = [c.ar.take(T, BF16, "b_qr") for _ in range(3)]
    WV = 66 * 8
    Vb = c.ar.take(16 * WV, BF16, "b_V")
    sh = [c.ar.take(T, BF16, "b_sh") for _ in range(2)]
    Ob = [c.ar.take(T, BF16, "b_O") for _ in range(4)]
    at2 = c.ar.take(PB, BF16, "b_at2")
    scr = c.ar.take(PB, F32, "b_scr")
    rotf = c.ar.take(128, F32, "b_rotf")
    wstab = c.ar.take(2048, BF16, "b_wsab")
    wst = [wstab[:, 0:1024], wstab[:, 1024:2048]]
    Ab3 = c.ar.take(T, BF16, "b_Ab3")
    colsc = c.ar.take(64, F32, "b_colsc")
    krow = Vb[:, 0:4096].bitcast(F32)

    CQ = c.smallf
    trot = p.dma("sync", rotf, c.const("l1a_rotT_f32"), sem="dL")
    tbl = {}
    tbl["C"] = p.dma("gpsimd", sh[0], c.const("l1a_ropeC"), sem="dTb" + sfx)
    tbl["S"] = p.dma("gpsimd", sh[1], c.const("l1a_ropeS"), sem="dTb" + sfx)
    tgk = p.dma("sync", CQ[:, 30:32], c.din("l1a_g_kv"), sem="dL")
    te6 = p.emit("vector", lambda e: e.memset(colsc[:, 0:1], 1e-6), inc="ve")

    tw = [None, None]
    wguard = [None, None]
    tkv = None
    for oc in range(2):
        pa = oc % 2
        tw[pa] = p.dma("gpsimd", wst[pa].rearrange("p (i q) -> p i q", i=8),
                       c.din("l1a_W_kv")[:, 128 * oc:128 * (oc + 1)].rearrange(
                           "(i p) q -> p i q", p=128), wait=(wguard[pa],), sem=f"dKV{pa}{sfx}")
        for g in range(4):
            s = slice(g * PB, (g + 1) * PB)
            tm = None
            for ic in range(8):
                tm = p.emit("tensor", lambda e, ic=ic, s=s, pa=pa: e.matmul(
                    c.p4[:, 0:PB], wst[pa][:, 128 * ic:128 * (ic + 1)], xn[ic][:, s],
                    start=(ic == 0), stop=(ic == 7)), wait=(tw[pa], tkv), inc="pe")
            tkv = p.emit("scalar", lambda e, oc=oc, s=s: e.copy(kvn[oc][:, s], c.p4[:, 0:PB]),
                         wait=(tm,), inc="se")
            wguard[pa] = tm
    tms = None
    tsg = None
    for oc in range(2):
        for g in range(4):
            s = slice(g * PB, (g + 1) * PB)
            ts_ = p.emit("scalar", lambda e, oc=oc, s=s: e.activation(
                at2, kvn[oc][:, s], AF.Square), wait=(tkv, tsg), inc="se")
            tms = p.emit("tensor", lambda e, g=g, oc=oc: e.matmul(
                c.pS[0:1, g * PB:(g + 1) * PB], c.onescol[:], at2,
                start=(oc == 0), stop=(oc == 1)), wait=(ts_,), inc="pe")
            tsg = tms
    r1 = krow[0:1, 0:T]
    t = p.emit("vector", lambda e: e.tensor_scalar_mul(r1, c.pS[0:1, 0:4 * PB], 1.0 / 256),
               wait=(tms, te6), inc="ve", drain=True)
    t = p.emit("vector", lambda e: e.tensor_scalar(r1, r1, colsc[0:1, 0:1], None, ALU.add),
               wait=(t,), inc="ve", drain=True)
    t = p.emit("scalar", lambda e: e.activation(r1, r1, AF.Ln), wait=(t,), inc="se")
    trs = p.emit("scalar", lambda e: e.activation(r1, r1, AF.Exp, scale=-0.5),
                 wait=(t,), inc="se", drain=True)
    tkn = None
    for g in range(4):
        s = slice(g * PB, (g + 1) * PB)
        tb = p.emit("tensor", lambda e, s=s: e.matmul(
            c.p5[:, 0:PB], c.onesrowf[:], r1[0:1, s], start=True, stop=True),
            wait=(trs, tkn), inc="pe")
        for oc in range(2):
            t1 = p.emit("vector", lambda e, oc=oc, s=s: e.tensor_mul(
                kvn[oc][:, s], kvn[oc][:, s], c.p5[:, 0:PB]), wait=(tb, tkv, tkn), inc="ve")
            tkn = p.emit("vector", lambda e, oc=oc, s=s: e.tensor_scalar(
                kvn[oc][:, s], kvn[oc][:, s], CQ[:, 30 + oc:31 + oc], None, ALU.mult),
                wait=(t1, tgk), inc="ve")

    tqc = None
    for oc in range(4):
        pa = oc % 2
        tw[pa] = p.dma("gpsimd", wst[pa].rearrange("p (i q) -> p i q", i=8),
                       c.din("l1a_W_q")[:, 128 * oc:128 * (oc + 1)].rearrange(
                           "(i p) q -> p i q", p=128), wait=(wguard[pa],), sem=f"dKV{pa}{sfx}")
        for g in range(4):
            s = slice(g * PB, (g + 1) * PB)
            tm = None
            for ic in range(8):
                tm = p.emit("tensor", lambda e, ic=ic, s=s, pa=pa: e.matmul(
                    c.p4[:, 0:PB], wst[pa][:, 128 * ic:128 * (ic + 1)], xn[ic][:, s],
                    start=(ic == 0), stop=(ic == 7)), wait=(tw[pa], tqc), inc="pe")
            tqc = p.emit("scalar", lambda e, oc=oc, s=s: e.copy(qcT[oc][:, s], c.p4[:, 0:PB]),
                         wait=(tm,), inc="se")
            wguard[pa] = tm
    p.barrier()

    krd = c.scratch("krTd", (384, T), BF16)

    def emit_rope(dst3, w_name, src_chunks, nsrc, half, sem):
        tlast = None
        tww = [None, None]
        wgr = [wguard[0], wguard[1]]
        for ch3 in range(3):
            pa = ch3 % 2
            tww[pa] = p.dma("gpsimd", wst[pa][:, 0:nsrc * 128].rearrange(
                "p (i q) -> p i q", i=nsrc),
                c.din(w_name)[:, 384 * half + 128 * ch3:384 * half + 128 * (ch3 + 1)].rearrange(
                    "(i p) q -> p i q", p=128),
                wait=(wgr[pa],), sem=f"dRp{pa}{sfx}")
            for g in range(4):
                s = slice(g * PB, (g + 1) * PB)
                tm = None
                for ic in range(nsrc):
                    tm = p.emit("tensor", lambda e, ic=ic, s=s, pa=pa: e.matmul(
                        c.p4[:, 0:PB], wst[pa][:, 128 * ic:128 * (ic + 1)],
                        src_chunks[ic][:, s], start=(ic == 0), stop=(ic == nsrc - 1)),
                        wait=(tww[pa], tlast), inc="pe")
                wgr[pa] = tm
                wguard[pa] = tm
                t1 = p.emit("scalar", lambda e: e.copy(scr, c.p4[:, 0:PB]),
                            wait=(tm, tlast), inc="se")
                t2 = p.emit("tensor", lambda e: e.matmul(c.p7[:, 0:PB], rotf, scr,
                                                         start=True, stop=True),
                            wait=(t1, trot, tlast), inc="pe")
                t3 = p.emit("vector", lambda e, s=s: e.tensor_mul(
                    c.ftmp[1][:, 0:PB], scr, sh[0][:, s]), wait=(t1, tbl["C"], tlast), inc="ve")
                t4 = p.emit("vector", lambda e, s=s: e.tensor_mul(
                    c.ftmp[0][:, 0:PB], c.p7[:, 0:PB], sh[1][:, s]),
                    wait=(t2, tbl["S"], t3), inc="ve")
                tlast = p.emit("vector", lambda e, ch3=ch3, s=s, dst3=dst3: e.tensor_add(
                    dst3[ch3][:, s], c.ftmp[1][:, 0:PB], c.ftmp[0][:, 0:PB]),
                    wait=(t3, t4), inc="ve")
        return tlast

    tk1 = emit_rope(krT, "l1a_W_krope", xn, 8, 1, "k1")
    tkd = None
    for ch3 in range(3):
        tkd = p.dma("sync", krd[128 * ch3:128 * (ch3 + 1), :], krT[ch3][:, :],
                    wait=(tk1,), sem="dKd")
    for _e in ENGS:
        p.emit(_e, None, wait=(tkd,))
    p.barrier()
    tk0 = emit_rope(krT, "l1a_W_krope", xn, 8, 0, "k0")
    p.barrier()
    tq0 = emit_rope(qrT, "l1a_W_qrope", qcT, 4, 0, "q0")
    p.barrier()

    if DBG_L1A:
        p.barrier()
        for nm, buf in (("kvn0", kvn[0]), ("qc0", qcT[0]), ("kr0", krT[0]),
                        ("qr0", qrT[0])):
            tdmp = p.dma("sync", c.scratch("dbg_" + nm, (128, T), BF16), buf[:, :], sem="dDBG")
        p.emit("sync", None, wait=(tdmp,))
        p.barrier()

    opj = c.din("l1a_W_o")
    scl1 = 1.0 / math.sqrt(96.0)
    for half in range(2):
        if half == 1:
            tkr = None
            for ch3 in range(3):
                tkr = p.dma("gpsimd", krT[ch3][:, :], krd[128 * ch3:128 * (ch3 + 1), :],
                            wait=(tkd,), sem="dKd2")
            p.barrier()
            tbl["C"] = p.dma("gpsimd", sh[0], c.const("l1a_ropeC"), sem="dTb" + sfx)
            tbl["S"] = p.dma("gpsimd", sh[1], c.const("l1a_ropeS"), sem="dTb" + sfx)
            tq1 = emit_rope(qrT, "l1a_W_qrope", qcT, 4, 1, "q1")
            p.emit("tensor", None, wait=(tkr, tq1))
            p.barrier()
            if DBG_L1A:
                tdmp = p.dma("sync", c.scratch("dbg_qrfresh", (128, T), BF16),
                             qrT[0][:, :], sem="dDBG")
                p.emit("sync", None, wait=(tdmp,))
                p.barrier()
        kc = xn[0:4]
        qc2 = xn[4:8]
        tdec = None
        for (dst, wname, srcs, nsrc) in ((kc, "l1a_W_kdec", kvn, 2),
                                         (qc2, "l1a_W_qdec", qcT, 4)):
            for oc in range(4):
                pa = oc % 2
                tw[pa] = p.dma("gpsimd", wst[pa][:, 0:nsrc * 128].rearrange(
                    "p (i q) -> p i q", i=nsrc),
                    c.din(wname)[:, 512 * half + 128 * oc:512 * half + 128 * (oc + 1)].rearrange(
                        "(i p) q -> p i q", p=128),
                    wait=(wguard[pa],), sem=f"dKV{pa}{sfx}")
                for g in range(4):
                    s = slice(g * PB, (g + 1) * PB)
                    tm = None
                    for ic in range(nsrc):
                        tm = p.emit("tensor", lambda e, ic=ic, s=s, pa=pa, srcs=srcs, nsrc=nsrc:
                                    e.matmul(c.p4[:, 0:PB],
                                             wst[pa][:, 128 * ic:128 * (ic + 1)],
                                             srcs[ic][:, s], start=(ic == 0),
                                             stop=(ic == nsrc - 1)),
                                    wait=(tw[pa], tdec), inc="pe")
                    tdec = p.emit("scalar", lambda e, dst=dst, oc=oc, s=s: e.copy(
                        dst[oc][:, s], c.p4[:, 0:PB]), wait=(tm,), inc="se")
                    wguard[pa] = tm
        twv = p.dma("gpsimd", wst[0][:, 0:1024].rearrange("p (i q) -> p i q", i=2),
                    c.din("l1a_W_vdec")[:, 512 * half:512 * (half + 1)].rearrange(
                        "(i p) q -> p i q", p=128), wait=(wguard[0],), sem=f"dKV0{sfx}")
        tvm = tdec
        tves = None
        for tt in range(16):
            tm = None
            for ic in range(2):
                tm = p.emit("tensor", lambda e, ic=ic, tt=tt: e.matmul(
                    c.p4[:, 0:PB], kvn[ic][:, 128 * tt:128 * (tt + 1)],
                    wst[0][:, 512 * ic:512 * (ic + 1)],
                    start=(ic == 0), stop=(ic == 1)), wait=(twv, tvm), inc="pe")
            tve = p.emit("vector", lambda e, tt=tt: e.tensor_copy(
                Vb[:, tt * WV:tt * WV + 8 * 66].rearrange("p (h d) -> p h d", d=66)[:, :, 0:64],
                c.p4[:, 0:PB].rearrange("p (h d) -> p h d", d=64)),
                wait=(tm, tves), inc="ve")
            tves = p.emit("vector", lambda e, tt=tt: e.memset(
                Vb[:, tt * WV + 64: tt * WV + 8 * 66:66], 1.0), wait=(tve,), inc="ve")
            tvm = tve
            wguard[0] = tm
        tv_done = tves
        if DBG_L1A:
            p.barrier()
            for nm, buf in (("kc0", kc[0]), ("qc20", qc2[0]), ("kr0x", krT[0]),
                            ("qr0x", qrT[0])):
                tdmp = p.dma("sync", c.scratch("dbg_%sh%d" % (nm, half), (128, T), BF16),
                             buf[:, :], sem="dDBG")
            tdmp = p.dma("sync", c.scratch("dbg_V0h%d" % half, (128, 512), BF16),
                         Vb[:, 0:528].rearrange("p (h d) -> p h d", d=66)[:, :, 0:64], sem="dDBG")
            p.emit("sync", None, wait=(tdmp,))
        p.barrier()
        A4 = [sh[0], sh[1], wstab, Ab3]
        aits = [(hh, tp) for hh in range(8) for tp in range(8)]
        NIT = len(aits)
        texp = {}
        tmask = {}
        tABr = {(-2, 0): None, (-2, 1): None, (-1, 0): None, (-1, 1): None}
        tsc_t = {}
        ttr = {0: None, 1: None}
        tat2r = [None, None]
        tavfin = {}
        trecip = {-2: None, -1: None}
        tavs = {}
        toutm = {}
        tbc_t = {}
        tscore_last = None

        def ageo(n):
            hh, tp = aits[n]
            return (hh, tp, hh // 2, (hh % 2) * 64, hh // 3, (hh % 3) * 32,
                    2 * tp + 2)

        def emit_scores(n, ti, ext, exts, cm, Ab, w0):
            nonlocal tscore_last
            hh, tp, cch, coff, rch, roff, nkt = ageo(n)
            qt = 2 * tp + ti
            tsc = None
            tmq = []
            ngr = (exts + PB - 1) // PB
            for g in range(ngr):
                ws = min(PB, exts - g * PB)
                tsc = p.emit("tensor", lambda e, g=g, ws=ws, qt=qt, cch=cch,
                             coff=coff: e.matmul(
                    c.pS[:, g * PB:g * PB + ws],
                    qc2[cch][coff:coff + 64, 128 * qt:128 * (qt + 1)],
                    kc[cch][coff:coff + 64, g * PB:g * PB + ws],
                    start=True, stop=False),
                    wait=(w0 if g == 0 else ()), inc="pe")
                tsc = p.emit("tensor", lambda e, g=g, ws=ws, qt=qt, rch=rch,
                             roff=roff: e.matmul(
                    c.pS[:, g * PB:g * PB + ws],
                    qrT[rch][roff:roff + 32, 128 * qt:128 * (qt + 1)],
                    krT[rch][roff:roff + 32, g * PB:g * PB + ws],
                    start=False, stop=True),
                    wait=(tsc,), inc="pe")
                tmq.append(p.emit("vector", lambda e, g=g, ws=ws, cm=cm: e.reduce_max(
                    colsc[:, cm + 2 + g:cm + 3 + g], c.pS[:, g * PB:g * PB + ws],
                    axis=AX), wait=(tsc, texp.get((n - 2, ti))), inc="ve"))
            tsc_t[(n, ti)] = tsc
            tscore_last = tsc
            tmx = p.emit("vector", lambda e, cm=cm, ngr=ngr: e.reduce_max(
                colsc[:, cm:cm + 1], colsc[:, cm + 2:cm + 2 + ngr], axis=AX,
                negate=True), wait=tuple(tmq), inc="ve")
            tmx2 = p.emit("vector", lambda e, cm=cm: e.tensor_scalar_mul(
                colsc[:, cm + 1:cm + 2], colsc[:, cm:cm + 1], scl1), wait=(tmx,), inc="ve")
            texp[(n, ti)] = p.emit("scalar", lambda e, exts=exts, Ab=Ab, cm=cm:
                                   e.activation(Ab[:, 0:exts], c.pS[:, 0:exts], AF.Exp,
                                                bias=colsc[:, cm + 1:cm + 2], scale=scl1),
                                   wait=(tmx2, tABr[(n - 2, ti)]), inc="se")
            return tsc

        def prod_first(n):
            hh, tp, cch, coff, rch, roff, nkt = ageo(n)
            ext = 128 * nkt
            pr = n % 2
            wprev = texp.get((n - 1, 1))
            w0 = (tdec, tv_done, wprev)
            emit_scores(n, 0, ext, ext - 128, 1 + 14 * pr, A4[2 * pr], w0)

        def prod_second(n):
            hh, tp, cch, coff, rch, roff, nkt = ageo(n)
            ext = 128 * nkt
            pr = n % 2
            Ab0, Ab1 = A4[2 * pr], A4[2 * pr + 1]
            qt0, qt1 = 2 * tp, 2 * tp + 1
            emit_scores(n, 1, ext, ext, 8 + 14 * pr, Ab1, (texp[(n, 0)],))
            tms0 = p.emit("gpsimd", lambda e, qt=qt0, Ab=Ab0: e.tensor_mul(
                Ab[:, 128 * qt:128 * (qt + 1)], Ab[:, 128 * qt:128 * (qt + 1)],
                c.diagmask[:]), wait=(texp[(n, 0)],), inc="ge")
            tmask[(n, 0)] = p.emit("gpsimd", lambda e, qt=qt0, ext=ext, Ab=Ab0:
                                   e.memset(Ab[:, 128 * (qt + 1):ext], 0.0),
                                   wait=(tms0,), inc="ge")
            tmask[(n, 1)] = p.emit("gpsimd", lambda e, qt=qt1, Ab=Ab1: e.tensor_mul(
                Ab[:, 128 * qt:128 * (qt + 1)], Ab[:, 128 * qt:128 * (qt + 1)],
                c.diagmask[:]), wait=(texp[(n, 1)],), inc="ge")

        cstate = {}

        def cons_kt(n, kt, pr):
            Ab0, Ab1 = A4[2 * pr], A4[2 * pr + 1]
            tb = c.p4 if kt % 2 == 0 else c.p7
            to1 = p.emit("tensor", lambda e, kt=kt, tb=tb, Ab0=Ab0: e.matmul(
                tb[:, 0:128], Ab0[:, 128 * kt:128 * (kt + 1)], c.ident[:],
                start=True, stop=True), wait=(tmask[(n, 0)], ttr[kt % 2]), inc="pe")
            to2 = p.emit("tensor", lambda e, kt=kt, tb=tb, Ab1=Ab1: e.matmul(
                tb[:, 128:256], Ab1[:, 128 * kt:128 * (kt + 1)], c.ident[:],
                start=True, stop=True), wait=(tmask[(n, 1)],), inc="pe")
            tABr[(n, 0)] = to1
            tABr[(n, 1)] = to2
            if kt % 2 == 0:
                tev = p.emit("vector", lambda e, kt=kt, tb=tb: e.tensor_copy(
                    at2[:, 256 * (kt % 2):256 * (kt % 2) + 256], tb[:, 0:256]),
                    wait=(to1, to2, tat2r[kt % 2]), inc="ve")
            else:
                tev = p.emit("scalar", lambda e, kt=kt, tb=tb: e.copy(
                    at2[:, 256 * (kt % 2):256 * (kt % 2) + 256], tb[:, 0:256]),
                    wait=(to1, to2, tat2r[kt % 2]), inc="se")
            ttr[kt % 2] = tev
            cstate[(n, kt)] = tev

        def cons_av(n, kt, nkt, hh, pr):
            avbank = c.p5 if pr == 0 else c.p6
            w = [cstate.pop((n, kt)), tv_done]
            if kt == 0:
                w += [trecip.get(n - 2), tavs.get(n - 2)]
            tav = p.emit("tensor", lambda e, kt=kt, hh=hh, nkt=nkt, avbank=avbank:
                         e.matmul(avbank[0:65, 0:256],
                                  Vb[:, kt * WV + hh * 66: kt * WV + hh * 66 + 65],
                                  at2[:, 256 * (kt % 2):256 * (kt % 2) + 256],
                                  start=(kt == 0), stop=(kt == nkt - 1)),
                         wait=tuple(w), inc="pe")
            tat2r[kt % 2] = tav
            if kt == nkt - 1:
                tavfin[n] = tav

        def cons(n, half_):
            hh, tp, cch, coff, rch, roff, nkt = ageo(n)
            pr = n % 2
            h1 = (nkt + 1) // 2
            kts = range(0, h1) if half_ == 0 else range(h1, nkt)
            for kt in kts:
                cons_kt(n, kt, pr)
                if kt >= 1:
                    cons_av(n, kt - 1, nkt, hh, pr)
            if half_ == 1:
                cons_av(n, nkt - 1, nkt, hh, pr)
                avbank = c.p5 if pr == 0 else c.p6
                rsl = c.ftmp[0][0:1, 256 * pr:256 * pr + 256]
                avs = scr[0:64, 0:256] if pr == 0 else scr[0:64, 256:512]
                tln = p.emit("scalar", lambda e, rsl=rsl, avbank=avbank: e.activation(
                    rsl, avbank[64:65, 0:256], AF.Ln),
                    wait=(tavfin[n], tbc_t.get(n - 2)), inc="se")
                tiv = p.emit("scalar", lambda e, rsl=rsl: e.activation(
                    rsl, rsl, AF.Exp, scale=-1.0), wait=(tln,), inc="se", drain=True)
                trecip[n] = tiv
                tas = p.emit("scalar", lambda e, avs=avs, avbank=avbank: e.copy(
                    avs, avbank[0:64, 0:256]),
                    wait=(tavfin[n], toutm.get(n - 2)), inc="se")
                tavs[n] = tas

        def cons_tail(n):
            hh, tp, cch, coff, rch, roff, nkt = ageo(n)
            pr = n % 2
            avbank = c.p5 if pr == 0 else c.p6
            rsl = c.ftmp[0][0:1, 256 * pr:256 * pr + 256]
            avs = scr[0:64, 0:256] if pr == 0 else scr[0:64, 256:512]
            tbc = p.emit("tensor", lambda e, rsl=rsl, avbank=avbank: e.matmul(
                avbank[0:64, 256:512], c.onesrowf[0:1, 0:64], rsl,
                start=True, stop=True),
                wait=(trecip[n], tavs[n], toutm.get(n - 2)), inc="pe")
            tbc_t[n] = tbc
            toutm[n] = p.emit("vector", lambda e, tp=tp, cch=cch, coff=coff,
                              avs=avs, avbank=avbank: e.tensor_mul(
                Ob[cch][coff:coff + 64, 256 * tp:256 * (tp + 1)],
                avs, avbank[0:64, 256:512]), wait=(tbc,), inc="ve")

        for n in range(NIT + 2):
            if n < NIT:
                prod_first(n)
            if n >= 2:
                cons_tail(n - 2)
            if 1 <= n <= NIT:
                cons(n - 1, 0)
            if n < NIT:
                prod_second(n)
            if 1 <= n <= NIT:
                cons(n - 1, 1)
        tO_done = toutm[NIT - 1]
        tp4_guard = tO_done
        if DBG_L1A:
            p.barrier()
            tdmp = p.dma("sync", c.scratch("dbg_O0h%d" % half, (128, T), BF16),
                         Ob[0][:, :], sem="dDBG")
            tdmp = p.dma("sync", c.scratch("dbg_O1h%d" % half, (128, T), BF16),
                         Ob[1][:, :], sem="dDBG")
            p.emit("sync", None, wait=(tdmp,))
            p.barrier()
        twg = [tO_done, tO_done]
        tpx = {"p4": tp4_guard}
        tXl = None
        for oc in range(8):
            pa = oc % 2
            tw[pa] = p.dma("gpsimd", wst[pa][:, 0:512].rearrange("p (i q) -> p i q", i=4),
                           opj[512 * half:512 * (half + 1),
                               128 * oc:128 * (oc + 1)].rearrange("(i p) q -> p i q", p=128),
                           wait=(twg[pa], wguard[pa]), sem=f"dWo{pa}{sfx}")
            for g in range(4):
                s = slice(g * PB, (g + 1) * PB)
                tm = None
                for ic in range(4):
                    tm = p.emit("tensor", lambda e, ic=ic, s=s, pa=pa: e.matmul(
                        c.p4[:, 0:PB], wst[pa][:, 128 * ic:128 * (ic + 1)], Ob[ic][:, s],
                        start=(ic == 0), stop=(ic == 3)),
                        wait=(tw[pa], tpx.get("p4"), tO_done), inc="pe")
                twg[pa] = tm
                wguard[pa] = tm
                tXl = p.emit("vector", lambda e, oc=oc, s=s: e.tensor_add(
                    c.X[oc][:, s], c.X[oc][:, s], c.p4[:, 0:PB]),
                    wait=(tm, tpx.get("X")), inc="ve")
                tpx["p4"] = tXl
                tpx["X"] = tXl
        p.barrier()
    c.ar.pop()


def phase_l1attn(c):
    emit_l1attn(c)



def phase_dumpx(c):
    p = c.prog
    nc = c.nc
    p.barrier()
    oX = nc.dram_tensor("dbg_X0", [128, T], F32, kind="ExternalOutput").ap()
    oI = nc.dram_tensor("dbg_idf", [128, 128], F32, kind="ExternalOutput").ap()
    oS = nc.dram_tensor("dbg_smallf", [128, 64], F32, kind="ExternalOutput").ap()
    t = p.dma("sync", oX, c.X[0][:], sem="dDBG")
    t = p.dma("sync", oI, c.identf[:], sem="dDBG")
    t = p.dma("sync", oS, c.smallf[:], sem="dDBG")
    p.emit("sync", None, wait=(t,))
    p.barrier()


def phase_lntest(c):
    p = c.prog
    c.ar.push()
    st = alloc_stats(c)
    xn = [c.ar.take(T, BF16, f"xn{i}") for i in range(8)]
    emit_layernorm(c, xn, st)
    for ch in range(8):
        p.emit("vector", lambda e, ch=ch: e.tensor_copy(c.X[ch][:], xn[ch]), inc="ve")
    p.barrier()
    c.ar.pop()


PHASE_EMITTERS = {
    "lntest": phase_lntest,
    "meff": phase_meff,
    "l0ffn": phase_l0ffn,
    "l0attn": phase_l0attn,
    "l1ffn": phase_l1ffn,
    "l1attn": phase_l1attn,
    "dumpx": phase_dumpx,
}



ALL_PHASES = ["meff", "l0ffn", "l0attn", "l0ffn", "l1ffn", "l1attn", "l1ffn"]
LAUNCH_GROUPS = [ALL_PHASES]


def kernel(**inputs):
    W = _prep_weights(inputs)
    x = np.asarray(inputs["L0_attn_x"], np.float32)
    cur = [np.ascontiguousarray(x[b].T) for b in range(NCORE)]

    outs = None
    for phases in LAUNCH_GROUPS:
        nc, used, _dbg = build_program(phases)
        in_maps = []
        for core in range(NCORE):
            m = {}
            for name in used:
                m[name] = cur[core] if name == "xT" else W[name]
            in_maps.append(m)
        res = run_bass_kernel_spmd(nc, in_maps, core_ids=list(range(NCORE)))
        outs = [np.asarray(res.results[i]["out"]) for i in range(NCORE)]
        cur = [np.ascontiguousarray(o.T) for o in outs]

    return np.stack(outs, axis=0)


# version 09
# speedup vs baseline: 1.7240x; 1.0201x over previous
import math
from contextlib import ExitStack

import numpy as np
import ml_dtypes

import concourse.bass as bass
import concourse.mybir as mybir
from concourse.bass_utils import run_bass_kernel_spmd

F32 = mybir.dt.float32
BF16 = mybir.dt.bfloat16
I32 = mybir.dt.int32
AF = mybir.ActivationFunctionType
ALU = mybir.AluOpType
AX = mybir.AxisListType.X
BF = ml_dtypes.bfloat16

B, T, D = 8, 2048, 1024
DBG_L1A = False
NCORE = 8
PB = 512
NG = T // PB

ENGS = ("sync", "gpsimd", "tensor", "vector", "scalar")


class Tok:
    __slots__ = ("sem", "val", "order")

    def __init__(self, sem, val, order):
        self.sem, self.val, self.order = sem, val, order


class Prog:
    def __init__(self, nc):
        self.nc = nc
        self.q = {e: [] for e in ENGS}
        self.sems = {}
        self.cnt = {}
        self.order = 0
        self.bar_round = 0

    def sem(self, name):
        if name not in self.sems:
            self.sems[name] = self.nc.alloc_semaphore(name)
            self.cnt[name] = 0
        return self.sems[name]

    def emit(self, eng, fn, wait=(), inc=None, inc_by=1, drain=False):
        self.order += 1
        o = self.order
        wsd = {}
        for t in wait:
            if t is None:
                continue
            assert t.order < o, "wait on later-planned token"
            wsd[t.sem] = max(wsd.get(t.sem, 0), t.val)
        ws = list(wsd.items())
        tok = None
        if inc is not None:
            self.sem(inc)
            self.cnt[inc] += inc_by
            tok = Tok(inc, self.cnt[inc], o)
        self.q[eng].append((ws, fn, drain, (inc, inc_by) if inc else None))
        return tok

    def barrier(self):
        self.bar_round += 1
        bar = "bar"
        self.sem(bar)
        for e in ENGS:
            self.order += 1
            self.cnt[bar] += 1
            self.q[e].append(([], lambda eng: eng.drain(), False, (bar, 1)))
        tgt = 5 * self.bar_round
        for e in ENGS:
            self.order += 1
            self.q[e].append(([(bar, tgt)], None, False, None))

    def finalize(self, block):
        for name in ENGS:
            ops = self.q[name]

            def run(eng, ops=ops):
                seen = {}
                for ws, fn, drain, inc in ops:
                    for (s, v) in ws:
                        if seen.get(s, -1) < v:
                            eng.wait_ge(self.sems[s], v)
                            seen[s] = v
                    if drain:
                        eng.drain()
                    if fn is None:
                        continue
                    instr = fn(eng)
                    if inc is not None:
                        s, by = inc
                        instr.then_inc(self.sems[s], by)

            getattr(block, name)(run)

    def dma(self, queue, out, in_, wait=(), sem="dL"):
        if not hasattr(self, "_last_dma"):
            self._last_dma = {}
        w = list(wait) + [self._last_dma.get(sem)]
        tok = self.emit(queue, lambda e, o=out, i=in_: e.dma_start(out=o, in_=i),
                        wait=w, inc=sem, inc_by=16)
        self._last_dma[sem] = tok
        return tok


class Arena:
    def __init__(self, handle, nbytes):
        self.h = handle
        self.nbytes = nbytes
        self.off = 0
        self.marks = []

    def take(self, cols, dtype, name=""):
        isz = {F32: 4, BF16: 2, I32: 4}[dtype]
        nb = cols * isz
        off = (self.off + 63) & ~63
        assert off + nb <= self.nbytes, f"arena overflow at {name}: {off + nb}>{self.nbytes}"
        self.off = off + nb
        return self.h[:, off:off + nb].bitcast(dtype)

    def push(self):
        self.marks.append(self.off)

    def pop(self):
        self.off = self.marks.pop()



def _prep_weights(I):
    W = {}
    f32 = lambda a: np.ascontiguousarray(np.asarray(a, dtype=np.float32))

    qhead = lambda cch, half: (cch // 4) * 8 + (cch % 4) + 4 * half
    qperm = np.empty(1024, np.int64)
    for cch in range(8):
        for half in range(2):
            h = qhead(cch, half)
            qperm[cch * 128 + half * 64: cch * 128 + half * 64 + 64] = \
                np.arange(h * 64, (h + 1) * 64)
    W["l0f_wq"] = f32(np.asarray(I["L0_ffn_wq"]).T[:, qperm])
    W["l0f_wk"] = f32(np.asarray(I["L0_ffn_wk"]).T)
    W["l0f_wv"] = f32(np.asarray(I["L0_ffn_wv"]).T)
    W["l0f_wo"] = f32(np.asarray(I["L0_ffn_wo"]).T[qperm, :])
    qg = np.asarray(I["L0_ffn_q_gain"], np.float32)
    W["l0f_qgain"] = f32(np.stack([[qg[qhead(cch, half)] for cch in range(8)]
                                   for half in range(2)]))
    W["l0f_kgain"] = f32(np.asarray(I["L0_ffn_k_gain"]).reshape(2, 2).T)
    W["l0f_fc_logits"] = f32(I["L0_ffn_fc_logits"]).reshape(1, 8)
    W["l0f_pj_logits"] = f32(I["L0_ffn_pj_logits"]).reshape(1, 8)

    s1 = np.asarray(I["L0_ffn_fc_s1"], np.float32)
    s2 = np.asarray(I["L0_ffn_fc_s2"], np.float32)
    mix = np.asarray(I["L0_ffn_fc_mix"], np.float32)
    p1 = np.asarray(I["L0_ffn_pj_s1"], np.float32)
    p2 = np.asarray(I["L0_ffn_pj_s2"], np.float32)
    pmx = np.asarray(I["L0_ffn_pj_mix"], np.float32)

    fcAT = np.zeros((8, 8, 128, 128), np.float32)
    s1r = s1.reshape(8, 16, 64, 64)
    for a in range(8):
        for gg in range(2):
            fcAT[:, a, gg * 64:(gg + 1) * 64, gg * 64:(gg + 1) * 64] = \
                s1r[:, 2 * a + gg].transpose(0, 2, 1)
    W["fcAT"] = f32(fcAT)
    u = np.arange(16)[:, None]
    cc = np.arange(64)[None, :]
    fcP = np.empty((8, 1024, 4096), np.float32)
    for j in range(8):
        perm = (cc * 16 + (u + j) % 16).reshape(-1)
        fcP[j] = mix[j].T[perm]
    W["fcP"] = f32(fcP)
    s2r = s2.reshape(8, 64, 64, 64)
    fcB = np.zeros((8, 32, 128, 128), np.float32)
    for dch in range(32):
        for qq in range(2):
            fcB[:, dch, qq * 64:(qq + 1) * 64, qq * 64:(qq + 1) * 64] = s2r[:, 2 * dch + qq]
    W["fcB"] = f32(fcB)

    pjAT = np.zeros((8, 32, 128, 128), np.float32)
    p1r = p1.reshape(8, 64, 64, 64)
    for a in range(32):
        for gg in range(2):
            pjAT[:, a, gg * 64:(gg + 1) * 64, gg * 64:(gg + 1) * 64] = \
                p1r[:, 2 * a + gg].transpose(0, 2, 1)
    W["pjAT"] = f32(pjAT)
    u = np.arange(64)[:, None]
    pjP = np.empty((8, 4096, 1024), np.float32)
    for j in range(8):
        perm = (cc * 64 + (u + j) % 64).reshape(-1)
        pjP[j] = pmx[j].T[perm]
    W["pjP"] = f32(pjP)
    p2r = p2.reshape(8, 16, 64, 64)
    pjB = np.zeros((8, 8, 128, 128), np.float32)
    for dch in range(8):
        for qq in range(2):
            pjB[:, dch, qq * 64:(qq + 1) * 64, qq * 64:(qq + 1) * 64] = p2r[:, 2 * dch + qq]
    W["pjB"] = f32(pjB)

    W["l0a_inproj"] = f32(I["L0_attn_in_proj_w"])
    W["l0a_outproj"] = f32(I["L0_attn_out_proj_w"])
    W["l0a_convk"] = f32(np.asarray(I["L0_attn_conv_k"]).reshape(4, 4128).T)
    W["l0a_convb"] = f32(I["L0_attn_conv_b"]).reshape(4128, 1)
    W["l0a_theta"] = f32(np.asarray(I["L0_attn_theta"]).reshape(2048, 1))
    W["l0a_decay"] = f32(I["L0_attn_decay_slopes"]).reshape(1, 28)
    W["l0a_anchor"] = f32(I["L0_attn_anchor_slopes"]).reshape(1, 4)
    W["l0a_sscale"] = f32(np.asarray(I["L0_attn_score_scale"]).reshape(32, 1))
    W["l0a_dlog"] = f32(I["L0_attn_deriv_logits"]).reshape(1, 3)
    W["l0a_wre"] = f32(I["L0_attn_W_re"])
    W["l0a_wim"] = f32(I["L0_attn_W_im"])
    nsc = np.asarray(I["L0_attn_norm_scale"], np.float32)
    W["l0a_nsc_re"] = f32(np.tile(nsc[:64], 2).reshape(128, 1))
    W["l0a_nsc_im"] = f32(np.tile(nsc[64:], 2).reshape(128, 1))

    for k in ("W_kv", "W_kdec", "W_vdec", "W_q", "W_qdec", "W_o"):
        W["l1a_" + k] = f32(I["L1_attn_" + k])
    W["l1a_g_kv"] = f32(np.asarray(I["L1_attn_g_kv"]).reshape(2, 128).T)

    def rope_cols(wsrc):
        src = f32(I[wsrc])
        dst = np.zeros((src.shape[0], 2 * 384), np.float32)
        for h in range(16):
            half, hh = divmod(h, 8)
            cidx, r = divmod(hh, 3)
            dst[:, half * 384 + cidx * 128 + r * 32: half * 384 + cidx * 128 + r * 32 + 32] = \
                src[:, h * 32:(h + 1) * 32]
        return np.ascontiguousarray(dst)
    W["l1a_W_krope"] = rope_cols("L1_attn_W_krope")
    W["l1a_W_qrope"] = rope_cols("L1_attn_W_qrope")

    W["l1f_colembT"] = f32(np.asarray(I["L1_ffn_col_emb"]).T)
    W["l1f_gw1"] = f32(I["L1_ffn_gate_w1"])
    W["l1f_gb1"] = f32(np.asarray(I["L1_ffn_gate_b1"]).reshape(4, 128).T)
    W["l1f_gw2"] = f32(I["L1_ffn_gate_w2"])
    W["l1f_gb2"] = f32(np.asarray(I["L1_ffn_gate_b2"]).reshape(1, 16))
    W["l1f_Wr"] = f32(np.asarray(I["L1_ffn_W_cols"]).transpose(1, 0, 2).reshape(1024, 4096))
    W["l1f_bcols"] = f32(np.asarray(I["L1_ffn_b_cols"]).reshape(32, 128).T)
    W["l1f_projw"] = f32(I["L1_ffn_proj_w"])
    W["l1f_projb"] = f32(np.asarray(I["L1_ffn_proj_b"]).reshape(8, 128).T)
    W["l1f_lng"] = f32(np.asarray(I["L1_ffn_ln_g"]).reshape(8, 128).T)
    W["l1f_lnb"] = f32(np.asarray(I["L1_ffn_ln_b"]).reshape(8, 128).T)
    return W


_INPUT_SHAPES = {
    "xT": (D, T),
    "l0f_wq": (1024, 1024), "l0f_wk": (1024, 256), "l0f_wv": (1024, 256),
    "l0f_wo": (1024, 1024), "l0f_qgain": (2, 8), "l0f_kgain": (2, 2),
    "l0f_fc_logits": (1, 8), "l0f_pj_logits": (1, 8),
    "fcAT": (8, 8, 128, 128), "fcP": (8, 1024, 4096), "fcB": (8, 32, 128, 128),
    "pjAT": (8, 32, 128, 128), "pjP": (8, 4096, 1024), "pjB": (8, 8, 128, 128),
    "l0a_inproj": (1024, 4128), "l0a_outproj": (2048, 1024),
    "l0a_convk": (4128, 4), "l0a_convb": (4128, 1), "l0a_theta": (2048, 1),
    "l0a_decay": (1, 28), "l0a_anchor": (1, 4), "l0a_sscale": (32, 1),
    "l0a_dlog": (1, 3), "l0a_wre": (64, 64), "l0a_wim": (64, 64),
    "l0a_nsc_re": (128, 1), "l0a_nsc_im": (128, 1),
    "l1a_W_kv": (1024, 256), "l1a_W_kdec": (256, 1024), "l1a_W_vdec": (256, 1024),
    "l1a_W_q": (1024, 512), "l1a_W_qdec": (512, 1024), "l1a_W_o": (1024, 1024),
    "l1a_g_kv": (128, 2), "l1a_W_krope": (1024, 768), "l1a_W_qrope": (512, 768),
    "l1f_colembT": (1024, 16), "l1f_gw1": (1024, 512), "l1f_gb1": (128, 4),
    "l1f_gw2": (512, 16), "l1f_gb2": (1, 16), "l1f_Wr": (1024, 4096),
    "l1f_bcols": (128, 32), "l1f_projw": (4096, 1024), "l1f_projb": (128, 8),
    "l1f_lng": (128, 8), "l1f_lnb": (128, 8),
}


def _host_consts():
    C = {}
    C["ident_bf"] = np.eye(128, dtype=BF)
    C["ident_f32"] = np.eye(128, dtype=np.float32)
    C["ones_col_bf"] = np.ones((128, 1), dtype=BF)
    C["ones_row_bf"] = np.ones((1, 128), dtype=BF)
    C["ones_row_f32"] = np.ones((1, 128), dtype=np.float32)
    r = np.arange(128)
    C["diagmask_bf"] = (r[None, :] <= r[:, None]).astype(BF)

    hd = 64
    inv = 1.0 / (10000.0 ** (np.arange(0, hd, 2, dtype=np.float64) / hd))
    fr = np.outer(np.arange(T, dtype=np.float64), inv)
    cosv = np.cos(fr).astype(np.float32).T
    sinv = np.sin(fr).astype(np.float32).T
    C["l0f_ropeC"] = np.ascontiguousarray(np.tile(np.concatenate([cosv, cosv], 0), (2, 1)))
    C["l0f_ropeS"] = np.ascontiguousarray(np.tile(np.concatenate([sinv, sinv], 0), (2, 1)))
    M64 = np.zeros((64, 64), np.float32)
    for j in range(32):
        M64[j, 32 + j] = 1.0
        M64[32 + j, j] = -1.0
    M128 = np.zeros((128, 128), np.float32)
    M128[:64, :64] = M64
    M128[64:, 64:] = M64
    C["l0f_rotT"] = np.ascontiguousarray(M128.T).astype(BF)

    rd = 32
    inv1 = 1.0 / (10000.0 ** (np.arange(0, rd, 2, dtype=np.float64) / rd))
    fr1 = np.outer(np.arange(T, dtype=np.float64), inv1)
    c1 = np.cos(fr1).astype(np.float32).T
    s1 = np.sin(fr1).astype(np.float32).T
    c32 = np.concatenate([c1, c1], 0)
    s32 = np.concatenate([s1, s1], 0)
    padC = np.zeros((128, T), np.float32)
    padS = np.zeros((128, T), np.float32)
    for rr in range(3):
        padC[rr * 32:(rr + 1) * 32] = c32
        padS[rr * 32:(rr + 1) * 32] = s32
    C["l1a_ropeC"] = np.ascontiguousarray(padC)
    C["l1a_ropeS"] = np.ascontiguousarray(padS)
    M32 = np.zeros((32, 32), np.float32)
    for j in range(16):
        M32[j, 16 + j] = -1.0
        M32[16 + j, j] = 1.0
    M1 = np.zeros((128, 128), np.float32)
    for rr in range(3):
        M1[rr * 32:(rr + 1) * 32, rr * 32:(rr + 1) * 32] = M32
    C["l1a_rotT"] = np.ascontiguousarray(M1.T).astype(BF)
    C["l1a_rotT_f32"] = np.ascontiguousarray(M1.T)

    hs = np.zeros((128, 2), np.float32)
    hs[:64, 0] = 1.0
    hs[64:, 1] = 1.0
    C["headsel_bf"] = hs.astype(BF)
    C["headbc_bf"] = np.ascontiguousarray(hs.T).astype(BF)
    C["headsel_f32"] = hs
    C["headbc_f32"] = np.ascontiguousarray(hs.T)
    C["l0f_rotT_f32"] = np.ascontiguousarray(M128.T)
    C["trow_f32"] = np.arange(T, dtype=np.float32).reshape(1, T)
    C["drow_f32"] = (T - 1.0 - np.arange(T, dtype=np.float32)).reshape(1, T)
    repl = np.zeros((32, 16 * 128), np.float32)
    for hp in range(16):
        repl[2 * hp, hp * 128:hp * 128 + 64] = 1.0
        repl[2 * hp + 1, hp * 128 + 64:(hp + 1) * 128] = 1.0
    C["replc_bf"] = repl.astype(BF)
    return C



class Ctx:
    pass


def build_program(phases, dbg=()):
    nc = bass.Bass("TRN2", target_bir_lowering=False, dynamic_dma_scratch_size=16384)
    prog = Prog(nc)
    c = Ctx()
    c.nc, c.prog = nc, prog
    c.used = set()
    c.hostc = _host_consts()
    c.inline = {}
    c.dbg_req = list(dbg)
    c.dbg_out = {}

    def din(name):
        if name not in c.used:
            c.used.add(name)
            c._dins[name] = nc.dram_tensor(name, list(_INPUT_SHAPES[name]), F32,
                                           kind="ExternalInput").ap()
        return c._dins[name]
    c._dins = {}
    c.din = din

    def const(name):
        if name not in c.inline:
            arr = np.ascontiguousarray(c.hostc[name])
            c.inline[name] = nc.inline_tensor(arr, name).ap()
        return c.inline[name]
    c.const = const

    c._scratch = {}

    def scratch(name, shape, dtype):
        if name not in c._scratch:
            c._scratch[name] = nc.dram_tensor(name, list(shape), dtype,
                                              kind="Internal").ap()
        return c._scratch[name]
    c.scratch = scratch

    c.out = nc.dram_tensor("out", [T, D], F32, kind="ExternalOutput").ap()

    with ExitStack() as stack:
        ec = stack.enter_context
        c.X = [ec(nc.sbuf_tensor(f"X{i}", [128, T], F32)) for i in range(8)]
        c.epscol = ec(nc.sbuf_tensor("epscol", [1, 1], F32))
        c.ident = ec(nc.sbuf_tensor("identt", [128, 128], BF16))
        c.identf = ec(nc.sbuf_tensor("identtf", [128, 128], F32))
        c.onescol = ec(nc.sbuf_tensor("onescol", [128, 1], BF16))
        c.onesrow = ec(nc.sbuf_tensor("onesrow", [1, 128], BF16))
        c.onesrowf = ec(nc.sbuf_tensor("onesrowf", [1, 128], F32))
        c.diagmask = ec(nc.sbuf_tensor("diagmask", [128, 128], BF16))
        c.smallf = ec(nc.sbuf_tensor("smallf", [128, 160], F32))
        c.headself = ec(nc.sbuf_tensor("headself", [128, 2], F32))
        c.headbcf = ec(nc.sbuf_tensor("headbcf", [2, 128], F32))
        c.rotTf = ec(nc.sbuf_tensor("rotTf", [128, 128], F32))
        c.ftmp = [ec(nc.sbuf_tensor(f"ftmp{i}", [128, PB], F32)) for i in range(2)]
        ARENA_BYTES = 136 * 1024
        c.ar = Arena(ec(nc.sbuf_tensor("arena", [128, ARENA_BYTES], mybir.dt.uint8)),
                     ARENA_BYTES)
        c.pS = ec(nc.psum_tensor("pS", [128, 4 * PB], F32))
        c.p4 = ec(nc.psum_tensor("p4", [128, PB], F32))
        c.p5 = ec(nc.psum_tensor("p5", [128, PB], F32))
        c.p6 = ec(nc.psum_tensor("p6", [128, PB], F32))
        c.p7 = ec(nc.psum_tensor("p7", [128, PB], F32))
        c.pb = [c.pS[:, 0:PB], c.pS[:, PB:2 * PB], c.pS[:, 2 * PB:3 * PB],
                c.pS[:, 3 * PB:4 * PB], c.p4[:, :], c.p5[:, :], c.p6[:, :], c.p7[:, :]]

        block = ec(nc.Block())

        _emit_preamble(c)
        for ph in phases:
            PHASE_EMITTERS[ph](c)
        _emit_output(c)

        prog.finalize(block)

    return nc, sorted(c.used), dict(c.dbg_out)


def _dbg_dump(c, name, src_ap, shape):
    if name not in c.dbg_req:
        return
    o = c.nc.dram_tensor("dbg_" + name, list(shape), F32, kind="ExternalOutput").ap()
    c.dbg_out[name] = "dbg_" + name
    c.prog.barrier()
    t = c.prog.dma("sync", o, src_ap, sem="dS")
    c.prog.emit("sync", None, wait=(t,))
    c.prog.barrier()


def _emit_preamble(c):
    p = c.prog
    toks = [
        p.dma("sync", c.ident[:], c.const("ident_bf")),
        p.dma("sync", c.identf[:], c.const("ident_f32")),
        p.dma("sync", c.onescol[:], c.const("ones_col_bf")),
        p.dma("sync", c.onesrow[:], c.const("ones_row_bf")),
        p.dma("sync", c.onesrowf[:], c.const("ones_row_f32")),
        p.dma("sync", c.diagmask[:], c.const("diagmask_bf")),
        p.dma("sync", c.headself[:], c.const("headsel_f32")),
        p.dma("sync", c.headbcf[:], c.const("headbc_f32")),
        p.dma("sync", c.rotTf[:], c.const("l0f_rotT_f32")),
    ]
    xr = c.din("xT").rearrange("(c p) t -> c p t", p=128)
    for i in range(8):
        toks.append(p.dma("sync", c.X[i][:], xr[i]))
    p.emit("vector", lambda e: e.memset(c.epscol[:], 1e-5), inc="ve")
    tok = toks[-1]
    for e in ENGS:
        p.emit(e, None, wait=(tok,))
    p.barrier()


def _emit_output(c):
    p = c.prog
    p.barrier()
    outr = c.out.rearrange("(tt p) d -> tt p d", p=128)
    evac = [c.ftmp[0], c.ftmp[1]]
    last = [None, None]
    tv_prev = [None, None]
    for tt in range(16):
        for half in range(2):
            bank = (c.p4 if half == 0 else c.p5)
            ev = evac[half]
            tk = None
            for k in range(4):
                ch = half * 4 + k
                tk = p.emit("tensor",
                            lambda e, ch=ch, tt=tt, bank=bank, k=k:
                            e.matmul(bank[:, k * 128:(k + 1) * 128],
                                     c.X[ch][:, tt * 128:(tt + 1) * 128],
                                     c.identf[:], start=True, stop=True),
                            wait=(tv_prev[half],), inc="pe")
            tv = p.emit("vector", lambda e, bank=bank, ev=ev: e.tensor_copy(ev[:], bank[:, 0:PB]),
                        wait=(tk, last[half]), inc="ve")
            tv_prev[half] = tv
            last[half] = p.dma("sync", outr[tt][:, half * 512:(half + 1) * 512], ev[:],
                               wait=(tv,), sem="dSo%d" % half)
    p.emit("sync", None, wait=(last[0], last[1]))



def emit_layernorm(c, xn_chunks, st, scratch4=None, eps=1e-5):
    p = c.prog
    own = scratch4 is None
    if own:
        c.ar.push()
        scratch4 = [c.ar.take(T, BF16, f"ln_s{i}") for i in range(4)]
    xb = scratch4[0:2]
    sq = scratch4[2:4]
    p.barrier()
    sqb = [c.p4, c.p5, c.p6, c.p7]
    last_use = [None, None]
    tsum = None
    for ch in range(8):
        pa = ch % 2
        t1 = p.emit("vector", lambda e, ch=ch, pa=pa: e.tensor_copy(xb[pa], c.X[ch][:]),
                    wait=(last_use[pa],), inc="ve")
        t2 = p.emit("scalar", lambda e, pa=pa: e.activation(sq[pa], xb[pa], AF.Square),
                    wait=(t1,), inc="se")
        for g in range(4):
            s = slice(g * PB, (g + 1) * PB)
            p.emit("tensor", lambda e, s=s, ch=ch, g=g, pa=pa: e.matmul(
                c.pS[0:1, s.start:s.stop], c.onescol[:], xb[pa][:, s],
                start=(ch == 0), stop=(ch == 7)), wait=(t1,), inc="pe")
            tsum = p.emit("tensor", lambda e, s=s, ch=ch, g=g, pa=pa: e.matmul(
                sqb[g][0:1, 0:PB], c.onescol[:], sq[pa][:, s],
                start=(ch == 0), stop=(ch == 7)), wait=(t2,), inc="pe")
        last_use[pa] = tsum
    r1 = c.ftmp[0][0:1, 0:PB]
    r2 = c.ftmp[1][0:1, 0:PB]
    tg_done = None
    mu, nmu, rstd = st["mu"], st["nmu"], st["rstd"]
    for g in range(4):
        s = slice(g * PB, (g + 1) * PB)
        t1 = p.emit("vector", lambda e, s=s: e.tensor_scalar_mul(
            mu[:, s], c.pS[0:1, s.start:s.stop], 1.0 / D), wait=(tsum, tg_done), inc="ve")
        t1b = p.emit("vector", lambda e, s=s, g=g: e.tensor_scalar_mul(
            r1, sqb[g][0:1, 0:PB], 1.0 / D), wait=(tsum, t1), inc="ve")
        t2 = p.emit("scalar", lambda e, s=s: e.activation(r2, mu[:, s], AF.Square),
                    wait=(t1, tg_done), inc="se")
        t3 = p.emit("vector", lambda e: e.tensor_sub(r1, r1, r2),
                    wait=(t2, t1b), inc="ve")
        t4 = p.emit("scalar", lambda e: e.activation(r2, r1, AF.Ln, bias=c.epscol[0:1, 0:1]),
                    wait=(t3,), inc="se")
        t5 = p.emit("scalar", lambda e, s=s: e.activation(rstd[:, s], r2, AF.Exp, scale=-0.5),
                    wait=(t4,), inc="se")
        tg_done = p.emit("vector", lambda e, s=s: e.tensor_scalar_mul(
            nmu[:, s], mu[:, s], -1.0), wait=(t5,), inc="ve")
    p.barrier()
    for g in range(4):
        s = slice(g * PB, (g + 1) * PB)
        tb1 = p.emit("tensor", lambda e, s=s: e.matmul(
            c.p4[:, 0:PB], c.onesrowf[:], nmu[0:1, s], start=True, stop=True), inc="pe")
        tb2 = p.emit("tensor", lambda e, s=s: e.matmul(
            c.p5[:, 0:PB], c.onesrowf[:], rstd[0:1, s], start=True, stop=True), inc="pe")
        tvs = [None, None]
        for ch in range(8):
            ft = c.ftmp[ch % 2]
            t1 = p.emit("vector", lambda e, ch=ch, s=s, ft=ft: e.tensor_add(
                ft[:], c.X[ch][:, s], c.p4[:, 0:PB]),
                wait=(tb1, tvs[ch % 2]), inc="ve")
            tvs[ch % 2] = p.emit("vector", lambda e, ch=ch, s=s, ft=ft: e.tensor_mul(
                xn_chunks[ch][:, s], ft[:], c.p5[:, 0:PB]),
                wait=(tb2, t1), inc="ve")
        p.barrier()
    if own:
        c.ar.pop()


def alloc_stats(c):
    return {"mu": c.ar.take(T, F32, "st_mu")[0:1, :],
            "nmu": c.ar.take(T, F32, "st_nmu")[0:1, :],
            "rstd": c.ar.take(T, F32, "st_rstd")[0:1, :]}



def emit_alpha(c, logits_name, out_cols, wait=()):
    p = c.prog
    nc = c.nc
    c.ar.push()
    row = c.ar.take(64, F32, "al_rows")
    lg = row[0:1, 0:8]
    m1 = row[0:1, 8:9]
    nm1 = row[0:1, 9:10]
    eq1 = row[0:1, 16:24]
    t1 = row[0:1, 24:32]
    t2 = row[0:1, 32:40]
    m2 = row[0:1, 10:11]
    mask = row[0:1, 40:48]
    expv = row[0:1, 48:56]
    ssum = row[0:1, 11:12]
    rsum = row[0:1, 12:13]
    alpha = row[0:1, 56:64]

    tl = p.dma("sync", lg, c.din(logits_name), wait=wait)
    v = lambda fn, wait, inc="ve": p.emit("vector", fn, wait=wait, inc=inc)
    t = v(lambda e: e.reduce_max(m1, lg, axis=AX), (tl,))
    t = v(lambda e: e.tensor_scalar(eq1, lg, m1, None, ALU.is_equal), (t,))
    t = v(lambda e: e.tensor_mul(t1, eq1, lg), (t,))
    t = v(lambda e: e.tensor_sub(t1, lg, t1), (t,))
    t = v(lambda e: e.tensor_scalar(t2, eq1, 1e38, None, ALU.mult), (t,))
    t = v(lambda e: e.tensor_sub(t2, t1, t2), (t,))
    t = v(lambda e: e.reduce_max(m2, t2, axis=AX), (t,))
    tmask = v(lambda e: e.tensor_scalar(mask, lg, m2, None, ALU.is_ge), (t,))
    tneg = v(lambda e: e.tensor_scalar_mul(nm1, m1, -1.0), (tmask,))
    te = p.emit("scalar", lambda e: e.activation(expv, lg, AF.Exp, bias=nm1),
                wait=(tneg,), inc="se")
    t = v(lambda e: e.tensor_mul(expv, expv, mask), (te,))
    t = v(lambda e: e.reduce_sum(ssum, expv, axis=AX), (t,))
    t = v(lambda e: e.reciprocal(rsum, ssum), (t,))
    t = v(lambda e: e.tensor_scalar(alpha, expv, rsum, None, ALU.mult), (t,))
    tb = p.emit("tensor", lambda e: e.matmul(c.p7[:, 0:8], c.onesrowf[:], alpha,
                                             start=True, stop=True), wait=(t,), inc="pe")
    tout = p.emit("vector", lambda e: e.tensor_copy(out_cols, c.p7[:, 0:8]),
                  wait=(tb,), inc="ve")
    c.ar.pop()
    return tout


def emit_meff_build(c):
    p = c.prog
    nc = c.nc
    mfc = c.scratch("Mfc_t", (128, 32768), BF16)
    mpj = c.scratch("Mpj_t", (128, 32768), BF16)
    afc = c.smallf[:, 0:8]
    apj = c.smallf[:, 8:16]
    p.barrier()

    c.ar.push()
    atbF = c.ar.take(8 * 8 * 128, BF16, "bd_atF")
    atbP = c.ar.take(32 * 8 * 128, BF16, "bd_atP")
    bb = [c.ar.take(8 * 512, BF16, "bd_b0"), c.ar.take(8 * 512, BF16, "bd_b1")]
    NPR = 10
    pring = [c.ar.take(512, BF16, "bd_p%d" % i) for i in range(NPR)]
    ztb = [c.ar.take(512, BF16, "bd_zt0"), c.ar.take(512, BF16, "bd_zt1")]
    mo = [c.ar.take(512, BF16, "bd_mo0"), c.ar.take(512, BF16, "bd_mo1")]

    t_at = {}
    for which, atb, NA in (("fc", atbF, 8), ("pj", atbP, 32)):
        AT_in = c.din("fcAT" if which == "fc" else "pjAT")
        for a in range(NA):
            t_at[(which, a)] = p.dma(
                "gpsimd", atb[:, 1024 * a:1024 * (a + 1)].rearrange("p (j q) -> p j q", j=8),
                AT_in[:, a].rearrange("j p q -> p j q"), sem="dW_at%d" % (a % 4))

    t_afc = emit_alpha(c, "l0f_fc_logits", afc)
    t_apj = emit_alpha(c, "l0f_pj_logits", apj, wait=(t_afc,))
    p.barrier()

    cfgs = {
        "fc": (c.din("fcP"), c.din("fcB"), atbF, afc, t_afc, 8, 8,
               mfc.rearrange("p (uc i q) -> p uc i q", uc=32, i=8, q=128)),
        "pj": (c.din("pjP"), c.din("pjB"), atbP, apj, t_apj, 32, 2,
               mpj.rearrange("p (o a q) -> p o a q", o=8, a=32, q=128)),
    }
    its = []
    dglist = []
    for which in ("fc", "pj"):
        _, _, _, _, _, NA, NDG, _ = cfgs[which]
        for dg in range(NDG):
            gdg = len(dglist)
            dglist.append((which, dg))
            for a in range(NA):
                its.append((which, dg, a, gdg))
    flat = []
    gacnt = 0
    for which, dg, a, gdg in its:
        for j in range(8):
            flat.append((which, dg, a, j, gdg, gacnt))
        gacnt += 1
    NI = len(flat)

    tstore = [None, None]
    tbload = {}
    tbscale = {}
    tbbfree = [None, None]
    tpl = {}
    tpfree = {}
    tev = {}
    ts2 = {}
    tmo_t = {}

    def bb_load(gdg):
        which, dg = dglist[gdg]
        _, B_in, _, acol, t_alpha, _, _, _ = cfgs[which]
        par = gdg % 2
        for j in range(8):
            tbload[(gdg, j)] = p.dma(
                "gpsimd", bb[par][:, 512 * j:512 * (j + 1)].rearrange(
                    "p (d q) -> p d q", d=4),
                B_in[j, 4 * dg:4 * dg + 4].rearrange("d p q -> p d q"),
                wait=(tbbfree[par],), sem="dW_b%d" % par)
        tsc = None
        for j in range(8):
            tsc = p.emit("vector", lambda e, par=par, j=j: e.tensor_scalar(
                bb[par][:, 512 * j:512 * (j + 1)], bb[par][:, 512 * j:512 * (j + 1)],
                acol[:, j:j + 1], None, ALU.mult),
                wait=(tbload[(gdg, j)], t_alpha), inc="ve")
        tbscale[gdg] = tsc

    def p_load(gi):
        which, dg, a, j, _, _ = flat[gi]
        P_in = cfgs[which][0]
        slot = gi % NPR
        tpl[gi] = p.dma("gpsimd", pring[slot],
                        P_in[j, 128 * a:128 * (a + 1), 512 * dg:512 * (dg + 1)],
                        wait=(tpfree.get(slot),), sem="dW_p%d" % slot)

    PF = 8
    bb_load(0)
    bb_load(1)
    for gi in range(PF):
        p_load(gi)

    for gi in range(NI + 1):
        if gi < NI:
            which, dg, a, j, gdg, gac = flat[gi]
            atb = cfgs[which][2]
            slot = gi % NPR
            ps1 = c.p4 if gi % 2 == 0 else c.p5
            t1 = None
            for tt in range(4):
                t1 = p.emit("tensor", lambda e, tt=tt, ps1=ps1, slot=slot, atb=atb, a=a, j=j:
                            e.matmul(ps1[:, 128 * tt:128 * (tt + 1)],
                                     pring[slot][:, 128 * tt:128 * (tt + 1)],
                                     atb[:, 1024 * a + 128 * j:1024 * a + 128 * (j + 1)],
                                     start=True, stop=True),
                            wait=(tpl[gi], t_at[(which, a)], tev.get(gi - 2)), inc="pe")
            tpfree[slot] = t1
            if gi + PF < NI:
                p_load(gi + PF)
            tev[gi] = p.emit("vector", lambda e, gi=gi, ps1=ps1: e.tensor_copy(
                ztb[gi % 2], ps1[:, 0:PB]), wait=(t1, ts2.get(gi - 2)), inc="ve")
        if gi >= 1:
            which1, dg1, a1, j1, gdg1, gac1 = flat[gi - 1]
            Mv = cfgs[which1][7]
            bank = c.p6 if gac1 % 2 == 0 else c.p7
            tl = None
            for tt in range(4):
                tl = p.emit("tensor", lambda e, tt=tt, bank=bank, gi=gi, j1=j1, gdg1=gdg1:
                            e.matmul(bank[:, 128 * tt:128 * (tt + 1)],
                                     ztb[(gi - 1) % 2][:, 128 * tt:128 * (tt + 1)],
                                     bb[gdg1 % 2][:, 512 * j1 + 128 * tt:512 * j1 + 128 * (tt + 1)],
                                     start=(j1 == 0 and tt == 0),
                                     stop=(j1 == 7 and tt == 3)),
                            wait=(tev[gi - 1], tbscale[gdg1],
                                  tmo_t.get(gac1 - 2)), inc="pe")
            ts2[gi - 1] = tl
            if j1 == 7:
                last_a = (gi == NI) or (flat[gi][4] != gdg1)
                if last_a:
                    tbbfree[gdg1 % 2] = tl
                    if gdg1 + 2 < len(dglist):
                        bb_load(gdg1 + 2)
                ms = gac1 % 2
                tmo = p.emit("vector", lambda e, ms=ms, bank=bank: e.tensor_copy(
                    mo[ms], bank[:, 0:PB]), wait=(tl, tstore[ms]), inc="ve")
                tmo_t[gac1] = tmo
                dst = Mv[:, 4 * dg1:4 * dg1 + 4, a1, :]
                src = mo[ms].rearrange("p (u q) -> p u q", q=128)
                tstore[ms] = p.dma("sync", dst, src, wait=(tmo,), sem="dS_mo%d" % ms)
    p.emit("sync", None, wait=(tstore[0], tstore[1]))
    p.barrier()
    c.ar.pop()


def phase_meff(c):
    emit_meff_build(c)



def emit_l0ffn(c, tag):
    p = c.prog
    nc = c.nc
    c.ar.push()
    st = alloc_stats(c)
    xnO = [c.ar.take(T, BF16, "xnO") for _ in range(8)]
    qTi = [c.ar.take(T, BF16, "qTi") for _ in range(8)]
    kxh = [c.ar.take(T, BF16, "kxh") for _ in range(2)]
    WV = 264
    Vb = c.ar.take(16 * WV, BF16, "Vb")
    sh = [c.ar.take(T, BF16, "sh") for _ in range(2)]
    at2 = c.ar.take(PB, BF16, "at2")
    sqf = c.ar.take(PB, F32, "sqf")
    qrawf = c.ar.take(PB, F32, "qrawf")
    qnf = c.ar.take(PB, F32, "qnf")
    avsf = c.ar.take(256, F32, "avsf")
    row1 = c.ar.take(PB, F32, "row1")
    row2 = c.ar.take(PB, F32, "row2")
    colsc = c.ar.take(64, F32, "colsc")
    wq2ab = c.ar.take(2048, BF16, "wq2ab")
    wq2 = [wq2ab[:, 0:1024], wq2ab[:, 1024:2048]]
    wkv = [c.ar.take(2048, BF16, "wkva")]

    emit_layernorm(c, xnO, st, scratch4=qTi[0:4])
    p.barrier()

    sfx = tag
    tg2 = p.dma("sync", c.smallf[0:2, 16:24], c.din("l0f_qgain"), sem="dL")
    tg3 = p.dma("sync", c.smallf[0:2, 24:26], c.din("l0f_kgain"), sem="dL")
    trC = p.dma("gpsimd", sh[0], c.const("l0f_ropeC"), sem="dW_rc" + sfx)
    trS = p.dma("gpsimd", sh[1], c.const("l0f_ropeS"), sem="dW_rs" + sfx)
    te6 = p.emit("vector", lambda e: e.memset(colsc[:, 0:1], 1e-6), inc="ve")
    qg2 = c.smallf[0:2, 16:24]
    kg2 = c.smallf[0:2, 24:26]
    eps6 = colsc[0:2, 0:1]

    def proj_normed(kind, tprev):
        if kind == "q":
            nco, wname, gcol, dst = 8, "l0f_wq", qg2, qTi
        else:
            nco, wname, gcol, dst = 2, "l0f_wk", kg2, kxh
        w_in = c.din(wname)
        tw = [None, None]
        guards = [tprev.get("wq0"), tprev.get("wq1")]
        tlast = None
        for oc in range(nco):
            pa = oc % 2
            wb = wq2[pa]
            tw[pa] = p.dma("gpsimd", wb.rearrange("p (i q) -> p i q", i=8),
                           w_in[:, 128 * oc:128 * (oc + 1)].rearrange("(i p) q -> p i q", p=128),
                           wait=(guards[pa],), sem=f"dW_{kind}{pa}{sfx}")
            for tgi in range(4):
                s = slice(tgi * PB, (tgi + 1) * PB)
                tq = None
                for ic in range(8):
                    tq = p.emit("tensor", lambda e, ic=ic, s=s, wb=wb:
                                e.matmul(c.p4[:, 0:PB], wb[:, 128 * ic:128 * (ic + 1)],
                                         xnO[ic][:, s], start=(ic == 0), stop=(ic == 7)),
                                wait=(tw[pa], tprev.get("p4")), inc="pe")
                guards[pa] = tq
                ts = p.emit("scalar", lambda e: e.activation(sqf, c.p4[:, 0:PB], AF.Square),
                            wait=(tq, tprev.get("sqf")), inc="se")
                tq2 = p.emit("scalar", lambda e: e.copy(qrawf, c.p4[:, 0:PB]),
                             wait=(tq, tprev.get("qrawf")), inc="se")
                tprev["p4"] = tq2
                tm = p.emit("tensor", lambda e: e.matmul(c.p5[0:2, 0:PB], c.headself[:],
                                                         sqf, start=True, stop=True),
                            wait=(ts, tprev.get("p5")), inc="pe")
                t1 = p.emit("vector", lambda e: e.tensor_scalar_mul(
                    row1[0:2, :], c.p5[0:2, 0:PB], 1.0 / 64),
                    wait=(tm, tprev.get("row1")), inc="ve")
                tprev["p5"] = t1
                t2 = p.emit("scalar", lambda e: e.activation(
                    row2[0:2, :], row1[0:2, :], AF.Ln, bias=eps6),
                    wait=(t1, te6, tprev.get("row2")), inc="se")
                t3 = p.emit("scalar", lambda e: e.activation(
                    row1[0:2, :], row2[0:2, :], AF.Exp, scale=-0.5),
                    wait=(t2,), inc="se")
                tprev["row2"] = t3
                t4 = p.emit("vector", lambda e, oc=oc, gcol=gcol: e.tensor_scalar(
                    row1[0:2, :], row1[0:2, :], gcol[:, oc:oc + 1], None, ALU.mult),
                    wait=(t3, tg2, tg3), inc="ve")
                tprev["row1"] = t4
                t5 = p.emit("tensor", lambda e: e.matmul(c.p6[:, 0:PB], c.headbcf[:],
                                                         row1[0:2, :], start=True, stop=True),
                            wait=(t4, tprev.get("p6")), inc="pe")
                t6 = p.emit("vector", lambda e: e.tensor_mul(qnf, qrawf, c.p6[:, 0:PB]),
                            wait=(t5, tq2, tprev.get("qnf")), inc="ve")
                tprev["p6"] = t6
                t7 = p.emit("tensor", lambda e: e.matmul(c.p7[:, 0:PB], c.rotTf[:], qnf,
                                                         start=True, stop=True),
                            wait=(t6, tprev.get("p7")), inc="pe")
                t8 = p.emit("vector", lambda e, s=s: e.tensor_mul(qrawf, qnf, sh[0][:, s]),
                            wait=(t6, trC, tq2, ts), inc="ve")
                tprev["qnf"] = t8
                t9 = p.emit("vector", lambda e, s=s: e.tensor_mul(sqf, c.p7[:, 0:PB], sh[1][:, s]),
                            wait=(t7, trS, ts), inc="ve")
                tprev["p7"] = t9
                tlast = p.emit("vector", lambda e, oc=oc, s=s, dst=dst: e.tensor_add(
                    dst[oc][:, s], qrawf, sqf), wait=(t8, t9), inc="ve")
                tprev["qrawf"] = tlast
                tprev["sqf"] = tlast
        tprev["wq0"], tprev["wq1"] = guards
        return tlast

    tprev = {}
    tq_done = proj_normed("q", tprev)
    tk_done = proj_normed("k", tprev)

    twv = p.dma("gpsimd", wkv[0].rearrange("p (i q) -> p i q", i=8),
                c.din("l0f_wv").rearrange("(i p) q -> p i q", p=128),
                wait=(tk_done,), sem="dW_wv" + sfx)
    tvslot = [None, None]
    tve_t = {}
    tvev = {}

    def v_evac(tt):
        off = 512 * (tt % 2)
        t1 = p.emit("vector", lambda e, tt=tt, off=off: e.tensor_copy(
            Vb[:, tt * WV:tt * WV + 260].rearrange("p (h d) -> p h d", d=65)[:, :, 0:64],
            c.pS[:, off:off + 256].rearrange("p (h d) -> p h d", d=64)),
            wait=(tve_t[tt],), inc="ve")
        tvslot[tt % 2] = t1
        tvev[tt] = p.emit("vector", lambda e, tt=tt: e.memset(
            Vb[:, tt * WV + 64: tt * WV + 261:65], 1.0), wait=(t1,), inc="ve")

    for tt in range(16):
        off = 512 * (tt % 2)
        tvm = None
        for ic in range(8):
            tvm = p.emit("tensor", lambda e, ic=ic, tt=tt, off=off: e.matmul(
                c.pS[:, off:off + 256], xnO[ic][:, 128 * tt:128 * (tt + 1)],
                wkv[0][:, 256 * ic:256 * (ic + 1)],
                start=(ic == 0), stop=(ic == 7)),
                wait=(twv, tvslot[tt % 2] if ic == 0 else None), inc="pe")
        tve_t[tt] = tvm
        if tt >= 1:
            v_evac(tt - 1)
    v_evac(15)
    tv_done = tvev[15]
    txn_free = tvev[15]
    tpS_init = (tvev[14], tvev[15])

    A4 = [sh[0], sh[1], wq2ab, wkv[0]]
    aits = [(h, tp) for h in range(16) for tp in range(8)]
    NIT = len(aits)
    scl = 0.125

    texp = {}
    tmask = {}
    tABr = {(-2, 0): tk_done, (-2, 1): tk_done,
            (-1, 0): tk_done, (-1, 1): tvev[15]}
    tsc_t = {}
    ttr = {0: tprev.get("p4"), 1: tprev.get("p7")}
    tat2r = [None, None]
    tavfin = {}
    trecip = {-2: tprev.get("p5"), -1: tprev.get("p6")}
    tavs = {}
    toutm = {}
    tbc_t = {}
    tO_last = txn_free
    tscore_last = None

    def ageo(n):
        h, tp = aits[n]
        kvh = h // 4
        return (h, tp, kvh, (h // 8) * 4 + h % 4, (kvh % 2) * 64,
                kvh // 2, (kvh % 2) * 64, 2 * tp + 2)

    def emit_scores(n, ti, qmat, qrow0, kmat, krow0, ext, exts, cm, Ab, w0):
        tsc = None
        tmq = []
        ngr = (exts + PB - 1) // PB
        qt = 2 * aits[n][1] + ti
        for g in range(ngr):
            ws = min(PB, exts - g * PB)
            tsc = p.emit("tensor", lambda e, g=g, ws=ws, qt=qt, qmat=qmat,
                         qrow0=qrow0, kmat=kmat, krow0=krow0: e.matmul(
                c.pS[:, g * PB:g * PB + ws],
                qmat[qrow0:qrow0 + 64, 128 * qt:128 * (qt + 1)],
                kmat[krow0:krow0 + 64, g * PB:g * PB + ws],
                start=True, stop=True),
                wait=w0 if g == 0 else (), inc="pe")
            tmq.append(p.emit("vector", lambda e, g=g, ws=ws, cm=cm: e.reduce_max(
                colsc[:, cm + 2 + g:cm + 3 + g], c.pS[:, g * PB:g * PB + ws], axis=AX),
                wait=(tsc, texp.get((n - 2, ti))), inc="ve"))
        tsc_t[(n, ti)] = tsc
        tmx = p.emit("vector", lambda e, cm=cm, ngr=ngr: e.reduce_max(
            colsc[:, cm:cm + 1], colsc[:, cm + 2:cm + 2 + ngr], axis=AX, negate=True),
            wait=tuple(tmq), inc="ve")
        tmx2 = p.emit("vector", lambda e, cm=cm: e.tensor_scalar_mul(
            colsc[:, cm + 1:cm + 2], colsc[:, cm:cm + 1], scl), wait=(tmx,), inc="ve")
        texp[(n, ti)] = p.emit("scalar", lambda e, exts=exts, Ab=Ab, cm=cm: e.activation(
            Ab[:, 0:exts], c.pS[:, 0:exts], AF.Exp, bias=colsc[:, cm + 1:cm + 2],
            scale=scl), wait=(tmx2, tABr[(n - 2, ti)]), inc="se")
        return tsc

    def prod_first(n):
        h, tp, kvh, qch, qoff, kch, koff, nkt = ageo(n)
        ext = 128 * nkt
        pr = n % 2
        wprev = texp.get((n - 1, 1))
        w0 = (tq_done, tk_done, wprev) if wprev is not None else \
            (tq_done, tk_done) + tpS_init
        emit_scores(n, 0, qTi[qch], qoff, kxh[kch], koff, ext,
                    ext - 128 if nkt > 1 else ext, 1 + 14 * pr, A4[2 * pr], w0)

    def prod_second(n):
        nonlocal tscore_last
        h, tp, kvh, qch, qoff, kch, koff, nkt = ageo(n)
        ext = 128 * nkt
        pr = n % 2
        Ab0, Ab1 = A4[2 * pr], A4[2 * pr + 1]
        qt0, qt1 = 2 * tp, 2 * tp + 1
        tscore_last = emit_scores(n, 1, qTi[qch], qoff, kxh[kch], koff, ext, ext,
                                  8 + 14 * pr, Ab1, (texp[(n, 0)],))
        tms0 = p.emit("gpsimd", lambda e, qt=qt0, Ab=Ab0: e.tensor_mul(
            Ab[:, 128 * qt:128 * (qt + 1)], Ab[:, 128 * qt:128 * (qt + 1)],
            c.diagmask[:]), wait=(texp[(n, 0)],), inc="ge")
        tmask[(n, 0)] = p.emit("gpsimd", lambda e, qt=qt0, ext=ext, Ab=Ab0: e.memset(
            Ab[:, 128 * (qt + 1):ext], 0.0), wait=(tms0,), inc="ge")
        tmask[(n, 1)] = p.emit("gpsimd", lambda e, qt=qt1, Ab=Ab1: e.tensor_mul(
            Ab[:, 128 * qt:128 * (qt + 1)], Ab[:, 128 * qt:128 * (qt + 1)],
            c.diagmask[:]), wait=(texp[(n, 1)],), inc="ge")


    cstate = {}

    def cons_kt(n, kt, nkt, kvh, pr):
        Ab0, Ab1 = A4[2 * pr], A4[2 * pr + 1]
        tb = c.p4 if kt % 2 == 0 else c.p7
        to1 = p.emit("tensor", lambda e, kt=kt, tb=tb, Ab0=Ab0: e.matmul(
            tb[:, 0:128], Ab0[:, 128 * kt:128 * (kt + 1)], c.ident[:],
            start=True, stop=True), wait=(tmask[(n, 0)], ttr[kt % 2]), inc="pe")
        to2 = p.emit("tensor", lambda e, kt=kt, tb=tb, Ab1=Ab1: e.matmul(
            tb[:, 128:256], Ab1[:, 128 * kt:128 * (kt + 1)], c.ident[:],
            start=True, stop=True), wait=(tmask[(n, 1)],), inc="pe")
        tABr[(n, 0)] = to1
        tABr[(n, 1)] = to2
        if kt % 2 == 0:
            tev = p.emit("vector", lambda e, kt=kt, tb=tb: e.tensor_copy(
                at2[:, 256 * (kt % 2):256 * (kt % 2) + 256], tb[:, 0:256]),
                wait=(to1, to2, tat2r[kt % 2]), inc="ve")
        else:
            tev = p.emit("scalar", lambda e, kt=kt, tb=tb: e.copy(
                at2[:, 256 * (kt % 2):256 * (kt % 2) + 256], tb[:, 0:256]),
                wait=(to1, to2, tat2r[kt % 2]), inc="se")
        ttr[kt % 2] = tev
        cstate[(n, kt)] = tev

    def cons_av(n, kt, nkt, kvh, pr):
        avbank = c.p5 if pr == 0 else c.p6
        w = [cstate.pop((n, kt)), tv_done]
        if kt == 0:
            w += [trecip.get(n - 2), tavs.get(n - 2)]
        tav = p.emit("tensor", lambda e, kt=kt, kvh=kvh, nkt=nkt, avbank=avbank: e.matmul(
            avbank[0:65, 0:256], Vb[:, kt * WV + kvh * 65: kt * WV + kvh * 65 + 65],
            at2[:, 256 * (kt % 2):256 * (kt % 2) + 256],
            start=(kt == 0), stop=(kt == nkt - 1)), wait=tuple(w), inc="pe")
        tat2r[kt % 2] = tav
        if kt == nkt - 1:
            tavfin[n] = tav

    def cons(n, half):
        h, tp, kvh, qch, qoff, kch, koff, nkt = ageo(n)
        pr = n % 2
        h1 = (nkt + 1) // 2
        kts = range(0, h1) if half == 0 else range(h1, nkt)
        for kt in kts:
            cons_kt(n, kt, nkt, kvh, pr)
            if kt >= 1:
                cons_av(n, kt - 1, nkt, kvh, pr)
        if half == 1:
            cons_av(n, nkt - 1, nkt, kvh, pr)
            avbank = c.p5 if pr == 0 else c.p6
            rsl = row1[0:1, 256 * pr:256 * pr + 256]
            tln = p.emit("scalar", lambda e, rsl=rsl, avbank=avbank: e.activation(
                rsl, avbank[64:65, 0:256], AF.Ln),
                wait=(tavfin[n], tbc_t.get(n - 2)), inc="se")
            tiv = p.emit("scalar", lambda e, rsl=rsl: e.activation(
                rsl, rsl, AF.Exp, scale=-1.0), wait=(tln,), inc="se", drain=True)
            trecip[n] = tiv
            avs = avsf[0:64, 0:256] if pr == 0 else row2[0:64, 0:256]
            tas = p.emit("scalar", lambda e, avs=avs, avbank=avbank: e.copy(
                avs, avbank[0:64, 0:256]),
                wait=(tavfin[n], toutm.get(n - 2)), inc="se")
            tavs[n] = tas

    def cons_tail(n):
        h, tp, kvh, qch, qoff, kch, koff, nkt = ageo(n)
        pr = n % 2
        avbank = c.p5 if pr == 0 else c.p6
        rsl = row1[0:1, 256 * pr:256 * pr + 256]
        avs = avsf[0:64, 0:256] if pr == 0 else row2[0:64, 0:256]
        tbc = p.emit("tensor", lambda e, rsl=rsl, avbank=avbank: e.matmul(
            avbank[0:64, 256:512], c.onesrowf[0:1, 0:64], rsl,
            start=True, stop=True),
            wait=(trecip[n], tavs[n], toutm.get(n - 2)), inc="pe")
        tbc_t[n] = tbc
        toutm[n] = p.emit("vector", lambda e, tp=tp, qch=qch, qoff=qoff,
                          avs=avs, avbank=avbank: e.tensor_mul(
            xnO[qch][qoff:qoff + 64, 256 * tp:256 * (tp + 1)],
            avs, avbank[0:64, 256:512]), wait=(tbc,), inc="ve")

    for n in range(NIT + 2):
        if n < NIT:
            prod_first(n)
        if n >= 2:
            cons_tail(n - 2)
        if 1 <= n <= NIT:
            cons(n - 1, 0)
        if n < NIT:
            prod_second(n)
        if 1 <= n <= NIT:
            cons(n - 1, 1)
    tO_done = toutm[NIT - 1]

    w_in = c.din("l0f_wo")
    tw = [None, None]
    twg = [tO_done, tO_done]
    tup = {"p4": tO_done, "p5": tO_done, "p6": tO_done}
    tinner_last = None
    tu5 = None
    for oc in range(8):
        pa = oc % 2
        tw[pa] = p.dma("gpsimd", wq2[pa].rearrange("p (i q) -> p i q", i=8),
                       w_in[:, 128 * oc:128 * (oc + 1)].rearrange("(i p) q -> p i q", p=128),
                       wait=(twg[pa],), sem=f"dW_wo{pa}{sfx}")
        for tgi in range(4):
            s = slice(tgi * PB, (tgi + 1) * PB)
            tb1 = p.emit("tensor", lambda e, s=s: e.matmul(
                c.p5[:, 0:PB], c.onesrowf[:], st["nmu"][0:1, s], start=True, stop=True),
                wait=(tup.get("p5"),), inc="pe")
            tb2 = p.emit("tensor", lambda e, s=s: e.matmul(
                c.p6[:, 0:PB], c.onesrowf[:], st["rstd"][0:1, s], start=True, stop=True),
                wait=(tup.get("p6"),), inc="pe")
            tmm = None
            for ic in range(8):
                tmm = p.emit("tensor", lambda e, ic=ic, s=s, pa=pa: e.matmul(
                    c.p4[:, 0:PB], wq2[pa][:, 128 * ic:128 * (ic + 1)], xnO[ic][:, s],
                    start=(ic == 0), stop=(ic == 7)),
                    wait=(tw[pa], tup.get("p4"), tO_done), inc="pe")
            twg[pa] = tmm
            ft = c.ftmp[pa]
            tu1 = p.emit("vector", lambda e, oc=oc, s=s, ft=ft: e.tensor_add(
                ft[:], c.X[oc][:, s], c.p5[:, 0:PB]),
                wait=(tb1, tup.get("ft%d" % pa)), inc="ve")
            tup["p5"] = tu1
            tu2 = p.emit("vector", lambda e, ft=ft: e.tensor_mul(
                ft[:], ft[:], c.p6[:, 0:PB]), wait=(tb2, tu1), inc="ve")
            tup["p6"] = tu2
            tu3 = p.emit("vector", lambda e, ft=ft: e.tensor_add(
                ft[:], ft[:], c.p4[:, 0:PB]), wait=(tmm, tu2), inc="ve")
            tup["p4"] = tu3
            ti4 = p.emit("vector", lambda e, oc=oc, s=s, ft=ft: e.tensor_copy(
                qTi[oc][:, s], ft[:]), wait=(tu3, tscore_last), inc="ve")
            tinner_last = ti4
            tu5 = p.emit("vector", lambda e, oc=oc, s=s, ft=ft: e.scalar_tensor_tensor(
                c.X[oc][:, s], ft[:], 0.5, c.X[oc][:, s], ALU.mult, ALU.add),
                wait=(ti4,), inc="ve")
            tup["ft%d" % pa] = tu5
    p.barrier()

    inner = qTi
    mfc = c.scratch("Mfc_t", (128, 32768), BF16)
    mpj = c.scratch("Mpj_t", (128, 32768), BF16)
    msq_row = st["mu"][0:1, :]
    rms_row = st["rstd"][0:1, :]
    tmsq = None
    tsq_guard = [None, None]
    sqbf = [A4[0][:, 0:PB], A4[0][:, PB:2 * PB]]
    for chn in range(8):
        pa = chn % 2
        sb = sqbf[pa]
        for g in range(4):
            sg = slice(g * PB, (g + 1) * PB)
            tsq = p.emit("scalar", lambda e, chn=chn, sg=sg, sb=sb: e.activation(
                sb, inner[chn][:, sg], AF.Square), wait=(tinner_last, tsq_guard[pa]), inc="se")
            tmsq = p.emit("tensor", lambda e, g=g, chn=chn, sb=sb: e.matmul(
                c.pS[0:1, g * PB:(g + 1) * PB], c.onescol[:], sb,
                start=(chn == 0), stop=(chn == 7)), wait=(tsq,), inc="pe")
            tsq_guard[pa] = tmsq
    t = p.emit("vector", lambda e: e.tensor_scalar_mul(msq_row, c.pS[0:1, 0:4 * PB], 1.0 / D),
               wait=(tmsq,), inc="ve")
    t = p.emit("scalar", lambda e: e.activation(st["nmu"][0:1, :], msq_row, AF.Ln, bias=eps6[0:1, 0:1]),
               wait=(t,), inc="se")
    trms = p.emit("scalar", lambda e: e.activation(rms_row, st["nmu"][0:1, :], AF.Exp, scale=-0.5),
                  wait=(t,), inc="se")

    def xh_ap(ic):
        return kxh[ic // 4][:, (ic % 4) * PB:(ic % 4 + 1) * PB]

    u2v = []
    for uc in range(32):
        u2v.append(xnO[uc // 4][:, (uc % 4) * PB:(uc % 4 + 1) * PB])

    NMR = 8
    mring = [Vb[:, 1024 * i:1024 * (i + 1)] for i in range(4)] + \
            [A4[2][:, 0:1024], A4[2][:, 1024:2048],
             A4[3][:, 0:1024], A4[3][:, 1024:2048]]
    mloads = []
    for tgi in range(4):
        for uc in range(32):
            mloads.append(mfc[:, 1024 * uc:1024 * (uc + 1)])
        for oc in range(8):
            for ug4 in range(4):
                mloads.append(mpj[:, 4096 * oc + 1024 * ug4:4096 * oc + 1024 * (ug4 + 1)])
    tmld = {}
    tmread = {}
    nxt = [0]

    def m_load():
        k = nxt[0]
        nxt[0] += 1
        sl = k % NMR
        tmld[k] = p.dma("sync", mring[sl], mloads[k],
                        wait=(tmread.get(k - NMR, tO_done),), sem=f"dM{sl}{sfx}")

    for _ in range(6):
        m_load()

    sq2 = [sqf, qnf]
    tmg = {"b0": trms, "b1": trms, "b2": trms, "b3": trms,
           "sq0": None, "sq1": None}
    tX_last = tu5
    for tgi in range(4):
        s = slice(tgi * PB, (tgi + 1) * PB)
        tb = p.emit("tensor", lambda e, s=s: e.matmul(
            c.p7[:, 0:PB], c.onesrowf[:], rms_row[0:1, s], start=True, stop=True),
            wait=(trms, tmg.get("p7")), inc="pe")
        txh = None
        for ic in range(8):
            txh = p.emit("vector", lambda e, ic=ic, s=s: e.tensor_mul(
                xh_ap(ic), inner[ic][:, s], c.p7[:, 0:PB]),
                wait=(tb, tmg.get("xh"), tscore_last), inc="ve")
        tmg["p7"] = txh
        tlast_u2 = None
        for uc in range(32):
            k = tgi * 64 + uc
            sl = k % NMR
            bi = uc % 4
            bk = c.pb[bi]
            sb = sq2[uc % 2]
            tfc = None
            for ic in range(8):
                tfc = p.emit("tensor", lambda e, ic=ic, sl=sl, bk=bk: e.matmul(
                    bk, mring[sl][:, 128 * ic:128 * (ic + 1)], xh_ap(ic),
                    start=(ic == 0), stop=(ic == 7)),
                    wait=(txh, tmld[k], tmg.get("b%d" % bi)), inc="pe")
            tmread[k] = tfc
            if nxt[0] < len(mloads):
                m_load()
            trl = p.emit("scalar", lambda e, bk=bk, sb=sb: e.activation(sb, bk, AF.Relu),
                         wait=(tfc, tmg.get("sq%d" % (uc % 2))), inc="se")
            tu2 = p.emit("gpsimd", lambda e, uc=uc, sb=sb: e.tensor_mul(
                u2v[uc], sb, sb), wait=(trl, tmg.get("u2")), inc="ge")
            tmg["sq%d" % (uc % 2)] = tu2
            tmg["b%d" % bi] = tu2
            tlast_u2 = tu2
        for oc in range(8):
            pa = oc % 2
            bk = c.p5 if pa == 0 else c.p6
            tpj = None
            for ug4 in range(4):
                k = tgi * 64 + 32 + oc * 4 + ug4
                sl = k % NMR
                for du in range(8):
                    uc = ug4 * 8 + du
                    tpj = p.emit("tensor", lambda e, du=du, sl=sl, bk=bk, uc=uc: e.matmul(
                        bk[:, 0:PB], mring[sl][:, 128 * du:128 * (du + 1)],
                        u2v[uc], start=(uc == 0), stop=(uc == 31)),
                        wait=(tmld[k], tlast_u2, tmg.get("pj%d" % pa)), inc="pe")
                tmread[k] = tpj
                if nxt[0] < len(mloads):
                    m_load()
            tX = p.emit("vector", lambda e, oc=oc, s=s, bk=bk: e.scalar_tensor_tensor(
                c.X[oc][:, s], bk[:, 0:PB], 0.5, c.X[oc][:, s], ALU.mult, ALU.add),
                wait=(tpj, tX_last), inc="ve")
            tmg["pj%d" % pa] = tX
            tX_last = tX
        tmg["xh"] = tX_last
        tmg["u2"] = tX_last
    p.barrier()
    c.ar.pop()


def phase_l0ffn(c):
    emit_l0ffn(c, "l0f")



def emit_l0attn(c):
    p = c.prog
    nc = c.nc
    c.ar.push()
    xn = [c.ar.take(T, BF16, "a_xn") for _ in range(8)]
    c.ar.push()
    st = alloc_stats(c)
    emit_layernorm(c, xn, st)
    c.ar.pop()
    p.barrier()
    yg = [c.ar.take(T, BF16, "a_yg") for _ in range(8)]
    zin = c.ar.take(T + 16, F32, "a_zin")
    xval = c.ar.take(T, F32, "a_xval")
    W1 = c.ar.take(T, F32, "a_w1")
    W2 = c.ar.take(T, F32, "a_w2")
    W3 = c.ar.take(T, F32, "a_w3")
    tA_ = c.ar.take(T, F32, "a_p")
    tbf = c.ar.take(2 * T, BF16, "a_pbf")
    replc = c.ar.take(2048, BF16, "a_replc")
    wre2 = c.ar.take(64, BF16, "a_wre2")
    wim2 = c.ar.take(64, BF16, "a_wim2")
    scrb = [zin[:, 0:256].bitcast(BF16), zin[:, 256:512].bitcast(BF16)]
    wst = [c.ar.take(1024, BF16, "a_ws0"), c.ar.take(1024, BF16, "a_ws1")]
    wrow = c.smallf[:, 96:160]
    pA = tA_[0:32, :]
    pB = xval[0:32, :]
    pbf = tbf[0:32, 0:T]
    invbf = tbf[0:32, T:2 * T]

    inproj = c.din("l0a_inproj")
    CQ = c.smallf
    t1 = p.emit("vector", lambda e: e.memset(CQ[:, 13:14], 1e-5), inc="ve")
    t1b = p.emit("vector", lambda e: e.memset(CQ[:, 14:15], 1.0), wait=(t1,), inc="ve")
    tpin = p.emit("vector", lambda e: e.memset(CQ[:, 35:36], np.pi), wait=(t1b,), inc="ve")
    tpin = p.emit("vector", lambda e: e.memset(CQ[:, 36:37], -np.pi), wait=(tpin,), inc="ve")
    tpin = p.emit("vector", lambda e: e.memset(CQ[:, 37:38], -2 * np.pi), wait=(tpin,), inc="ve")
    trepl = p.dma("gpsimd", replc[0:32, :], c.const("replc_bf"), sem="dWrp")
    ttrow = p.dma("sync", c.ftmp[0][0:1, 0:PB] if False else wrow[0:1, 0:3], c.din("l0a_dlog"), sem="dL")

    r = wrow
    t = p.emit("vector", lambda e: e.reduce_max(r[0:1, 8:9], r[0:1, 0:3], axis=AX),
               wait=(ttrow,), inc="ve")
    t = p.emit("vector", lambda e: e.tensor_scalar_mul(r[0:1, 9:10], r[0:1, 8:9], -1.0),
               wait=(t,), inc="ve")
    t = p.emit("scalar", lambda e: e.activation(r[0:1, 4:7], r[0:1, 0:3], AF.Exp,
                                                bias=r[0:1, 9:10]), wait=(t,), inc="se")
    t = p.emit("vector", lambda e: e.reduce_sum(r[0:1, 8:9], r[0:1, 4:7], axis=AX),
               wait=(t,), inc="ve")
    t = p.emit("vector", lambda e: e.reciprocal(r[0:1, 9:10], r[0:1, 8:9]),
               wait=(t,), inc="ve")
    t = p.emit("vector", lambda e: e.tensor_scalar(r[0:1, 4:7], r[0:1, 4:7],
                                                   r[0:1, 9:10], None, ALU.mult),
               wait=(t,), inc="ve")
    t = p.emit("vector", lambda e: e.tensor_scalar_mul(r[0:1, 6:7], r[0:1, 6:7], -1.0),
               wait=(t,), inc="ve")
    tbw = p.emit("tensor", lambda e: e.matmul(c.p7[:, 0:3], c.onesrowf[:], r[0:1, 4:7],
                                              start=True, stop=True), wait=(t,), inc="pe")
    twcols = p.emit("vector", lambda e: e.tensor_copy(CQ[:, 10:13], c.p7[:, 0:3]),
                    wait=(tbw,), inc="ve")

    tdec = p.dma("sync", r[0:1, 32:60], c.din("l0a_decay"), wait=(t,), sem="dL")
    tanc = p.dma("sync", r[0:1, 60:64], c.din("l0a_anchor"), wait=(tdec,), sem="dL")
    t = p.emit("scalar", lambda e: e.activation(r[0:1, 0:32], r[0:1, 32:64], AF.Abs),
               wait=(tanc, twcols), inc="se")
    t = p.emit("scalar", lambda e: e.activation(r[0:1, 0:32], r[0:1, 0:32], AF.Exp,
                                                scale=-1.0), wait=(t,), inc="se", drain=True)
    t = p.emit("scalar", lambda e: e.activation(r[0:1, 0:32], r[0:1, 0:32], AF.Ln,
                                                bias=CQ[0:1, 14:15]), wait=(t1b, t), inc="se",
               drain=True)
    t2 = p.emit("vector", lambda e: e.tensor_scalar_max(r[0:1, 32:64], r[0:1, 32:64], 0.0),
                wait=(t,), inc="ve")
    tsp = p.emit("vector", lambda e: e.tensor_add(r[0:1, 32:64], r[0:1, 32:64], r[0:1, 0:32]),
                 wait=(t2,), inc="ve")
    t = p.emit("vector", lambda e: e.memset(r[0:1, 0:32], 0.0), wait=(tsp,), inc="ve")
    t = p.emit("vector", lambda e: e.tensor_copy(r[0:1, 0:28], r[0:1, 32:60]),
               wait=(t,), inc="ve")
    tspd = p.emit("vector", lambda e: e.memset(r[0:1, 32:60], 0.0), wait=(tsp, t), inc="ve")
    tdr = p.dma("sync", c.ftmp[0][0:1, 0:T] if False else zin[0:1, 0:T], c.const("drow_f32"), sem="dL")
    ttr = p.dma("sync", zin[1:2, 0:T] if False else xval[0:1, 0:T], c.const("trow_f32"), sem="dL")
    targ = None
    for g in range(4):
        s = slice(g * PB, (g + 1) * PB)
        p.emit("tensor", lambda e, s=s: e.matmul(
            c.pS[0:32, s.start:s.stop], r[0:1, 0:32], zin[0:1, s],
            start=True, stop=False), wait=(t, tdr, targ), inc="pe")
        targ = p.emit("tensor", lambda e, s=s: e.matmul(
            c.pS[0:32, s.start:s.stop], r[0:1, 32:64], xval[0:1, s],
            start=False, stop=True), wait=(tspd, ttr), inc="pe")
    ttw = p.emit("scalar", lambda e: e.activation(pB, c.pS[0:32, 0:T], AF.Exp, scale=-1.0),
                 wait=(targ,), inc="se")
    tzs = emit_inproj_chunk(c, zin, xn, 4096, wst[0], "dWsr", wait=(ttw,), rows=32)
    tck = p.dma("sync", CQ[0:32, 4:8], c.din("l0a_convk")[4096:4128, :],
                sem="dL")
    tcb = p.dma("sync", CQ[0:32, 8:9], c.din("l0a_convb")[4096:4128, :],
                sem="dL")
    tss = p.dma("sync", CQ[0:32, 9:10], c.din("l0a_sscale"), sem="dL")
    tcv = emit_conv4(c, pA, zin[0:32, :], CQ[0:32, 4:8], CQ[0:32, 8:9],
                     wait=(tzs, tck, tcb), rows=32)
    t = p.emit("vector", lambda e: e.tensor_scalar(pA, pA, CQ[0:32, 9:10], None,
                                                   ALU.mult),
               wait=(tcv, tss), inc="ve", drain=True)
    t = p.emit("vector", lambda e: e.tensor_scalar_min(pA, pA, 20.0), wait=(t,), inc="ve")
    t = p.emit("vector", lambda e: e.tensor_scalar_max(pA, pA, -20.0), wait=(t,), inc="ve")
    t = p.emit("scalar", lambda e: e.activation(pA, pA, AF.Exp), wait=(t,), inc="se")
    tp_ = p.emit("vector", lambda e: e.tensor_mul(pA, pA, pB), wait=(t, ttw), inc="ve")
    tone = p.emit("vector", lambda e: e.memset(W1[0:32, :], 1.0), inc="ve")
    t = p.emit("vector", lambda e: e.tensor_tensor_scan(pB, W1[0:32, :], pA, 0.0,
                                                        ALU.mult, ALU.add),
               wait=(tp_, tone, ttw), inc="ve", drain=True)
    t = p.emit("vector", lambda e: e.tensor_scalar_max(pB, pB, 1e-4), wait=(t,), inc="ve",
               drain=True)
    t = p.emit("vector", lambda e: e.reciprocal(pB, pB), wait=(t,), inc="ve", drain=True)
    tinv = p.emit("vector", lambda e: e.tensor_copy(invbf, pB), wait=(t,), inc="ve")
    tpbf = p.emit("vector", lambda e: e.tensor_copy(pbf, pA), wait=(tp_,), inc="ve")

    tnre = p.dma("sync", CQ[:, 15:16], c.din("l0a_nsc_re"), sem="dL")
    tnim = p.dma("sync", CQ[:, 16:17], c.din("l0a_nsc_im"), sem="dL")
    twr1 = p.dma("sync", W2[0:64, 0:64].bitcast(F32) if False else W2[0:64, 0:64], c.din("l0a_wre"), sem="dL")
    twr2 = p.dma("sync", W2[64:128, 0:64], c.din("l0a_wre"), sem="dL")
    twi1 = p.dma("sync", W3[0:64, 0:64], c.din("l0a_wim"), sem="dL")
    twi2 = p.dma("sync", W3[64:128, 0:64], c.din("l0a_wim"), sem="dL")
    t = p.emit("vector", lambda e: e.tensor_scalar(W2[:, 0:64], W2[:, 0:64],
                                                   CQ[:, 15:16], None, ALU.mult),
               wait=(twr2, tnre), inc="ve")
    twre = p.emit("vector", lambda e: e.tensor_copy(wre2, W2[:, 0:64]), wait=(t,), inc="ve")
    t = p.emit("vector", lambda e: e.tensor_scalar(W3[:, 0:64], W3[:, 0:64],
                                                   CQ[:, 16:17], None, ALU.mult),
               wait=(twi2, tnim), inc="ve")
    twim = p.emit("vector", lambda e: e.tensor_copy(wim2, W3[:, 0:64]), wait=(t,), inc="ve")
    p.barrier()

    opj = c.din("l0a_outproj")
    tw_g = [None, None]
    W4 = c.ar.take(1024, F32, "a_w4")
    scrb = [c.ar.take(512, BF16, "a_sc0"), c.ar.take(512, BF16, "a_sc1")]
    xvbuf = [xval, tA_]
    GT = (4, 17)
    VT = (23, 28)
    TH = (33, 34)

    for half in range(2):
        gg = {}
        st1 = {}
        tth2 = {}
        tzpad = p.emit("vector", lambda e: e.memset(zin[:, 0:3], 0.0), inc="ve")
        gg["zin"] = tzpad

        def inproj2(col0, wbuf, sem, wguard, zguard):
            tw = p.dma("gpsimd", wbuf.rearrange("p (i q) -> p i q", i=8),
                       inproj[:, col0:col0 + 128].rearrange("(i p) q -> p i q", p=128),
                       wait=(wguard,), sem=sem)
            gtoks = []
            tmlast = None
            for g in range(4):
                s = slice(g * PB, (g + 1) * PB)
                po = (g % 2) * PB
                tm = None
                for ic in range(8):
                    tm = p.emit("tensor", lambda e, ic=ic, s=s, po=po, wbuf=wbuf:
                                e.matmul(c.pS[:, po:po + PB],
                                         wbuf[:, 128 * ic:128 * (ic + 1)], xn[ic][:, s],
                                         start=(ic == 0), stop=(ic == 7)),
                                wait=(tw, gg.get("pS%d" % (g % 2)), zguard), inc="pe")
                tmlast = tm
                tz = p.emit("scalar", lambda e, s=s, po=po: e.copy(
                    zin[:, 3 + s.start:3 + s.stop], c.pS[:, po:po + PB]),
                    wait=(tm, zguard), inc="se")
                gg["pS%d" % (g % 2)] = tz
                gtoks.append(tz)
            return gtoks, tmlast

        def conv_pool(out, lo, hi, taps, bias, wait):
            t = p.emit("vector", lambda e, out=out, lo=lo, hi=hi, taps=taps: e.tensor_scalar(
                out, zin[:, lo:hi], taps[:, 0:1], None, ALU.mult),
                wait=wait, inc="ve")
            for tau in range(1, 4):
                t = p.emit("vector", lambda e, out=out, lo=lo, hi=hi, taps=taps, tau=tau:
                           e.scalar_tensor_tensor(out, zin[:, lo + tau:hi + tau],
                                                  taps[:, tau:tau + 1], out,
                                                  ALU.mult, ALU.add),
                           wait=(t,), inc="ve", drain=True)
            t = p.emit("vector", lambda e, out=out, bias=bias: e.tensor_scalar(
                out, out, bias[:, 0:1], None, ALU.add), wait=(t,), inc="ve", drain=True)
            return t

        def stage1(hp):
            hpi = hp % 8
            par = hp % 2
            gch = 2048 + 128 * hp
            vch = 128 * hp
            gt0, vt0, th = GT[par], VT[par], TH[par]
            sem = "dLck%d" % par
            tck1 = p.dma("sync", CQ[:, gt0:gt0 + 4], c.din("l0a_convk")[gch:gch + 128, :],
                         wait=(gg.get("tapg%d" % par),), sem=sem)
            tcb1 = p.dma("sync", CQ[:, gt0 + 4:gt0 + 5],
                         c.din("l0a_convb")[gch:gch + 128, :], wait=(tck1,), sem=sem)
            tck2 = p.dma("sync", CQ[:, vt0:vt0 + 4], c.din("l0a_convk")[vch:vch + 128, :],
                         wait=(gg.get("tapv%d" % par), tcb1), sem=sem)
            tcb2 = p.dma("sync", CQ[:, vt0 + 4:vt0 + 5],
                         c.din("l0a_convb")[vch:vch + 128, :], wait=(tck2,), sem=sem)
            tth = p.dma("sync", CQ[:, th:th + 1], c.din("l0a_theta")[vch:vch + 128, :],
                        wait=(gg.get("th%d" % par), tcb2), sem=sem)
            tth2[par] = p.emit("vector", lambda e, th=th, par=par: e.tensor_scalar_mul(
                CQ[:, 40 + (th - 33):41 + (th - 33)], CQ[:, th:th + 1], 1.0 / (2 * np.pi)),
                wait=(tth, gg.get("th%d" % par)), inc="ve")
            gz, wl = inproj2(gch, wst[0], "dWip0", gg.get("wg0"), gg.get("zin"))
            gg["wg0"] = wl
            tsilu = []
            tcvg = None
            for g in range(4):
                sg = slice(g * PB, (g + 1) * PB)
                w4 = W4[:, (g % 2) * PB:(g % 2) * PB + PB]
                tcv = conv_pool(w4, sg.start, sg.stop, CQ[:, gt0:gt0 + 5],
                                CQ[:, gt0 + 4:gt0 + 5],
                                (gz[g], tck1, tcb1, tcvg,
                                 tsilu[g - 2] if g >= 2 else gg.get("w4%d" % (g % 2))))
                tcvg = tcv
                tsl = p.emit("scalar", lambda e, sg=sg, w4=w4, hpi=hpi: e.activation(
                    yg[hpi][:, sg], w4, AF.Silu), wait=(tcv,), inc="se")
                tsilu.append(tsl)
                gg["w4%d" % (g % 2)] = tsl
            gg["tapg%d" % par] = tcvg
            vz, wl2 = inproj2(vch, wst[1], "dWip1", gg.get("wg1"), tcvg)
            gg["wg1"] = wl2
            gg["zin"] = None
            xv = xvbuf[par]
            tcv2 = conv_pool(xv, 0, T, CQ[:, vt0:vt0 + 5], CQ[:, vt0 + 4:vt0 + 5],
                             (vz[3], tck2, tcb2, gg.get("xv%d" % par)))
            gg["tapv%d" % par] = tcv2
            gg["zin"] = tcv2
            st1[hp] = (tcv2, tsilu, tth2[par], th)

        def stage2(hp):
            hpi = hp % 8
            par = hp % 2
            tcv2, tsilu, tth2c, th = st1.pop(hp)
            xv = xvbuf[par]
            gd = {"p4b": gg.get("p4"), "p5g": gg.get("p5"), "p6g": gg.get("p6"),
                  "p7g": gg.get("p7"), "rowa": gg.get("rowa"),
                  "sb0": gg.get("scrb"), "sb1": gg.get("scrb")}
            MAGIC = 12582912.0
            t = p.emit("vector", lambda e, xv=xv, th=th: e.tensor_scalar(
                W2, xv, CQ[:, 40 + (th - 33):41 + (th - 33)], None, ALU.mult),
                wait=(tcv2, tth2c, gg.get("W2")), inc="ve")
            gg["th%d" % par] = t
            t = p.emit("vector", lambda e: e.tensor_scalar_add(W1, W2, MAGIC),
                       wait=(t, gg.get("W1")), inc="ve")
            t = p.emit("vector", lambda e: e.tensor_scalar_add(W1, W1, -MAGIC),
                       wait=(t,), inc="ve", drain=True)
            t = p.emit("vector", lambda e: e.tensor_sub(W2, W2, W1), wait=(t,), inc="ve")
            t = p.emit("vector", lambda e: e.tensor_scalar_mul(W2, W2, 2 * np.pi),
                       wait=(t,), inc="ve")
            t = p.emit("vector", lambda e: e.tensor_scalar(W2, W2, CQ[:, 35:36],
                                                           CQ[:, 36:37], ALU.min, ALU.max),
                       wait=(t, tpin), inc="ve")
            tsn = p.emit("scalar", lambda e: e.activation(W3, W2, AF.Sin),
                         wait=(t, gg.get("W3")), inc="se")
            t2 = p.emit("vector", lambda e: e.tensor_scalar_add(W1, W2, np.pi / 2),
                        wait=(tsn,), inc="ve")
            t2b = p.emit("vector", lambda e: e.tensor_scalar(W2, W1, CQ[:, 35:36],
                                                             CQ[:, 37:38], ALU.is_gt, ALU.mult),
                         wait=(t2, tpin), inc="ve")
            t2 = p.emit("vector", lambda e: e.tensor_add(W1, W1, W2), wait=(t2b,), inc="ve")
            tcs = p.emit("scalar", lambda e: e.activation(W2, W1, AF.Sin), wait=(t2,), inc="se")
            tq = p.emit("scalar", lambda e, xv=xv: e.activation(W1, xv, AF.Square),
                        wait=(t2, tcs), inc="se")
            t3 = p.emit("vector", lambda e, xv=xv: e.tensor_scalar(xv, xv, CQ[:, 11:12],
                                                                   CQ[:, 10:11], ALU.mult, ALU.add),
                        wait=(tq, twcols), inc="ve", drain=True)
            t3 = p.emit("vector", lambda e, xv=xv: e.scalar_tensor_tensor(
                xv, W1, CQ[:, 12:13], xv, ALU.mult, ALU.add), wait=(t3,), inc="ve")
            t4 = p.emit("vector", lambda e, xv=xv: e.tensor_mul(W2, W2, xv), wait=(t3, tcs), inc="ve")
            t5 = p.emit("vector", lambda e, xv=xv: e.tensor_mul(W3, W3, xv), wait=(t3, tsn), inc="ve")
            tlast = None
            for g in range(4):
                s = slice(g * PB, (g + 1) * PB)
                tb = p.emit("tensor", lambda e, s=s, hp=hp: e.matmul(
                    c.p4[:, 0:PB], replc[0:32, 128 * hp:128 * (hp + 1)], pbf[:, s],
                    start=True, stop=True), wait=(trepl, tpbf, t4, t5, gd.get("p4b")),
                    inc="pe")
                t6 = p.emit("vector", lambda e, s=s: e.tensor_mul(W2[:, s], W2[:, s],
                                                                 c.p4[:, 0:PB]),
                            wait=(tb,), inc="ve")
                tlast = p.emit("vector", lambda e, s=s: e.tensor_mul(W3[:, s], W3[:, s],
                                                                    c.p4[:, 0:PB]),
                               wait=(tb, t6), inc="ve")
                gd["p4b"] = tlast
            tone2 = p.emit("vector", lambda e: e.memset(W1, 1.0), wait=(tlast,), inc="ve")
            tsc1 = p.emit("vector", lambda e, xv=xv: e.tensor_tensor_scan(xv, W1, W2, 0.0,
                                                                          ALU.mult, ALU.add),
                          wait=(tone2,), inc="ve", drain=True)
            tsc2 = p.emit("vector", lambda e: e.tensor_tensor_scan(W2, W1, W3, 0.0,
                                                                   ALU.mult, ALU.add),
                          wait=(tsc1,), inc="ve", drain=True)
            gg["W3"] = tsc2
            tl2 = None
            for g in range(4):
                s = slice(g * PB, (g + 1) * PB)
                tb = p.emit("tensor", lambda e, s=s, hp=hp: e.matmul(
                    c.p4[:, 0:PB], replc[0:32, 128 * hp:128 * (hp + 1)], invbf[:, s],
                    start=True, stop=True), wait=(tinv, tsc2, gd.get("p4c")), inc="pe")
                t6 = p.emit("vector", lambda e, s=s, xv=xv: e.tensor_mul(xv[:, s], xv[:, s],
                                                                        c.p4[:, 0:PB]),
                            wait=(tb, tsc1), inc="ve")
                tl2 = p.emit("vector", lambda e, s=s: e.tensor_mul(W2[:, s], W2[:, s],
                                                                  c.p4[:, 0:PB]),
                             wait=(tb, t6), inc="ve")
                gd["p4c"] = tl2
            gg["p4"] = tl2
            for g in range(4):
                s = slice(g * PB, (g + 1) * PB)
                tq1 = p.emit("scalar", lambda e, s=s, xv=xv: e.activation(W1[:, 0:PB], xv[:, s],
                                                                          AF.Square),
                             wait=(tl2, gd.get("W1s"),), inc="se")
                tm1 = p.emit("tensor", lambda e: e.matmul(c.p5[0:2, 0:PB], c.headself[:],
                                                          W1[:, 0:PB], start=True, stop=False),
                             wait=(tq1, gd.get("p5g")), inc="pe")
                tq2 = p.emit("scalar", lambda e, s=s: e.activation(W1[:, 0:PB], W2[:, s],
                                                                   AF.Square),
                             wait=(tm1,), inc="se")
                tm2 = p.emit("tensor", lambda e: e.matmul(c.p5[0:2, 0:PB], c.headself[:],
                                                          W1[:, 0:PB], start=False, stop=True),
                             wait=(tq2,), inc="pe")
                gd["W1s"] = tm2
                gg["W1"] = tm2
                tr = p.emit("vector", lambda e: e.tensor_scalar_mul(
                    row_a(c)[0:2, :], c.p5[0:2, 0:PB], 1.0 / 128),
                    wait=(tm2, gd.get("rowa")), inc="ve")
                gg["p5"] = tr
                tr = p.emit("vector", lambda e: e.tensor_scalar(
                    row_a(c)[0:2, :], row_a(c)[0:2, :], CQ[0:2, 13:14], None, ALU.add),
                    wait=(tr,), inc="ve", drain=True)
                gd["p5g"] = tr
                tr = p.emit("scalar", lambda e: e.activation(row_b(c)[0:2, :], row_a(c)[0:2, :],
                                                             AF.Ln), wait=(tr,), inc="se")
                tr = p.emit("scalar", lambda e: e.activation(row_a(c)[0:2, :], row_b(c)[0:2, :],
                                                             AF.Exp, scale=-0.5),
                            wait=(tr,), inc="se", drain=True)
                gd["rowa"] = tr
                tbs = p.emit("tensor", lambda e: e.matmul(c.p6[:, 0:PB], c.headbcf[:],
                                                          row_a(c)[0:2, :], start=True, stop=True),
                             wait=(tr, gd.get("p6g")), inc="pe")
                gg["rowa"] = tbs
                tn1 = p.emit("vector", lambda e, s=s, xv=xv: e.tensor_mul(scrb[0], xv[:, s],
                                                                          c.p6[:, 0:PB]),
                             wait=(tbs, gd.get("sb0")), inc="ve")
                tn2 = p.emit("vector", lambda e, s=s: e.tensor_mul(scrb[1], W2[:, s],
                                                                  c.p6[:, 0:PB]),
                             wait=(tbs, tn1, gd.get("sb1")), inc="ve")
                gd["p6g"] = tn2
                gg["p6"] = tn2
                gg["W2"] = tn2
                gg["xv%d" % par] = tn1
                tyy = None
                for off in (0, 64):
                    tyy = p.emit("tensor", lambda e, off=off: e.matmul(
                        c.p7[off:off + 64, 0:PB], wre2[off:off + 64, :],
                        scrb[0][off:off + 64, :], start=True, stop=False),
                        wait=(twre, tn1, gd.get("p7g")), inc="pe")
                    tyy = p.emit("tensor", lambda e, off=off: e.matmul(
                        c.p7[off:off + 64, 0:PB], wim2[off:off + 64, :],
                        scrb[1][off:off + 64, :], start=False, stop=True),
                        wait=(twim, tn2, tyy), inc="pe")
                gd["sb0"] = tyy
                gd["sb1"] = tyy
                gg["scrb"] = tyy
                tyg2 = p.emit("vector", lambda e, s=s, hpi=hpi: e.tensor_mul(
                    yg[hpi][:, s], yg[hpi][:, s], c.p7[:, 0:PB]),
                    wait=(tyy, tsilu[g]), inc="ve")
                gd["p7g"] = tyg2
                gg["p7"] = tyg2

        for hpi in range(8):
            stage1(half * 8 + hpi)
            if hpi >= 1:
                stage2(half * 8 + hpi - 1)
        stage2(half * 8 + 7)
        p.barrier()
        p.barrier()
        tpx = {}
        for oc in range(8):
            pa = oc % 2
            tw_g[pa] = p.dma("gpsimd", wst[pa].rearrange("p (i q) -> p i q", i=8),
                             opj[1024 * half:1024 * (half + 1),
                                 128 * oc:128 * (oc + 1)].rearrange("(i p) q -> p i q", p=128),
                             wait=(tpx.get("w%d" % pa),), sem=f"dWop{pa}")
            for g in range(4):
                s = slice(g * PB, (g + 1) * PB)
                tm = None
                for ic in range(8):
                    tm = p.emit("tensor", lambda e, ic=ic, s=s, pa=pa: e.matmul(
                        c.p4[:, 0:PB], wst[pa][:, 128 * ic:128 * (ic + 1)], yg[ic][:, s],
                        start=(ic == 0), stop=(ic == 7)),
                        wait=(tw_g[pa], tpx.get("p4")), inc="pe")
                tpx["w%d" % pa] = tm
                tX = p.emit("vector", lambda e, oc=oc, s=s: e.tensor_add(
                    c.X[oc][:, s], c.X[oc][:, s], c.p4[:, 0:PB]),
                    wait=(tm, tpx.get("X")), inc="ve")
                tpx["p4"] = tX
                tpx["X"] = tX
        p.barrier()
    c.ar.pop()


def row_a(c):
    return c.ftmp[0][0:2, 0:PB]


def row_b(c):
    return c.ftmp[1][0:2, 0:PB]


def emit_inproj_chunk(c, dst_zin, xn, col0, wbuf, sem, wait=(), rows=128):
    p = c.prog
    inproj = c.din("l0a_inproj")
    tw = p.dma("gpsimd", wbuf[:, 0:8 * rows].rearrange("p (i q) -> p i q", i=8),
               inproj[:, col0:col0 + rows].rearrange("(i p) q -> p i q", p=128),
               wait=wait, sem=sem)
    tzp = p.emit("vector", lambda e: e.memset(dst_zin[0:rows, 0:3], 0.0), wait=wait, inc="ve")
    tz = None
    tprev = None
    for g in range(4):
        s = slice(g * PB, (g + 1) * PB)
        tm = None
        for ic in range(8):
            tm = p.emit("tensor", lambda e, ic=ic, s=s: e.matmul(
                c.p4[0:rows, 0:PB], wbuf[:, rows * ic:rows * (ic + 1)], xn[ic][:, s],
                start=(ic == 0), stop=(ic == 7)), wait=(tw, tprev), inc="pe")
        tz = p.emit("scalar", lambda e, s=s: e.copy(
            dst_zin[0:rows, 3 + s.start:3 + s.stop], c.p4[0:rows, 0:PB]),
            wait=(tm,), inc="se")
        tprev = tz
    p.emit("vector", None, wait=(tzp,))
    return tz


def emit_conv4(c, out, zin, taps, bias, wait=(), rows=128):
    p = c.prog
    ra = out[0:rows, :] if out.shape[0] != rows else out
    t = p.emit("vector", lambda e: e.tensor_scalar(ra, zin[0:rows, 0:T], taps[0:rows, 0:1],
                                                   None, ALU.mult), wait=wait, inc="ve")
    for tau in range(1, 4):
        t = p.emit("vector", lambda e, tau=tau: e.scalar_tensor_tensor(
            ra, zin[0:rows, tau:T + tau], taps[0:rows, tau:tau + 1], ra,
            ALU.mult, ALU.add), wait=(t,), inc="ve", drain=True)
    t = p.emit("vector", lambda e: e.tensor_scalar(ra, ra, bias[0:rows, 0:1], None,
                                                   ALU.add), wait=(t,), inc="ve", drain=True)
    return t


def phase_l0attn(c):
    emit_l0attn(c)



def emit_l1ffn(c, tag):
    p = c.prog
    nc = c.nc
    sfx = tag
    c.ar.push()
    st = alloc_stats(c)
    xn = [c.ar.take(T, BF16, "f_xn") for _ in range(8)]
    emit_layernorm(c, xn, st)
    p.barrier()
    colsT = [c.ar.take(T, BF16, "f_cols") for _ in range(8)]
    h1all = c.ar.take(4 * T, BF16, "f_h1")
    h1 = [h1all[:, T * i:T * (i + 1)] for i in range(4)]
    cembb = c.ar.take(256, BF16, "f_cemb")
    wTs = c.ar.take(128, BF16, "f_wTs")
    wTrow = c.ar.take(PB, BF16, "f_wTrow")
    smallv = c.ar.take(16, F32, "f_smallv")
    CA = c.ar.take(64, F32, "f_CA")
    wst = [c.ar.take(1024, BF16, "f_ws0"), c.ar.take(1024, BF16, "f_ws1")]
    yf0 = h1all.bitcast(F32)
    yf = [yf0[:, 0:T], yf0[:, T:2 * T]]
    sq1 = c.ar.take(PB, BF16, "f_sq")
    sq2 = c.ar.take(PB, BF16, "f_sq2")
    wtile = c.ar.take(16, BF16, "f_wtile")
    rowbf = c.ar.take(T, BF16, "f_rowbf")
    rrow = c.ar.take(PB, F32, "f_rrow")

    def col_u2(uc):
        return colsT[uc // 4][:, (uc % 4) * PB:(uc % 4 + 1) * PB]

    def y_ap(oc):
        return yf[oc // 4][:, (oc % 4) * PB:(oc % 4 + 1) * PB]

    tbc = p.dma("sync", CA[:, 0:32], c.din("l1f_bcols"), sem="dL")
    tpb = p.dma("sync", CA[:, 32:40], c.din("l1f_projb"), sem="dL")
    tlg = p.dma("sync", CA[:, 40:48], c.din("l1f_lng"), sem="dL")
    tlb = p.dma("sync", CA[:, 48:56], c.din("l1f_lnb"), sem="dL")
    tg1 = p.dma("sync", CA[:, 56:60], c.din("l1f_gb1"), sem="dL")
    te24 = p.emit("vector", lambda e: e.memset(CA[:, 60:61], 1e-24), inc="ve")

    tce = p.dma("gpsimd", cembb[:, 0:128].rearrange("p (i q) -> p i q", i=8),
                c.din("l1f_colembT").rearrange("(i p) q -> p i q", p=128), sem="dC" + sfx)
    tgw2 = p.dma("gpsimd", cembb[:, 128:192].rearrange("p (i q) -> p i q", i=4),
                 c.din("l1f_gw2").rearrange("(i p) q -> p i q", p=128), sem="dC" + sfx)
    tsq = None
    tss = None
    for ic in range(8):
        tss = p.emit("scalar", lambda e, ic=ic: e.activation(
            sq1[:, 0:16], cembb[:, 16 * ic:16 * (ic + 1)], AF.Square),
            wait=(tce, tsq), inc="se")
        tsq = p.emit("tensor", lambda e, ic=ic: e.matmul(
            c.p6[0:1, 0:16], c.onescol[:], sq1[:, 0:16],
            start=(ic == 0), stop=(ic == 7)), wait=(tss,), inc="pe")
    r1 = c.ftmp[0][0:1, 0:16]
    t = p.emit("scalar", lambda e: e.activation(r1, c.p6[0:1, 0:16], AF.Ln,
                                                bias=CA[0:1, 60:61]), wait=(tsq, te24), inc="se")
    t = p.emit("scalar", lambda e: e.activation(r1, r1, AF.Exp, scale=-0.5),
               wait=(t,), inc="se", drain=True)
    tinvc = p.emit("tensor", lambda e: e.matmul(c.p7[:, 0:16], c.onesrowf[:], r1,
                                                start=True, stop=True), wait=(t,), inc="pe")
    invcb = c.ftmp[1][:, 0:16]
    ticb = p.emit("vector", lambda e: e.tensor_copy(invcb, c.p7[:, 0:16]),
                  wait=(tinvc,), inc="ve")

    tms = None
    tsqg = None
    for ch in range(8):
        for g in range(4):
            s = slice(g * PB, (g + 1) * PB)
            ts_ = p.emit("scalar", lambda e, ch=ch, s=s: e.activation(
                sq1, xn[ch][:, s], AF.Square), wait=(tsqg, tss), inc="se")
            tms = p.emit("tensor", lambda e, g=g, ch=ch: e.matmul(
                c.pS[0:1, g * PB:(g + 1) * PB], c.onescol[:], sq1,
                start=(ch == 0), stop=(ch == 7)), wait=(ts_,), inc="pe")
            tsqg = tms
    t = p.emit("scalar", lambda e: e.activation(rowbf[0:1, :], c.pS[0:1, 0:4 * PB], AF.Ln,
                                                bias=CA[0:1, 60:61]), wait=(tms, te24), inc="se")
    tinvn = p.emit("scalar", lambda e: e.activation(rowbf[0:1, :], rowbf[0:1, :], AF.Exp,
                                                    scale=-0.5), wait=(t,), inc="se", drain=True)

    tw = [None, None]
    twg = [None, None]
    th1 = None
    for hc in range(4):
        pa = hc % 2
        tw[pa] = p.dma("gpsimd", wst[pa].rearrange("p (i q) -> p i q", i=8),
                       c.din("l1f_gw1")[:, 128 * hc:128 * (hc + 1)].rearrange(
                           "(i p) q -> p i q", p=128), wait=(twg[pa],), sem=f"dG{pa}{sfx}")
        for g in range(4):
            s = slice(g * PB, (g + 1) * PB)
            tm = None
            for ic in range(8):
                tm = p.emit("tensor", lambda e, ic=ic, s=s, pa=pa: e.matmul(
                    c.p4[:, 0:PB], wst[pa][:, 128 * ic:128 * (ic + 1)], xn[ic][:, s],
                    start=(ic == 0), stop=(ic == 7)), wait=(tw[pa], th1), inc="pe")
            th1 = p.emit("scalar", lambda e, hc=hc, s=s: e.activation(
                h1[hc][:, s], c.p4[:, 0:PB], AF.Gelu, bias=CA[:, 56 + hc:57 + hc]),
                wait=(tm, tg1), inc="se")
        twg[pa] = th1
    p.barrier()

    tgb2 = p.dma("sync", c.ftmp[0][0:1, 16:32], c.din("l1f_gb2"), sem="dL")
    tb2 = p.emit("tensor", lambda e: e.matmul(c.p7[:, 0:16], c.onesrowf[:],
                                              c.ftmp[0][0:1, 16:32], start=True, stop=True),
                 wait=(tgb2,), inc="pe")
    gb2b = c.ftmp[0][:, 32:48]
    tgb2b = p.emit("vector", lambda e: e.tensor_copy(gb2b, c.p7[:, 0:16]),
                   wait=(tb2,), inc="ve")
    gd = {}
    t15 = None
    for tt in range(16):
        tm1 = None
        for ic in range(8):
            tm1 = p.emit("tensor", lambda e, ic=ic, tt=tt: e.matmul(
                c.p4[:, 0:16], xn[ic][:, 128 * tt:128 * (tt + 1)],
                cembb[:, 16 * ic:16 * (ic + 1)], start=(ic == 0), stop=(ic == 7)),
                wait=(tce, gd.get("p4"),), inc="pe")
        tm2 = None
        for hc in range(4):
            tm2 = p.emit("tensor", lambda e, hc=hc, tt=tt: e.matmul(
                c.p5[:, 0:16], h1[hc][:, 128 * tt:128 * (tt + 1)],
                cembb[:, 128 + 16 * hc:128 + 16 * (hc + 1)],
                start=(hc == 0), stop=(hc == 3)), wait=(tgw2, th1, gd.get("p5"),), inc="pe")
        ttr = p.emit("tensor", lambda e, tt=tt: e.matmul(
            c.p6[0:128, 0:1], rowbf[0:1, 128 * tt:128 * (tt + 1)], c.onescol[0:1, 0:1],
            start=True, stop=True), wait=(tinvn, gd.get("p6"),), inc="pe")
        tcol = p.emit("vector", lambda e: e.tensor_copy(smallv[:, 0:1], c.p6[0:128, 0:1]),
                      wait=(ttr, gd.get("sv"),), inc="ve")
        f1 = c.ftmp[1][:, 16:32]
        f2 = c.ftmp[1][:, 32:48]
        t1 = p.emit("vector", lambda e: e.tensor_scalar(f1, c.p4[:, 0:16], smallv[:, 0:1],
                                                        None, ALU.mult),
                    wait=(tm1, tcol, gd.get("f1"),), inc="ve")
        gd["p4"] = t1
        t2 = p.emit("vector", lambda e: e.tensor_mul(f1, f1, invcb), wait=(t1, ticb), inc="ve")
        t3 = p.emit("vector", lambda e: e.tensor_add(f2, c.p5[:, 0:16], gb2b),
                    wait=(tm2, tgb2b, gd.get("f2"),), inc="ve")
        gd["p5"] = t3
        t4 = p.emit("scalar", lambda e: e.activation(f2, f2, AF.Sigmoid), wait=(t3,), inc="se")
        t5 = p.emit("vector", lambda e: e.tensor_add(f1, f1, f2), wait=(t2, t4), inc="ve")
        t6 = p.emit("vector", lambda e: e.reduce_max(smallv[:, 1:2], f1, axis=AX,
                                                     negate=True), wait=(t5,), inc="ve",
                    drain=True)
        t7 = p.emit("scalar", lambda e: e.activation(f2, f1, AF.Exp, bias=smallv[:, 1:2]),
                    wait=(t6,), inc="se")
        t8 = p.emit("vector", lambda e: e.tensor_scalar_mul(smallv[:, 2:3], smallv[:, 1:2],
                                                            -1.0), wait=(t7,), inc="ve",
                    drain=True)
        t9 = p.emit("vector", lambda e: e.tensor_scalar(f1, f1, smallv[:, 2:3], None,
                                                        ALU.is_ge), wait=(t8,), inc="ve")
        t10 = p.emit("vector", lambda e: e.reduce_sum(smallv[:, 3:4], f2, axis=AX),
                     wait=(t7, t9), inc="ve", drain=True)
        t11 = p.emit("vector", lambda e: e.reciprocal(smallv[:, 4:5], smallv[:, 3:4]),
                     wait=(t10,), inc="ve", drain=True)
        t12 = p.emit("vector", lambda e: e.tensor_mul(f2, f2, f1), wait=(t9, t11), inc="ve")
        t13 = p.emit("vector", lambda e: e.tensor_scalar(wtile, f2, smallv[:, 4:5], None,
                                                         ALU.mult), wait=(t12,), inc="ve")
        gd["f1"] = t13
        gd["f2"] = t13
        gd["sv"] = t13
        t14 = p.emit("tensor", lambda e, tt=tt: e.matmul(
            c.p6[0:16, 0:128], wtile, c.ident[:], start=True, stop=True),
            wait=(t13, ttr), inc="pe")
        t15 = p.emit("vector", lambda e: e.tensor_copy(
            wTs[0:16, 0:128], c.p6[0:16, 0:128]),
            wait=(t14, gd.get("wTg"),), inc="ve")
        gd["p6"] = t15
        gd["wTg"] = p.dma("sync", c.scratch("wTd", (16, T), BF16)[:, 128 * tt:128 * (tt + 1)],
                          wTs[0:16, 0:128], wait=(t15,), sem="dWT")
    twT = gd["wTg"]
    p.barrier()

    Wr = c.din("l1f_Wr")
    pw = c.din("l1f_projw")
    wring = wst + [c.ar.take(1024, BF16, "f_ws2"), c.ar.take(1024, BF16, "f_ws3")]
    wloads = []
    for tg in range(4):
        for uc in range(32):
            wloads.append(("wr", uc, 0))
        for oc in range(8):
            for ug4 in range(4):
                wloads.append(("pj", oc, ug4))
    twld = {}
    twread = {}
    wnxt = [0]

    def w_load():
        k = wnxt[0]
        wnxt[0] += 1
        sl = k % 4
        d = wloads[k]
        if d[0] == "wr":
            src = Wr[:, 128 * d[1]:128 * (d[1] + 1)].rearrange("(i p) q -> p i q", p=128)
        else:
            src = pw[1024 * d[2]:1024 * (d[2] + 1),
                     128 * d[1]:128 * (d[1] + 1)].rearrange("(u p) q -> p u q", p=128)
        twld[k] = p.dma("gpsimd", wring[sl].rearrange("p (i q) -> p i q", i=8), src,
                        wait=(twread.get(k - 4),), sem=f"dR{sl}{sfx}")

    for _ in range(3):
        w_load()
    wtrows = c.ar.take(4096, BF16, "f_wtr")
    tslotr = {}

    def wt_load(cc, s):
        sl8 = cc % 8
        return p.dma("sync", wtrows[0:1, 512 * sl8:512 * sl8 + 512],
                     c.scratch("wTd", (16, T), BF16)[cc:cc + 1, s],
                     wait=(twT, tslotr.get(sl8)), sem="dWT%d" % (cc % 4))

    for tg in range(4):
        s = slice(tg * PB, (tg + 1) * PB)
        gm = {}
        twl = {}
        for cc in range(8):
            twl[cc] = wt_load(cc, s)
        for uc in range(32):
            k = tg * 64 + uc
            sl = k % 4
            bi = uc % 4
            bk = c.pb[bi]
            tm = None
            for ic in range(8):
                tm = p.emit("tensor", lambda e, ic=ic, sl=sl, s=s, bk=bk: e.matmul(
                    bk, wring[sl][:, 128 * ic:128 * (ic + 1)], xn[ic][:, s],
                    start=(ic == 0), stop=(ic == 7)),
                    wait=(twld[k], gm.get("b%d" % bi),), inc="pe")
            twread[k] = tm
            if wnxt[0] < len(wloads):
                w_load()
            tcl = p.emit("scalar", lambda e, uc=uc, bk=bk: e.activation(
                col_u2(uc), bk, AF.Gelu, bias=CA[:, uc:uc + 1]),
                wait=(tm, tbc), inc="se")
            gm["b%d" % bi] = tcl
            gm["c%d" % uc] = tcl
        tww = None
        for cc in range(16):
            sl8 = cc % 8
            tb = p.emit("tensor", lambda e, sl8=sl8: e.matmul(
                c.p6[:, 0:PB], c.onesrow[:], wtrows[0:1, 512 * sl8:512 * sl8 + 512],
                start=True, stop=True),
                wait=(twl[cc], gm.get("p6w"),), inc="pe")
            tslotr[sl8] = tb
            if cc < 8:
                twl[cc + 8] = wt_load(cc + 8, s)
            for du in range(2):
                uc = 2 * cc + du
                tww = p.emit("vector", lambda e, uc=uc: e.tensor_mul(
                    col_u2(uc), col_u2(uc), c.p6[:, 0:PB]),
                    wait=(tb, gm.get("c%d" % uc),), inc="ve")
                gm["c%d" % uc] = tww
            gm["p6w"] = tww
        tb1 = p.emit("tensor", lambda e, s=s: e.matmul(
            c.p6[:, 0:PB], c.onesrowf[:], st["nmu"][0:1, s], start=True, stop=True),
            wait=(tww,), inc="pe")
        tb2 = p.emit("tensor", lambda e, s=s: e.matmul(
            c.p7[:, 0:PB], c.onesrowf[:], st["rstd"][0:1, s], start=True, stop=True),
            wait=(tinvc, tb1), inc="pe")
        tyl = None
        for oc in range(8):
            pa = oc % 2
            bk = c.p4 if pa == 0 else c.p5
            tpj = None
            for ug4 in range(4):
                k = tg * 64 + 32 + oc * 4 + ug4
                sl = k % 4
                for du in range(8):
                    uc = ug4 * 8 + du
                    tpj = p.emit("tensor", lambda e, du=du, sl=sl, bk=bk, uc=uc: e.matmul(
                        bk[:, 0:PB], wring[sl][:, 128 * du:128 * (du + 1)], col_u2(uc),
                        start=(uc == 0), stop=(uc == 31)),
                        wait=(twld[k], tww, gm.get("pj%d" % pa),), inc="pe")
                twread[k] = tpj
                if wnxt[0] < len(wloads):
                    w_load()
            t1 = p.emit("vector", lambda e, oc=oc, s=s: e.tensor_add(
                y_ap(oc), c.X[oc][:, s], c.p6[:, 0:PB]), wait=(tb1, gm.get("y%d" % oc),),
                inc="ve")
            t2 = p.emit("vector", lambda e, oc=oc: e.tensor_mul(
                y_ap(oc), y_ap(oc), c.p7[:, 0:PB]), wait=(tb2, t1), inc="ve")
            t3 = p.emit("vector", lambda e, oc=oc: e.tensor_scalar(
                y_ap(oc), y_ap(oc), CA[:, 32 + oc:33 + oc], None, ALU.add),
                wait=(t2, tpb), inc="ve")
            tyl = p.emit("vector", lambda e, oc=oc, bk=bk: e.tensor_add(
                y_ap(oc), y_ap(oc), bk[:, 0:PB]), wait=(t3, tpj), inc="ve")
            gm["pj%d" % pa] = tyl
            gm["y%d" % oc] = tyl
        tst = None
        for oc in range(8):
            tyb = p.emit("vector", lambda e, oc=oc: e.tensor_copy(sq1, y_ap(oc)),
                         wait=(gm.get("y%d" % oc), gm.get("sq1"),), inc="ve")
            tsq2 = p.emit("scalar", lambda e: e.activation(sq2, sq1, AF.Square),
                          wait=(tyb, gm.get("sq2"),), inc="se")
            p.emit("tensor", lambda e, oc=oc: e.matmul(
                c.pS[0:1, 0:PB], c.onescol[:], sq1, start=(oc == 0), stop=(oc == 7)),
                wait=(tyb, gm.get("st1"),), inc="pe")
            tst = p.emit("tensor", lambda e, oc=oc: e.matmul(
                c.pS[0:1, PB:2 * PB], c.onescol[:], sq2,
                start=(oc == 0), stop=(oc == 7)), wait=(tsq2,), inc="pe")
            gm["sq1"] = tst
            gm["sq2"] = tst
        gm["st1"] = tst
        ra = c.ftmp[1][0:1, 0:PB]
        rb = c.ftmp[0][0:1, 0:PB]
        rc = rrow[0:1, 0:PB]
        t1 = p.emit("vector", lambda e: e.tensor_scalar_mul(ra, c.pS[0:1, 0:PB], 1.0 / D),
                    wait=(tst,), inc="ve", drain=True)
        t2 = p.emit("vector", lambda e: e.tensor_scalar_mul(rb, c.pS[0:1, PB:2 * PB], 1.0 / D),
                    wait=(tst, t1), inc="ve", drain=True)
        t3 = p.emit("scalar", lambda e: e.activation(rc, ra, AF.Square),
                    wait=(t1, t2), inc="se")
        t4 = p.emit("vector", lambda e: e.tensor_sub(rb, rb, rc), wait=(t3,), inc="ve",
                    drain=True)
        t5 = p.emit("scalar", lambda e: e.activation(rb, rb, AF.Ln, bias=c.epscol[0:1, 0:1]),
                    wait=(t4,), inc="se", drain=True)
        t6 = p.emit("scalar", lambda e: e.activation(rb, rb, AF.Exp, scale=-0.5),
                    wait=(t5,), inc="se", drain=True)
        t7 = p.emit("vector", lambda e: e.tensor_scalar_mul(ra, ra, -1.0), wait=(t6,),
                    inc="ve", drain=True)
        tb3 = p.emit("tensor", lambda e: e.matmul(c.p6[:, 0:PB], c.onesrowf[:], ra,
                                                  start=True, stop=True),
                     wait=(t7, tb2, tyl), inc="pe")
        tb4 = p.emit("tensor", lambda e: e.matmul(c.p7[:, 0:PB], c.onesrowf[:], rb,
                                                  start=True, stop=True),
                     wait=(t6, tb3), inc="pe")
        for oc in range(8):
            t1 = p.emit("vector", lambda e, oc=oc: e.tensor_add(
                y_ap(oc), y_ap(oc), c.p6[:, 0:PB]), wait=(tb3, gm.get("yo%d" % oc),), inc="ve")
            t2 = p.emit("vector", lambda e, oc=oc: e.tensor_mul(
                y_ap(oc), y_ap(oc), c.p7[:, 0:PB]), wait=(tb4, t1), inc="ve")
            t3 = p.emit("vector", lambda e, oc=oc: e.tensor_scalar(
                y_ap(oc), y_ap(oc), CA[:, 40 + oc:41 + oc], CA[:, 48 + oc:49 + oc],
                ALU.mult, ALU.add), wait=(t2, tlg, tlb), inc="ve")
            p.emit("vector", lambda e, oc=oc, s=s: e.scalar_tensor_tensor(
                c.X[oc][:, s], y_ap(oc), 0.5, c.X[oc][:, s], ALU.mult, ALU.add),
                wait=(t3,), inc="ve")
        p.barrier()
    c.ar.pop()


def phase_l1ffn(c):
    emit_l1ffn(c, "f1")



def emit_l1attn(c):
    p = c.prog
    nc = c.nc
    sfx = "a1"
    c.ar.push()
    xn = [c.ar.take(T, BF16, "b_xn") for _ in range(8)]
    c.ar.push()
    st = alloc_stats(c)
    emit_layernorm(c, xn, st)
    c.ar.pop()
    p.barrier()
    kvn = [c.ar.take(T, BF16, "b_kvn") for _ in range(2)]
    qcT = [c.ar.take(T, BF16, "b_qc") for _ in range(4)]
    krT = [c.ar.take(T, BF16, "b_kr") for _ in range(3)]
    qrT = [c.ar.take(T, BF16, "b_qr") for _ in range(3)]
    WV = 66 * 8
    Vb = c.ar.take(16 * WV, BF16, "b_V")
    sh = [c.ar.take(T, BF16, "b_sh") for _ in range(2)]
    Ob = [c.ar.take(T, BF16, "b_O") for _ in range(4)]
    at2 = c.ar.take(PB, BF16, "b_at2")
    scr = c.ar.take(PB, F32, "b_scr")
    rotf = c.ar.take(128, F32, "b_rotf")
    wstab = c.ar.take(2048, BF16, "b_wsab")
    wst = [wstab[:, 0:1024], wstab[:, 1024:2048]]
    Ab3 = c.ar.take(T, BF16, "b_Ab3")
    colsc = c.ar.take(64, F32, "b_colsc")
    krow = Vb[:, 0:4096].bitcast(F32)

    CQ = c.smallf
    trot = p.dma("sync", rotf, c.const("l1a_rotT_f32"), sem="dL")
    tbl = {}
    tbl["C"] = p.dma("gpsimd", sh[0], c.const("l1a_ropeC"), sem="dTb" + sfx)
    tbl["S"] = p.dma("gpsimd", sh[1], c.const("l1a_ropeS"), sem="dTb" + sfx)
    tgk = p.dma("sync", CQ[:, 30:32], c.din("l1a_g_kv"), sem="dL")
    te6 = p.emit("vector", lambda e: e.memset(colsc[:, 0:1], 1e-6), inc="ve")

    tw = [None, None]
    wguard = [None, None]
    tkv = None
    for oc in range(2):
        pa = oc % 2
        tw[pa] = p.dma("gpsimd", wst[pa].rearrange("p (i q) -> p i q", i=8),
                       c.din("l1a_W_kv")[:, 128 * oc:128 * (oc + 1)].rearrange(
                           "(i p) q -> p i q", p=128), wait=(wguard[pa],), sem=f"dKV{pa}{sfx}")
        for g in range(4):
            s = slice(g * PB, (g + 1) * PB)
            tm = None
            for ic in range(8):
                tm = p.emit("tensor", lambda e, ic=ic, s=s, pa=pa: e.matmul(
                    c.p4[:, 0:PB], wst[pa][:, 128 * ic:128 * (ic + 1)], xn[ic][:, s],
                    start=(ic == 0), stop=(ic == 7)), wait=(tw[pa], tkv), inc="pe")
            tkv = p.emit("scalar", lambda e, oc=oc, s=s: e.copy(kvn[oc][:, s], c.p4[:, 0:PB]),
                         wait=(tm,), inc="se")
            wguard[pa] = tm
    tms = None
    tsg = None
    for oc in range(2):
        for g in range(4):
            s = slice(g * PB, (g + 1) * PB)
            ts_ = p.emit("scalar", lambda e, oc=oc, s=s: e.activation(
                at2, kvn[oc][:, s], AF.Square), wait=(tkv, tsg), inc="se")
            tms = p.emit("tensor", lambda e, g=g, oc=oc: e.matmul(
                c.pS[0:1, g * PB:(g + 1) * PB], c.onescol[:], at2,
                start=(oc == 0), stop=(oc == 1)), wait=(ts_,), inc="pe")
            tsg = tms
    r1 = krow[0:1, 0:T]
    t = p.emit("vector", lambda e: e.tensor_scalar_mul(r1, c.pS[0:1, 0:4 * PB], 1.0 / 256),
               wait=(tms, te6), inc="ve", drain=True)
    t = p.emit("vector", lambda e: e.tensor_scalar(r1, r1, colsc[0:1, 0:1], None, ALU.add),
               wait=(t,), inc="ve", drain=True)
    t = p.emit("scalar", lambda e: e.activation(r1, r1, AF.Ln), wait=(t,), inc="se")
    trs = p.emit("scalar", lambda e: e.activation(r1, r1, AF.Exp, scale=-0.5),
                 wait=(t,), inc="se", drain=True)
    tkn = None
    for g in range(4):
        s = slice(g * PB, (g + 1) * PB)
        tb = p.emit("tensor", lambda e, s=s: e.matmul(
            c.p5[:, 0:PB], c.onesrowf[:], r1[0:1, s], start=True, stop=True),
            wait=(trs, tkn), inc="pe")
        for oc in range(2):
            t1 = p.emit("vector", lambda e, oc=oc, s=s: e.tensor_mul(
                kvn[oc][:, s], kvn[oc][:, s], c.p5[:, 0:PB]), wait=(tb, tkv, tkn), inc="ve")
            tkn = p.emit("vector", lambda e, oc=oc, s=s: e.tensor_scalar(
                kvn[oc][:, s], kvn[oc][:, s], CQ[:, 30 + oc:31 + oc], None, ALU.mult),
                wait=(t1, tgk), inc="ve")

    tqc = None
    for oc in range(4):
        pa = oc % 2
        tw[pa] = p.dma("gpsimd", wst[pa].rearrange("p (i q) -> p i q", i=8),
                       c.din("l1a_W_q")[:, 128 * oc:128 * (oc + 1)].rearrange(
                           "(i p) q -> p i q", p=128), wait=(wguard[pa],), sem=f"dKV{pa}{sfx}")
        for g in range(4):
            s = slice(g * PB, (g + 1) * PB)
            tm = None
            for ic in range(8):
                tm = p.emit("tensor", lambda e, ic=ic, s=s, pa=pa: e.matmul(
                    c.p4[:, 0:PB], wst[pa][:, 128 * ic:128 * (ic + 1)], xn[ic][:, s],
                    start=(ic == 0), stop=(ic == 7)), wait=(tw[pa], tqc), inc="pe")
            tqc = p.emit("scalar", lambda e, oc=oc, s=s: e.copy(qcT[oc][:, s], c.p4[:, 0:PB]),
                         wait=(tm,), inc="se")
            wguard[pa] = tm
    p.barrier()

    krd = c.scratch("krTd", (384, T), BF16)

    def emit_rope(dst3, w_name, src_chunks, nsrc, half, sem):
        tlast = None
        tww = [None, None]
        wgr = [wguard[0], wguard[1]]
        for ch3 in range(3):
            pa = ch3 % 2
            tww[pa] = p.dma("gpsimd", wst[pa][:, 0:nsrc * 128].rearrange(
                "p (i q) -> p i q", i=nsrc),
                c.din(w_name)[:, 384 * half + 128 * ch3:384 * half + 128 * (ch3 + 1)].rearrange(
                    "(i p) q -> p i q", p=128),
                wait=(wgr[pa],), sem=f"dRp{pa}{sfx}")
            for g in range(4):
                s = slice(g * PB, (g + 1) * PB)
                tm = None
                for ic in range(nsrc):
                    tm = p.emit("tensor", lambda e, ic=ic, s=s, pa=pa: e.matmul(
                        c.p4[:, 0:PB], wst[pa][:, 128 * ic:128 * (ic + 1)],
                        src_chunks[ic][:, s], start=(ic == 0), stop=(ic == nsrc - 1)),
                        wait=(tww[pa], tlast), inc="pe")
                wgr[pa] = tm
                wguard[pa] = tm
                t1 = p.emit("scalar", lambda e: e.copy(scr, c.p4[:, 0:PB]),
                            wait=(tm, tlast), inc="se")
                t2 = p.emit("tensor", lambda e: e.matmul(c.p7[:, 0:PB], rotf, scr,
                                                         start=True, stop=True),
                            wait=(t1, trot, tlast), inc="pe")
                t3 = p.emit("vector", lambda e, s=s: e.tensor_mul(
                    c.ftmp[1][:, 0:PB], scr, sh[0][:, s]), wait=(t1, tbl["C"], tlast), inc="ve")
                t4 = p.emit("vector", lambda e, s=s: e.tensor_mul(
                    c.ftmp[0][:, 0:PB], c.p7[:, 0:PB], sh[1][:, s]),
                    wait=(t2, tbl["S"], t3), inc="ve")
                tlast = p.emit("vector", lambda e, ch3=ch3, s=s, dst3=dst3: e.tensor_add(
                    dst3[ch3][:, s], c.ftmp[1][:, 0:PB], c.ftmp[0][:, 0:PB]),
                    wait=(t3, t4), inc="ve")
        return tlast

    tk1 = emit_rope(krT, "l1a_W_krope", xn, 8, 1, "k1")
    tkd = None
    for ch3 in range(3):
        tkd = p.dma("sync", krd[128 * ch3:128 * (ch3 + 1), :], krT[ch3][:, :],
                    wait=(tk1,), sem="dKd")
    for _e in ENGS:
        p.emit(_e, None, wait=(tkd,))
    p.barrier()
    tk0 = emit_rope(krT, "l1a_W_krope", xn, 8, 0, "k0")
    p.barrier()
    tq0 = emit_rope(qrT, "l1a_W_qrope", qcT, 4, 0, "q0")
    p.barrier()

    if DBG_L1A:
        p.barrier()
        for nm, buf in (("kvn0", kvn[0]), ("qc0", qcT[0]), ("kr0", krT[0]),
                        ("qr0", qrT[0])):
            tdmp = p.dma("sync", c.scratch("dbg_" + nm, (128, T), BF16), buf[:, :], sem="dDBG")
        p.emit("sync", None, wait=(tdmp,))
        p.barrier()

    opj = c.din("l1a_W_o")
    scl1 = 1.0 / math.sqrt(96.0)
    for half in range(2):
        if half == 1:
            tkr = None
            for ch3 in range(3):
                tkr = p.dma("gpsimd", krT[ch3][:, :], krd[128 * ch3:128 * (ch3 + 1), :],
                            wait=(tkd,), sem="dKd2")
            p.barrier()
            tbl["C"] = p.dma("gpsimd", sh[0], c.const("l1a_ropeC"), sem="dTb" + sfx)
            tbl["S"] = p.dma("gpsimd", sh[1], c.const("l1a_ropeS"), sem="dTb" + sfx)
            tq1 = emit_rope(qrT, "l1a_W_qrope", qcT, 4, 1, "q1")
            p.emit("tensor", None, wait=(tkr, tq1))
            p.barrier()
            if DBG_L1A:
                tdmp = p.dma("sync", c.scratch("dbg_qrfresh", (128, T), BF16),
                             qrT[0][:, :], sem="dDBG")
                p.emit("sync", None, wait=(tdmp,))
                p.barrier()
        kc = xn[0:4]
        qc2 = xn[4:8]
        tdec = None
        for (dst, wname, srcs, nsrc) in ((kc, "l1a_W_kdec", kvn, 2),
                                         (qc2, "l1a_W_qdec", qcT, 4)):
            for oc in range(4):
                pa = oc % 2
                tw[pa] = p.dma("gpsimd", wst[pa][:, 0:nsrc * 128].rearrange(
                    "p (i q) -> p i q", i=nsrc),
                    c.din(wname)[:, 512 * half + 128 * oc:512 * half + 128 * (oc + 1)].rearrange(
                        "(i p) q -> p i q", p=128),
                    wait=(wguard[pa],), sem=f"dKV{pa}{sfx}")
                for g in range(4):
                    s = slice(g * PB, (g + 1) * PB)
                    tm = None
                    for ic in range(nsrc):
                        tm = p.emit("tensor", lambda e, ic=ic, s=s, pa=pa, srcs=srcs, nsrc=nsrc:
                                    e.matmul(c.p4[:, 0:PB],
                                             wst[pa][:, 128 * ic:128 * (ic + 1)],
                                             srcs[ic][:, s], start=(ic == 0),
                                             stop=(ic == nsrc - 1)),
                                    wait=(tw[pa], tdec), inc="pe")
                    tdec = p.emit("scalar", lambda e, dst=dst, oc=oc, s=s: e.copy(
                        dst[oc][:, s], c.p4[:, 0:PB]), wait=(tm,), inc="se")
                    wguard[pa] = tm
        twv = p.dma("gpsimd", wst[0][:, 0:1024].rearrange("p (i q) -> p i q", i=2),
                    c.din("l1a_W_vdec")[:, 512 * half:512 * (half + 1)].rearrange(
                        "(i p) q -> p i q", p=128), wait=(wguard[0],), sem=f"dKV0{sfx}")
        tvm = tdec
        tves = None
        for tt in range(16):
            tm = None
            for ic in range(2):
                tm = p.emit("tensor", lambda e, ic=ic, tt=tt: e.matmul(
                    c.p4[:, 0:PB], kvn[ic][:, 128 * tt:128 * (tt + 1)],
                    wst[0][:, 512 * ic:512 * (ic + 1)],
                    start=(ic == 0), stop=(ic == 1)), wait=(twv, tvm), inc="pe")
            tve = p.emit("vector", lambda e, tt=tt: e.tensor_copy(
                Vb[:, tt * WV:tt * WV + 8 * 66].rearrange("p (h d) -> p h d", d=66)[:, :, 0:64],
                c.p4[:, 0:PB].rearrange("p (h d) -> p h d", d=64)),
                wait=(tm, tves), inc="ve")
            tves = p.emit("vector", lambda e, tt=tt: e.memset(
                Vb[:, tt * WV + 64: tt * WV + 8 * 66:66], 1.0), wait=(tve,), inc="ve")
            tvm = tve
            wguard[0] = tm
        tv_done = tves
        if DBG_L1A:
            p.barrier()
            for nm, buf in (("kc0", kc[0]), ("qc20", qc2[0]), ("kr0x", krT[0]),
                            ("qr0x", qrT[0])):
                tdmp = p.dma("sync", c.scratch("dbg_%sh%d" % (nm, half), (128, T), BF16),
                             buf[:, :], sem="dDBG")
            tdmp = p.dma("sync", c.scratch("dbg_V0h%d" % half, (128, 512), BF16),
                         Vb[:, 0:528].rearrange("p (h d) -> p h d", d=66)[:, :, 0:64], sem="dDBG")
            p.emit("sync", None, wait=(tdmp,))
        p.barrier()
        A4 = [sh[0], sh[1], wstab, Ab3]
        aits = [(hh, tp) for hh in range(8) for tp in range(8)]
        NIT = len(aits)
        texp = {}
        tmask = {}
        tABr = {(-2, 0): None, (-2, 1): None, (-1, 0): None, (-1, 1): None}
        tsc_t = {}
        ttr = {0: None, 1: None}
        tat2r = [None, None]
        tavfin = {}
        trecip = {-2: None, -1: None}
        tavs = {}
        toutm = {}
        tbc_t = {}
        tscore_last = None

        def ageo(n):
            hh, tp = aits[n]
            return (hh, tp, hh // 2, (hh % 2) * 64, hh // 3, (hh % 3) * 32,
                    2 * tp + 2)

        def emit_scores(n, ti, ext, exts, cm, Ab, w0):
            nonlocal tscore_last
            hh, tp, cch, coff, rch, roff, nkt = ageo(n)
            qt = 2 * tp + ti
            tsc = None
            tmq = []
            ngr = (exts + PB - 1) // PB
            for g in range(ngr):
                ws = min(PB, exts - g * PB)
                tsc = p.emit("tensor", lambda e, g=g, ws=ws, qt=qt, cch=cch,
                             coff=coff: e.matmul(
                    c.pS[:, g * PB:g * PB + ws],
                    qc2[cch][coff:coff + 64, 128 * qt:128 * (qt + 1)],
                    kc[cch][coff:coff + 64, g * PB:g * PB + ws],
                    start=True, stop=False),
                    wait=(w0 if g == 0 else ()), inc="pe")
                tsc = p.emit("tensor", lambda e, g=g, ws=ws, qt=qt, rch=rch,
                             roff=roff: e.matmul(
                    c.pS[:, g * PB:g * PB + ws],
                    qrT[rch][roff:roff + 32, 128 * qt:128 * (qt + 1)],
                    krT[rch][roff:roff + 32, g * PB:g * PB + ws],
                    start=False, stop=True),
                    wait=(tsc,), inc="pe")
                tmq.append(p.emit("vector", lambda e, g=g, ws=ws, cm=cm: e.reduce_max(
                    colsc[:, cm + 2 + g:cm + 3 + g], c.pS[:, g * PB:g * PB + ws],
                    axis=AX), wait=(tsc, texp.get((n - 2, ti))), inc="ve"))
            tsc_t[(n, ti)] = tsc
            tscore_last = tsc
            tmx = p.emit("vector", lambda e, cm=cm, ngr=ngr: e.reduce_max(
                colsc[:, cm:cm + 1], colsc[:, cm + 2:cm + 2 + ngr], axis=AX,
                negate=True), wait=tuple(tmq), inc="ve")
            tmx2 = p.emit("vector", lambda e, cm=cm: e.tensor_scalar_mul(
                colsc[:, cm + 1:cm + 2], colsc[:, cm:cm + 1], scl1), wait=(tmx,), inc="ve")
            texp[(n, ti)] = p.emit("scalar", lambda e, exts=exts, Ab=Ab, cm=cm:
                                   e.activation(Ab[:, 0:exts], c.pS[:, 0:exts], AF.Exp,
                                                bias=colsc[:, cm + 1:cm + 2], scale=scl1),
                                   wait=(tmx2, tABr[(n - 2, ti)]), inc="se")
            return tsc

        def prod_first(n):
            hh, tp, cch, coff, rch, roff, nkt = ageo(n)
            ext = 128 * nkt
            pr = n % 2
            wprev = texp.get((n - 1, 1))
            w0 = (tdec, tv_done, wprev)
            emit_scores(n, 0, ext, ext - 128, 1 + 14 * pr, A4[2 * pr], w0)

        def prod_second(n):
            hh, tp, cch, coff, rch, roff, nkt = ageo(n)
            ext = 128 * nkt
            pr = n % 2
            Ab0, Ab1 = A4[2 * pr], A4[2 * pr + 1]
            qt0, qt1 = 2 * tp, 2 * tp + 1
            emit_scores(n, 1, ext, ext, 8 + 14 * pr, Ab1, (texp[(n, 0)],))
            tms0 = p.emit("gpsimd", lambda e, qt=qt0, Ab=Ab0: e.tensor_mul(
                Ab[:, 128 * qt:128 * (qt + 1)], Ab[:, 128 * qt:128 * (qt + 1)],
                c.diagmask[:]), wait=(texp[(n, 0)],), inc="ge")
            tmask[(n, 0)] = p.emit("gpsimd", lambda e, qt=qt0, ext=ext, Ab=Ab0:
                                   e.memset(Ab[:, 128 * (qt + 1):ext], 0.0),
                                   wait=(tms0,), inc="ge")
            tmask[(n, 1)] = p.emit("gpsimd", lambda e, qt=qt1, Ab=Ab1: e.tensor_mul(
                Ab[:, 128 * qt:128 * (qt + 1)], Ab[:, 128 * qt:128 * (qt + 1)],
                c.diagmask[:]), wait=(texp[(n, 1)],), inc="ge")

        cstate = {}

        def cons_kt(n, kt, pr):
            Ab0, Ab1 = A4[2 * pr], A4[2 * pr + 1]
            tb = c.p4 if kt % 2 == 0 else c.p7
            to1 = p.emit("tensor", lambda e, kt=kt, tb=tb, Ab0=Ab0: e.matmul(
                tb[:, 0:128], Ab0[:, 128 * kt:128 * (kt + 1)], c.ident[:],
                start=True, stop=True), wait=(tmask[(n, 0)], ttr[kt % 2]), inc="pe")
            to2 = p.emit("tensor", lambda e, kt=kt, tb=tb, Ab1=Ab1: e.matmul(
                tb[:, 128:256], Ab1[:, 128 * kt:128 * (kt + 1)], c.ident[:],
                start=True, stop=True), wait=(tmask[(n, 1)],), inc="pe")
            tABr[(n, 0)] = to1
            tABr[(n, 1)] = to2
            if kt % 2 == 0:
                tev = p.emit("vector", lambda e, kt=kt, tb=tb: e.tensor_copy(
                    at2[:, 256 * (kt % 2):256 * (kt % 2) + 256], tb[:, 0:256]),
                    wait=(to1, to2, tat2r[kt % 2]), inc="ve")
            else:
                tev = p.emit("scalar", lambda e, kt=kt, tb=tb: e.copy(
                    at2[:, 256 * (kt % 2):256 * (kt % 2) + 256], tb[:, 0:256]),
                    wait=(to1, to2, tat2r[kt % 2]), inc="se")
            ttr[kt % 2] = tev
            cstate[(n, kt)] = tev

        def cons_av(n, kt, nkt, hh, pr):
            avbank = c.p5 if pr == 0 else c.p6
            w = [cstate.pop((n, kt)), tv_done]
            if kt == 0:
                w += [trecip.get(n - 2), tavs.get(n - 2)]
            tav = p.emit("tensor", lambda e, kt=kt, hh=hh, nkt=nkt, avbank=avbank:
                         e.matmul(avbank[0:65, 0:256],
                                  Vb[:, kt * WV + hh * 66: kt * WV + hh * 66 + 65],
                                  at2[:, 256 * (kt % 2):256 * (kt % 2) + 256],
                                  start=(kt == 0), stop=(kt == nkt - 1)),
                         wait=tuple(w), inc="pe")
            tat2r[kt % 2] = tav
            if kt == nkt - 1:
                tavfin[n] = tav

        def cons(n, half_):
            hh, tp, cch, coff, rch, roff, nkt = ageo(n)
            pr = n % 2
            h1 = (nkt + 1) // 2
            kts = range(0, h1) if half_ == 0 else range(h1, nkt)
            for kt in kts:
                cons_kt(n, kt, pr)
                if kt >= 1:
                    cons_av(n, kt - 1, nkt, hh, pr)
            if half_ == 1:
                cons_av(n, nkt - 1, nkt, hh, pr)
                avbank = c.p5 if pr == 0 else c.p6
                rsl = c.ftmp[0][0:1, 256 * pr:256 * pr + 256]
                avs = scr[0:64, 0:256] if pr == 0 else scr[0:64, 256:512]
                tln = p.emit("scalar", lambda e, rsl=rsl, avbank=avbank: e.activation(
                    rsl, avbank[64:65, 0:256], AF.Ln),
                    wait=(tavfin[n], tbc_t.get(n - 2)), inc="se")
                tiv = p.emit("scalar", lambda e, rsl=rsl: e.activation(
                    rsl, rsl, AF.Exp, scale=-1.0), wait=(tln,), inc="se", drain=True)
                trecip[n] = tiv
                tas = p.emit("scalar", lambda e, avs=avs, avbank=avbank: e.copy(
                    avs, avbank[0:64, 0:256]),
                    wait=(tavfin[n], toutm.get(n - 2)), inc="se")
                tavs[n] = tas

        def cons_tail(n):
            hh, tp, cch, coff, rch, roff, nkt = ageo(n)
            pr = n % 2
            avbank = c.p5 if pr == 0 else c.p6
            rsl = c.ftmp[0][0:1, 256 * pr:256 * pr + 256]
            avs = scr[0:64, 0:256] if pr == 0 else scr[0:64, 256:512]
            tbc = p.emit("tensor", lambda e, rsl=rsl, avbank=avbank: e.matmul(
                avbank[0:64, 256:512], c.onesrowf[0:1, 0:64], rsl,
                start=True, stop=True),
                wait=(trecip[n], tavs[n], toutm.get(n - 2)), inc="pe")
            tbc_t[n] = tbc
            toutm[n] = p.emit("vector", lambda e, tp=tp, cch=cch, coff=coff,
                              avs=avs, avbank=avbank: e.tensor_mul(
                Ob[cch][coff:coff + 64, 256 * tp:256 * (tp + 1)],
                avs, avbank[0:64, 256:512]), wait=(tbc,), inc="ve")

        for n in range(NIT + 2):
            if n < NIT:
                prod_first(n)
            if n >= 2:
                cons_tail(n - 2)
            if 1 <= n <= NIT:
                cons(n - 1, 0)
            if n < NIT:
                prod_second(n)
            if 1 <= n <= NIT:
                cons(n - 1, 1)
        tO_done = toutm[NIT - 1]
        tp4_guard = tO_done
        if DBG_L1A:
            p.barrier()
            tdmp = p.dma("sync", c.scratch("dbg_O0h%d" % half, (128, T), BF16),
                         Ob[0][:, :], sem="dDBG")
            tdmp = p.dma("sync", c.scratch("dbg_O1h%d" % half, (128, T), BF16),
                         Ob[1][:, :], sem="dDBG")
            p.emit("sync", None, wait=(tdmp,))
            p.barrier()
        twg = [tO_done, tO_done]
        tpx = {"p4": tp4_guard}
        tXl = None
        for oc in range(8):
            pa = oc % 2
            tw[pa] = p.dma("gpsimd", wst[pa][:, 0:512].rearrange("p (i q) -> p i q", i=4),
                           opj[512 * half:512 * (half + 1),
                               128 * oc:128 * (oc + 1)].rearrange("(i p) q -> p i q", p=128),
                           wait=(twg[pa], wguard[pa]), sem=f"dWo{pa}{sfx}")
            for g in range(4):
                s = slice(g * PB, (g + 1) * PB)
                tm = None
                for ic in range(4):
                    tm = p.emit("tensor", lambda e, ic=ic, s=s, pa=pa: e.matmul(
                        c.p4[:, 0:PB], wst[pa][:, 128 * ic:128 * (ic + 1)], Ob[ic][:, s],
                        start=(ic == 0), stop=(ic == 3)),
                        wait=(tw[pa], tpx.get("p4"), tO_done), inc="pe")
                twg[pa] = tm
                wguard[pa] = tm
                tXl = p.emit("vector", lambda e, oc=oc, s=s: e.tensor_add(
                    c.X[oc][:, s], c.X[oc][:, s], c.p4[:, 0:PB]),
                    wait=(tm, tpx.get("X")), inc="ve")
                tpx["p4"] = tXl
                tpx["X"] = tXl
        p.barrier()
    c.ar.pop()


def phase_l1attn(c):
    emit_l1attn(c)



def phase_dumpx(c):
    p = c.prog
    nc = c.nc
    p.barrier()
    oX = nc.dram_tensor("dbg_X0", [128, T], F32, kind="ExternalOutput").ap()
    oI = nc.dram_tensor("dbg_idf", [128, 128], F32, kind="ExternalOutput").ap()
    oS = nc.dram_tensor("dbg_smallf", [128, 64], F32, kind="ExternalOutput").ap()
    t = p.dma("sync", oX, c.X[0][:], sem="dDBG")
    t = p.dma("sync", oI, c.identf[:], sem="dDBG")
    t = p.dma("sync", oS, c.smallf[:], sem="dDBG")
    p.emit("sync", None, wait=(t,))
    p.barrier()


def phase_lntest(c):
    p = c.prog
    c.ar.push()
    st = alloc_stats(c)
    xn = [c.ar.take(T, BF16, f"xn{i}") for i in range(8)]
    emit_layernorm(c, xn, st)
    for ch in range(8):
        p.emit("vector", lambda e, ch=ch: e.tensor_copy(c.X[ch][:], xn[ch]), inc="ve")
    p.barrier()
    c.ar.pop()


PHASE_EMITTERS = {
    "lntest": phase_lntest,
    "meff": phase_meff,
    "l0ffn": phase_l0ffn,
    "l0attn": phase_l0attn,
    "l1ffn": phase_l1ffn,
    "l1attn": phase_l1attn,
    "dumpx": phase_dumpx,
}



ALL_PHASES = ["meff", "l0ffn", "l0attn", "l0ffn", "l1ffn", "l1attn", "l1ffn"]
LAUNCH_GROUPS = [ALL_PHASES]


def kernel(**inputs):
    W = _prep_weights(inputs)
    x = np.asarray(inputs["L0_attn_x"], np.float32)
    cur = [np.ascontiguousarray(x[b].T) for b in range(NCORE)]

    outs = None
    for phases in LAUNCH_GROUPS:
        nc, used, _dbg = build_program(phases)
        in_maps = []
        for core in range(NCORE):
            m = {}
            for name in used:
                m[name] = cur[core] if name == "xT" else W[name]
            in_maps.append(m)
        res = run_bass_kernel_spmd(nc, in_maps, core_ids=list(range(NCORE)))
        outs = [np.asarray(res.results[i]["out"]) for i in range(NCORE)]
        cur = [np.ascontiguousarray(o.T) for o in outs]

    return np.stack(outs, axis=0)
